# version 19
# baseline (speedup 1.0000x reference)
import numpy as np
import ml_dtypes
from contextlib import ExitStack

import concourse.bass as bass
import concourse.mybir as mybir
from concourse.bass_utils import run_bass_kernel_spmd

F32 = mybir.dt.float32
BF16 = mybir.dt.bfloat16
AF = mybir.ActivationFunctionType
ALU = mybir.AluOpType
AX = mybir.AxisListType

ENGINES = ("tensor", "vector", "scalar", "gpsimd", "sync")


def _rname(x):
    if isinstance(x, str):
        return x
    name = getattr(x, "name", None)
    if name is None:
        t = getattr(x, "tensor", None)
        name = t.name
    return name


def rg(t, key=None):
    return (_rname(t), key)


class _Rec:
    __slots__ = ("engine", "fn", "waits", "clock", "completion", "mark",
                 "dma_inc", "label", "seq")

    def __init__(self, engine, fn, label):
        self.engine = engine
        self.fn = fn
        self.waits = {}
        self.clock = {}
        self.completion = None
        self.mark = False
        self.dma_inc = None
        self.label = label
        self.seq = -1


class Sched:
    DMA_SLOTS = 8

    def __init__(self, nc):
        self.nc = nc
        self.recs = []
        self.wr = {}
        self.rd = {}
        self.eng_count = {e: 0 for e in ENGINES}
        self.last_clock = {e: {} for e in ENGINES}
        self.pending_barrier = {e: None for e in ENGINES}
        self.dma_slot_i = {}
        self.dma_slot_uses = {}
        self.open_group_reads = None
        self.group_reads = []
        self.group_psum = None

    def _conflict_writers(self, name, key):
        d = self.wr.get(name)
        if not d:
            return ()
        if key is None:
            return list(d.values())
        out = []
        if key in d:
            out.append(d[key])
        if None in d:
            out.append(d[None])
        return out

    def _conflict_readers(self, name, key):
        d = self.rd.get(name)
        if not d:
            return ()
        if key is None:
            out = []
            for lst in d.values():
                out.extend(lst)
            return out
        out = list(d.get(key, ()))
        out.extend(d.get(None, ()))
        return out

    def _edge(self, rec, prod):
        if prod is None or prod is rec:
            return
        comp = prod.completion
        assert comp is not None, (
            f"dependency on {prod.label} which has no completion event "
            f"(needed by {rec.label})")
        sem, val = comp
        c = rec.clock
        if c.get(sem, 0) < val:
            w = rec.waits
            if w.get(sem, 0) < val:
                w[sem] = val
        for k, v in prod.clock.items():
            if c.get(k, 0) < v:
                c[k] = v
        if c.get(sem, 0) < val:
            c[sem] = val

    def _reads(self, rec, reads, register=True):
        for name, key in reads:
            if self.group_psum is not None:
                gname, gkey = self.group_psum
                assert not (name == gname and (key == gkey or key is None or
                                               gkey is None)), \
                    f"{rec.label}: read of open matmul-group psum {name}/{key}"
            for w in self._conflict_writers(name, key):
                self._edge(rec, w)
            if register:
                self.rd.setdefault(name, {}).setdefault(key, []).append(rec)

    def _writes(self, rec, writes, register=True):
        for name, key in writes:
            if self.open_group_reads is not None:
                assert (name, key) not in self.open_group_reads and \
                       (name, None) not in self.open_group_reads, \
                    f"write to {name}/{key} while matmul group reading it is open"
            for w in self._conflict_writers(name, key):
                self._edge(rec, w)
            for r in self._conflict_readers(name, key):
                self._edge(rec, r)
            if register:
                wd = self.wr.setdefault(name, {})
                rd = self.rd.setdefault(name, {})
                if key is None:
                    wd.clear()
                    rd.clear()
                else:
                    rd.pop(key, None)
                wd[key] = rec

    def _new_rec(self, engine, fn, label):
        rec = _Rec(engine, fn, label)
        rec.clock = dict(self.last_clock[engine])
        pb = self.pending_barrier[engine]
        if pb is not None:
            for comp in pb:
                sem, val = comp
                if rec.clock.get(sem, 0) < val:
                    rec.waits[sem] = max(rec.waits.get(sem, 0), val)
                    rec.clock[sem] = val
            self.pending_barrier[engine] = None
        return rec

    def op(self, engine, fn, reads=(), writes=(), label=""):
        rec = self._new_rec(engine, fn, label)
        self._reads(rec, reads)
        self._writes(rec, writes, register=False)
        self.eng_count[engine] += 1
        rec.completion = (("E", engine), self.eng_count[engine])
        rec.mark = True
        for name, key in writes:
            wd = self.wr.setdefault(name, {})
            rd = self.rd.setdefault(name, {})
            if key is None:
                wd.clear()
                rd.clear()
            else:
                rd.pop(key, None)
            wd[key] = rec
        self.last_clock[engine] = rec.clock
        rec.seq = len(self.recs)
        self.recs.append(rec)
        self._bg_tick()
        return rec

    def dma(self, engine, out_ap, in_ap, reads=None, writes=None, label="", **kw):
        if reads is None:
            reads = [rg(in_ap)]
        if writes is None:
            writes = [rg(out_ap)]
        slot = self.dma_slot_i.get(engine, 0)
        self.dma_slot_i[engine] = (slot + 1) % self.DMA_SLOTS
        uses = self.dma_slot_uses.get((engine, slot), 0) + 1
        self.dma_slot_uses[(engine, slot)] = uses
        semkey = ("D", engine, slot)

        def fn(eng, out_ap=out_ap, in_ap=in_ap, kw=kw):
            return eng.dma_start(out_ap, in_ap, **kw)

        rec = self._new_rec(engine, fn, label)
        if uses > 1 and rec.clock.get(semkey, 0) < 16 * (uses - 1):
            rec.waits[semkey] = max(rec.waits.get(semkey, 0), 16 * (uses - 1))
            rec.clock[semkey] = 16 * (uses - 1)
        rec.completion = (semkey, 16 * uses)
        rec.dma_inc = (semkey, 16)
        self._reads(rec, reads)
        self._writes(rec, writes)
        self.last_clock[rec.engine] = rec.clock
        rec.seq = len(self.recs)
        self.recs.append(rec)
        self._bg_tick()
        return rec

    def mm(self, fn, reads=(), psum_rgn=None, start=False, stop=False, label=""):
        rec = self._new_rec("tensor", fn, label)
        if start:
            assert self.group_psum is None, "nested matmul group"
            self.group_psum = psum_rgn
            self.group_reads = []
            self.open_group_reads = set()
            self._writes(rec, [psum_rgn], register=False)
        self._reads(rec, reads, register=False)
        self.group_reads.extend(reads)
        self.open_group_reads.update(reads)
        if stop:
            self.eng_count["tensor"] += 1
            rec.completion = (("E", "tensor"), self.eng_count["tensor"])
            rec.mark = True
            name, key = self.group_psum
            self.wr.setdefault(name, {})[key] = rec
            self.rd.setdefault(name, {}).pop(key, None)
            for rname, rkey in self.group_reads:
                self.rd.setdefault(rname, {}).setdefault(rkey, []).append(rec)
            self.group_psum = None
            self.group_reads = []
            self.open_group_reads = None
        self.last_clock[rec.engine] = rec.clock
        rec.seq = len(self.recs)
        self.recs.append(rec)
        return rec

    def add_background(self, fns, every=8):
        from collections import deque
        if not hasattr(self, "_bg"):
            self._bg = deque()
            self._bg_count = 0
            self._bg_active = False
        self._bg_every = every
        self._bg.extend(fns)

    def _bg_tick(self):
        if not getattr(self, "_bg", None) or getattr(self, "_bg_active", False):
            return
        self._bg_count += 1
        if self._bg_count % self._bg_every == 0:
            self._bg_pending = True
        if getattr(self, "_bg_pending", False) and self.group_psum is None:
            self._bg_active = True
            try:
                fn = self._bg.popleft()
                fn()
            finally:
                self._bg_active = False
            self._bg_pending = False

    def drain_background(self):
        while getattr(self, "_bg", None):
            self._bg_active = True
            try:
                fn = self._bg.popleft()
                fn()
            finally:
                self._bg_active = False

    def barrier(self):
        comps = []
        for e in ENGINES:
            if self.eng_count[e] > 0:
                comps.append((("E", e), self.eng_count[e]))
        for (eng, slot), uses in self.dma_slot_uses.items():
            comps.append((("D", eng, slot), 16 * uses))
        for e in ENGINES:
            prev = self.pending_barrier[e] or []
            self.pending_barrier[e] = list(prev) + comps

    def finalize(self):
        nc = self.nc
        semkeys = set()
        for rec in self.recs:
            semkeys.update(rec.waits.keys())
            if rec.completion is not None:
                semkeys.add(rec.completion[0])
        final_waits = {}
        for e in ENGINES:
            if self.eng_count[e] > 0 and e != "sync":
                final_waits[("E", e)] = self.eng_count[e]
                semkeys.add(("E", e))
        for (eng, slot), uses in self.dma_slot_uses.items():
            final_waits[("D", eng, slot)] = 16 * uses
            semkeys.add(("D", eng, slot))

        with ExitStack() as stack:
            sems = {}
            for k in sorted(semkeys, key=repr):
                sems[k] = stack.enter_context(
                    nc.semaphore("s_" + "_".join(str(p) for p in k)))
            block = stack.enter_context(nc.Block())
            by_eng = {e: [r for r in self.recs if r.engine == e]
                      for e in ENGINES}

            def make_section(e):
                recs = by_eng[e]

                def section(eng):
                    for rec in recs:
                        for sem, val in sorted(rec.waits.items(), key=repr):
                            eng.wait_ge(sems[sem], val)
                        try:
                            ins = rec.fn(eng)
                        except Exception:
                            print(f"[Sched] failing instruction: {rec.label}")
                            raise
                        if rec.dma_inc is not None:
                            ins.then_inc(sems[rec.dma_inc[0]], rec.dma_inc[1])
                        elif rec.mark:
                            ins.then_inc(sems[rec.completion[0]], 1)
                    if e == "sync":
                        for sem, val in sorted(final_waits.items(), key=repr):
                            eng.wait_ge(sems[sem], val)
                return section

            for e in ENGINES:
                if by_eng[e] or e == "sync":
                    getattr(block, e)(make_section(e))


class ConstPool:

    def __init__(self, S, nc, stack, n=16, tag="cst"):
        self.S = S
        self.t = stack.enter_context(nc.sbuf_tensor(tag, [128, n], F32))
        self.cols = {}
        self.n = n

    def __call__(self, value):
        value = float(value)
        if value not in self.cols:
            j = len(self.cols)
            assert j < self.n
            self.cols[value] = j
            self.S.op("vector",
                      lambda e, j=j, value=value: e.memset(self.t[:, j:j + 1],
                                                           value),
                      reads=[], writes=[rg(self.t, j)],
                      label=f"cst.{value}")
        j = self.cols[value]
        return self.t[:, j:j + 1]

    def r(self, value):
        self(value)
        return (self.t.name, self.cols[float(value)])



LN_EPS = 1e-5


def mk_ln_bufs(nc, stack, D, tag, nps=2):
    return dict(
        xa=[stack.enter_context(nc.sbuf_tensor(f"{tag}_x{j}", [128, D], F32))
            for j in range(2)],
        junk_s=stack.enter_context(
            nc.sbuf_tensor(f"{tag}_junks", [128, D], BF16)),
        junk_v=stack.enter_context(
            nc.sbuf_tensor(f"{tag}_junkv", [128, D], BF16)),
        xnb=[stack.enter_context(
            nc.sbuf_tensor(f"{tag}_xnb{j}", [128, D], BF16))
            for j in range(2)],
        st=stack.enter_context(nc.sbuf_tensor(f"{tag}_st", [128, 8], F32)),
        ps=[stack.enter_context(
            nc.psum_tensor(f"{tag}_pt{j}", [128, 1024], BF16))
            for j in range(nps)],
    )


def mk_sink_sbuf(S, xnT_sb, T, tag):
    def sink(i, k0, nblk, pst, reads):
        dst = xnT_sb[:, :].rearrange("p (k t) -> p k t", t=T)[
            :, k0:k0 + nblk, i * 128:(i + 1) * 128]
        src = pst[:, :].rearrange("p (b j) -> p b j", j=128)[:, 0:nblk, :]
        S.op("vector",
             lambda e, dst=dst, src=src: e.tensor_copy(dst, src),
             reads=reads, writes=[rg(xnT_sb, ("c", i))],
             label=f"{tag}.ev{i}.{k0}")
    return sink


def mk_sink_hbm(S, nc, stack, xnT_hbm, T, tag):
    stg = [stack.enter_context(
        nc.sbuf_tensor(f"{tag}_sg{j}", [128, 1024], BF16)) for j in range(2)]
    cnt = [0]

    def sink(i, k0, nblk, pst, reads):
        s = stg[cnt[0] % 2]
        cnt[0] += 1
        S.op("vector",
             lambda e, s=s, pst=pst, nblk=nblk: e.tensor_copy(
                 s[:, 0:nblk * 128], pst[:, 0:nblk * 128]),
             reads=reads, writes=[rg(s)], label=f"{tag}.ev{i}.{k0}")
        dst = xnT_hbm.rearrange("p (k t) -> p k t", t=T)[
            :, k0:k0 + nblk, i * 128:(i + 1) * 128]
        src = s[:, :].rearrange("p (b j) -> p b j", j=128)[:, 0:nblk, :]
        S.dma("sync", dst, src, reads=[rg(s)],
              writes=[rg(xnT_hbm, ("c", i, k0))], label=f"{tag}.sx{i}.{k0}")
    return sink


def ph_ln_to_xnT(S, nc, stack, x_hbm, xnT_sb, ident_bf, T, D, tag, C,
                 eps=LN_EPS, bufs=None, sink=None):
    BD = D // 128
    TT = T // 128
    if bufs is None:
        bufs = mk_ln_bufs(nc, stack, D, tag)
    if sink is None:
        sink = mk_sink_sbuf(S, xnT_sb, T, tag)
    xa = bufs["xa"]
    junk_s = bufs["junk_s"]
    junk_v = bufs["junk_v"]
    xnb = bufs["xnb"]
    st = bufs["st"]
    ps = bufs["ps"]
    for i in range(TT):
        xt = xa[i % 2]
        xn = xnb[i % 2]
        pst = ps[i % 2]
        S.dma("sync", xt[:, :], x_hbm[i * 128:(i + 1) * 128, :],
              label=f"{tag}.ldx{i}")
        S.op("scalar",
             lambda e, xt=xt: e.activation(junk_s[:, :], xt[:, :], AF.Square,
                                           accum_out=st[:, 1:2]),
             reads=[rg(xt)], writes=[rg(junk_s), rg(st, 1)],
             label=f"{tag}.sq{i}")
        S.op("vector",
             lambda e, xt=xt: e.tensor_scalar(junk_v[:, :], xt[:, :], 1.0, None,
                                              ALU.mult, op1=ALU.add,
                                              accum_out=st[:, 0:1]),
             reads=[rg(xt)], writes=[rg(junk_v), rg(st, 0)],
             label=f"{tag}.sum{i}")
        S.op("vector",
             lambda e: e.tensor_scalar(st[:, 2:3], st[:, 0:1], 1.0 / D, None,
                                       ALU.mult),
             reads=[rg(st, 0)], writes=[rg(st, 2)], label=f"{tag}.m{i}")
        S.op("vector",
             lambda e: e.tensor_scalar(st[:, 3:4], st[:, 1:2], 1.0 / D, None,
                                       ALU.mult),
             reads=[rg(st, 1)], writes=[rg(st, 3)], label=f"{tag}.msq{i}")
        S.op("vector",
             lambda e: e.tensor_tensor(st[:, 4:5], st[:, 2:3], st[:, 2:3],
                                       ALU.mult),
             reads=[rg(st, 2)], writes=[rg(st, 4)], label=f"{tag}.m2{i}")
        S.op("vector",
             lambda e: e.tensor_tensor(st[:, 5:6], st[:, 3:4], st[:, 4:5],
                                       ALU.subtract),
             reads=[rg(st, 3), rg(st, 4)], writes=[rg(st, 5)],
             label=f"{tag}.var{i}")
        S.op("scalar",
             lambda e: e.activation(st[:, 6:7], st[:, 5:6], AF.Ln, bias=C(eps)),
             reads=[rg(st, 5), C.r(eps)], writes=[rg(st, 6)],
             label=f"{tag}.ln{i}")
        S.op("scalar",
             lambda e: e.activation(st[:, 7:8], st[:, 6:7], AF.Exp, scale=-0.5),
             reads=[rg(st, 6)], writes=[rg(st, 7)], label=f"{tag}.rstd{i}")
        S.op("vector",
             lambda e, xt=xt, xn=xn: e.tensor_scalar(
                 xn[:, :], xt[:, :], st[:, 2:3], st[:, 7:8],
                 ALU.subtract, ALU.mult),
             reads=[rg(xt), rg(st, 2), rg(st, 7)], writes=[rg(xn)],
             label=f"{tag}.norm{i}")
        for k in range(BD):
            S.op("tensor",
                 lambda e, xn=xn, pst=pst, k=k: e.transpose(
                     pst[:, (k % 8) * 128:(k % 8 + 1) * 128],
                     xn[:, k * 128:(k + 1) * 128], ident_bf[:, :]),
                 reads=[rg(xn), rg(ident_bf)], writes=[rg(pst, k % 8)],
                 label=f"{tag}.tr{i}.{k}")
            if k % 8 == 7 or k == BD - 1:
                k0 = (k // 8) * 8
                nblk = k - k0 + 1
                sink(i, k0, nblk, pst,
                     [rg(pst, kk % 8) for kk in range(k0, k + 1)])


def _chunks(total, size):
    out = []
    c0 = 0
    while c0 < total:
        out.append((c0, min(size, total - c0)))
        c0 += size
    return out


def ph_mm_F(S, nc, stack, w_hbm, Din, ncols, xnT_sb, T, tag, evac_cb,
            col_scale=None, nchunk=512):
    BK = Din // 128
    NB = ncols // 128
    wb = [stack.enter_context(nc.sbuf_tensor(f"{tag}_wb{j}", [128, BK * 128],
                                             BF16)) for j in range(2)]
    ps = [stack.enter_context(nc.psum_tensor(f"{tag}_ps{j}", [128, 512], F32))
          for j in range(2)]
    xnT3 = xnT_sb[:, :].rearrange("p (k t) -> p k t", t=T)
    pi = 0
    for m in range(NB):
        w = wb[m % 2]
        src = w_hbm[:, m * 128:(m + 1) * 128].rearrange("(k p) j -> p k j",
                                                        p=128)
        dst = w[:, :].rearrange("p (k j) -> p k j", j=128)
        S.dma("gpsimd", dst, src, reads=[rg(w_hbm)], writes=[rg(w)],
              label=f"{tag}.ldw{m}")
        for (c0, cn) in _chunks(T, nchunk):
            p = ps[pi % 2]
            pi += 1
            for k in range(BK):
                S.mm(lambda e, w=w, k=k, c0=c0, cn=cn, p=p: e.matmul(
                        p[:, 0:cn], w[:, k * 128:(k + 1) * 128],
                        xnT3[:, k, c0:c0 + cn],
                        start=(k == 0), stop=(k == BK - 1)),
                     reads=[rg(w), rg(xnT_sb)], psum_rgn=rg(p),
                     start=(k == 0), stop=(k == BK - 1),
                     label=f"{tag}.mm{m}.{c0}.{k}")
            evac_cb(m, c0, cn, p)


def ph_mm_T(S, nc, stack, actT_sb, BK, T, w_hbm, Dout, tag, evac_cb,
            nchunk=256, extra_mm=None, w_rows0=0, pre_cb=None,
            w_tiled_bf=False, w_tiled_blk=None):
    wb = [stack.enter_context(nc.sbuf_tensor(f"{tag}_wb{j}",
                                             [128, BK * nchunk], BF16))
          for j in range(2)]
    ps = [stack.enter_context(nc.psum_tensor(f"{tag}_ps{j}", [128, 512], F32))
          for j in range(2)]
    act3 = actT_sb[:, :].rearrange("p (k t) -> p k t", t=T)
    TT = T // 128
    pi = 0
    for ci, (n0, nn) in enumerate(_chunks(Dout, nchunk)):
        w = wb[ci % 2]
        if w_tiled_bf:
            slab = BK * nchunk
            S.dma("scalar", w[:, 0:BK * nn],
                  w_hbm[:, ci * slab:ci * slab + BK * nn],
                  reads=[rg(w_hbm, ci * (nchunk // (w_tiled_blk or nchunk))
                            + kk)
                         for kk in range(nchunk // (w_tiled_blk or nchunk))],
                  writes=[rg(w)],
                  label=f"{tag}.ldw{ci}")
            blk = w_tiled_blk or nchunk
            w4 = w[:, :].rearrange("p (s k j) -> p s k j", k=BK, j=blk)
        else:
            src = w_hbm[w_rows0:w_rows0 + BK * 128, n0:n0 + nn].rearrange(
                "(k p) j -> p k j", p=128)
            dst = w[:, :].rearrange("p (k j) -> p k j", j=nchunk)[:, :, 0:nn]
            S.dma("gpsimd", dst, src, reads=[rg(w_hbm)], writes=[rg(w)],
                  label=f"{tag}.ldw{ci}")
        for i in range(TT):
            p = ps[pi % 2]
            pi += 1
            if pre_cb is not None:
                pre_cb(i, n0, nn)
            extras = extra_mm(i, n0, nn) if extra_mm is not None else []
            for k in range(BK):
                last = (k == BK - 1) and not extras
                if w_tiled_bf and (w_tiled_blk or nchunk) != nchunk:
                    blk = w_tiled_blk
                    rhs = w4[:, 0:nn // blk, k, :]
                else:
                    rhs = w[:, :].rearrange("p (k j) -> p k j", j=nchunk)[
                        :, k, 0:nn]
                S.mm(lambda e, w=w, k=k, i=i, nn=nn, p=p, last=last, rhs=rhs:
                     e.matmul(
                        p[:, 0:nn],
                        act3[:, k, i * 128:(i + 1) * 128],
                        rhs,
                        start=(k == 0), stop=last),
                     reads=[rg(w), rg(actT_sb)], psum_rgn=rg(p),
                     start=(k == 0), stop=last,
                     label=f"{tag}.mm{ci}.{i}.{k}")
            for j, (fn, rds) in enumerate(extras):
                last = (j == len(extras) - 1)
                S.mm(lambda e, fn=fn, p=p: fn(e, p),
                     reads=rds, psum_rgn=rg(p), start=False, stop=last,
                     label=f"{tag}.xmm{ci}.{i}.{j}")
            evac_cb(i, n0, nn, p)




class NCView:

    def __init__(self, nc, pfx):
        self._nc = nc
        self._pfx = pfx

    def sbuf_tensor(self, name, *a, **k):
        return self._nc.sbuf_tensor(self._pfx + name, *a, **k)

    def psum_tensor(self, name, *a, **k):
        return self._nc.psum_tensor(self._pfx + name, *a, **k)

    def dram_tensor(self, name, *a, **k):
        return self._nc.dram_tensor(self._pfx + name, *a, **k)



def _alibi_slopes(n):
    import math
    cp2 = 2 ** int(math.floor(math.log2(n)))
    base = 2.0 ** (-2.0 ** (-(math.log2(cp2) - 3)))
    slopes = base ** np.arange(1, cp2 + 1, dtype=np.float64)
    if cp2 != n:
        extra_base = 2.0 ** (-2.0 ** (-(math.log2(2 * cp2) - 3)))
        rem = min(cp2, n - cp2)
        slopes = np.concatenate(
            [slopes, extra_base ** np.arange(1, 1 + 2 * rem, 2,
                                             dtype=np.float64)])
    return slopes.astype(np.float32)


def _alib3_rows(T, slopes):
    H = len(slopes)
    out = np.zeros((3 * H, T), dtype=ml_dtypes.bfloat16)
    k = np.arange(T, dtype=np.float64)
    for h, s in enumerate(slopes):
        v = np.float64(np.float32(s)) * k
        hi = v.astype(ml_dtypes.bfloat16)
        r1 = v - hi.astype(np.float64)
        mid = r1.astype(ml_dtypes.bfloat16)
        lo = (r1 - mid.astype(np.float64)).astype(ml_dtypes.bfloat16)
        out[3 * h] = hi
        out[3 * h + 1] = mid
        out[3 * h + 2] = lo
    return out


_BUILD_CACHE = {}


def _get_nc(build_fn, *args):
    key = (build_fn.__name__,) + tuple(args)
    if key not in _BUILD_CACHE:
        _BUILD_CACHE[key] = build_fn(*args)
    return _BUILD_CACHE[key]


def _chain(inputs, dims, run):
    T, D, NH, DI, LAT = (dims["T"], dims["D"], dims["NH"], dims["DI"],
                         dims["LAT"])
    NB = NH + 2
    BD = D // 128
    x = np.ascontiguousarray(inputs["L0_attn_hidden_states"],
                             dtype=np.float32)
    NC = x.shape[0]
    cn = np.ascontiguousarray
    f32 = np.float32
    identbf = np.eye(128, dtype=ml_dtypes.bfloat16)
    identf32 = np.eye(128, dtype=f32)
    kramp = cn(np.broadcast_to(np.arange(T, dtype=f32), (128, T)))
    maskdiag = np.where(np.arange(128)[:, None] >= np.arange(128)[None, :],
                        0.0, -1e30).astype(f32)
    W0 = inputs["L0_ffn_W"].astype(f32)
    WT0 = cn(W0.T)
    A0 = inputs["L0_ffn_A"].astype(f32)
    b0 = inputs["L0_ffn_b"].astype(f32)
    LiMEs = inputs["L0_ffn_LiMEs"].astype(f32)
    G0 = cn(np.concatenate(
        [WT0[:, :8], np.zeros((D, 24), f32), A0], axis=1))
    WQKV0 = cn(inputs["L0_attn_w_qkv"].astype(f32))
    WQKV1 = cn(np.concatenate([inputs["L1_attn_wq"], inputs["L1_attn_wk"],
                               inputs["L1_attn_wv"]], axis=1).astype(f32))
    wc0 = cn(inputs["L0_attn_w_c"].astype(f32))
    Bm0 = cn(inputs["L0_ffn_Bmat"].astype(f32))
    wo1 = cn(inputs["L1_attn_wo"].astype(f32))
    up1 = cn(inputs["L1_ffn_up_w"].astype(f32))
    dn1 = cn(inputs["L1_ffn_down_w"].astype(f32))
    gt1 = cn(inputs["L1_ffn_gate_w"].astype(f32))
    f11 = cn(inputs["L1_ffn_fc1_w"].astype(f32))
    ew1 = cn(inputs["L1_ffn_expert_w"].astype(f32))
    f21 = cn(inputs["L1_ffn_fc2_w"].astype(f32))
    gammaB = cn(np.broadcast_to(inputs["L1_ffn_ln_gamma"].astype(f32),
                                (128, D)))
    beta_row = cn(inputs["L1_ffn_ln_beta"].astype(f32)[None, :])
    slopes0 = _alibi_slopes(NH)
    slopes1 = np.zeros(NH, f32)
    xa = cn(x[:, 1::2, :])

    def rn(build_fn, args, maps):
        return run(_get_nc(build_fn, *args), maps)

    r = rn(build_pre0_launch, (T, D, NC),
           [{"x": x[c], "x_anch": cn(np.concatenate(
                 [xa[c:c + 1], xa[:c], xa[c + 1:]]).reshape(-1, D)),
             "G0": G0, "b8_col": cn(b0[:8, None]),
             "LiT": cn(LiMEs.T), "Li": LiMEs,
             "c_identbf": identbf, "c_identf32": identf32}
            for c in range(NC)])
    xnT0 = [r[c]["xnT"] for c in range(NC)]
    pmixT = [r[c]["pmixT"] for c in range(NC)]
    r = rn(build_qkv_launch, (T, D, NB),
           [{"xnT": xnT0[c], "wqkv": WQKV0} for c in range(NC)])
    qkT = [r[c]["qkT"] for c in range(NC)]
    r = rn(build_attn_launch, (T, NH, tuple(float(s) for s in slopes0)),
           [{"qkT": qkT[c], "c_kramp": kramp, "c_maskdiag": maskdiag,
             "c_identbf": identbf} for c in range(NC)])
    attnT = [r[c]["attnT"] for c in range(NC)]
    r = rn(build_h0_launch, (T, D),
           [{"xnT": xnT0[c], "WT": WT0, "b_row": cn(b0[None, :]), "A0": A0,
             "x": x[c]} for c in range(NC)])
    ACC = [r[c]["ACC"] for c in range(NC)]
    zT = [r[c]["zT"] for c in range(NC)]
    r = rn(build_combine0_launch, (T, D),
           [{"ACC": ACC[c], "attnT": attnT[c], "wc": wc0, "Bmat": Bm0,
             "zT": zT[c], "pmixT": pmixT[c]} for c in range(NC)])
    out1 = [r[c]["out1"] for c in range(NC)]
    r = rn(build_ln_launch, (T, D),
           [{"x": out1[c], "c_identbf": identbf} for c in range(NC)])
    xnT1 = [r[c]["xnT"] for c in range(NC)]
    r = rn(build_qkv_launch, (T, D, NB),
           [{"xnT": xnT1[c], "wqkv": WQKV1} for c in range(NC)])
    qkT1 = [r[c]["qkT"] for c in range(NC)]
    r = rn(build_attn_launch, (T, NH, tuple(float(s) for s in slopes1)),
           [{"qkT": qkT1[c], "c_kramp": kramp, "c_maskdiag": maskdiag,
             "c_identbf": identbf} for c in range(NC)])
    attn1T = [r[c]["attnT"] for c in range(NC)]
    r = rn(build_wcast_launch, (D, DI, LAT),
           [{"up": up1, "down": dn1, "fc1": f11, "expw": ew1, "fc2": f21,
             "wo": wo1} for c in range(NC)])
    wb = r
    r = rn(build_ffn1a_launch, (T, D, DI),
           [{"xnT": xnT1[c], "up_bf": wb[c]["up_bf"],
             "down_bf": wb[c]["down_bf"]} for c in range(NC)])
    SH = [r[c]["SH"] for c in range(NC)]
    r = rn(build_ffn1b_launch, (T, D, LAT),
           [{"xnT": xnT1[c], "gate": gt1, "fc1_bf": wb[c]["fc1_bf"],
             "expw_bf": wb[c]["expw_bf"], "fc2_bf": wb[c]["fc2_bf"],
             "c_identf32": identf32} for c in range(NC)])
    RT = [r[c]["RT"] for c in range(NC)]
    r = rn(build_combine1_launch, (T, D),
           [{"x1": out1[c], "attnT": attn1T[c], "wo_bf": wb[c]["wo_bf"],
             "SH": SH[c], "RT": RT[c], "gammaB": gammaB,
             "beta_row": beta_row} for c in range(NC)])
    return np.stack([r[c]["OUT"] for c in range(NC)]).astype(np.float32)


def _dev_run(nc, in_maps):
    res = run_bass_kernel_spmd(nc, in_maps, core_ids=list(range(len(in_maps))))
    return res.results


def build_fused(T, D, NH, DI, LAT, NBAT, slopes0, slopes1):
    nc = bass.Bass("TRN2", target_bir_lowering=False)
    BD = D // 128
    NBq = NH + 2
    EI, EO, IN = "ExternalInput", "ExternalOutput", "Internal"

    def dt(n, shp, dty, kind):
        return nc.dram_tensor(n, shp, dty, kind=kind).ap()

    A = {
        "x": dt("x", [T, D], F32, EI),
        "x_anch": dt("x_anch", [NBAT * (T // 2), D], F32, EI),
        "G0": dt("G0", [D, 64], F32, EI),
        "b8_col": dt("b8_col", [8, 1], F32, EI),
        "LiT": dt("LiT", [32, 8], F32, EI),
        "Li": dt("Li", [8, 32], F32, EI),
        "c_identbf": dt("c_identbf", [128, 128], BF16, EI),
        "c_identf32": dt("c_identf32", [128, 128], F32, EI),
        "c_maskdiag": dt("c_maskdiag", [128, 128], F32, EI),
        "alib3": dt("alib3", [3 * NH, T], BF16, EI),
        "wqkv0": dt("wqkv0", [D, NBq * 128], F32, EI),
        "wqkv1": dt("wqkv1", [D, NBq * 128], F32, EI),
        "WT": dt("WT", [D, D], F32, EI),
        "b_row": dt("b_row", [1, D], F32, EI),
        "A0": dt("A0", [D, 32], F32, EI),
        "wc": dt("wc", [D, D], F32, EI),
        "Bmat": dt("Bmat", [32, D], F32, EI),
        "up": dt("up", [D, DI], F32, EI),
        "down": dt("down", [DI, D], F32, EI),
        "gate": dt("gate", [D, 8], F32, EI),
        "fc1": dt("fc1", [D, LAT], F32, EI),
        "expw": dt("expw", [LAT, LAT], F32, EI),
        "fc2": dt("fc2", [LAT, D], F32, EI),
        "wo": dt("wo", [D, D], F32, EI),
        "gammaB": dt("gammaB", [128, D], F32, EI),
        "beta_row": dt("beta_row", [1, D], F32, EI),
        "OUT": dt("OUT", [T, D], F32, EO),
        "xnT0": dt("xnT0", [128, BD * T], BF16, IN),
        "pmixT": dt("pmixT", [32, T], F32, IN),
        "qkT0": dt("qkT0", [128, NBq * T], BF16, IN),
        "attnT0": dt("attnT0", [128, BD * T], BF16, IN),
        "ACC": dt("ACC", [T, D], F32, IN),
        "zT": dt("zT", [32, T], F32, IN),
        "out1": dt("out1", [T, D], F32, IN),
        "xnT1": dt("xnT1", [128, BD * T], BF16, IN),
        "qkT1": dt("qkT1", [128, NBq * T], BF16, IN),
        "attn1T": dt("attn1T", [128, BD * T], BF16, IN),
        "SH": dt("SH", [T, D], F32, IN),
        "RT": dt("RT", [T, D], F32, IN),
        "up_bf": dt("up_bf", [128, (D // 128) * DI], BF16, IN),
        "down_bf": dt("down_bf", [128, (DI // 128) * D], BF16, IN),
        "fc1_bf": dt("fc1_bf", [128, (D // 128) * LAT], BF16, IN),
        "expw_bf": dt("expw_bf", [128, (LAT // 128) * LAT], BF16, IN),
        "fc2_bf": dt("fc2_bf", [128, (LAT // 128) * D], BF16, IN),
        "wo_bf": dt("wo_bf", [128, (D // 128) * D], BF16, IN),
    }
    S = Sched(nc)
    pc = [0]

    def P(fn, aps, *args):
        pc[0] += 1
        with ExitStack() as st:
            fn(S, NCView(nc, f"p{pc[0]}_"), st, aps, *args)
        S.barrier()

    g = lambda *names: {n: A[m] for n, m in
                        (nm.split(":") if ":" in nm else (nm, nm)
                         for nm in names)}
    P(ph_pre0_ln, g("x", "c_identbf", "xnT:xnT0"), T, D)
    P(ph_qkv, g("xnT:xnT0", "wqkv:wqkv0", "qkT:qkT0"), T, D, NBq)
    anch_stack = ExitStack()
    anch_fns = ph_anchors(S, NCView(nc, "anc_"), anch_stack,
                          g("x_anch", "G0", "b8_col", "LiT", "Li",
                            "c_identbf", "c_identf32", "pmixT"), T, D, NBAT)
    S.add_background(anch_fns, every=90)
    P(ph_attn2, g("qkT:qkT0", "alib3", "c_maskdiag", "c_identbf",
                  "attnT:attnT0"), T, NH, slopes0)
    S.drain_background()
    S.barrier()
    anch_stack.close()
    wc_stack = ExitStack()
    wc_fns = ph_wcast(S, NCView(nc, "wcv_"), wc_stack, [
        (A["up"], A["up_bf"], D, DI, 128),
        (A["down"], A["down_bf"], DI, D, 64),
        (A["fc1"], A["fc1_bf"], D, LAT, 128),
        (A["expw"], A["expw_bf"], LAT, LAT, 128),
        (A["fc2"], A["fc2_bf"], LAT, D, 256),
        (A["wo"], A["wo_bf"], D, D, 128),
    ])
    S.add_background(wc_fns, every=32)
    P(ph_h0, g("xnT:xnT0", "WT", "b_row", "A0", "x", "ACC", "zT"), T, D)
    P(ph_combine0, g("ACC", "attnT:attnT0", "wc", "Bmat", "zT", "pmixT",
                     "out1"), T, D)
    P(ph_lnL, g("x:out1", "c_identbf", "xnT:xnT1"), T, D)
    P(ph_qkv, g("xnT:xnT1", "wqkv:wqkv1", "qkT:qkT1"), T, D, NBq)
    P(ph_attn2, g("qkT:qkT1", "c_maskdiag", "c_identbf",
                  "attnT:attn1T"), T, NH, slopes1)
    S.drain_background()
    S.barrier()
    wc_stack.close()
    P(ph_ffn1a, g("xnT:xnT1", "up_bf", "down_bf", "SH"), T, D, DI)
    P(ph_ffn1b, g("xnT:xnT1", "gate", "fc1_bf", "expw_bf", "fc2_bf",
                  "c_identf32", "RT"), T, D, LAT)
    P(ph_combine1, g("x1:out1", "attnT:attn1T", "wo_bf", "SH", "RT",
                     "gammaB", "beta_row", "OUT"), T, D)
    S.finalize()
    return nc


def _fused_chain(inputs, dims, run):
    T, D, NH, DI, LAT = (dims["T"], dims["D"], dims["NH"], dims["DI"],
                         dims["LAT"])
    x = np.ascontiguousarray(inputs["L0_attn_hidden_states"],
                             dtype=np.float32)
    NC = x.shape[0]
    cn = np.ascontiguousarray
    f32 = np.float32
    W0 = inputs["L0_ffn_W"].astype(f32)
    WT0 = cn(W0.T)
    A0 = inputs["L0_ffn_A"].astype(f32)
    b0 = inputs["L0_ffn_b"].astype(f32)
    LiMEs = inputs["L0_ffn_LiMEs"].astype(f32)
    xa = cn(x[:, 1::2, :])
    nc = _get_nc(build_fused, T, D, NH, DI, LAT, NC,
                 tuple(float(s) for s in _alibi_slopes(NH)),
                 tuple(0.0 for _ in range(NH)))
    shared = {
        "G0": cn(np.concatenate([WT0[:, :8], np.zeros((D, 24), f32), A0],
                                axis=1)),
        "b8_col": cn(b0[:8, None]),
        "LiT": cn(LiMEs.T),
        "Li": LiMEs,
        "c_identbf": np.eye(128, dtype=ml_dtypes.bfloat16),
        "c_identf32": np.eye(128, dtype=f32),
        "alib3": _alib3_rows(T, _alibi_slopes(NH)),
        "c_maskdiag": np.where(
            np.arange(128)[:, None] >= np.arange(128)[None, :],
            0.0, -1e30).astype(f32),
        "wqkv0": cn(inputs["L0_attn_w_qkv"].astype(f32)),
        "wqkv1": cn(np.concatenate(
            [inputs["L1_attn_wq"], inputs["L1_attn_wk"],
             inputs["L1_attn_wv"]], axis=1).astype(f32)),
        "WT": WT0,
        "b_row": cn(b0[None, :]),
        "A0": cn(A0),
        "wc": cn(inputs["L0_attn_w_c"].astype(f32)),
        "Bmat": cn(inputs["L0_ffn_Bmat"].astype(f32)),
        "up": cn(inputs["L1_ffn_up_w"].astype(f32)),
        "down": cn(inputs["L1_ffn_down_w"].astype(f32)),
        "gate": cn(inputs["L1_ffn_gate_w"].astype(f32)),
        "fc1": cn(inputs["L1_ffn_fc1_w"].astype(f32)),
        "expw": cn(inputs["L1_ffn_expert_w"].astype(f32)),
        "fc2": cn(inputs["L1_ffn_fc2_w"].astype(f32)),
        "wo": cn(inputs["L1_attn_wo"].astype(f32)),
        "gammaB": cn(np.broadcast_to(
            inputs["L1_ffn_ln_gamma"].astype(f32), (128, D))),
        "beta_row": cn(inputs["L1_ffn_ln_beta"].astype(f32)[None, :]),
    }
    in_maps = []
    for c in range(NC):
        m = dict(shared)
        m["x"] = x[c]
        m["x_anch"] = cn(np.concatenate(
            [xa[c:c + 1], xa[:c], xa[c + 1:]]).reshape(-1, D))
        in_maps.append(m)
    r = run(nc, in_maps)
    return np.stack([r[c]["OUT"] for c in range(NC)]).astype(np.float32)


def kernel(**inputs):
    import os
    dims = dict(T=2048, D=4096, NH=32, DI=8192, LAT=2048)
    if os.environ.get("KERNEL_UNFUSED"):
        return _chain(inputs, dims, _dev_run)
    return _fused_chain(inputs, dims, _dev_run)


def ph_wcast(S, nc, stack, specs, tag="wc"):
    bufs = [stack.enter_context(
        nc.sbuf_tensor(f"{tag}_b{j}", [128, 4096], BF16)) for j in range(2)]
    cnt = [0]
    fns = []

    def emit_slab(src, dst, blk, slab, ib, si):
        b = bufs[cnt[0] % 2]
        cnt[0] += 1
        S.dma("gpsimd",
              b[:, 0:slab].rearrange("p (k j) -> p k j", j=blk),
              src[:, ib * blk:(ib + 1) * blk].rearrange(
                  "(k p) j -> p k j", p=128),
              reads=[rg(src)], writes=[rg(b)],
              label=f"{tag}.c{si}.{ib}")
        S.dma("scalar", dst[:, ib * slab:(ib + 1) * slab], b[:, 0:slab],
              reads=[rg(b)], writes=[rg(dst, ib)],
              label=f"{tag}.s{si}.{ib}")

    for si, (src, dst, Din, ncols, blk) in enumerate(specs):
        BK = Din // 128
        slab = BK * blk
        assert slab <= 4096, (slab, si)
        for ib in range(ncols // blk):
            fns.append(lambda src=src, dst=dst, blk=blk, slab=slab, ib=ib,
                       si=si: emit_slab(src, dst, blk, slab, ib, si))
    return fns


def ph_ffn1a(S, nc, stack, aps, T, D, DI):
    BD = D // 128
    BI = DI // 128
    xnT_h = aps["xnT"]
    up = aps["up_bf"]
    down = aps["down_bf"]
    SH = aps["SH"]
    sbt = lambda name, shp, dt: stack.enter_context(
        nc.sbuf_tensor(name, shp, dt))
    xcb = sbt("xcb", [128, BD * 512], BF16)
    hTn = sbt("hTn", [128, BI * 512], BF16)
    h1 = sbt("h1", [128, 512], BF16)
    ub = [sbt(f"ub{j}", [128, BD * 128], BF16) for j in range(2)]
    db = [sbt(f"db{j}", [128, BI * 256], BF16) for j in range(2)]
    stg = [sbt(f"stg{j}", [128, 256], F32) for j in range(2)]
    pu = [stack.enter_context(nc.psum_tensor(f"pu{j}", [128, 512], F32))
          for j in range(2)]
    pd = [stack.enter_context(nc.psum_tensor(f"pd{j}", [128, 512], F32))
          for j in range(2)]
    xc3 = xcb[:, :].rearrange("p (k t) -> p k t", t=512)
    hT3 = hTn[:, :].rearrange("p (k t) -> p k t", t=512)
    si = [0]
    for (t0, tn) in _chunks(T, 512):
        S.dma("sync", xc3[:, :, 0:tn],
              xnT_h.rearrange("p (k t) -> p k t", t=T)[:, :, t0:t0 + tn],
              reads=[rg(xnT_h)], writes=[rg(xcb)], label=f"f1.ldx{t0}")
        for ib in range(BI):
            w = ub[ib % 2]
            S.dma("sync", w[:, :],
                  up[:, ib * BD * 128:(ib + 1) * BD * 128],
                  reads=[rg(up, ib)], writes=[rg(w)],
                  label=f"f1.ldu{t0}.{ib}")
            p = pu[ib % 2]
            for k in range(BD):
                S.mm(lambda e, w=w, k=k, tn=tn, p=p: e.matmul(
                        p[:, 0:tn], w[:, k * 128:(k + 1) * 128],
                        xc3[:, k, 0:tn],
                        start=(k == 0), stop=(k == BD - 1)),
                     reads=[rg(w), rg(xcb)], psum_rgn=rg(p),
                     start=(k == 0), stop=(k == BD - 1),
                     label=f"f1.up{t0}.{ib}.{k}")
            S.op("scalar",
                 lambda e, p=p, tn=tn: e.activation(h1[:, 0:tn],
                                                    p[:, 0:tn], AF.Relu),
                 reads=[rg(p)], writes=[rg(h1)], label=f"f1.rl{t0}.{ib}")
            S.op("vector",
                 lambda e, ib=ib, tn=tn: e.tensor_tensor(
                     hT3[:, ib, 0:tn], h1[:, 0:tn], h1[:, 0:tn],
                     ALU.mult),
                 reads=[rg(h1)], writes=[rg(hTn, ib)],
                 label=f"f1.sq{t0}.{ib}")
        for mi, (m0, mn) in enumerate(_chunks(D, 256)):
            dbv = db[mi % 2]
            s0 = (m0 // 64) * BI * 64
            S.dma("sync", dbv[:, 0:BI * mn], down[:, s0:s0 + BI * mn],
                  reads=[rg(down, m0 // 64 + i) for i in range(mn // 64)],
                  writes=[rg(dbv)],
                  label=f"f1.ldd{t0}.{m0}")
            db4 = dbv[:, :].rearrange("p (s k j) -> p s k j", k=BI, j=64)
            for tt in range(tn // 128):
                p = pd[tt % 2]
                for ib in range(BI):
                    S.mm(lambda e, ib=ib, tt=tt, mn=mn, p=p, db4=db4:
                         e.matmul(
                            p[:, 0:mn],
                            hT3[:, ib, tt * 128:(tt + 1) * 128],
                            db4[:, 0:mn // 64, ib, :],
                            start=(ib == 0), stop=(ib == BI - 1)),
                         reads=[rg(dbv)] + [rg(hTn, kk)
                                           for kk in range(BI)],
                         psum_rgn=rg(p), start=(ib == 0),
                         stop=(ib == BI - 1),
                         label=f"f1.dn{t0}.{m0}.{tt}.{ib}")
                s = stg[si[0] % 2]
                si[0] += 1
                S.op("vector",
                     lambda e, p=p, s=s, mn=mn: e.tensor_copy(
                         s[:, 0:mn], p[:, 0:mn]),
                     reads=[rg(p)], writes=[rg(s)],
                     label=f"f1.ev{t0}.{m0}.{tt}")
                S.dma("sync",
                      SH[t0 + tt * 128:t0 + (tt + 1) * 128, m0:m0 + mn],
                      s[:, 0:mn], reads=[rg(s)],
                      writes=[rg(SH, (t0, m0, tt))],
                      label=f"f1.st{t0}.{m0}.{tt}")


def build_wcast_launch(D, DI, LAT):
    nc = bass.Bass("TRN2", target_bir_lowering=False)
    aps = {}
    aps["up"] = nc.dram_tensor("up", [D, DI], F32, kind="ExternalInput").ap()
    aps["down"] = nc.dram_tensor("down", [DI, D], F32,
                                 kind="ExternalInput").ap()
    aps["fc1"] = nc.dram_tensor("fc1", [D, LAT], F32,
                                kind="ExternalInput").ap()
    aps["expw"] = nc.dram_tensor("expw", [LAT, LAT], F32,
                                 kind="ExternalInput").ap()
    aps["fc2"] = nc.dram_tensor("fc2", [LAT, D], F32,
                                kind="ExternalInput").ap()
    aps["wo"] = nc.dram_tensor("wo", [D, D], F32,
                               kind="ExternalInput").ap()
    outs = {
        "up_bf": nc.dram_tensor("up_bf", [128, (D // 128) * DI], BF16,
                                kind="ExternalOutput").ap(),
        "down_bf": nc.dram_tensor("down_bf", [128, (DI // 128) * D], BF16,
                                  kind="ExternalOutput").ap(),
        "fc1_bf": nc.dram_tensor("fc1_bf", [128, (D // 128) * LAT], BF16,
                                 kind="ExternalOutput").ap(),
        "expw_bf": nc.dram_tensor("expw_bf", [128, (LAT // 128) * LAT], BF16,
                                  kind="ExternalOutput").ap(),
        "fc2_bf": nc.dram_tensor("fc2_bf", [128, (LAT // 128) * D], BF16,
                                 kind="ExternalOutput").ap(),
        "wo_bf": nc.dram_tensor("wo_bf", [128, (D // 128) * D], BF16,
                                kind="ExternalOutput").ap(),
    }
    S = Sched(nc)
    with ExitStack() as stack:
        for fn in ph_wcast(S, NCView(nc, ""), stack, [
                (aps["up"], outs["up_bf"], D, DI, 128),
                (aps["down"], outs["down_bf"], DI, D, 64),
                (aps["fc1"], outs["fc1_bf"], D, LAT, 128),
                (aps["expw"], outs["expw_bf"], LAT, LAT, 128),
                (aps["fc2"], outs["fc2_bf"], LAT, D, 256),
                (aps["wo"], outs["wo_bf"], D, D, 128)]):
            fn()
        S.finalize()
    return nc


def build_ffn1a_launch(T, D, DI):
    nc = bass.Bass("TRN2", target_bir_lowering=False)
    aps = {}
    BD = D // 128
    BI = DI // 128
    aps["xnT"] = nc.dram_tensor("xnT", [128, BD * T], BF16,
                           kind="ExternalInput").ap()
    aps["up_bf"] = nc.dram_tensor("up_bf", [128, (D // 128) * DI], BF16,
                                  kind="ExternalInput").ap()
    aps["down_bf"] = nc.dram_tensor("down_bf", [128, (DI // 128) * D], BF16,
                                    kind="ExternalInput").ap()
    aps["SH"] = nc.dram_tensor("SH", [T, D], F32, kind="ExternalOutput").ap()
    S = Sched(nc)
    with ExitStack() as stack:
        ph_ffn1a(S, NCView(nc, ""), stack, aps, T, D, DI)
        S.finalize()
    return nc


def ph_ffn1b(S, nc, stack, aps, T, D, LAT):
    BD = D // 128
    BL = LAT // 128
    TT = T // 128
    xnT_h = aps["xnT"]
    gate = aps["gate"]
    fc1 = aps["fc1_bf"]
    expw = aps["expw_bf"]
    fc2 = aps["fc2_bf"]
    identf_h = aps["c_identf32"]
    RT = aps["RT"]
    sbt = lambda name, shp, dt: stack.enter_context(
        nc.sbuf_tensor(name, shp, dt))
    identf = sbt("identf", [128, 128], F32)
    S.dma("sync", identf[:, :], identf_h, label="ldidf")
    gb = sbt("gb", [128, BD * 8], BF16)
    xlT = sbt("xlT", [128, BL * T], BF16)
    lgT = sbt("lgT", [8, T], F32)
    LGt = sbt("LGt", [128, 8], F32)
    rw = sbt("rw", [128, TT], F32)
    wst = sbt("wst1", [128, 8], F32)
    E8 = sbt("E8b", [128, 8], F32)
    M8 = sbt("M8b", [128, 8], F32)
    stack2 = ExitStack()
    sb2 = lambda name, shp, dt: stack2.enter_context(
        nc.sbuf_tensor(name, shp, dt))
    xcb = sb2("xcb", [128, BD * 512], BF16)
    fb = [sb2(f"fb{j}", [128, BD * 128], BF16) for j in range(2)]
    pg = stack.enter_context(nc.psum_tensor("pg", [128, 512], F32))
    pf = [stack.enter_context(nc.psum_tensor(f"pf{j}", [128, 512], F32))
          for j in range(2)]
    xc3 = xcb[:, :].rearrange("p (k t) -> p k t", t=512)
    xl3 = xlT[:, :].rearrange("p (k t) -> p k t", t=T)
    S.dma("gpsimd", gb[:, :].rearrange("p (k j) -> p k j", j=8),
          gate.rearrange("(k p) j -> p k j", p=128),
          reads=[rg(gate)], writes=[rg(gb)], label="ldg")
    for (t0, tn) in _chunks(T, 512):
        S.dma("sync", xc3[:, :, 0:tn],
              xnT_h.rearrange("p (k t) -> p k t", t=T)[:, :, t0:t0 + tn],
              reads=[rg(xnT_h)], writes=[rg(xcb)], label=f"f2.ldx{t0}")
        for k in range(BD):
            S.mm(lambda e, k=k, tn=tn: e.matmul(
                    pg[0:8, 0:tn], gb[:, k * 8:(k + 1) * 8],
                    xc3[:, k, 0:tn],
                    start=(k == 0), stop=(k == BD - 1)),
                 reads=[rg(gb), rg(xcb)], psum_rgn=rg(pg),
                 start=(k == 0), stop=(k == BD - 1),
                 label=f"f2.g{t0}.{k}")
        S.op("vector",
             lambda e, t0=t0, tn=tn: e.tensor_copy(
                 lgT[:, t0:t0 + tn], pg[0:8, 0:tn]),
             reads=[rg(pg)], writes=[rg(lgT, t0)], label=f"f2.gev{t0}")
        for lb in range(BL):
            w = fb[lb % 2]
            S.dma("sync", w[:, :],
                  fc1[:, lb * BD * 128:(lb + 1) * BD * 128],
                  reads=[rg(fc1, lb)], writes=[rg(w)],
                  label=f"f2.ldf{t0}.{lb}")
            p = pf[lb % 2]
            for k in range(BD):
                S.mm(lambda e, w=w, k=k, tn=tn, p=p: e.matmul(
                        p[:, 0:tn], w[:, k * 128:(k + 1) * 128],
                        xc3[:, k, 0:tn],
                        start=(k == 0), stop=(k == BD - 1)),
                     reads=[rg(w), rg(xcb)], psum_rgn=rg(p),
                     start=(k == 0), stop=(k == BD - 1),
                     label=f"f2.fc1.{t0}.{lb}.{k}")
            S.op("vector",
                 lambda e, lb=lb, t0=t0, tn=tn, p=p: e.tensor_copy(
                     xl3[:, lb, t0:t0 + tn], p[:, 0:tn]),
                 reads=[rg(p)], writes=[rg(xlT, lb)],
                 label=f"f2.fev{t0}.{lb}")
    stack2.close()
    S.barrier()
    eoT = sbt("eoT", [128, BL * T], BF16)
    eb = [sbt(f"eb{j}", [128, BL * 128], BF16) for j in range(2)]
    eo3 = eoT[:, :].rearrange("p (k t) -> p k t", t=T)
    for i in range(TT):
        S.op("tensor",
             lambda e, i=i: e.transpose(
                 pg[:, 0:8], lgT[:, i * 128:(i + 1) * 128],
                 identf[0:8, 0:8]),
             reads=[rg(lgT, kk) for kk in range(0, T, 512)]
             + [rg(identf)],
             writes=[rg(pg)], label=f"rw.tr{i}")
        S.op("vector", lambda e: e.tensor_copy(LGt[:, :], pg[:, 0:8]),
             reads=[rg(pg)], writes=[rg(LGt)], label=f"rw.ev{i}")
        S.op("vector",
             lambda e: e.reduce_max(wst[:, 0:1], LGt[:, :], axis=AX.X),
             reads=[rg(LGt)], writes=[rg(wst, 0)], label=f"rw.m1{i}")
        S.op("vector",
             lambda e: e.tensor_scalar(E8[:, :], LGt[:, :], wst[:, 0:1],
                                       None, ALU.is_ge),
             reads=[rg(LGt), rg(wst, 0)], writes=[rg(E8)],
             label=f"rw.eq{i}")
        S.op("vector",
             lambda e: e.reduce_sum(wst[:, 1:2], E8[:, :], axis=AX.X),
             reads=[rg(E8)], writes=[rg(wst, 1)], label=f"rw.cnt{i}")
        S.op("vector",
             lambda e: e.scalar_tensor_tensor(M8[:, :], E8[:, :], -1e30,
                                              LGt[:, :], ALU.mult,
                                              ALU.add),
             reads=[rg(E8), rg(LGt)], writes=[rg(M8)], label=f"rw.mk{i}")
        S.op("vector",
             lambda e: e.reduce_max(wst[:, 2:3], M8[:, :], axis=AX.X),
             reads=[rg(M8)], writes=[rg(wst, 2)], label=f"rw.m2{i}")
        S.op("vector",
             lambda e: e.tensor_scalar(wst[:, 3:4], wst[:, 1:2], 1.5,
                                       None, ALU.is_ge),
             reads=[rg(wst, 1)], writes=[rg(wst, 3)], label=f"rw.g2{i}")
        S.op("vector",
             lambda e: e.tensor_tensor(wst[:, 4:5], wst[:, 0:1],
                                       wst[:, 2:3], ALU.subtract),
             reads=[rg(wst, 0), rg(wst, 2)], writes=[rg(wst, 4)],
             label=f"rw.df{i}")
        S.op("vector",
             lambda e: e.tensor_tensor(wst[:, 5:6], wst[:, 3:4],
                                       wst[:, 4:5], ALU.mult),
             reads=[rg(wst, 3), rg(wst, 4)], writes=[rg(wst, 5)],
             label=f"rw.cr{i}")
        S.op("vector",
             lambda e: e.tensor_tensor(wst[:, 6:7], wst[:, 0:1],
                                       wst[:, 2:3], ALU.add),
             reads=[rg(wst, 0), rg(wst, 2)], writes=[rg(wst, 6)],
             label=f"rw.sm{i}")
        S.op("vector",
             lambda e, i=i: e.tensor_tensor(rw[:, i:i + 1], wst[:, 6:7],
                                            wst[:, 5:6], ALU.add),
             reads=[rg(wst, 6), rg(wst, 5)], writes=[rg(rw, i)],
             label=f"rw.rw{i}")
    for (t0, tn) in _chunks(T, 512):
        for l2 in range(BL):
            w = eb[l2 % 2]
            S.dma("sync", w[:, :],
                  expw[:, l2 * BL * 128:(l2 + 1) * BL * 128],
                  reads=[rg(expw, l2)], writes=[rg(w)],
                  label=f"f2.lde{t0}.{l2}")
            p = pf[l2 % 2]
            for lb in range(BL):
                S.mm(lambda e, w=w, lb=lb, t0=t0, tn=tn, p=p: e.matmul(
                        p[:, 0:tn], w[:, lb * 128:(lb + 1) * 128],
                        xl3[:, lb, t0:t0 + tn],
                        start=(lb == 0), stop=(lb == BL - 1)),
                     reads=[rg(w)] + [rg(xlT, kk) for kk in range(BL)],
                     psum_rgn=rg(p), start=(lb == 0),
                     stop=(lb == BL - 1), label=f"f2.ex{t0}.{l2}.{lb}")
            S.op("vector",
                 lambda e, l2=l2, t0=t0, tn=tn, p=p: e.tensor_copy(
                     eo3[:, l2, t0:t0 + tn], p[:, 0:tn]),
                 reads=[rg(p)], writes=[rg(eoT, l2)],
                 label=f"f2.eev{t0}.{l2}")
    stg = [sbt(f"stg{j}", [128, 512], F32) for j in range(2)]
    si = [0]

    def evac(i, n0, nn, p):
        s = stg[si[0] % 2]
        si[0] += 1
        S.op("vector",
             lambda e, p=p, s=s, nn=nn, i=i: e.tensor_scalar(
                 s[:, 0:nn], p[:, 0:nn], rw[:, i:i + 1], None, ALU.mult),
             reads=[rg(p), rg(rw, i)], writes=[rg(s)],
             label=f"f2.c2ev{i}.{n0}")
        S.dma("sync", RT[i * 128:(i + 1) * 128, n0:n0 + nn], s[:, 0:nn],
              reads=[rg(s)], writes=[rg(RT, (i, n0))],
              label=f"f2.c2st{i}.{n0}")

    ph_mm_T(S, nc, stack, eoT, BL, T, fc2, D, "c2", evac, nchunk=256,
            w_tiled_bf=True)


def build_ffn1b_launch(T, D, LAT):
    nc = bass.Bass("TRN2", target_bir_lowering=False)
    aps = {}
    BD = D // 128
    BL = LAT // 128
    TT = T // 128
    aps["xnT"] = nc.dram_tensor("xnT", [128, BD * T], BF16,
                           kind="ExternalInput").ap()
    aps["gate"] = nc.dram_tensor("gate", [D, 8], F32, kind="ExternalInput").ap()
    aps["fc1_bf"] = nc.dram_tensor("fc1_bf", [128, (D // 128) * LAT], BF16,
                                   kind="ExternalInput").ap()
    aps["expw_bf"] = nc.dram_tensor("expw_bf", [128, (LAT // 128) * LAT],
                                    BF16, kind="ExternalInput").ap()
    aps["fc2_bf"] = nc.dram_tensor("fc2_bf", [128, (LAT // 128) * D], BF16,
                                   kind="ExternalInput").ap()
    aps["c_identf32"] = nc.dram_tensor("c_identf32", [128, 128], F32,
                              kind="ExternalInput").ap()
    aps["RT"] = nc.dram_tensor("RT", [T, D], F32, kind="ExternalOutput").ap()
    S = Sched(nc)
    with ExitStack() as stack:
        ph_ffn1b(S, NCView(nc, ""), stack, aps, T, D, LAT)
        S.finalize()
    return nc


def ph_combine1(S, nc, stack, aps, T, D):
    BD = D // 128
    TT = T // 128
    x1 = aps["x1"]
    attnT_h = aps["attnT"]
    wo = aps["wo_bf"]
    SH = aps["SH"]
    RT = aps["RT"]
    gammaB = aps["gammaB"]
    beta_row = aps["beta_row"]
    OUT = aps["OUT"]
    Y = nc.dram_tensor("Yscr", [T, D], F32, kind="Internal").ap()
    sbt = lambda name, shp, dt: stack.enter_context(
        nc.sbuf_tensor(name, shp, dt))
    C = ConstPool(S, nc, stack)
    btb = sbt("btb", [1, D], BF16)
    ones_bf = sbt("onesb", [1, 128], BF16)
    S.dma("gpsimd", btb[:, :], beta_row, label="ldbt")
    S.op("gpsimd", lambda e: e.memset(ones_bf[:, :], 1.0),
         reads=[], writes=[rg(ones_bf)], label="ones")
    st = sbt("st3", [128, 8], F32)
    my = sbt("my", [128, TT], F32)
    mr = sbt("mr", [128, TT], F32)
    stack2 = ExitStack()
    sb2 = lambda name, shp, dt: stack2.enter_context(
        nc.sbuf_tensor(name, shp, dt))
    ya = [sb2(f"ya{j}", [128, D], F32) for j in range(2)]
    yb = [sb2(f"yb{j}", [128, D], F32) for j in range(2)]
    junk_s = sb2("junks3", [128, D], BF16)
    junk_v = sb2("junkv3", [128, D], BF16)
    for i in range(TT):
        sh = ya[i % 2]
        y = yb[i % 2]
        S.dma("sync", sh[:, :], SH[i * 128:(i + 1) * 128, :],
              reads=[rg(SH)], writes=[rg(sh)], label=f"y.ldsh{i}")
        S.dma("scalar", y[:, :], RT[i * 128:(i + 1) * 128, :],
              reads=[rg(RT)], writes=[rg(y)], label=f"y.ldrt{i}")
        S.op("vector",
             lambda e, sh=sh, y=y: e.tensor_tensor(y[:, :], y[:, :],
                                                   sh[:, :], ALU.add),
             reads=[rg(sh), rg(y)], writes=[rg(y)], label=f"y.add{i}")
        S.op("scalar",
             lambda e, y=y: e.activation(junk_s[:, :], y[:, :], AF.Square,
                                         accum_out=st[:, 1:2]),
             reads=[rg(y)], writes=[rg(junk_s), rg(st, 1)],
             label=f"y.sq{i}")
        S.op("vector",
             lambda e, y=y: e.tensor_scalar(junk_v[:, :], y[:, :], 1.0,
                                            None, ALU.mult, op1=ALU.add,
                                            accum_out=st[:, 0:1]),
             reads=[rg(y)], writes=[rg(junk_v), rg(st, 0)],
             label=f"y.sum{i}")
        S.op("vector",
             lambda e, i=i: e.tensor_scalar(my[:, i:i + 1], st[:, 0:1],
                                            1.0 / D, None, ALU.mult),
             reads=[rg(st, 0)], writes=[rg(my, i)], label=f"y.m{i}")
        S.op("vector",
             lambda e: e.tensor_scalar(st[:, 3:4], st[:, 1:2], 1.0 / D,
                                       None, ALU.mult),
             reads=[rg(st, 1)], writes=[rg(st, 3)], label=f"y.msq{i}")
        S.op("vector",
             lambda e, i=i: e.tensor_tensor(st[:, 4:5], my[:, i:i + 1],
                                            my[:, i:i + 1], ALU.mult),
             reads=[rg(my, i)], writes=[rg(st, 4)], label=f"y.m2{i}")
        S.op("vector",
             lambda e: e.tensor_tensor(st[:, 5:6], st[:, 3:4],
                                       st[:, 4:5], ALU.subtract),
             reads=[rg(st, 3), rg(st, 4)], writes=[rg(st, 5)],
             label=f"y.var{i}")
        S.op("scalar",
             lambda e: e.activation(st[:, 6:7], st[:, 5:6], AF.Ln,
                                    bias=C(LN_EPS)),
             reads=[rg(st, 5), C.r(LN_EPS)], writes=[rg(st, 6)],
             label=f"y.ln{i}")
        S.op("scalar",
             lambda e, i=i: e.activation(mr[:, i:i + 1], st[:, 6:7],
                                         AF.Exp, scale=-0.5),
             reads=[rg(st, 6)], writes=[rg(mr, i)], label=f"y.rstd{i}")
        S.dma("sync", Y[i * 128:(i + 1) * 128, :], y[:, :],
              reads=[rg(y)], writes=[rg(Y, i)], label=f"y.st{i}")
    stack2.close()
    S.barrier()
    aT = sbt("aT", [128, BD * T], BF16)
    gmb = sbt("gmb", [128, D], F32)
    S.dma("sync", aT[:, :], attnT_h, label="ldaT")
    S.dma("sync", gmb[:, :], gammaB, label="ldgm")
    yc = [sbt(f"yc{j}", [128, 256], F32) for j in range(3)]
    xc = [sbt(f"xc{j}", [128, 256], F32) for j in range(3)]
    t1b = [sbt(f"t1{j}", [128, 256], F32) for j in range(2)]
    og = [sbt(f"og{j}", [128, 256], F32) for j in range(3)]
    bi = [0]

    def pre(i, n0, nn):
        yt = yc[bi[0] % 3]
        xt = xc[bi[0] % 3]
        S.dma("sync", yt[:, 0:nn], Y[i * 128:(i + 1) * 128, n0:n0 + nn],
              reads=[rg(Y, i)], writes=[rg(yt)], label=f"b.ldy{i}.{n0}")
        S.dma("scalar", xt[:, 0:nn], x1[i * 128:(i + 1) * 128, n0:n0 + nn],
              reads=[rg(x1)], writes=[rg(xt)], label=f"b.ldx{i}.{n0}")

    def extra(i, n0, nn):
        return [(lambda e, p, n0=n0, nn=nn: e.matmul(
                    p[:, 0:nn], ones_bf[0:1, :], btb[0:1, n0:n0 + nn],
                    start=False, stop=True),
                 [rg(ones_bf), rg(btb)])]

    def evac(i, n0, nn, p):
        yt = yc[bi[0] % 3]
        xt = xc[bi[0] % 3]
        o = og[bi[0] % 3]
        t1 = t1b[bi[0] % 2]
        bi[0] += 1
        S.op("vector",
             lambda e, yt=yt, nn=nn, i=i: e.tensor_scalar(
                 t1[:, 0:nn], yt[:, 0:nn], my[:, i:i + 1],
                 mr[:, i:i + 1], ALU.subtract, ALU.mult),
             reads=[rg(yt), rg(my, i), rg(mr, i)], writes=[rg(t1)],
             label=f"b.t1{i}.{n0}")
        S.op("vector",
             lambda e, nn=nn, n0=n0: e.tensor_tensor(
                 t1[:, 0:nn], t1[:, 0:nn], gmb[:, n0:n0 + nn], ALU.mult),
             reads=[rg(t1), rg(gmb)], writes=[rg(t1)],
             label=f"b.t2{i}.{n0}")
        S.op("vector",
             lambda e, p=p, nn=nn: e.tensor_tensor(
                 t1[:, 0:nn], t1[:, 0:nn], p[:, 0:nn], ALU.add),
             reads=[rg(t1), rg(p)], writes=[rg(t1)],
             label=f"b.t3{i}.{n0}")
        S.op("vector",
             lambda e, xt=xt, o=o, nn=nn: e.tensor_tensor(
                 o[:, 0:nn], t1[:, 0:nn], xt[:, 0:nn], ALU.add),
             reads=[rg(t1), rg(xt)], writes=[rg(o)],
             label=f"b.t4{i}.{n0}")
        S.dma("sync", OUT[i * 128:(i + 1) * 128, n0:n0 + nn], o[:, 0:nn],
              reads=[rg(o)], writes=[rg(OUT, (i, n0))],
              label=f"b.st{i}.{n0}")

    ph_mm_T(S, nc, stack, aT, BD, T, wo, D, "b", evac,
            nchunk=256, extra_mm=extra, pre_cb=pre,
            w_tiled_bf=True, w_tiled_blk=128)


def build_combine1_launch(T, D):
    nc = bass.Bass("TRN2", target_bir_lowering=False)
    aps = {}
    BD = D // 128
    TT = T // 128
    aps["x1"] = nc.dram_tensor("x1", [T, D], F32, kind="ExternalInput").ap()
    aps["attnT"] = nc.dram_tensor("attnT", [128, BD * T], BF16,
                             kind="ExternalInput").ap()
    aps["wo_bf"] = nc.dram_tensor("wo_bf", [128, (D // 128) * D], BF16,
                                  kind="ExternalInput").ap()
    aps["SH"] = nc.dram_tensor("SH", [T, D], F32, kind="ExternalInput").ap()
    aps["RT"] = nc.dram_tensor("RT", [T, D], F32, kind="ExternalInput").ap()
    aps["gammaB"] = nc.dram_tensor("gammaB", [128, D], F32,
                            kind="ExternalInput").ap()
    aps["beta_row"] = nc.dram_tensor("beta_row", [1, D], F32,
                              kind="ExternalInput").ap()
    aps["OUT"] = nc.dram_tensor("OUT", [T, D], F32, kind="ExternalOutput").ap()
    S = Sched(nc)
    with ExitStack() as stack:
        ph_combine1(S, NCView(nc, ""), stack, aps, T, D)
        S.finalize()
    return nc


def ph_pre0_ln(S, nc, stack, aps, T, D):
    identb = stack.enter_context(nc.sbuf_tensor("identb", [128, 128], BF16))
    S.dma("sync", identb[:, :], aps["c_identbf"], label="ldidb")
    C = ConstPool(S, nc, stack)
    ph_ln_to_xnT(S, nc, stack, aps["x"], None, identb, T, D, "ln", C,
                 sink=mk_sink_hbm(S, nc, stack, aps["xnT"], T, "lnsk"))


def ph_anchors(S, nc, stack, aps, T, D, NBAT):
    BD = D // 128
    TA = T // 2
    OT = TA // 128
    NAT = NBAT * OT
    xa_h = aps["x_anch"]
    G0 = aps["G0"]
    b8_h = aps["b8_col"]
    LiT_h = aps["LiT"]
    Li_h = aps["Li"]
    identb_h = aps["c_identbf"]
    identf_h = aps["c_identf32"]
    pmixT_out = aps["pmixT"]
    sb = lambda name, shp, dt=F32: stack.enter_context(
        nc.sbuf_tensor(name, shp, dt))
    identb = sb("identb", [128, 128], BF16)
    identf = sb("identf", [128, 128], F32)
    C = ConstPool(S, nc, stack)
    lnb = mk_ln_bufs(nc, stack, D, "lnb", nps=1)
    fns = []
    fns.append(lambda: (
        S.dma("sync", identb[:, :], identb_h, label="ldidb"),
        S.dma("sync", identf[:, :], identf_h, label="ldidf")))
    Gb = sb("Gb", [128, BD * 64], BF16)
    b8 = sb("b8", [8, 1], F32)
    LiT = sb("LiT_s", [32, 8], BF16)
    Li = sb("Li_s", [8, 32], BF16)
    fns.append(lambda: (
        S.dma("gpsimd", Gb[:, :].rearrange("p (k j) -> p k j", j=64),
              G0.rearrange("(k p) j -> p k j", p=128),
              reads=[rg(G0)], writes=[rg(Gb)], label="ldG"),
        S.dma("sync", b8[:, :], b8_h, label="ldb8"),
        S.dma("gpsimd", LiT[:, :], LiT_h, label="ldLiT"),
        S.dma("gpsimd", Li[:, :], Li_h, label="ldLi")))
    xat = lnb["xa"]
    xab = lnb["xnb"]
    junk_s2 = lnb["junk_s"]
    junk_v2 = lnb["junk_v"]
    ast = lnb["st"]
    xaT = sb("xaT", [128, BD * 128], BF16)
    ptr = lnb["ps"]
    psG = stack.enter_context(nc.psum_tensor("psG", [128, 512], F32))
    HsK = sb("HsK", [8, OT * 128])
    dsK = sb("dsK", [8, OT * 128])
    Hst = sb("Hst", [8, 128])
    dst_t = sb("dst", [8, 128])
    zat = sb("zat", [32, 128], BF16)
    mx = sb("mx", [8, 2])
    red = sb("red", [8, 2])
    fns.append(lambda: S.op("gpsimd", lambda e: e.memset(mx[:, :], 0.0),
                            reads=[], writes=[rg(mx)], label="mx0"))

    def seg_tile(a):
        xt = xat[a % 2]
        xb = xab[a % 2]
        pst = ptr[0]
        S.dma("sync", xt[:, :], xa_h[a * 128:(a + 1) * 128, :],
              reads=[rg(xa_h)], writes=[rg(xt)], label=f"a.ldx{a}")
        S.op("scalar",
             lambda e, xt=xt: e.activation(junk_s2[:, :], xt[:, :],
                                           AF.Square,
                                           accum_out=ast[:, 1:2]),
             reads=[rg(xt)], writes=[rg(junk_s2), rg(ast, 1)],
             label=f"a.sq{a}")
        S.op("vector",
             lambda e, xt=xt: e.tensor_scalar(junk_v2[:, :], xt[:, :],
                                              1.0, None, ALU.mult,
                                              op1=ALU.add,
                                              accum_out=ast[:, 0:1]),
             reads=[rg(xt)], writes=[rg(junk_v2), rg(ast, 0)],
             label=f"a.sum{a}")
        S.op("vector",
             lambda e: e.tensor_scalar(ast[:, 2:3], ast[:, 0:1], 1.0 / D,
                                       None, ALU.mult),
             reads=[rg(ast, 0)], writes=[rg(ast, 2)], label=f"a.m{a}")
        S.op("vector",
             lambda e: e.tensor_scalar(ast[:, 3:4], ast[:, 1:2], 1.0 / D,
                                       None, ALU.mult),
             reads=[rg(ast, 1)], writes=[rg(ast, 3)], label=f"a.msq{a}")
        S.op("vector",
             lambda e: e.tensor_tensor(ast[:, 4:5], ast[:, 2:3],
                                       ast[:, 2:3], ALU.mult),
             reads=[rg(ast, 2)], writes=[rg(ast, 4)], label=f"a.m2{a}")
        S.op("vector",
             lambda e: e.tensor_tensor(ast[:, 5:6], ast[:, 3:4],
                                       ast[:, 4:5], ALU.subtract),
             reads=[rg(ast, 3), rg(ast, 4)], writes=[rg(ast, 5)],
             label=f"a.var{a}")
        S.op("scalar",
             lambda e: e.activation(ast[:, 6:7], ast[:, 5:6], AF.Ln,
                                    bias=C(LN_EPS)),
             reads=[rg(ast, 5), C.r(LN_EPS)], writes=[rg(ast, 6)],
             label=f"a.ln{a}")
        S.op("scalar",
             lambda e: e.activation(ast[:, 7:8], ast[:, 6:7], AF.Exp,
                                    scale=-0.5),
             reads=[rg(ast, 6)], writes=[rg(ast, 7)], label=f"a.rstd{a}")
        S.op("vector",
             lambda e, xt=xt, xb=xb: e.tensor_scalar(
                 xb[:, :], xt[:, :], ast[:, 2:3], ast[:, 7:8],
                 ALU.subtract, ALU.mult),
             reads=[rg(xt), rg(ast, 2), rg(ast, 7)], writes=[rg(xb)],
             label=f"a.nrm{a}")
        for k in range(BD):
            S.op("tensor",
                 lambda e, xb=xb, pst=pst, k=k: e.transpose(
                     pst[:, (k % 8) * 128:(k % 8 + 1) * 128],
                     xb[:, k * 128:(k + 1) * 128], identb[:, :]),
                 reads=[rg(xb), rg(identb)], writes=[rg(pst, k % 8)],
                 label=f"a.tr{a}.{k}")
            if k % 8 == 7 or k == BD - 1:
                k0 = (k // 8) * 8
                nb = k - k0 + 1
                S.op("vector",
                     lambda e, k0=k0, nb=nb, pst=pst: e.tensor_copy(
                         xaT[:, k0 * 128:(k0 + nb) * 128],
                         pst[:, 0:nb * 128]),
                     reads=[rg(pst, kk % 8) for kk in range(k0, k + 1)],
                     writes=[rg(xaT, ("c", k0))],
                     label=f"a.ev{a}.{k0}")
        for k in range(BD):
            S.mm(lambda e, k=k: e.matmul(
                    psG[0:64, 0:128], Gb[:, k * 64:(k + 1) * 64],
                    xaT[:, k * 128:(k + 1) * 128],
                    start=(k == 0), stop=(k == BD - 1)),
                 reads=[rg(Gb)] + [rg(xaT, ("c", kk))
                                   for kk in range(0, BD, 8)],
                 psum_rgn=rg(psG), start=(k == 0), stop=(k == BD - 1),
                 label=f"a.g{a}.{k}")
        own = a < OT
        hdst = HsK[:, a * 128:(a + 1) * 128] if own else Hst[:, :]
        hrgn = rg(HsK, a) if own else rg(Hst)
        S.op("vector",
             lambda e, hdst=hdst: e.tensor_scalar(
                 hdst, psG[0:8, 0:128], b8[:, :], None, ALU.add),
             reads=[rg(psG), rg(b8)], writes=[hrgn], label=f"a.hev{a}")
        S.op("vector",
             lambda e: e.tensor_copy(zat[:, :], psG[32:64, 0:128]),
             reads=[rg(psG)], writes=[rg(zat)], label=f"a.zev{a}")
        S.mm(lambda e: e.matmul(psG[0:8, 128:256], LiT[:, :], zat[:, :],
                                start=True, stop=True),
             reads=[rg(LiT), rg(zat)], psum_rgn=rg(psG, "D"),
             start=True, stop=True, label=f"a.d{a}")
        ddst = dsK[:, a * 128:(a + 1) * 128] if own else dst_t[:, :]
        drgn = rg(dsK, a) if own else rg(dst_t)
        S.op("vector",
             lambda e, ddst=ddst: e.tensor_copy(ddst, psG[0:8, 128:256]),
             reads=[rg(psG, "D")], writes=[drgn], label=f"a.dev{a}")
        S.op("vector",
             lambda e, hdst=hdst: e.tensor_reduce(
                 red[:, 0:1], hdst, AX.X, ALU.max,
                 apply_absolute_value=True),
             reads=[hrgn], writes=[rg(red, 0)], label=f"a.hmx{a}")
        S.op("vector",
             lambda e, ddst=ddst: e.tensor_reduce(
                 red[:, 1:2], ddst, AX.X, ALU.max,
                 apply_absolute_value=True),
             reads=[drgn], writes=[rg(red, 1)], label=f"a.dmx{a}")
        S.op("vector",
             lambda e: e.tensor_tensor(mx[:, :], mx[:, :], red[:, :],
                                       ALU.max),
             reads=[rg(red, 0), rg(red, 1), rg(mx)], writes=[rg(mx)],
             label=f"a.mxa{a}")

    for a in range(NAT):
        fns.append(lambda a=a: seg_tile(a))

    ps8 = psG
    mxT = sb("mxT", [2, 8])
    scl = sb("scl", [2, 1])
    sclT = sb("sclT", [1, 2])
    rb = sb("rb", [8, 2])
    ones8 = sb("ones8", [1, 8])
    lgT = sb("lgT", [8, OT * 128])
    tmp8 = sb("tmp8", [8, OT * 128])
    rb1h = sb("rb1h", [8, 1])
    LG = sb("LG", [128, OT * 8])
    WGT = sb("WGT", [128, OT * 8], BF16)
    ps9 = ps8

    def seg_scales():
        S.op("gpsimd", lambda e: e.memset(ones8[:, :], 1.0),
             reads=[], writes=[rg(ones8)], label="ones8")
        S.op("tensor", lambda e: e.transpose(ps8[0:2, 0:8], mx[:, :],
                                             identf[0:8, 0:8]),
             reads=[rg(mx), rg(identf)], writes=[rg(ps8)], label="mxtr")
        S.op("vector", lambda e: e.tensor_copy(mxT[:, :], ps8[0:2, 0:8]),
             reads=[rg(ps8)], writes=[rg(mxT)], label="mxTev")
        S.op("vector",
             lambda e: e.tensor_reduce(scl[:, :], mxT[:, :], AX.X, ALU.max),
             reads=[rg(mxT)], writes=[rg(scl)], label="sclred")
        S.op("vector",
             lambda e: e.tensor_scalar(scl[:, :], scl[:, :], 1e-6, None,
                                       ALU.max),
             reads=[rg(scl)], writes=[rg(scl)], label="scleps")
        S.op("vector", lambda e: e.reciprocal(scl[:, :], scl[:, :]),
             reads=[rg(scl)], writes=[rg(scl)], label="sclrcp")
        S.op("tensor", lambda e: e.transpose(ps8[0:1, 0:2], scl[:, :],
                                             identf[0:2, 0:2]),
             reads=[rg(scl), rg(identf)], writes=[rg(ps8)], label="scltr")
        S.op("vector", lambda e: e.tensor_copy(sclT[:, :], ps8[0:1, 0:2]),
             reads=[rg(ps8)], writes=[rg(sclT)], label="sclTev")
        S.mm(lambda e: e.matmul(ps8[0:8, 0:2], ones8[:, :], sclT[:, :],
                                start=True, stop=True),
             reads=[rg(ones8), rg(sclT)], psum_rgn=rg(ps8),
             start=True, stop=True, label="sclbc")
        S.op("vector", lambda e: e.tensor_copy(rb[:, :], ps8[0:8, 0:2]),
             reads=[rg(ps8)], writes=[rg(rb)], label="rbev")
        S.op("vector",
             lambda e: e.tensor_scalar(tmp8[:, :], HsK[:, :], rb[:, 0:1], 0.5,
                                       ALU.mult, op1=ALU.mult),
             reads=[rg(HsK, kk) for kk in range(OT)] + [rg(rb)],
             writes=[rg(tmp8)], label="lg1")
        S.op("vector",
             lambda e: e.tensor_scalar(rb1h[:, :], rb[:, 1:2], 0.5, None,
                                       ALU.mult),
             reads=[rg(rb)], writes=[rg(rb1h)], label="lg2")
        S.op("vector",
             lambda e: e.scalar_tensor_tensor(lgT[:, :], dsK[:, :],
                                              rb1h[:, :], tmp8[:, :],
                                              ALU.mult, ALU.add),
             reads=[rg(dsK, kk) for kk in range(OT)] + [rg(rb1h), rg(tmp8)],
             writes=[rg(lgT)], label="lg3")
        for a in range(OT):
            S.op("tensor",
                 lambda e, a=a: e.transpose(
                     ps9[:, (a % 8) * 8:(a % 8) * 8 + 8],
                     lgT[:, a * 128:(a + 1) * 128], identf[0:8, 0:8]),
                 reads=[rg(lgT), rg(identf)], writes=[rg(ps9, a % 8)],
                 label=f"w.tr{a}")
            if a % 8 == 7 or a == OT - 1:
                a0 = (a // 8) * 8
                nb = a - a0 + 1
                S.op("vector",
                     lambda e, a0=a0, nb=nb: e.tensor_copy(
                         LG[:, a0 * 8:(a0 + nb) * 8],
                         ps9[:, 0:nb * 8]),
                     reads=[rg(ps9, kk % 8) for kk in range(a0, a + 1)],
                     writes=[rg(LG, ("c", a0))], label=f"w.ev{a0}")

    fns.append(seg_scales)

    wst = sb("wst", [128, 16])
    P8 = sb("P8", [128, 8])
    E8 = sb("E8", [128, 8])
    M8 = sb("M8", [128, 8])
    K8 = sb("K8", [128, 8])
    def seg_rout(a):
        L = LG[:, a * 8:(a + 1) * 8]
        lrg = rg(LG, ("c", (a // 8) * 8))
        S.op("vector",
             lambda e, L=L: e.reduce_max(wst[:, 0:1], L, axis=AX.X),
             reads=[lrg], writes=[rg(wst, 0)], label=f"w.m1{a}")
        S.op("vector",
             lambda e: e.tensor_scalar(wst[:, 1:2], wst[:, 0:1], -1.0,
                                       None, ALU.mult),
             reads=[rg(wst, 0)], writes=[rg(wst, 1)], label=f"w.nm{a}")
        S.op("scalar",
             lambda e, L=L: e.activation(P8[:, :], L, AF.Exp,
                                         bias=wst[:, 1:2],
                                         accum_out=wst[:, 2:3]),
             reads=[lrg, rg(wst, 1)], writes=[rg(P8), rg(wst, 2)],
             label=f"w.exp{a}")
        S.op("vector", lambda e: e.reciprocal(wst[:, 3:4], wst[:, 2:3]),
             reads=[rg(wst, 2)], writes=[rg(wst, 3)], label=f"w.rs{a}")
        S.op("vector",
             lambda e: e.tensor_scalar(P8[:, :], P8[:, :], wst[:, 3:4],
                                       None, ALU.mult),
             reads=[rg(P8), rg(wst, 3)], writes=[rg(P8)],
             label=f"w.pn{a}")
        S.op("vector",
             lambda e: e.reduce_max(wst[:, 4:5], P8[:, :], axis=AX.X),
             reads=[rg(P8)], writes=[rg(wst, 4)], label=f"w.m1p{a}")
        S.op("vector",
             lambda e: e.tensor_scalar(E8[:, :], P8[:, :], wst[:, 4:5],
                                       None, ALU.is_ge),
             reads=[rg(P8), rg(wst, 4)], writes=[rg(E8)],
             label=f"w.eq{a}")
        S.op("vector",
             lambda e: e.reduce_sum(wst[:, 5:6], E8[:, :], axis=AX.X),
             reads=[rg(E8)], writes=[rg(wst, 5)], label=f"w.cnt{a}")
        S.op("vector",
             lambda e: e.scalar_tensor_tensor(M8[:, :], E8[:, :], -1e30,
                                              P8[:, :], ALU.mult,
                                              ALU.add),
             reads=[rg(E8), rg(P8)], writes=[rg(M8)], label=f"w.msk{a}")
        S.op("vector",
             lambda e: e.reduce_max(wst[:, 6:7], M8[:, :], axis=AX.X),
             reads=[rg(M8)], writes=[rg(wst, 6)], label=f"w.m2{a}")
        S.op("vector",
             lambda e: e.tensor_scalar(wst[:, 7:8], wst[:, 5:6], 1.5,
                                       None, ALU.is_ge),
             reads=[rg(wst, 5)], writes=[rg(wst, 7)], label=f"w.ge2{a}")
        S.op("vector",
             lambda e: e.tensor_tensor(wst[:, 8:9], wst[:, 4:5],
                                       wst[:, 6:7], ALU.subtract),
             reads=[rg(wst, 4), rg(wst, 6)], writes=[rg(wst, 8)],
             label=f"w.df{a}")
        S.op("vector",
             lambda e: e.tensor_tensor(wst[:, 9:10], wst[:, 7:8],
                                       wst[:, 8:9], ALU.mult),
             reads=[rg(wst, 7), rg(wst, 8)], writes=[rg(wst, 9)],
             label=f"w.cr{a}")
        S.op("vector",
             lambda e: e.tensor_tensor(wst[:, 10:11], wst[:, 6:7],
                                       wst[:, 9:10], ALU.add),
             reads=[rg(wst, 6), rg(wst, 9)], writes=[rg(wst, 10)],
             label=f"w.th{a}")
        S.op("vector",
             lambda e: e.tensor_scalar(wst[:, 11:12], wst[:, 10:11], 2.0,
                                       None, ALU.mult),
             reads=[rg(wst, 10)], writes=[rg(wst, 11)], label=f"w.t2{a}")
        S.op("scalar",
             lambda e: e.activation(K8[:, :], P8[:, :], AF.Exp,
                                    scale=-2.0, bias=wst[:, 11:12]),
             reads=[rg(P8), rg(wst, 11)], writes=[rg(K8)],
             label=f"w.sg1{a}")
        S.op("vector",
             lambda e: e.tensor_scalar(K8[:, :], K8[:, :], 1.0, None,
                                       ALU.add),
             reads=[rg(K8)], writes=[rg(K8)], label=f"w.sg2{a}")
        S.op("vector", lambda e: e.reciprocal(K8[:, :], K8[:, :]),
             reads=[rg(K8)], writes=[rg(K8)], label=f"w.sg3{a}")
        S.op("vector",
             lambda e: e.tensor_tensor(M8[:, :], P8[:, :], K8[:, :],
                                       ALU.mult),
             reads=[rg(P8), rg(K8)], writes=[rg(M8)], label=f"w.w0{a}")
        S.op("vector",
             lambda e: e.tensor_scalar(E8[:, :], M8[:, :], 1.0, None,
                                       ALU.mult, op1=ALU.add,
                                       accum_out=wst[:, 13:14]),
             reads=[rg(M8)], writes=[rg(E8), rg(wst, 13)],
             label=f"w.ws{a}")
        S.op("vector",
             lambda e: e.tensor_scalar(wst[:, 14:15], wst[:, 13:14],
                                       1e-9, None, ALU.add),
             reads=[rg(wst, 13)], writes=[rg(wst, 14)],
             label=f"w.wse{a}")
        S.op("vector",
             lambda e: e.reciprocal(wst[:, 15:16], wst[:, 14:15]),
             reads=[rg(wst, 14)], writes=[rg(wst, 15)],
             label=f"w.wr{a}")
        S.op("vector",
             lambda e, a=a: e.tensor_scalar(
                 WGT[:, a * 8:(a + 1) * 8], M8[:, :], wst[:, 15:16],
                 None, ALU.mult),
             reads=[rg(M8), rg(wst, 15)], writes=[rg(WGT, a)],
             label=f"w.wf{a}")

    for a in range(OT):
        fns.append(lambda a=a: seg_rout(a))

    wT = sb("wT", [8, OT * 128], BF16)
    wfT = sb("wfT", [8, T], BF16)
    psA = ptr[0]
    pmx_sb = sb("pmx_sb", [32, T])
    psP = psG

    def seg_pmix():
        for a in range(OT):
            S.op("tensor",
                 lambda e, a=a: e.transpose(
                     psA[0:8, (a % 8) * 128:(a % 8 + 1) * 128],
                     WGT[:, a * 8:(a + 1) * 8], identb[:, :]),
                 reads=[rg(WGT, a), rg(identb)], writes=[rg(psA, a % 8)],
                 label=f"p.tr{a}")
            if a % 8 == 7 or a == OT - 1:
                a0 = (a // 8) * 8
                nb = a - a0 + 1
                S.op("vector",
                     lambda e, a0=a0, nb=nb: e.tensor_copy(
                         wT[:, a0 * 128:(a0 + nb) * 128],
                         psA[0:8, 0:nb * 128]),
                     reads=[rg(psA, kk % 8) for kk in range(a0, a + 1)],
                     writes=[rg(wT, ("c", a0))], label=f"p.ev{a0}")
        wf3 = wfT[:, :].rearrange("p (t two) -> p two t", two=2)
        for two in range(2):
            S.op("vector",
                 lambda e, two=two: e.tensor_copy(wf3[:, two, :], wT[:, :]),
                 reads=[rg(wT, ("c", kk)) for kk in range(0, OT, 8)],
                 writes=[rg(wfT, two)], label=f"p.rep{two}")
        for (c0, cn) in _chunks(T, 512):
            S.mm(lambda e, c0=c0, cn=cn: e.matmul(
                    psP[0:32, 0:cn], Li[:, :], wfT[:, c0:c0 + cn],
                    start=True, stop=True),
                 reads=[rg(Li), rg(wfT, 0), rg(wfT, 1)], psum_rgn=rg(psP),
                 start=True, stop=True, label=f"p.mm{c0}")
            S.op("vector",
                 lambda e, c0=c0, cn=cn: e.tensor_copy(
                     pmx_sb[:, c0:c0 + cn], psP[0:32, 0:cn]),
                 reads=[rg(psP)], writes=[rg(pmx_sb)], label=f"p.pev{c0}")
        S.dma("sync", pmixT_out, pmx_sb[:, :], reads=[rg(pmx_sb)],
              writes=[rg(pmixT_out)], label="p.st")

    fns.append(seg_pmix)
    return fns



def build_pre0_launch(T, D, NBAT):
    nc = bass.Bass("TRN2", target_bir_lowering=False)
    aps = {}
    BD = D // 128
    TA = T // 2
    OT = TA // 128
    NAT = NBAT * OT
    NE, RK = 8, 32
    aps["x"] = nc.dram_tensor("x", [T, D], F32, kind="ExternalInput").ap()
    aps["x_anch"] = nc.dram_tensor("x_anch", [NBAT * TA, D], F32,
                          kind="ExternalInput").ap()
    aps["G0"] = nc.dram_tensor("G0", [D, 64], F32, kind="ExternalInput").ap()
    aps["b8_col"] = nc.dram_tensor("b8_col", [8, 1], F32, kind="ExternalInput").ap()
    aps["LiT"] = nc.dram_tensor("LiT", [32, 8], F32, kind="ExternalInput").ap()
    aps["Li"] = nc.dram_tensor("Li", [8, 32], F32, kind="ExternalInput").ap()
    aps["c_identbf"] = nc.dram_tensor("c_identbf", [128, 128], BF16,
                              kind="ExternalInput").ap()
    aps["c_identf32"] = nc.dram_tensor("c_identf32", [128, 128], F32,
                              kind="ExternalInput").ap()
    aps["xnT"] = nc.dram_tensor("xnT", [128, BD * T], BF16,
                             kind="ExternalOutput").ap()
    aps["pmixT"] = nc.dram_tensor("pmixT", [32, T], F32,
                               kind="ExternalOutput").ap()
    S = Sched(nc)
    with ExitStack() as stack:
        ph_pre0_ln(S, NCView(nc, ""), stack, aps, T, D)
        for fn in ph_anchors(S, NCView(nc, "anc_"), stack, aps, T, D, NBAT):
            fn()
        S.finalize()
    return nc


def ph_combine0(S, nc, stack, aps, T, D):
    BD = D // 128
    ACC = aps["ACC"]
    attnT_h = aps["attnT"]
    wc = aps["wc"]
    Bmat = aps["Bmat"]
    zT_h = aps["zT"]
    pmixT_h = aps["pmixT"]
    out1 = aps["out1"]
    Bm = stack.enter_context(nc.sbuf_tensor("Bm", [32, D], BF16))
    zp = stack.enter_context(nc.sbuf_tensor("zp", [32, T], BF16))
    with ExitStack() as zstack:
        zT = zstack.enter_context(nc.sbuf_tensor("zTs", [32, T], F32))
        pmx = zstack.enter_context(nc.sbuf_tensor("pmx", [32, T], F32))
        S.dma("sync", zT[:, :], zT_h, label="ldz")
        S.dma("sync", pmx[:, :], pmixT_h, label="ldpm")
        S.op("vector",
             lambda e: e.scalar_tensor_tensor(zp[:, :], zT[:, :], 0.5,
                                              pmx[:, :], ALU.mult, ALU.mult),
             reads=[rg(zT), rg(pmx)], writes=[rg(zp)], label="zp")
    aT = stack.enter_context(nc.sbuf_tensor("aT", [128, BD * T], BF16))
    ac = [stack.enter_context(nc.sbuf_tensor(f"ac{j}", [128, 256], F32))
          for j in range(3)]
    og = [stack.enter_context(nc.sbuf_tensor(f"og{j}", [128, 256], F32))
          for j in range(3)]
    S.dma("sync", aT[:, :], attnT_h, reads=[rg(attnT_h), rg(zp)],
          writes=[rg(aT)], label="ldaT")
    S.dma("gpsimd", Bm[:, :], Bmat, label="ldB")
    bi = [0]

    def pre(i, n0, nn):
        at = ac[bi[0] % 3]
        S.dma("sync", at[:, 0:nn], ACC[i * 128:(i + 1) * 128, n0:n0 + nn],
              reads=[rg(ACC)], writes=[rg(at)], label=f"c0.lda{i}.{n0}")

    def extra(i, n0, nn):
        return [(lambda e, p, i=i, n0=n0, nn=nn: e.matmul(
                    p[:, 0:nn], zp[:, i * 128:(i + 1) * 128],
                    Bm[:, n0:n0 + nn], start=False, stop=True),
                 [rg(zp), rg(Bm)])]

    def evac(i, n0, nn, p):
        at = ac[bi[0] % 3]
        o = og[bi[0] % 3]
        bi[0] += 1
        S.op("vector",
             lambda e, p=p, at=at, o=o, nn=nn: e.tensor_tensor(
                 o[:, 0:nn], p[:, 0:nn], at[:, 0:nn], ALU.add),
             reads=[rg(p), rg(at)], writes=[rg(o)], label=f"c0.ev{i}.{n0}")
        S.dma("sync", out1[i * 128:(i + 1) * 128, n0:n0 + nn], o[:, 0:nn],
              reads=[rg(o)], writes=[rg(out1, (i, n0))],
              label=f"c0.st{i}.{n0}")

    ph_mm_T(S, nc, stack, aT, BD, T, wc, D, "c0", evac,
            nchunk=256, extra_mm=extra, pre_cb=pre)


def build_combine0_launch(T, D):
    nc = bass.Bass("TRN2", target_bir_lowering=False)
    aps = {}
    BD = D // 128
    aps["ACC"] = nc.dram_tensor("ACC", [T, D], F32, kind="ExternalInput").ap()
    aps["attnT"] = nc.dram_tensor("attnT", [128, BD * T], BF16,
                             kind="ExternalInput").ap()
    aps["wc"] = nc.dram_tensor("wc", [D, D], F32, kind="ExternalInput").ap()
    aps["Bmat"] = nc.dram_tensor("Bmat", [32, D], F32, kind="ExternalInput").ap()
    aps["zT"] = nc.dram_tensor("zT", [32, T], F32, kind="ExternalInput").ap()
    aps["pmixT"] = nc.dram_tensor("pmixT", [32, T], F32, kind="ExternalInput").ap()
    aps["out1"] = nc.dram_tensor("out1", [T, D], F32, kind="ExternalOutput").ap()
    S = Sched(nc)
    with ExitStack() as stack:
        ph_combine0(S, NCView(nc, ""), stack, aps, T, D)
        S.finalize()
    return nc

def ph_attn(S, nc, stack, aps, T, NH, slopes):
    TT = T // 128
    qkT = aps["qkT"]
    kramp_h = aps["c_kramp"]
    maskd_h = aps["c_maskdiag"]
    ident_h = aps["c_identbf"]
    attnT_h = aps["attnT"]
    sbt = lambda n, shp, dt: stack.enter_context(nc.sbuf_tensor(n, shp, dt))
    pst = lambda n, shp, dt: stack.enter_context(nc.psum_tensor(n, shp, dt))
    kT = sbt("kT", [128, T], BF16)
    vF = sbt("vF", [128, T], BF16)
    vT = sbt("vT", [128, T], BF16)
    kramp = sbt("kramp", [128, T], F32)
    maskd = sbt("maskd", [128, 128], F32)
    identb = sbt("identb", [128, 128], BF16)
    ones_cb = sbt("ones_cb", [128, 1], BF16)
    ones_rf = sbt("ones_rf", [1, 128], F32)
    qb = [sbt(f"qb{j}", [128, T], BF16) for j in range(2)]
    q2 = [sbt(f"q2{j}", [128, T], BF16) for j in range(2)]
    Ssb = [sbt(f"Ssb{j}", [128, T], F32) for j in range(2)]
    Psb = [sbt(f"Psb{j}", [128, T], BF16) for j in range(4)]
    PT = [sbt(f"PT{j}", [128, T], BF16) for j in range(2)]
    stq = [sbt(f"ast{j}", [128, 8], F32) for j in range(4)]
    nb = [sbt(f"nb{j}", [1, 8], F32) for j in range(2)]
    negB = [sbt(f"negB{j}", [128, 1], F32) for j in range(2)]
    ah = [sbt(f"ah{j}", [128, T], BF16) for j in range(2)]
    scw = min(T, 2048)
    scb = pst("scb", [128, scw], F32)
    sc = [scb[:, c0:c0 + cn] for (c0, cn) in _chunks(scw, 512)]
    tp = [pst("tp0", [128, 1024], BF16)] * 2
    av = [pst("av0", [128, 512], F32)] * 2
    S.dma("sync", kT[:, :], qkT[:, NH * T:(NH + 1) * T], label="ldk")
    S.dma("sync", vF[:, :], qkT[:, (NH + 1) * T:(NH + 2) * T], label="ldv")
    S.dma("sync", kramp[:, :], kramp_h, label="ldkr")
    S.dma("sync", maskd[:, :], maskd_h, label="ldmd")
    S.dma("sync", identb[:, :], ident_h, label="ldid")
    S.op("gpsimd", lambda e: e.memset(ones_cb[:, :], 1.0),
         reads=[], writes=[rg(ones_cb)], label="onescb")
    S.op("gpsimd", lambda e: e.memset(ones_rf[:, :], 1.0),
         reads=[], writes=[rg(ones_rf)], label="onesrf")
    qiota = sbt("qiota", [128, 1], F32)
    S.op("gpsimd", lambda e: e.iota(qiota[:, :], [[0, 1]], base=0,
                                    channel_multiplier=1,
                                    allow_small_or_imprecise_dtypes=True),
         reads=[], writes=[rg(qiota)], label="qiota")

    def sq_norms_max(src, dst_sq, out_st, out_col, tag):
        S.op("vector",
             lambda e: e.tensor_tensor(dst_sq[:, :], src[:, :], src[:, :],
                                       ALU.mult),
             reads=[rg(src)], writes=[rg(dst_sq)], label=f"{tag}.sq")
        for ci, (c0, cn) in enumerate(_chunks(T, 512)):
            S.mm(lambda e, ci=ci, c0=c0, cn=cn: e.matmul(
                    sc[ci][0:1, 0:cn], ones_cb[:, :], dst_sq[:, c0:c0 + cn],
                    start=True, stop=True),
                 reads=[rg(ones_cb), rg(dst_sq)], psum_rgn=rg(scb, ci),
                 start=True, stop=True, label=f"{tag}.nm{ci}")
            S.op("vector",
                 lambda e, ci=ci, cn=cn: e.reduce_max(
                     out_st[0:1, out_col + ci:out_col + ci + 1],
                     sc[ci][0:1, 0:cn], axis=AX.X),
                 reads=[rg(scb, ci)], writes=[rg(out_st, out_col + ci)],
                 label=f"{tag}.mx{ci}")
        nchk = len(_chunks(T, 512))
        S.op("vector",
             lambda e, nchk=nchk: e.tensor_reduce(
                 out_st[0:1, out_col:out_col + 1],
                 out_st[0:1, out_col:out_col + nchk], AX.X, ALU.max),
             reads=[rg(out_st, out_col + ci) for ci in range(nchk)],
             writes=[rg(out_st, out_col)], label=f"{tag}.mxf")

    def transpose_blocks(src_sb, dst_sb, nblk, tpp, tag, evac_eng):
        for cb in range(nblk):
            S.op("tensor",
                 lambda e, cb=cb: e.transpose(
                     tpp[:, (cb % 8) * 128:(cb % 8 + 1) * 128],
                     src_sb[:, cb * 128:(cb + 1) * 128], identb[:, :]),
                 reads=[rg(src_sb), rg(identb)], writes=[rg(tpp, cb % 8)],
                 label=f"{tag}.tr{cb}")
            if cb % 8 == 7 or cb == nblk - 1:
                k0 = (cb // 8) * 8
                nbk = cb - k0 + 1
                if evac_eng == "scalar":
                    fn = lambda e, k0=k0, nbk=nbk: e.copy(
                        dst_sb[:, k0 * 128:(k0 + nbk) * 128],
                        tpp[:, 0:nbk * 128])
                else:
                    fn = lambda e, k0=k0, nbk=nbk: e.tensor_copy(
                        dst_sb[:, k0 * 128:(k0 + nbk) * 128],
                        tpp[:, 0:nbk * 128])
                S.op(evac_eng, fn,
                     reads=[rg(tpp, kk % 8) for kk in range(k0, cb + 1)],
                     writes=[rg(dst_sb, ("c", k0))],
                     label=f"{tag}.ev{k0}")

    transpose_blocks(vF, vT, TT, tp[0], "vtr", "vector")
    sq_norms_max(kT, q2[0], nb[0], 0, "knrm")

    def emit_A(h, qt, slope, q, hp):
        it = h * TT + qt
        Sb = Ssb[it % 2]
        Pb, st = Psb[it % 4], stq[it % 4]
        W = (qt + 1) * 128
        chs = _chunks(W, 512)
        nchs = len(chs)
        for ci, (c0, cn) in enumerate(chs):
            p = sc[ci]
            S.mm(lambda e, p=p, q=q, qt=qt, c0=c0, cn=cn: e.matmul(
                    p[:, 0:cn], q[:, qt * 128:(qt + 1) * 128],
                    kT[:, c0:c0 + cn], start=True, stop=True),
                 reads=[rg(q), rg(kT)], psum_rgn=rg(scb, ci),
                 start=True, stop=True, label=f"sc{h}.{qt}.{ci}")
        if slope != 0.0:
            S.op("vector",
                 lambda e, W=W, Sb=Sb: e.scalar_tensor_tensor(
                     Sb[:, 0:W], kramp[:, 0:W], slope,
                     scb[:, 0:W], ALU.mult, ALU.add),
                 reads=[rg(kramp)] + [rg(scb, ci) for ci in range(nchs)],
                 writes=[rg(Sb)] + [rg(Sb, ci) for ci in range(nchs)],
                 label=f"sev{h}.{qt}")
            S.op("vector",
                 lambda e, W=W, Sb=Sb: e.tensor_tensor(
                     Sb[:, W - 128:W], Sb[:, W - 128:W], maskd[:, :],
                     ALU.add),
                 reads=[rg(Sb, nchs - 1), rg(maskd)],
                 writes=[rg(Sb, ("d", nchs - 1))],
                 label=f"msk{h}.{qt}")
            S.op("vector",
                 lambda e, st=st, hp=hp, slope=slope:
                 e.scalar_tensor_tensor(st[:, 6:7], qiota[:, :], -slope,
                                        negB[hp][:, :], ALU.mult,
                                        ALU.add),
                 reads=[rg(qiota), rg(negB[hp])], writes=[rg(st, 6)],
                 label=f"bi1{h}.{qt}")
            S.op("vector",
                 lambda e, st=st, slope=slope, qt=qt: e.tensor_scalar(
                     st[:, 6:7], st[:, 6:7], -slope * qt * 128.0, None,
                     ALU.add),
                 reads=[rg(st, 6)], writes=[rg(st, 6)],
                 label=f"bi2{h}.{qt}")
        else:
            S.op("vector",
                 lambda e, W=W: e.tensor_tensor(
                     scb[:, W - 128:W], scb[:, W - 128:W],
                     maskd[:, :], ALU.add),
                 reads=[rg(scb, nchs - 1), rg(maskd)],
                 writes=[rg(scb, ("d", nchs - 1))],
                 label=f"msk{h}.{qt}")
            S.op("scalar",
                 lambda e, W=W, Pb=Pb, st=st, hp=hp: e.activation(
                     Pb[:, 0:W], scb[:, 0:W], AF.Exp,
                     bias=negB[hp][:, :], accum_out=st[:, 1:2]),
                 reads=[rg(negB[hp]), rg(scb, ("d", nchs - 1))]
                 + [rg(scb, ci) for ci in range(nchs)],
                 writes=[rg(Pb), rg(st, 1)], label=f"exp{h}.{qt}")

    def emit_Bx(h, qt, slope):
        if slope == 0.0:
            return
        it = h * TT + qt
        Sb = Ssb[it % 2]
        Pb, st = Psb[it % 4], stq[it % 4]
        W = (qt + 1) * 128
        nchs = len(_chunks(W, 512))
        S.op("scalar",
             lambda e, W=W, Sb=Sb, Pb=Pb, st=st: e.activation(
                 Pb[:, 0:W], Sb[:, 0:W], AF.Exp,
                 bias=st[:, 6:7], accum_out=st[:, 1:2]),
             reads=[rg(Sb), rg(Sb, ("d", nchs - 1)), rg(st, 6)]
             + [rg(Sb, ci) for ci in range(nchs)],
             writes=[rg(Pb), rg(st, 1)], label=f"exp{h}.{qt}")

    def emit_C(h, qt, slope):
        it = h * TT + qt
        Pb, st = Psb[it % 4], stq[it % 4]
        W = (qt + 1) * 128
        nchs = len(_chunks(W, 512))
        S.op("vector",
             lambda e, st=st: e.reciprocal(st[:, 0:1], st[:, 1:2]),
             reads=[rg(st, 1)], writes=[rg(st, 0)],
             label=f"rs{h}.{qt}")
        S.op("vector",
             lambda e, W=W, Pb=Pb, st=st: e.tensor_scalar(
                 Pb[:, 0:W], Pb[:, 0:W], st[:, 0:1], None, ALU.mult),
             reads=[rg(Pb), rg(st, 0)]
             + [rg(Pb, ci) for ci in range(nchs)],
             writes=[rg(Pb)] + [rg(Pb, ci) for ci in range(nchs)],
             label=f"nrm{h}.{qt}")

    def emit_D(h, qt, slope):
        it = h * TT + qt
        par = it % 2
        Pb, PTb = Psb[it % 4], PT[par]
        evac_eng = "scalar" if par == 0 else "vector"
        transpose_blocks(Pb, PTb, qt + 1, tp[par], f"ptr{h}.{qt}", evac_eng)
        pav = av[par]
        for cb in range(qt + 1):
            S.mm(lambda e, pav=pav, cb=cb, qt=qt, PTb=PTb: e.matmul(
                    pav[:, 0:128], vT[:, cb * 128:(cb + 1) * 128],
                    PTb[:, cb * 128:(cb + 1) * 128],
                    start=(cb == 0), stop=(cb == qt)),
                 reads=[rg(vT)] + [rg(PTb, ("c", kk))
                                   for kk in range(0, qt + 1, 8)],
                 psum_rgn=rg(pav), start=(cb == 0), stop=(cb == qt),
                 label=f"av{h}.{qt}.{cb}")

    def emit_E(h, qt, slope):
        it = h * TT + qt
        par = it % 2
        a = ah[h % 2]
        pav = av[par]
        eng = "scalar" if par == 0 else "vector"
        if eng == "scalar":
            fn = lambda e, a=a, pav=pav, qt=qt: e.copy(
                a[:, qt * 128:(qt + 1) * 128], pav[:, 0:128])
        else:
            fn = lambda e, a=a, pav=pav, qt=qt: e.tensor_copy(
                a[:, qt * 128:(qt + 1) * 128], pav[:, 0:128])
        S.op(eng, fn, reads=[rg(pav)], writes=[rg(a, qt)],
             label=f"aev{h}.{qt}")
        if qt == TT - 1:
            S.dma("sync", attnT_h[:, h * T:(h + 1) * T], a[:, :],
                  reads=[rg(a, qq) for qq in range(TT)],
                  writes=[rg(attnT_h, h)], label=f"sta{h}")

    def emit_head_prologue(h):
        hp = h % 2
        q = qb[hp]
        S.dma("sync", q[:, :], qkT[:, h * T:(h + 1) * T], label=f"ldq{h}")
        sq_norms_max(q, q2[hp], nb[hp], 4, f"qn{h}")
        S.op("vector",
             lambda e, hp=hp: e.tensor_tensor(nb[hp][0:1, 1:2],
                                              nb[hp][0:1, 4:5],
                                              nb[0][0:1, 0:1], ALU.mult),
             reads=[rg(nb[hp], 4), rg(nb[0], 0)], writes=[rg(nb[hp], 1)],
             label=f"bm{h}")
        S.op("vector",
             lambda e, hp=hp: e.tensor_scalar(nb[hp][0:1, 1:2],
                                              nb[hp][0:1, 1:2], 1e-30, None,
                                              ALU.add),
             reads=[rg(nb[hp], 1)], writes=[rg(nb[hp], 1)], label=f"bz{h}")
        S.op("scalar",
             lambda e, hp=hp: e.activation(nb[hp][0:1, 2:3],
                                           nb[hp][0:1, 1:2], AF.Ln),
             reads=[rg(nb[hp], 1)], writes=[rg(nb[hp], 2)], label=f"bl{h}")
        S.op("scalar",
             lambda e, hp=hp: e.activation(nb[hp][0:1, 3:4],
                                           nb[hp][0:1, 2:3], AF.Exp,
                                           scale=0.5),
             reads=[rg(nb[hp], 2)], writes=[rg(nb[hp], 3)], label=f"be{h}")
        S.op("vector",
             lambda e, hp=hp: e.tensor_scalar(nb[hp][0:1, 3:4],
                                              nb[hp][0:1, 3:4], -1.0, -2.0,
                                              ALU.mult, op1=ALU.add),
             reads=[rg(nb[hp], 3)], writes=[rg(nb[hp], 3)], label=f"bn{h}")
        S.mm(lambda e, hp=hp: e.matmul(scb[0:128, 0:1], ones_rf[:, :],
                                       nb[hp][0:1, 3:4],
                                       start=True, stop=True),
             reads=[rg(ones_rf), rg(nb[hp], 3)], psum_rgn=rg(scb, 0),
             start=True, stop=True, label=f"bb{h}")
        S.op("vector",
             lambda e, hp=hp: e.tensor_copy(negB[hp][:, :],
                                            scb[0:128, 0:1]),
             reads=[rg(scb, 0)], writes=[rg(negB[hp])], label=f"bv{h}")
        return q, hp

    stages = [(emit_Bx, 1), (emit_C, 2), (emit_E, 4), (emit_D, 3)]
    its = []
    for h in range(NH):
        for qt in range(TT):
            its.append((h, qt, float(slopes[h])))
    total = len(its)
    for g in range(total + 5):
        if g < total:
            h, qt, slope = its[g]
            if qt == 0:
                q, hp = emit_head_prologue(h)
            emit_A(h, qt, slope, qb[h % 2], h % 2)
        for fn, lag in stages:
            jj = g - lag
            if 0 <= jj < total:
                fn(*its[jj])


def ph_attn2(S, nc, stack, aps, T, NH, slopes):
    TT = T // 128
    qkT = aps["qkT"]
    maskd_h = aps["c_maskdiag"]
    ident_h = aps["c_identbf"]
    attnT_h = aps["attnT"]
    has_alibi = any(float(s) != 0.0 for s in slopes)
    sbt = lambda n, shp, dt: stack.enter_context(nc.sbuf_tensor(n, shp, dt))
    pst = lambda n, shp, dt: stack.enter_context(nc.psum_tensor(n, shp, dt))
    kT = sbt("kT", [128, T], BF16)
    vF = sbt("vF", [128, T], BF16)
    vT = sbt("vT", [128, T], BF16)
    maskd = sbt("maskd", [128, 128], F32)
    identb = sbt("identb", [128, 128], BF16)
    ones_cb = sbt("ones_cb", [128, 1], BF16)
    ones_rf = sbt("ones_rf", [1, 128], F32)
    qb = [sbt(f"qb{j}", [128, T], BF16) for j in range(2)]
    q2 = [sbt(f"q2{j}", [128, T], BF16) for j in range(2)]
    Psb = [sbt(f"Psb{j}", [128, T], BF16) for j in range(4)]
    PT = [sbt(f"PT{j}", [128, T], BF16) for j in range(2)]
    stq = [sbt(f"ast{j}", [128, 8], F32) for j in range(4)]
    nb = [sbt(f"nb{j}", [1, 8], F32) for j in range(2)]
    negB = [sbt(f"negB{j}", [128, 1], F32) for j in range(2)]
    ah = [sbt(f"ah{j}", [128, T], BF16) for j in range(2)]
    scb = pst("scb", [128, 1024], F32)
    sc = [scb[:, 0:512], scb[:, 512:1024]]
    tp = [pst(f"tp{j}", [128, 1024], BF16) for j in range(2)]
    av = [pst(f"av{j}", [128, 512], F32) for j in range(2)]
    if has_alibi:
        alib3_h = aps["alib3"]
        alb = [sbt(f"alb{j}", [3, T], BF16) for j in range(2)]
        ones3 = sbt("ones3", [3, 128], BF16)
        S.op("gpsimd", lambda e: e.memset(ones3[:, :], 1.0),
             reads=[], writes=[rg(ones3)], label="ones3")
    S.dma("sync", kT[:, :], qkT[:, NH * T:(NH + 1) * T], label="ldk")
    S.dma("sync", vF[:, :], qkT[:, (NH + 1) * T:(NH + 2) * T], label="ldv")
    S.dma("sync", maskd[:, :], maskd_h, label="ldmd")
    S.dma("sync", identb[:, :], ident_h, label="ldid")
    S.op("gpsimd", lambda e: e.memset(ones_cb[:, :], 1.0),
         reads=[], writes=[rg(ones_cb)], label="onescb")
    S.op("gpsimd", lambda e: e.memset(ones_rf[:, :], 1.0),
         reads=[], writes=[rg(ones_rf)], label="onesrf")
    qiota = sbt("qiota", [128, 1], F32)
    S.op("gpsimd", lambda e: e.iota(qiota[:, :], [[0, 1]], base=0,
                                    channel_multiplier=1,
                                    allow_small_or_imprecise_dtypes=True),
         reads=[], writes=[rg(qiota)], label="qiota")

    def sq_norms_max(src, dst_sq, out_st, out_col, tag):
        S.op("vector",
             lambda e: e.tensor_tensor(dst_sq[:, :], src[:, :], src[:, :],
                                       ALU.mult),
             reads=[rg(src)], writes=[rg(dst_sq)], label=f"{tag}.sq")
        for ci, (c0, cn) in enumerate(_chunks(T, 512)):
            S.mm(lambda e, ci=ci, c0=c0, cn=cn: e.matmul(
                    sc[ci % 2][0:1, 0:cn], ones_cb[:, :],
                    dst_sq[:, c0:c0 + cn], start=True, stop=True),
                 reads=[rg(ones_cb), rg(dst_sq)], psum_rgn=rg(scb, ci % 2),
                 start=True, stop=True, label=f"{tag}.nm{ci}")
            S.op("vector",
                 lambda e, ci=ci, cn=cn: e.reduce_max(
                     out_st[0:1, out_col + ci:out_col + ci + 1],
                     sc[ci % 2][0:1, 0:cn], axis=AX.X),
                 reads=[rg(scb, ci % 2)], writes=[rg(out_st, out_col + ci)],
                 label=f"{tag}.mx{ci}")
        nchk = len(_chunks(T, 512))
        S.op("vector",
             lambda e, nchk=nchk: e.tensor_reduce(
                 out_st[0:1, out_col:out_col + 1],
                 out_st[0:1, out_col:out_col + nchk], AX.X, ALU.max),
             reads=[rg(out_st, out_col + ci) for ci in range(nchk)],
             writes=[rg(out_st, out_col)], label=f"{tag}.mxf")

    def transpose_blocks(src_sb, dst_sb, nblk, base_par, tag):
        for cb in range(nblk):
            tpp = tp[(base_par + (cb // 8)) % 2]
            S.op("tensor",
                 lambda e, cb=cb, tpp=tpp: e.transpose(
                     tpp[:, (cb % 8) * 128:(cb % 8 + 1) * 128],
                     src_sb[:, cb * 128:(cb + 1) * 128], identb[:, :]),
                 reads=[rg(src_sb), rg(identb)], writes=[rg(tpp, cb % 8)],
                 label=f"{tag}.tr{cb}")
            if cb % 8 == 7 or cb == nblk - 1:
                k0 = (cb // 8) * 8
                nbk = cb - k0 + 1
                evac_eng = "scalar" if (base_par + (cb // 8)) % 2 == 0 \
                    else "vector"
                if evac_eng == "scalar":
                    fn = lambda e, k0=k0, nbk=nbk, tpp=tpp: e.copy(
                        dst_sb[:, k0 * 128:(k0 + nbk) * 128],
                        tpp[:, 0:nbk * 128])
                else:
                    fn = lambda e, k0=k0, nbk=nbk, tpp=tpp: e.tensor_copy(
                        dst_sb[:, k0 * 128:(k0 + nbk) * 128],
                        tpp[:, 0:nbk * 128])
                S.op(evac_eng, fn,
                     reads=[rg(tpp, kk % 8) for kk in range(k0, cb + 1)],
                     writes=[rg(dst_sb, ("c", k0))],
                     label=f"{tag}.ev{k0}")

    transpose_blocks(vF, vT, TT, 0, "vtr")
    sq_norms_max(kT, q2[0], nb[0], 0, "knrm")

    def emit_A(h, qt, slope, q, hp):
        it = h * TT + qt
        st = stq[it % 4]
        Pb = Psb[it % 4]
        W = (qt + 1) * 128
        chs = _chunks(W, 512)
        nchs = len(chs)
        if slope != 0.0:
            S.op("vector",
                 lambda e, st=st, hp=hp, slope=slope:
                 e.scalar_tensor_tensor(st[:, 6:7], qiota[:, :], -slope,
                                        negB[hp][:, :], ALU.mult,
                                        ALU.add),
                 reads=[rg(qiota), rg(negB[hp])], writes=[rg(st, 6)],
                 label=f"bi1{h}.{qt}")
            S.op("vector",
                 lambda e, st=st, slope=slope, qt=qt: e.tensor_scalar(
                     st[:, 6:7], st[:, 6:7], -slope * qt * 128.0, None,
                     ALU.add),
                 reads=[rg(st, 6)], writes=[rg(st, 6)],
                 label=f"bi2{h}.{qt}")
            bias = st[:, 6:7]
            bias_r = rg(st, 6)
        else:
            bias = negB[hp][:, :]
            bias_r = rg(negB[hp])
        for ci, (c0, cn) in enumerate(chs):
            b = (it + ci) % 2
            p = sc[b]
            S.mm(lambda e, p=p, q=q, qt=qt, c0=c0, cn=cn: e.matmul(
                    p[:, 0:cn], q[:, qt * 128:(qt + 1) * 128],
                    kT[:, c0:c0 + cn], start=True, stop=(slope == 0.0)),
                 reads=[rg(q), rg(kT)], psum_rgn=rg(scb, b),
                 start=True, stop=(slope == 0.0), label=f"sc{h}.{qt}.{ci}")
            if slope != 0.0:
                S.mm(lambda e, p=p, hp=hp, c0=c0, cn=cn: e.matmul(
                        p[:, 0:cn], ones3[:, :],
                        alb[hp][:, c0:c0 + cn],
                        start=False, stop=True),
                     reads=[rg(ones3), rg(alb[hp])], psum_rgn=rg(scb, b),
                     start=False, stop=True, label=f"al{h}.{qt}.{ci}")
            last = ci == nchs - 1
            if last:
                S.op("vector",
                     lambda e, p=p, cn=cn: e.tensor_tensor(
                         p[:, cn - 128:cn], p[:, cn - 128:cn],
                         maskd[:, :], ALU.add),
                     reads=[rg(scb, b), rg(maskd)],
                     writes=[rg(scb, ("d", b))],
                     label=f"msk{h}.{qt}")
            S.op("scalar",
                 lambda e, p=p, Pb=Pb, st=st, ci=ci, c0=c0, cn=cn,
                 bias=bias: e.activation(
                     Pb[:, c0:c0 + cn], p[:, 0:cn], AF.Exp,
                     bias=bias, accum_out=st[:, 1 + ci:2 + ci]),
                 reads=[bias_r, rg(scb, b)]
                 + ([rg(scb, ("d", b))] if last else []),
                 writes=[rg(Pb, ("e", ci)), rg(st, 1 + ci)],
                 label=f"exp{h}.{qt}.{ci}")

    def emit_C(h, qt, slope):
        it = h * TT + qt
        Pb, st = Psb[it % 4], stq[it % 4]
        W = (qt + 1) * 128
        nchs = len(_chunks(W, 512))
        if nchs > 1:
            S.op("vector",
                 lambda e, st=st, nchs=nchs: e.tensor_reduce(
                     st[:, 5:6], st[:, 1:1 + nchs], AX.X, ALU.add),
                 reads=[rg(st, 1 + ci) for ci in range(nchs)],
                 writes=[rg(st, 5)], label=f"sms{h}.{qt}")
            sum_col, sum_r = 5, rg(st, 5)
        else:
            sum_col, sum_r = 1, rg(st, 1)
        S.op("vector",
             lambda e, st=st, sum_col=sum_col: e.reciprocal(
                 st[:, 0:1], st[:, sum_col:sum_col + 1]),
             reads=[sum_r], writes=[rg(st, 0)],
             label=f"rs{h}.{qt}")
        S.op("vector",
             lambda e, W=W, Pb=Pb, st=st: e.tensor_scalar(
                 Pb[:, 0:W], Pb[:, 0:W], st[:, 0:1], None, ALU.mult),
             reads=[rg(Pb), rg(st, 0)],
             writes=[rg(Pb)],
             label=f"nrm{h}.{qt}")

    def emit_D(h, qt, slope):
        it = h * TT + qt
        par = it % 2
        Pb, PTb = Psb[it % 4], PT[par]
        transpose_blocks(Pb, PTb, qt + 1, par, f"ptr{h}.{qt}")
        pav = av[par]
        for cb in range(qt + 1):
            S.mm(lambda e, pav=pav, cb=cb, qt=qt, PTb=PTb: e.matmul(
                    pav[:, 0:128], vT[:, cb * 128:(cb + 1) * 128],
                    PTb[:, cb * 128:(cb + 1) * 128],
                    start=(cb == 0), stop=(cb == qt)),
                 reads=[rg(vT)] + [rg(PTb, ("c", kk))
                                   for kk in range(0, qt + 1, 8)],
                 psum_rgn=rg(pav), start=(cb == 0), stop=(cb == qt),
                 label=f"av{h}.{qt}.{cb}")

    def emit_E(h, qt, slope):
        it = h * TT + qt
        par = it % 2
        a = ah[h % 2]
        pav = av[par]
        eng = "scalar" if par == 0 else "vector"
        if eng == "scalar":
            fn = lambda e, a=a, pav=pav, qt=qt: e.copy(
                a[:, qt * 128:(qt + 1) * 128], pav[:, 0:128])
        else:
            fn = lambda e, a=a, pav=pav, qt=qt: e.tensor_copy(
                a[:, qt * 128:(qt + 1) * 128], pav[:, 0:128])
        S.op(eng, fn, reads=[rg(pav)], writes=[rg(a, qt)],
             label=f"aev{h}.{qt}")
        if qt == TT - 1:
            S.dma("sync", attnT_h[:, h * T:(h + 1) * T], a[:, :],
                  reads=[rg(a, qq) for qq in range(TT)],
                  writes=[rg(attnT_h, h)], label=f"sta{h}")

    def emit_head_prologue(h):
        hp = h % 2
        q = qb[hp]
        S.dma("sync", q[:, :], qkT[:, h * T:(h + 1) * T], label=f"ldq{h}")
        if has_alibi and float(slopes[h]) != 0.0:
            S.dma("sync", alb[hp][:, :], alib3_h[3 * h:3 * h + 3, :],
                  reads=[rg(alib3_h)], writes=[rg(alb[hp])],
                  label=f"ldal{h}")
        sq_norms_max(q, q2[hp], nb[hp], 4, f"qn{h}")
        S.op("vector",
             lambda e, hp=hp: e.tensor_tensor(nb[hp][0:1, 1:2],
                                              nb[hp][0:1, 4:5],
                                              nb[0][0:1, 0:1], ALU.mult),
             reads=[rg(nb[hp], 4), rg(nb[0], 0)], writes=[rg(nb[hp], 1)],
             label=f"bm{h}")
        S.op("vector",
             lambda e, hp=hp: e.tensor_scalar(nb[hp][0:1, 1:2],
                                              nb[hp][0:1, 1:2], 1e-30, None,
                                              ALU.add),
             reads=[rg(nb[hp], 1)], writes=[rg(nb[hp], 1)], label=f"bz{h}")
        S.op("scalar",
             lambda e, hp=hp: e.activation(nb[hp][0:1, 2:3],
                                           nb[hp][0:1, 1:2], AF.Ln),
             reads=[rg(nb[hp], 1)], writes=[rg(nb[hp], 2)], label=f"bl{h}")
        S.op("scalar",
             lambda e, hp=hp: e.activation(nb[hp][0:1, 3:4],
                                           nb[hp][0:1, 2:3], AF.Exp,
                                           scale=0.5),
             reads=[rg(nb[hp], 2)], writes=[rg(nb[hp], 3)], label=f"be{h}")
        S.op("vector",
             lambda e, hp=hp: e.tensor_scalar(nb[hp][0:1, 3:4],
                                              nb[hp][0:1, 3:4], -1.0, -2.0,
                                              ALU.mult, op1=ALU.add),
             reads=[rg(nb[hp], 3)], writes=[rg(nb[hp], 3)], label=f"bn{h}")
        S.mm(lambda e, hp=hp: e.matmul(scb[0:128, 0:1], ones_rf[:, :],
                                       nb[hp][0:1, 3:4],
                                       start=True, stop=True),
             reads=[rg(ones_rf), rg(nb[hp], 3)], psum_rgn=rg(scb, 0),
             start=True, stop=True, label=f"bb{h}")
        S.op("vector",
             lambda e, hp=hp: e.tensor_copy(negB[hp][:, :],
                                            scb[0:128, 0:1]),
             reads=[rg(scb, 0)], writes=[rg(negB[hp])], label=f"bv{h}")
        return q, hp

    stages = [(emit_C, 1), (emit_D, 2), (emit_E, 3)]
    its = []
    for h in range(NH):
        for qt in range(TT):
            its.append((h, qt, float(slopes[h])))
    total = len(its)
    qhp = {}
    for g in range(total + 5):
        if g < total:
            h, qt, slope = its[g]
            if g == 0:
                qhp[0] = emit_head_prologue(0)
            if qt == TT - 2 and h + 1 < NH:
                qhp[h + 1] = emit_head_prologue(h + 1)
            q, hp = qhp[h]
            emit_A(h, qt, slope, q, hp)
        for fn, lag in stages:
            jj = g - lag
            if 0 <= jj < total:
                fn(*its[jj])


def build_attn_launch(T, NH, slopes):
    nc = bass.Bass("TRN2", target_bir_lowering=False)
    aps = {}
    TT = T // 128
    aps["qkT"] = nc.dram_tensor("qkT", [128, (NH + 2) * T], BF16,
                         kind="ExternalInput").ap()
    aps["c_kramp"] = nc.dram_tensor("c_kramp", [128, T], F32,
                             kind="ExternalInput").ap()
    aps["c_maskdiag"] = nc.dram_tensor("c_maskdiag", [128, 128], F32,
                             kind="ExternalInput").ap()
    aps["c_identbf"] = nc.dram_tensor("c_identbf", [128, 128], BF16,
                             kind="ExternalInput").ap()
    aps["attnT"] = nc.dram_tensor("attnT", [128, NH * T], BF16,
                             kind="ExternalOutput").ap()
    S = Sched(nc)
    with ExitStack() as stack:
        ph_attn(S, NCView(nc, ""), stack, aps, T, NH, slopes)
        S.finalize()
    return nc


def ph_lnL(S, nc, stack, aps, T, D):
    BD = D // 128
    x = aps["x"]
    ident = aps["c_identbf"]
    xnT_h = aps["xnT"]
    ident_bf = stack.enter_context(nc.sbuf_tensor("idb", [128, 128], BF16))
    S.dma("sync", ident_bf[:, :], ident, label="ldid")
    C = ConstPool(S, nc, stack)
    ph_ln_to_xnT(S, nc, stack, x, None, ident_bf, T, D, "ln", C,
                 sink=mk_sink_hbm(S, nc, stack, xnT_h, T, "lnsk"))


def build_ln_launch(T, D):
    nc = bass.Bass("TRN2", target_bir_lowering=False)
    aps = {}
    BD = D // 128
    aps["x"] = nc.dram_tensor("x", [T, D], F32, kind="ExternalInput").ap()
    aps["c_identbf"] = nc.dram_tensor("c_identbf", [128, 128], BF16,
                           kind="ExternalInput").ap()
    aps["xnT"] = nc.dram_tensor("xnT", [128, BD * T], BF16,
                           kind="ExternalOutput").ap()
    S = Sched(nc)
    with ExitStack() as stack:
        ph_lnL(S, NCView(nc, ""), stack, aps, T, D)
        S.finalize()
    return nc


def ph_qkv(S, nc, stack, aps, T, D, NB):
    BD = D // 128
    c1 = float(128 ** -0.5)
    xnT_h = aps["xnT"]
    wqkv = aps["wqkv"]
    qkT_h = aps["qkT"]
    xnT_sb = stack.enter_context(
        nc.sbuf_tensor("xnT_sb", [128, BD * T], BF16))
    stg = [stack.enter_context(nc.sbuf_tensor(f"stg{j}", [128, 512], BF16))
           for j in range(2)]
    S.dma("sync", xnT_sb[:, :], xnT_h, label="ldxnT")
    si = [0]

    def evac(m, c0, cn, p):
        s = stg[si[0] % 2]
        si[0] += 1
        scale = c1 if m < NB - 2 else 1.0
        S.op("vector",
             lambda e, s=s, p=p, cn=cn, scale=scale: e.tensor_scalar(
                 s[:, 0:cn], p[:, 0:cn], scale, None, ALU.mult),
             reads=[rg(p)], writes=[rg(s)], label=f"qkv.ev{m}.{c0}")
        S.dma("sync", qkT_h[:, m * T + c0:m * T + c0 + cn], s[:, 0:cn],
              reads=[rg(s)], writes=[rg(qkT_h, m)], label=f"qkv.st{m}.{c0}")

    ph_mm_F(S, nc, stack, wqkv, D, NB * 128, xnT_sb, T, "qkv", evac)


def build_qkv_launch(T, D, NB):
    nc = bass.Bass("TRN2", target_bir_lowering=False)
    aps = {}
    BD = D // 128
    aps["xnT"] = nc.dram_tensor("xnT", [128, BD * T], BF16,
                           kind="ExternalInput").ap()
    aps["wqkv"] = nc.dram_tensor("wqkv", [D, NB * 128], F32,
                          kind="ExternalInput").ap()
    aps["qkT"] = nc.dram_tensor("qkT", [128, NB * T], BF16,
                           kind="ExternalOutput").ap()
    c1 = float(128 ** -0.5)
    S = Sched(nc)
    with ExitStack() as stack:
        ph_qkv(S, NCView(nc, ""), stack, aps, T, D, NB)
        S.finalize()
    return nc


def ph_h0(S, nc, stack, aps, T, D):
    BD = D // 128
    xnT_h = aps["xnT"]
    WT = aps["WT"]
    b_row = aps["b_row"]
    A0 = aps["A0"]
    x = aps["x"]
    ACC = aps["ACC"]
    zT_h = aps["zT"]
    xnT_sb = stack.enter_context(
        nc.sbuf_tensor("xnT_sb", [128, BD * T], BF16))
    ones_bf = stack.enter_context(nc.sbuf_tensor("onesb", [1, 128], BF16))
    b_bf = stack.enter_context(nc.sbuf_tensor("b_bf", [1, D], BF16))
    ab = stack.enter_context(nc.sbuf_tensor("ab", [128, BD * 32], BF16))
    zps = stack.enter_context(nc.psum_tensor("zps", [32, 512], F32))
    zT_sb = stack.enter_context(nc.sbuf_tensor("zT_sb", [32, T], F32))
    xc = [stack.enter_context(nc.sbuf_tensor(f"xc{j}", [128, 256], F32))
          for j in range(3)]
    og = [stack.enter_context(nc.sbuf_tensor(f"og{j}", [128, 256], F32))
          for j in range(3)]
    S.dma("sync", xnT_sb[:, :], xnT_h, label="ldxnT")
    S.dma("gpsimd", b_bf[:, :], b_row, label="ldb")
    S.dma("gpsimd",
          ab[:, :].rearrange("p (k j) -> p k j", j=32),
          A0.rearrange("(k p) j -> p k j", p=128),
          reads=[rg(A0)], writes=[rg(ab)], label="lda")
    S.op("gpsimd", lambda e: e.memset(ones_bf[:, :], 1.0),
         reads=[], writes=[rg(ones_bf)], label="ones")
    xnT3 = xnT_sb[:, :].rearrange("p (k t) -> p k t", t=T)
    for (c0, cn) in _chunks(T, 512):
        for k in range(BD):
            S.mm(lambda e, k=k, c0=c0, cn=cn: e.matmul(
                    zps[:, 0:cn], ab[:, k * 32:(k + 1) * 32],
                    xnT3[:, k, c0:c0 + cn],
                    start=(k == 0), stop=(k == BD - 1)),
                 reads=[rg(ab), rg(xnT_sb)], psum_rgn=rg(zps),
                 start=(k == 0), stop=(k == BD - 1), label=f"z.mm{c0}.{k}")
        S.op("vector",
             lambda e, c0=c0, cn=cn: e.tensor_copy(
                 zT_sb[:, c0:c0 + cn], zps[0:32, 0:cn]),
             reads=[rg(zps)], writes=[rg(zT_sb)], label=f"z.ev{c0}")
    S.dma("sync", zT_h, zT_sb[:, :], reads=[rg(zT_sb)], writes=[rg(zT_h)],
          label="z.st")
    bi = [0]

    def pre(i, n0, nn):
        xt = xc[bi[0] % 3]
        S.dma("sync", xt[:, 0:nn], x[i * 128:(i + 1) * 128, n0:n0 + nn],
              reads=[rg(x)], writes=[rg(xt)], label=f"h.ldx{i}.{n0}")

    def extra(i, n0, nn):
        return [(lambda e, p, n0=n0, nn=nn: e.matmul(
                    p[:, 0:nn], ones_bf[0:1, :], b_bf[0:1, n0:n0 + nn],
                    start=False, stop=True),
                 [rg(ones_bf), rg(b_bf)])]

    def evac(i, n0, nn, p):
        xt = xc[bi[0] % 3]
        o = og[bi[0] % 3]
        bi[0] += 1
        S.op("vector",
             lambda e, p=p, xt=xt, o=o, nn=nn: e.tensor_tensor(
                 o[:, 0:nn], p[:, 0:nn], xt[:, 0:nn], ALU.add),
             reads=[rg(p), rg(xt)], writes=[rg(o)],
             label=f"h.ev{i}.{n0}")
        S.dma("sync", ACC[i * 128:(i + 1) * 128, n0:n0 + nn], o[:, 0:nn],
              reads=[rg(o)], writes=[rg(ACC, (i, n0))],
              label=f"h.st{i}.{n0}")

    ph_mm_T(S, nc, stack, xnT_sb, BD, T, WT, D, "h", evac,
            nchunk=256, extra_mm=extra, pre_cb=pre)


def build_h0_launch(T, D):
    nc = bass.Bass("TRN2", target_bir_lowering=False)
    aps = {}
    BD = D // 128
    aps["xnT"] = nc.dram_tensor("xnT", [128, BD * T], BF16,
                           kind="ExternalInput").ap()
    aps["WT"] = nc.dram_tensor("WT", [D, D], F32, kind="ExternalInput").ap()
    aps["b_row"] = nc.dram_tensor("b_row", [1, D], F32, kind="ExternalInput").ap()
    aps["A0"] = nc.dram_tensor("A0", [D, 32], F32, kind="ExternalInput").ap()
    aps["x"] = nc.dram_tensor("x", [T, D], F32, kind="ExternalInput").ap()
    aps["ACC"] = nc.dram_tensor("ACC", [T, D], F32, kind="ExternalOutput").ap()
    aps["zT"] = nc.dram_tensor("zT", [32, T], F32, kind="ExternalOutput").ap()
    S = Sched(nc)
    with ExitStack() as stack:
        ph_h0(S, NCView(nc, ""), stack, aps, T, D)
        S.finalize()
    return nc



# version 20
# speedup vs baseline: 1.0291x; 1.0291x over previous
import numpy as np
import ml_dtypes
from contextlib import ExitStack

import concourse.bass as bass
import concourse.mybir as mybir
from concourse.bass_utils import run_bass_kernel_spmd

F32 = mybir.dt.float32
BF16 = mybir.dt.bfloat16
AF = mybir.ActivationFunctionType
ALU = mybir.AluOpType
AX = mybir.AxisListType

ENGINES = ("tensor", "vector", "scalar", "gpsimd", "sync")


def _rname(x):
    if isinstance(x, str):
        return x
    name = getattr(x, "name", None)
    if name is None:
        t = getattr(x, "tensor", None)
        name = t.name
    return name


def rg(t, key=None):
    return (_rname(t), key)


class _Rec:
    __slots__ = ("engine", "fn", "waits", "clock", "completion", "mark",
                 "dma_inc", "label", "seq")

    def __init__(self, engine, fn, label):
        self.engine = engine
        self.fn = fn
        self.waits = {}
        self.clock = {}
        self.completion = None
        self.mark = False
        self.dma_inc = None
        self.label = label
        self.seq = -1


class Sched:
    DMA_SLOTS = 8

    def __init__(self, nc):
        self.nc = nc
        self.recs = []
        self.wr = {}
        self.rd = {}
        self.eng_count = {e: 0 for e in ENGINES}
        self.last_clock = {e: {} for e in ENGINES}
        self.pending_barrier = {e: None for e in ENGINES}
        self.dma_slot_i = {}
        self.dma_slot_uses = {}
        self.open_group_reads = None
        self.group_reads = []
        self.group_psum = None

    def _conflict_writers(self, name, key):
        d = self.wr.get(name)
        if not d:
            return ()
        if key is None:
            return list(d.values())
        out = []
        if key in d:
            out.append(d[key])
        if None in d:
            out.append(d[None])
        return out

    def _conflict_readers(self, name, key):
        d = self.rd.get(name)
        if not d:
            return ()
        if key is None:
            out = []
            for lst in d.values():
                out.extend(lst)
            return out
        out = list(d.get(key, ()))
        out.extend(d.get(None, ()))
        return out

    def _edge(self, rec, prod):
        if prod is None or prod is rec:
            return
        comp = prod.completion
        assert comp is not None, (
            f"dependency on {prod.label} which has no completion event "
            f"(needed by {rec.label})")
        sem, val = comp
        c = rec.clock
        if c.get(sem, 0) < val:
            w = rec.waits
            if w.get(sem, 0) < val:
                w[sem] = val
        for k, v in prod.clock.items():
            if c.get(k, 0) < v:
                c[k] = v
        if c.get(sem, 0) < val:
            c[sem] = val

    def _reads(self, rec, reads, register=True):
        for name, key in reads:
            if self.group_psum is not None:
                gname, gkey = self.group_psum
                assert not (name == gname and (key == gkey or key is None or
                                               gkey is None)), \
                    f"{rec.label}: read of open matmul-group psum {name}/{key}"
            for w in self._conflict_writers(name, key):
                self._edge(rec, w)
            if register:
                self.rd.setdefault(name, {}).setdefault(key, []).append(rec)

    def _writes(self, rec, writes, register=True):
        for name, key in writes:
            if self.open_group_reads is not None:
                assert (name, key) not in self.open_group_reads and \
                       (name, None) not in self.open_group_reads, \
                    f"write to {name}/{key} while matmul group reading it is open"
            for w in self._conflict_writers(name, key):
                self._edge(rec, w)
            for r in self._conflict_readers(name, key):
                self._edge(rec, r)
            if register:
                wd = self.wr.setdefault(name, {})
                rd = self.rd.setdefault(name, {})
                if key is None:
                    wd.clear()
                    rd.clear()
                else:
                    rd.pop(key, None)
                wd[key] = rec

    def _new_rec(self, engine, fn, label):
        rec = _Rec(engine, fn, label)
        rec.clock = dict(self.last_clock[engine])
        pb = self.pending_barrier[engine]
        if pb is not None:
            for comp in pb:
                sem, val = comp
                if rec.clock.get(sem, 0) < val:
                    rec.waits[sem] = max(rec.waits.get(sem, 0), val)
                    rec.clock[sem] = val
            self.pending_barrier[engine] = None
        return rec

    def op(self, engine, fn, reads=(), writes=(), label=""):
        rec = self._new_rec(engine, fn, label)
        self._reads(rec, reads)
        self._writes(rec, writes, register=False)
        self.eng_count[engine] += 1
        rec.completion = (("E", engine), self.eng_count[engine])
        rec.mark = True
        for name, key in writes:
            wd = self.wr.setdefault(name, {})
            rd = self.rd.setdefault(name, {})
            if key is None:
                wd.clear()
                rd.clear()
            else:
                rd.pop(key, None)
            wd[key] = rec
        self.last_clock[engine] = rec.clock
        rec.seq = len(self.recs)
        self.recs.append(rec)
        self._bg_tick()
        return rec

    def dma(self, engine, out_ap, in_ap, reads=None, writes=None, label="", **kw):
        if reads is None:
            reads = [rg(in_ap)]
        if writes is None:
            writes = [rg(out_ap)]
        slot = self.dma_slot_i.get(engine, 0)
        self.dma_slot_i[engine] = (slot + 1) % self.DMA_SLOTS
        uses = self.dma_slot_uses.get((engine, slot), 0) + 1
        self.dma_slot_uses[(engine, slot)] = uses
        semkey = ("D", engine, slot)

        def fn(eng, out_ap=out_ap, in_ap=in_ap, kw=kw):
            return eng.dma_start(out_ap, in_ap, **kw)

        rec = self._new_rec(engine, fn, label)
        if uses > 1 and rec.clock.get(semkey, 0) < 16 * (uses - 1):
            rec.waits[semkey] = max(rec.waits.get(semkey, 0), 16 * (uses - 1))
            rec.clock[semkey] = 16 * (uses - 1)
        rec.completion = (semkey, 16 * uses)
        rec.dma_inc = (semkey, 16)
        self._reads(rec, reads)
        self._writes(rec, writes)
        self.last_clock[rec.engine] = rec.clock
        rec.seq = len(self.recs)
        self.recs.append(rec)
        self._bg_tick()
        return rec

    def mm(self, fn, reads=(), psum_rgn=None, start=False, stop=False, label=""):
        rec = self._new_rec("tensor", fn, label)
        if start:
            assert self.group_psum is None, "nested matmul group"
            self.group_psum = psum_rgn
            self.group_reads = []
            self.open_group_reads = set()
            self._writes(rec, [psum_rgn], register=False)
        self._reads(rec, reads, register=False)
        self.group_reads.extend(reads)
        self.open_group_reads.update(reads)
        if stop:
            self.eng_count["tensor"] += 1
            rec.completion = (("E", "tensor"), self.eng_count["tensor"])
            rec.mark = True
            name, key = self.group_psum
            self.wr.setdefault(name, {})[key] = rec
            self.rd.setdefault(name, {}).pop(key, None)
            for rname, rkey in self.group_reads:
                self.rd.setdefault(rname, {}).setdefault(rkey, []).append(rec)
            self.group_psum = None
            self.group_reads = []
            self.open_group_reads = None
        self.last_clock[rec.engine] = rec.clock
        rec.seq = len(self.recs)
        self.recs.append(rec)
        return rec

    def add_background(self, fns, every=8):
        from collections import deque
        if not hasattr(self, "_bg"):
            self._bg = deque()
            self._bg_count = 0
            self._bg_active = False
        self._bg_every = every
        self._bg.extend(fns)

    def _bg_tick(self):
        if not getattr(self, "_bg", None) or getattr(self, "_bg_active", False):
            return
        self._bg_count += 1
        if self._bg_count % self._bg_every == 0:
            self._bg_pending = True
        if getattr(self, "_bg_pending", False) and self.group_psum is None:
            self._bg_active = True
            try:
                fn = self._bg.popleft()
                fn()
            finally:
                self._bg_active = False
            self._bg_pending = False

    def drain_background(self):
        while getattr(self, "_bg", None):
            self._bg_active = True
            try:
                fn = self._bg.popleft()
                fn()
            finally:
                self._bg_active = False

    def barrier(self):
        comps = []
        for e in ENGINES:
            if self.eng_count[e] > 0:
                comps.append((("E", e), self.eng_count[e]))
        for (eng, slot), uses in self.dma_slot_uses.items():
            comps.append((("D", eng, slot), 16 * uses))
        for e in ENGINES:
            prev = self.pending_barrier[e] or []
            self.pending_barrier[e] = list(prev) + comps

    def finalize(self):
        nc = self.nc
        semkeys = set()
        for rec in self.recs:
            semkeys.update(rec.waits.keys())
            if rec.completion is not None:
                semkeys.add(rec.completion[0])
        final_waits = {}
        for e in ENGINES:
            if self.eng_count[e] > 0 and e != "sync":
                final_waits[("E", e)] = self.eng_count[e]
                semkeys.add(("E", e))
        for (eng, slot), uses in self.dma_slot_uses.items():
            final_waits[("D", eng, slot)] = 16 * uses
            semkeys.add(("D", eng, slot))

        with ExitStack() as stack:
            sems = {}
            for k in sorted(semkeys, key=repr):
                sems[k] = stack.enter_context(
                    nc.semaphore("s_" + "_".join(str(p) for p in k)))
            block = stack.enter_context(nc.Block())
            by_eng = {e: [r for r in self.recs if r.engine == e]
                      for e in ENGINES}

            def make_section(e):
                recs = by_eng[e]

                def section(eng):
                    for rec in recs:
                        for sem, val in sorted(rec.waits.items(), key=repr):
                            eng.wait_ge(sems[sem], val)
                        try:
                            ins = rec.fn(eng)
                        except Exception:
                            print(f"[Sched] failing instruction: {rec.label}")
                            raise
                        if rec.dma_inc is not None:
                            ins.then_inc(sems[rec.dma_inc[0]], rec.dma_inc[1])
                        elif rec.mark:
                            ins.then_inc(sems[rec.completion[0]], 1)
                    if e == "sync":
                        for sem, val in sorted(final_waits.items(), key=repr):
                            eng.wait_ge(sems[sem], val)
                return section

            for e in ENGINES:
                if by_eng[e] or e == "sync":
                    getattr(block, e)(make_section(e))


class ConstPool:

    def __init__(self, S, nc, stack, n=16, tag="cst"):
        self.S = S
        self.t = stack.enter_context(nc.sbuf_tensor(tag, [128, n], F32))
        self.cols = {}
        self.n = n

    def __call__(self, value):
        value = float(value)
        if value not in self.cols:
            j = len(self.cols)
            assert j < self.n
            self.cols[value] = j
            self.S.op("vector",
                      lambda e, j=j, value=value: e.memset(self.t[:, j:j + 1],
                                                           value),
                      reads=[], writes=[rg(self.t, j)],
                      label=f"cst.{value}")
        j = self.cols[value]
        return self.t[:, j:j + 1]

    def r(self, value):
        self(value)
        return (self.t.name, self.cols[float(value)])



LN_EPS = 1e-5


def mk_ln_bufs(nc, stack, D, tag, nps=2):
    return dict(
        xa=[stack.enter_context(nc.sbuf_tensor(f"{tag}_x{j}", [128, D], F32))
            for j in range(2)],
        junk_s=stack.enter_context(
            nc.sbuf_tensor(f"{tag}_junks", [128, D], BF16)),
        junk_v=stack.enter_context(
            nc.sbuf_tensor(f"{tag}_junkv", [128, D], BF16)),
        xnb=[stack.enter_context(
            nc.sbuf_tensor(f"{tag}_xnb{j}", [128, D], BF16))
            for j in range(2)],
        st=stack.enter_context(nc.sbuf_tensor(f"{tag}_st", [128, 8], F32)),
        ps=[stack.enter_context(
            nc.psum_tensor(f"{tag}_pt{j}", [128, 1024], BF16))
            for j in range(nps)],
    )


def mk_sink_sbuf(S, xnT_sb, T, tag):
    def sink(i, k0, nblk, pst, reads):
        dst = xnT_sb[:, :].rearrange("p (k t) -> p k t", t=T)[
            :, k0:k0 + nblk, i * 128:(i + 1) * 128]
        src = pst[:, :].rearrange("p (b j) -> p b j", j=128)[:, 0:nblk, :]
        S.op("vector",
             lambda e, dst=dst, src=src: e.tensor_copy(dst, src),
             reads=reads, writes=[rg(xnT_sb, ("c", i))],
             label=f"{tag}.ev{i}.{k0}")
    return sink


def mk_sink_hbm(S, nc, stack, xnT_hbm, T, tag):
    stg = [stack.enter_context(
        nc.sbuf_tensor(f"{tag}_sg{j}", [128, 1024], BF16)) for j in range(2)]
    cnt = [0]

    def sink(i, k0, nblk, pst, reads):
        s = stg[cnt[0] % 2]
        cnt[0] += 1
        S.op("vector",
             lambda e, s=s, pst=pst, nblk=nblk: e.tensor_copy(
                 s[:, 0:nblk * 128], pst[:, 0:nblk * 128]),
             reads=reads, writes=[rg(s)], label=f"{tag}.ev{i}.{k0}")
        dst = xnT_hbm.rearrange("p (k t) -> p k t", t=T)[
            :, k0:k0 + nblk, i * 128:(i + 1) * 128]
        src = s[:, :].rearrange("p (b j) -> p b j", j=128)[:, 0:nblk, :]
        S.dma("sync", dst, src, reads=[rg(s)],
              writes=[rg(xnT_hbm, ("c", i, k0))], label=f"{tag}.sx{i}.{k0}")
    return sink


def ph_ln_to_xnT(S, nc, stack, x_hbm, xnT_sb, ident_bf, T, D, tag, C,
                 eps=LN_EPS, bufs=None, sink=None):
    BD = D // 128
    TT = T // 128
    if bufs is None:
        bufs = mk_ln_bufs(nc, stack, D, tag)
    if sink is None:
        sink = mk_sink_sbuf(S, xnT_sb, T, tag)
    xa = bufs["xa"]
    junk_s = bufs["junk_s"]
    junk_v = bufs["junk_v"]
    xnb = bufs["xnb"]
    st = bufs["st"]
    ps = bufs["ps"]
    for i in range(TT):
        xt = xa[i % 2]
        xn = xnb[i % 2]
        pst = ps[i % 2]
        S.dma("sync", xt[:, :], x_hbm[i * 128:(i + 1) * 128, :],
              label=f"{tag}.ldx{i}")
        S.op("scalar",
             lambda e, xt=xt: e.activation(junk_s[:, :], xt[:, :], AF.Square,
                                           accum_out=st[:, 1:2]),
             reads=[rg(xt)], writes=[rg(junk_s), rg(st, 1)],
             label=f"{tag}.sq{i}")
        S.op("vector",
             lambda e, xt=xt: e.tensor_scalar(junk_v[:, :], xt[:, :], 1.0, None,
                                              ALU.mult, op1=ALU.add,
                                              accum_out=st[:, 0:1]),
             reads=[rg(xt)], writes=[rg(junk_v), rg(st, 0)],
             label=f"{tag}.sum{i}")
        S.op("vector",
             lambda e: e.tensor_scalar(st[:, 2:3], st[:, 0:1], 1.0 / D, None,
                                       ALU.mult),
             reads=[rg(st, 0)], writes=[rg(st, 2)], label=f"{tag}.m{i}")
        S.op("vector",
             lambda e: e.tensor_scalar(st[:, 3:4], st[:, 1:2], 1.0 / D, None,
                                       ALU.mult),
             reads=[rg(st, 1)], writes=[rg(st, 3)], label=f"{tag}.msq{i}")
        S.op("vector",
             lambda e: e.tensor_tensor(st[:, 4:5], st[:, 2:3], st[:, 2:3],
                                       ALU.mult),
             reads=[rg(st, 2)], writes=[rg(st, 4)], label=f"{tag}.m2{i}")
        S.op("vector",
             lambda e: e.tensor_tensor(st[:, 5:6], st[:, 3:4], st[:, 4:5],
                                       ALU.subtract),
             reads=[rg(st, 3), rg(st, 4)], writes=[rg(st, 5)],
             label=f"{tag}.var{i}")
        S.op("scalar",
             lambda e: e.activation(st[:, 6:7], st[:, 5:6], AF.Ln, bias=C(eps)),
             reads=[rg(st, 5), C.r(eps)], writes=[rg(st, 6)],
             label=f"{tag}.ln{i}")
        S.op("scalar",
             lambda e: e.activation(st[:, 7:8], st[:, 6:7], AF.Exp, scale=-0.5),
             reads=[rg(st, 6)], writes=[rg(st, 7)], label=f"{tag}.rstd{i}")
        S.op("vector",
             lambda e, xt=xt, xn=xn: e.tensor_scalar(
                 xn[:, :], xt[:, :], st[:, 2:3], st[:, 7:8],
                 ALU.subtract, ALU.mult),
             reads=[rg(xt), rg(st, 2), rg(st, 7)], writes=[rg(xn)],
             label=f"{tag}.norm{i}")
        for k in range(BD):
            S.op("tensor",
                 lambda e, xn=xn, pst=pst, k=k: e.transpose(
                     pst[:, (k % 8) * 128:(k % 8 + 1) * 128],
                     xn[:, k * 128:(k + 1) * 128], ident_bf[:, :]),
                 reads=[rg(xn), rg(ident_bf)], writes=[rg(pst, k % 8)],
                 label=f"{tag}.tr{i}.{k}")
            if k % 8 == 7 or k == BD - 1:
                k0 = (k // 8) * 8
                nblk = k - k0 + 1
                sink(i, k0, nblk, pst,
                     [rg(pst, kk % 8) for kk in range(k0, k + 1)])


def _chunks(total, size):
    out = []
    c0 = 0
    while c0 < total:
        out.append((c0, min(size, total - c0)))
        c0 += size
    return out


def ph_mm_F(S, nc, stack, w_hbm, Din, ncols, xnT_sb, T, tag, evac_cb,
            col_scale=None, nchunk=512):
    BK = Din // 128
    NB = ncols // 128
    wb = [stack.enter_context(nc.sbuf_tensor(f"{tag}_wb{j}", [128, BK * 128],
                                             BF16)) for j in range(2)]
    ps = [stack.enter_context(nc.psum_tensor(f"{tag}_ps{j}", [128, 512], F32))
          for j in range(2)]
    xnT3 = xnT_sb[:, :].rearrange("p (k t) -> p k t", t=T)
    pi = 0
    for m in range(NB):
        w = wb[m % 2]
        src = w_hbm[:, m * 128:(m + 1) * 128].rearrange("(k p) j -> p k j",
                                                        p=128)
        dst = w[:, :].rearrange("p (k j) -> p k j", j=128)
        S.dma("gpsimd", dst, src, reads=[rg(w_hbm)], writes=[rg(w)],
              label=f"{tag}.ldw{m}")
        for (c0, cn) in _chunks(T, nchunk):
            p = ps[pi % 2]
            pi += 1
            for k in range(BK):
                S.mm(lambda e, w=w, k=k, c0=c0, cn=cn, p=p: e.matmul(
                        p[:, 0:cn], w[:, k * 128:(k + 1) * 128],
                        xnT3[:, k, c0:c0 + cn],
                        start=(k == 0), stop=(k == BK - 1)),
                     reads=[rg(w), rg(xnT_sb)], psum_rgn=rg(p),
                     start=(k == 0), stop=(k == BK - 1),
                     label=f"{tag}.mm{m}.{c0}.{k}")
            evac_cb(m, c0, cn, p)


def ph_mm_T(S, nc, stack, actT_sb, BK, T, w_hbm, Dout, tag, evac_cb,
            nchunk=256, extra_mm=None, w_rows0=0, pre_cb=None,
            w_tiled_bf=False, w_tiled_blk=None):
    wb = [stack.enter_context(nc.sbuf_tensor(f"{tag}_wb{j}",
                                             [128, BK * nchunk], BF16))
          for j in range(2)]
    ps = [stack.enter_context(nc.psum_tensor(f"{tag}_ps{j}", [128, 512], F32))
          for j in range(2)]
    act3 = actT_sb[:, :].rearrange("p (k t) -> p k t", t=T)
    TT = T // 128
    pi = 0
    for ci, (n0, nn) in enumerate(_chunks(Dout, nchunk)):
        w = wb[ci % 2]
        if w_tiled_bf:
            slab = BK * nchunk
            S.dma("scalar", w[:, 0:BK * nn],
                  w_hbm[:, ci * slab:ci * slab + BK * nn],
                  reads=[rg(w_hbm, ci * (nchunk // (w_tiled_blk or nchunk))
                            + kk)
                         for kk in range(nchunk // (w_tiled_blk or nchunk))],
                  writes=[rg(w)],
                  label=f"{tag}.ldw{ci}")
            blk = w_tiled_blk or nchunk
            w4 = w[:, :].rearrange("p (s k j) -> p s k j", k=BK, j=blk)
        else:
            src = w_hbm[w_rows0:w_rows0 + BK * 128, n0:n0 + nn].rearrange(
                "(k p) j -> p k j", p=128)
            dst = w[:, :].rearrange("p (k j) -> p k j", j=nchunk)[:, :, 0:nn]
            S.dma("gpsimd", dst, src, reads=[rg(w_hbm)], writes=[rg(w)],
                  label=f"{tag}.ldw{ci}")
        for i in range(TT):
            p = ps[pi % 2]
            pi += 1
            if pre_cb is not None:
                pre_cb(i, n0, nn)
            extras = extra_mm(i, n0, nn) if extra_mm is not None else []
            for k in range(BK):
                last = (k == BK - 1) and not extras
                if w_tiled_bf and (w_tiled_blk or nchunk) != nchunk:
                    blk = w_tiled_blk
                    rhs = w4[:, 0:nn // blk, k, :]
                else:
                    rhs = w[:, :].rearrange("p (k j) -> p k j", j=nchunk)[
                        :, k, 0:nn]
                S.mm(lambda e, w=w, k=k, i=i, nn=nn, p=p, last=last, rhs=rhs:
                     e.matmul(
                        p[:, 0:nn],
                        act3[:, k, i * 128:(i + 1) * 128],
                        rhs,
                        start=(k == 0), stop=last),
                     reads=[rg(w), rg(actT_sb)], psum_rgn=rg(p),
                     start=(k == 0), stop=last,
                     label=f"{tag}.mm{ci}.{i}.{k}")
            for j, (fn, rds) in enumerate(extras):
                last = (j == len(extras) - 1)
                S.mm(lambda e, fn=fn, p=p: fn(e, p),
                     reads=rds, psum_rgn=rg(p), start=False, stop=last,
                     label=f"{tag}.xmm{ci}.{i}.{j}")
            evac_cb(i, n0, nn, p)




class NCView:

    def __init__(self, nc, pfx):
        self._nc = nc
        self._pfx = pfx

    def sbuf_tensor(self, name, *a, **k):
        return self._nc.sbuf_tensor(self._pfx + name, *a, **k)

    def psum_tensor(self, name, *a, **k):
        return self._nc.psum_tensor(self._pfx + name, *a, **k)

    def dram_tensor(self, name, *a, **k):
        return self._nc.dram_tensor(self._pfx + name, *a, **k)



def _alibi_slopes(n):
    import math
    cp2 = 2 ** int(math.floor(math.log2(n)))
    base = 2.0 ** (-2.0 ** (-(math.log2(cp2) - 3)))
    slopes = base ** np.arange(1, cp2 + 1, dtype=np.float64)
    if cp2 != n:
        extra_base = 2.0 ** (-2.0 ** (-(math.log2(2 * cp2) - 3)))
        rem = min(cp2, n - cp2)
        slopes = np.concatenate(
            [slopes, extra_base ** np.arange(1, 1 + 2 * rem, 2,
                                             dtype=np.float64)])
    return slopes.astype(np.float32)


def _alib3_rows(T, slopes):
    H = len(slopes)
    out = np.zeros((3 * H, T), dtype=ml_dtypes.bfloat16)
    k = np.arange(T, dtype=np.float64)
    for h, s in enumerate(slopes):
        v = np.float64(np.float32(s)) * k
        hi = v.astype(ml_dtypes.bfloat16)
        r1 = v - hi.astype(np.float64)
        mid = r1.astype(ml_dtypes.bfloat16)
        lo = (r1 - mid.astype(np.float64)).astype(ml_dtypes.bfloat16)
        out[3 * h] = hi
        out[3 * h + 1] = mid
        out[3 * h + 2] = lo
    return out


_BUILD_CACHE = {}


def _get_nc(build_fn, *args):
    key = (build_fn.__name__,) + tuple(args)
    if key not in _BUILD_CACHE:
        _BUILD_CACHE[key] = build_fn(*args)
    return _BUILD_CACHE[key]


def _chain(inputs, dims, run):
    T, D, NH, DI, LAT = (dims["T"], dims["D"], dims["NH"], dims["DI"],
                         dims["LAT"])
    NB = NH + 2
    BD = D // 128
    x = np.ascontiguousarray(inputs["L0_attn_hidden_states"],
                             dtype=np.float32)
    NC = x.shape[0]
    cn = np.ascontiguousarray
    f32 = np.float32
    identbf = np.eye(128, dtype=ml_dtypes.bfloat16)
    identf32 = np.eye(128, dtype=f32)
    kramp = cn(np.broadcast_to(np.arange(T, dtype=f32), (128, T)))
    maskdiag = np.where(np.arange(128)[:, None] >= np.arange(128)[None, :],
                        0.0, -1e30).astype(f32)
    W0 = inputs["L0_ffn_W"].astype(f32)
    WT0 = cn(W0.T)
    A0 = inputs["L0_ffn_A"].astype(f32)
    b0 = inputs["L0_ffn_b"].astype(f32)
    LiMEs = inputs["L0_ffn_LiMEs"].astype(f32)
    G0 = cn(np.concatenate(
        [WT0[:, :8], np.zeros((D, 24), f32), A0], axis=1))
    WQKV0 = cn(inputs["L0_attn_w_qkv"].astype(f32))
    WQKV1 = cn(np.concatenate([inputs["L1_attn_wq"], inputs["L1_attn_wk"],
                               inputs["L1_attn_wv"]], axis=1).astype(f32))
    wc0 = cn(inputs["L0_attn_w_c"].astype(f32))
    Bm0 = cn(inputs["L0_ffn_Bmat"].astype(f32))
    wo1 = cn(inputs["L1_attn_wo"].astype(f32))
    up1 = cn(inputs["L1_ffn_up_w"].astype(f32))
    dn1 = cn(inputs["L1_ffn_down_w"].astype(f32))
    gt1 = cn(inputs["L1_ffn_gate_w"].astype(f32))
    f11 = cn(inputs["L1_ffn_fc1_w"].astype(f32))
    ew1 = cn(inputs["L1_ffn_expert_w"].astype(f32))
    f21 = cn(inputs["L1_ffn_fc2_w"].astype(f32))
    gammaB = cn(np.broadcast_to(inputs["L1_ffn_ln_gamma"].astype(f32),
                                (128, D)))
    beta_row = cn(inputs["L1_ffn_ln_beta"].astype(f32)[None, :])
    slopes0 = _alibi_slopes(NH)
    slopes1 = np.zeros(NH, f32)
    xa = cn(x[:, 1::2, :])

    def rn(build_fn, args, maps):
        return run(_get_nc(build_fn, *args), maps)

    r = rn(build_pre0_launch, (T, D, NC),
           [{"x": x[c], "x_anch": cn(np.concatenate(
                 [xa[c:c + 1], xa[:c], xa[c + 1:]]).reshape(-1, D)),
             "G0": G0, "b8_col": cn(b0[:8, None]),
             "LiT": cn(LiMEs.T), "Li": LiMEs,
             "c_identbf": identbf, "c_identf32": identf32}
            for c in range(NC)])
    xnT0 = [r[c]["xnT"] for c in range(NC)]
    pmixT = [r[c]["pmixT"] for c in range(NC)]
    r = rn(build_qkv_launch, (T, D, NB),
           [{"xnT": xnT0[c], "wqkv": WQKV0} for c in range(NC)])
    qkT = [r[c]["qkT"] for c in range(NC)]
    r = rn(build_attn_launch, (T, NH, tuple(float(s) for s in slopes0)),
           [{"qkT": qkT[c], "c_kramp": kramp, "c_maskdiag": maskdiag,
             "c_identbf": identbf} for c in range(NC)])
    attnT = [r[c]["attnT"] for c in range(NC)]
    r = rn(build_h0_launch, (T, D),
           [{"xnT": xnT0[c], "WT": WT0, "b_row": cn(b0[None, :]), "A0": A0,
             "x": x[c]} for c in range(NC)])
    ACC = [r[c]["ACC"] for c in range(NC)]
    zT = [r[c]["zT"] for c in range(NC)]
    r = rn(build_combine0_launch, (T, D),
           [{"ACC": ACC[c], "attnT": attnT[c], "wc": wc0, "Bmat": Bm0,
             "zT": zT[c], "pmixT": pmixT[c]} for c in range(NC)])
    out1 = [r[c]["out1"] for c in range(NC)]
    r = rn(build_ln_launch, (T, D),
           [{"x": out1[c], "c_identbf": identbf} for c in range(NC)])
    xnT1 = [r[c]["xnT"] for c in range(NC)]
    r = rn(build_qkv_launch, (T, D, NB),
           [{"xnT": xnT1[c], "wqkv": WQKV1} for c in range(NC)])
    qkT1 = [r[c]["qkT"] for c in range(NC)]
    r = rn(build_attn_launch, (T, NH, tuple(float(s) for s in slopes1)),
           [{"qkT": qkT1[c], "c_kramp": kramp, "c_maskdiag": maskdiag,
             "c_identbf": identbf} for c in range(NC)])
    attn1T = [r[c]["attnT"] for c in range(NC)]
    r = rn(build_wcast_launch, (D, DI, LAT),
           [{"up": up1, "down": dn1, "fc1": f11, "expw": ew1, "fc2": f21,
             "wo": wo1} for c in range(NC)])
    wb = r
    r = rn(build_ffn1a_launch, (T, D, DI),
           [{"xnT": xnT1[c], "up_bf": wb[c]["up_bf"],
             "down_bf": wb[c]["down_bf"]} for c in range(NC)])
    SH = [r[c]["SH"] for c in range(NC)]
    r = rn(build_ffn1b_launch, (T, D, LAT),
           [{"xnT": xnT1[c], "gate": gt1, "fc1_bf": wb[c]["fc1_bf"],
             "expw_bf": wb[c]["expw_bf"], "fc2_bf": wb[c]["fc2_bf"],
             "c_identf32": identf32} for c in range(NC)])
    RT = [r[c]["RT"] for c in range(NC)]
    r = rn(build_combine1_launch, (T, D),
           [{"x1": out1[c], "attnT": attn1T[c], "wo_bf": wb[c]["wo_bf"],
             "SH": SH[c], "RT": RT[c], "gammaB": gammaB,
             "beta_row": beta_row} for c in range(NC)])
    return np.stack([r[c]["OUT"] for c in range(NC)]).astype(np.float32)


def _dev_run(nc, in_maps):
    res = run_bass_kernel_spmd(nc, in_maps, core_ids=list(range(len(in_maps))))
    return res.results


def build_fused(T, D, NH, DI, LAT, NBAT, slopes0, slopes1):
    nc = bass.Bass("TRN2", target_bir_lowering=False)
    BD = D // 128
    NBq = NH + 2
    EI, EO, IN = "ExternalInput", "ExternalOutput", "Internal"

    def dt(n, shp, dty, kind):
        return nc.dram_tensor(n, shp, dty, kind=kind).ap()

    A = {
        "x": dt("x", [T, D], F32, EI),
        "x_anch": dt("x_anch", [NBAT * (T // 2), D], F32, EI),
        "G0": dt("G0", [D, 64], F32, EI),
        "b8_col": dt("b8_col", [8, 1], F32, EI),
        "LiT": dt("LiT", [32, 8], F32, EI),
        "Li": dt("Li", [8, 32], F32, EI),
        "c_identbf": dt("c_identbf", [128, 128], BF16, EI),
        "c_identf32": dt("c_identf32", [128, 128], F32, EI),
        "c_maskdiag": dt("c_maskdiag", [128, 128], F32, EI),
        "alib3": dt("alib3", [3 * NH, T], BF16, EI),
        "wqkv0": dt("wqkv0", [D, NBq * 128], F32, EI),
        "wqkv1": dt("wqkv1", [D, NBq * 128], F32, EI),
        "WT": dt("WT", [D, D], F32, EI),
        "b_row": dt("b_row", [1, D], F32, EI),
        "A0": dt("A0", [D, 32], F32, EI),
        "wc": dt("wc", [D, D], F32, EI),
        "Bmat": dt("Bmat", [32, D], F32, EI),
        "up": dt("up", [D, DI], F32, EI),
        "down": dt("down", [DI, D], F32, EI),
        "gate": dt("gate", [D, 8], F32, EI),
        "fc1": dt("fc1", [D, LAT], F32, EI),
        "expw": dt("expw", [LAT, LAT], F32, EI),
        "fc2": dt("fc2", [LAT, D], F32, EI),
        "wo": dt("wo", [D, D], F32, EI),
        "gammaB": dt("gammaB", [128, D], F32, EI),
        "beta_row": dt("beta_row", [1, D], F32, EI),
        "OUT": dt("OUT", [T, D], F32, EO),
        "xnT0": dt("xnT0", [128, BD * T], BF16, IN),
        "pmixT": dt("pmixT", [32, T], F32, IN),
        "qkT0": dt("qkT0", [128, NBq * T], BF16, IN),
        "attnT0": dt("attnT0", [128, BD * T], BF16, IN),
        "ACC": dt("ACC", [T, D], F32, IN),
        "zT": dt("zT", [32, T], F32, IN),
        "out1": dt("out1", [T, D], F32, IN),
        "xnT1": dt("xnT1", [128, BD * T], BF16, IN),
        "qkT1": dt("qkT1", [128, NBq * T], BF16, IN),
        "attn1T": dt("attn1T", [128, BD * T], BF16, IN),
        "SH": dt("SH", [T, D], F32, IN),
        "RT": dt("RT", [T, D], F32, IN),
        "up_bf": dt("up_bf", [128, (D // 128) * DI], BF16, IN),
        "down_bf": dt("down_bf", [128, (DI // 128) * D], BF16, IN),
        "fc1_bf": dt("fc1_bf", [128, (D // 128) * LAT], BF16, IN),
        "expw_bf": dt("expw_bf", [128, (LAT // 128) * LAT], BF16, IN),
        "fc2_bf": dt("fc2_bf", [128, (LAT // 128) * D], BF16, IN),
        "wo_bf": dt("wo_bf", [128, (D // 128) * D], BF16, IN),
    }
    S = Sched(nc)
    pc = [0]

    def P(fn, aps, *args):
        pc[0] += 1
        with ExitStack() as st:
            fn(S, NCView(nc, f"p{pc[0]}_"), st, aps, *args)
        S.barrier()

    g = lambda *names: {n: A[m] for n, m in
                        (nm.split(":") if ":" in nm else (nm, nm)
                         for nm in names)}
    P(ph_pre0_ln, g("x", "c_identbf", "xnT:xnT0"), T, D)
    P(ph_qkv, g("xnT:xnT0", "wqkv:wqkv0", "qkT:qkT0"), T, D, NBq)
    anch_stack = ExitStack()
    anch_fns = ph_anchors(S, NCView(nc, "anc_"), anch_stack,
                          g("x_anch", "G0", "b8_col", "LiT", "Li",
                            "c_identbf", "c_identf32", "pmixT"), T, D, NBAT)
    S.add_background(anch_fns, every=90)
    P(ph_attn2, g("qkT:qkT0", "alib3", "c_maskdiag", "c_identbf",
                  "attnT:attnT0"), T, NH, slopes0)
    S.drain_background()
    S.barrier()
    anch_stack.close()
    wc_stack = ExitStack()
    wc_fns = ph_wcast(S, NCView(nc, "wcv_"), wc_stack, [
        (A["up"], A["up_bf"], D, DI, 128),
        (A["down"], A["down_bf"], DI, D, 64),
        (A["fc1"], A["fc1_bf"], D, LAT, 128),
        (A["expw"], A["expw_bf"], LAT, LAT, 128),
        (A["fc2"], A["fc2_bf"], LAT, D, 256),
        (A["wo"], A["wo_bf"], D, D, 128),
    ])
    S.add_background(wc_fns, every=32)
    P(ph_h0, g("xnT:xnT0", "WT", "b_row", "A0", "x", "ACC", "zT"), T, D)
    P(ph_combine0, g("ACC", "attnT:attnT0", "wc", "Bmat", "zT", "pmixT",
                     "out1"), T, D)
    P(ph_lnL, g("x:out1", "c_identbf", "xnT:xnT1"), T, D)
    P(ph_qkv, g("xnT:xnT1", "wqkv:wqkv1", "qkT:qkT1"), T, D, NBq)
    P(ph_attn2, g("qkT:qkT1", "c_maskdiag", "c_identbf",
                  "attnT:attn1T"), T, NH, slopes1)
    S.drain_background()
    S.barrier()
    wc_stack.close()
    P(ph_ffn1a, g("xnT:xnT1", "up_bf", "down_bf", "SH"), T, D, DI)
    P(ph_ffn1b, g("xnT:xnT1", "gate", "fc1_bf", "expw_bf", "fc2_bf",
                  "c_identf32", "RT"), T, D, LAT)
    P(ph_combine1, g("x1:out1", "attnT:attn1T", "wo_bf", "SH", "RT",
                     "gammaB", "beta_row", "OUT"), T, D)
    S.finalize()
    return nc


def _fused_chain(inputs, dims, run):
    T, D, NH, DI, LAT = (dims["T"], dims["D"], dims["NH"], dims["DI"],
                         dims["LAT"])
    x = np.ascontiguousarray(inputs["L0_attn_hidden_states"],
                             dtype=np.float32)
    NC = x.shape[0]
    cn = np.ascontiguousarray
    f32 = np.float32
    W0 = inputs["L0_ffn_W"].astype(f32)
    WT0 = cn(W0.T)
    A0 = inputs["L0_ffn_A"].astype(f32)
    b0 = inputs["L0_ffn_b"].astype(f32)
    LiMEs = inputs["L0_ffn_LiMEs"].astype(f32)
    xa = cn(x[:, 1::2, :])
    nc = _get_nc(build_fused, T, D, NH, DI, LAT, NC,
                 tuple(float(s) for s in _alibi_slopes(NH)),
                 tuple(0.0 for _ in range(NH)))
    shared = {
        "G0": cn(np.concatenate([WT0[:, :8], np.zeros((D, 24), f32), A0],
                                axis=1)),
        "b8_col": cn(b0[:8, None]),
        "LiT": cn(LiMEs.T),
        "Li": LiMEs,
        "c_identbf": np.eye(128, dtype=ml_dtypes.bfloat16),
        "c_identf32": np.eye(128, dtype=f32),
        "alib3": _alib3_rows(T, _alibi_slopes(NH)),
        "c_maskdiag": np.where(
            np.arange(128)[:, None] >= np.arange(128)[None, :],
            0.0, -1e30).astype(f32),
        "wqkv0": cn(inputs["L0_attn_w_qkv"].astype(f32)),
        "wqkv1": cn(np.concatenate(
            [inputs["L1_attn_wq"], inputs["L1_attn_wk"],
             inputs["L1_attn_wv"]], axis=1).astype(f32)),
        "WT": WT0,
        "b_row": cn(b0[None, :]),
        "A0": cn(A0),
        "wc": cn(inputs["L0_attn_w_c"].astype(f32)),
        "Bmat": cn(inputs["L0_ffn_Bmat"].astype(f32)),
        "up": cn(inputs["L1_ffn_up_w"].astype(f32)),
        "down": cn(inputs["L1_ffn_down_w"].astype(f32)),
        "gate": cn(inputs["L1_ffn_gate_w"].astype(f32)),
        "fc1": cn(inputs["L1_ffn_fc1_w"].astype(f32)),
        "expw": cn(inputs["L1_ffn_expert_w"].astype(f32)),
        "fc2": cn(inputs["L1_ffn_fc2_w"].astype(f32)),
        "wo": cn(inputs["L1_attn_wo"].astype(f32)),
        "gammaB": cn(np.broadcast_to(
            inputs["L1_ffn_ln_gamma"].astype(f32), (128, D))),
        "beta_row": cn(inputs["L1_ffn_ln_beta"].astype(f32)[None, :]),
    }
    in_maps = []
    for c in range(NC):
        m = dict(shared)
        m["x"] = x[c]
        m["x_anch"] = cn(np.concatenate(
            [xa[c:c + 1], xa[:c], xa[c + 1:]]).reshape(-1, D))
        in_maps.append(m)
    r = run(nc, in_maps)
    return np.stack([r[c]["OUT"] for c in range(NC)]).astype(np.float32)


def kernel(**inputs):
    import os
    dims = dict(T=2048, D=4096, NH=32, DI=8192, LAT=2048)
    if os.environ.get("KERNEL_UNFUSED"):
        return _chain(inputs, dims, _dev_run)
    return _fused_chain(inputs, dims, _dev_run)


def ph_wcast(S, nc, stack, specs, tag="wc"):
    bufs = [stack.enter_context(
        nc.sbuf_tensor(f"{tag}_b{j}", [128, 4096], BF16)) for j in range(2)]
    cnt = [0]
    fns = []

    def emit_slab(src, dst, blk, slab, ib, si):
        b = bufs[cnt[0] % 2]
        cnt[0] += 1
        S.dma("gpsimd",
              b[:, 0:slab].rearrange("p (k j) -> p k j", j=blk),
              src[:, ib * blk:(ib + 1) * blk].rearrange(
                  "(k p) j -> p k j", p=128),
              reads=[rg(src)], writes=[rg(b)],
              label=f"{tag}.c{si}.{ib}")
        S.dma("gpsimd", dst[:, ib * slab:(ib + 1) * slab], b[:, 0:slab],
              reads=[rg(b)], writes=[rg(dst, ib)],
              label=f"{tag}.s{si}.{ib}")

    for si, (src, dst, Din, ncols, blk) in enumerate(specs):
        BK = Din // 128
        slab = BK * blk
        assert slab <= 4096, (slab, si)
        for ib in range(ncols // blk):
            fns.append(lambda src=src, dst=dst, blk=blk, slab=slab, ib=ib,
                       si=si: emit_slab(src, dst, blk, slab, ib, si))
    return fns


def ph_ffn1a(S, nc, stack, aps, T, D, DI):
    BD = D // 128
    BI = DI // 128
    xnT_h = aps["xnT"]
    up = aps["up_bf"]
    down = aps["down_bf"]
    SH = aps["SH"]
    sbt = lambda name, shp, dt: stack.enter_context(
        nc.sbuf_tensor(name, shp, dt))
    xcb = sbt("xcb", [128, BD * 512], BF16)
    hTn = sbt("hTn", [128, BI * 512], BF16)
    h1 = sbt("h1", [128, 512], BF16)
    ub = [sbt(f"ub{j}", [128, BD * 128], BF16) for j in range(2)]
    db = [sbt(f"db{j}", [128, BI * 256], BF16) for j in range(2)]
    stg = [sbt(f"stg{j}", [128, 256], F32) for j in range(2)]
    pu = [stack.enter_context(nc.psum_tensor(f"pu{j}", [128, 512], F32))
          for j in range(2)]
    pd = [stack.enter_context(nc.psum_tensor(f"pd{j}", [128, 512], F32))
          for j in range(2)]
    xc3 = xcb[:, :].rearrange("p (k t) -> p k t", t=512)
    hT3 = hTn[:, :].rearrange("p (k t) -> p k t", t=512)
    si = [0]
    for (t0, tn) in _chunks(T, 512):
        S.dma("sync", xc3[:, :, 0:tn],
              xnT_h.rearrange("p (k t) -> p k t", t=T)[:, :, t0:t0 + tn],
              reads=[rg(xnT_h)], writes=[rg(xcb)], label=f"f1.ldx{t0}")
        for ib in range(BI):
            w = ub[ib % 2]
            S.dma("sync", w[:, :],
                  up[:, ib * BD * 128:(ib + 1) * BD * 128],
                  reads=[rg(up, ib)], writes=[rg(w)],
                  label=f"f1.ldu{t0}.{ib}")
            p = pu[ib % 2]
            for k in range(BD):
                S.mm(lambda e, w=w, k=k, tn=tn, p=p: e.matmul(
                        p[:, 0:tn], w[:, k * 128:(k + 1) * 128],
                        xc3[:, k, 0:tn],
                        start=(k == 0), stop=(k == BD - 1)),
                     reads=[rg(w), rg(xcb)], psum_rgn=rg(p),
                     start=(k == 0), stop=(k == BD - 1),
                     label=f"f1.up{t0}.{ib}.{k}")
            S.op("scalar",
                 lambda e, p=p, tn=tn: e.activation(h1[:, 0:tn],
                                                    p[:, 0:tn], AF.Relu),
                 reads=[rg(p)], writes=[rg(h1)], label=f"f1.rl{t0}.{ib}")
            S.op("vector",
                 lambda e, ib=ib, tn=tn: e.tensor_tensor(
                     hT3[:, ib, 0:tn], h1[:, 0:tn], h1[:, 0:tn],
                     ALU.mult),
                 reads=[rg(h1)], writes=[rg(hTn, ib)],
                 label=f"f1.sq{t0}.{ib}")
        for mi, (m0, mn) in enumerate(_chunks(D, 256)):
            dbv = db[mi % 2]
            s0 = (m0 // 64) * BI * 64
            S.dma("sync", dbv[:, 0:BI * mn], down[:, s0:s0 + BI * mn],
                  reads=[rg(down, m0 // 64 + i) for i in range(mn // 64)],
                  writes=[rg(dbv)],
                  label=f"f1.ldd{t0}.{m0}")
            db4 = dbv[:, :].rearrange("p (s k j) -> p s k j", k=BI, j=64)
            for tt in range(tn // 128):
                p = pd[tt % 2]
                for ib in range(BI):
                    S.mm(lambda e, ib=ib, tt=tt, mn=mn, p=p, db4=db4:
                         e.matmul(
                            p[:, 0:mn],
                            hT3[:, ib, tt * 128:(tt + 1) * 128],
                            db4[:, 0:mn // 64, ib, :],
                            start=(ib == 0), stop=(ib == BI - 1)),
                         reads=[rg(dbv)] + [rg(hTn, kk)
                                           for kk in range(BI)],
                         psum_rgn=rg(p), start=(ib == 0),
                         stop=(ib == BI - 1),
                         label=f"f1.dn{t0}.{m0}.{tt}.{ib}")
                s = stg[si[0] % 2]
                si[0] += 1
                S.op("vector",
                     lambda e, p=p, s=s, mn=mn: e.tensor_copy(
                         s[:, 0:mn], p[:, 0:mn]),
                     reads=[rg(p)], writes=[rg(s)],
                     label=f"f1.ev{t0}.{m0}.{tt}")
                S.dma("sync",
                      SH[t0 + tt * 128:t0 + (tt + 1) * 128, m0:m0 + mn],
                      s[:, 0:mn], reads=[rg(s)],
                      writes=[rg(SH, (t0, m0, tt))],
                      label=f"f1.st{t0}.{m0}.{tt}")


def build_wcast_launch(D, DI, LAT):
    nc = bass.Bass("TRN2", target_bir_lowering=False)
    aps = {}
    aps["up"] = nc.dram_tensor("up", [D, DI], F32, kind="ExternalInput").ap()
    aps["down"] = nc.dram_tensor("down", [DI, D], F32,
                                 kind="ExternalInput").ap()
    aps["fc1"] = nc.dram_tensor("fc1", [D, LAT], F32,
                                kind="ExternalInput").ap()
    aps["expw"] = nc.dram_tensor("expw", [LAT, LAT], F32,
                                 kind="ExternalInput").ap()
    aps["fc2"] = nc.dram_tensor("fc2", [LAT, D], F32,
                                kind="ExternalInput").ap()
    aps["wo"] = nc.dram_tensor("wo", [D, D], F32,
                               kind="ExternalInput").ap()
    outs = {
        "up_bf": nc.dram_tensor("up_bf", [128, (D // 128) * DI], BF16,
                                kind="ExternalOutput").ap(),
        "down_bf": nc.dram_tensor("down_bf", [128, (DI // 128) * D], BF16,
                                  kind="ExternalOutput").ap(),
        "fc1_bf": nc.dram_tensor("fc1_bf", [128, (D // 128) * LAT], BF16,
                                 kind="ExternalOutput").ap(),
        "expw_bf": nc.dram_tensor("expw_bf", [128, (LAT // 128) * LAT], BF16,
                                  kind="ExternalOutput").ap(),
        "fc2_bf": nc.dram_tensor("fc2_bf", [128, (LAT // 128) * D], BF16,
                                 kind="ExternalOutput").ap(),
        "wo_bf": nc.dram_tensor("wo_bf", [128, (D // 128) * D], BF16,
                                kind="ExternalOutput").ap(),
    }
    S = Sched(nc)
    with ExitStack() as stack:
        for fn in ph_wcast(S, NCView(nc, ""), stack, [
                (aps["up"], outs["up_bf"], D, DI, 128),
                (aps["down"], outs["down_bf"], DI, D, 64),
                (aps["fc1"], outs["fc1_bf"], D, LAT, 128),
                (aps["expw"], outs["expw_bf"], LAT, LAT, 128),
                (aps["fc2"], outs["fc2_bf"], LAT, D, 256),
                (aps["wo"], outs["wo_bf"], D, D, 128)]):
            fn()
        S.finalize()
    return nc


def build_ffn1a_launch(T, D, DI):
    nc = bass.Bass("TRN2", target_bir_lowering=False)
    aps = {}
    BD = D // 128
    BI = DI // 128
    aps["xnT"] = nc.dram_tensor("xnT", [128, BD * T], BF16,
                           kind="ExternalInput").ap()
    aps["up_bf"] = nc.dram_tensor("up_bf", [128, (D // 128) * DI], BF16,
                                  kind="ExternalInput").ap()
    aps["down_bf"] = nc.dram_tensor("down_bf", [128, (DI // 128) * D], BF16,
                                    kind="ExternalInput").ap()
    aps["SH"] = nc.dram_tensor("SH", [T, D], F32, kind="ExternalOutput").ap()
    S = Sched(nc)
    with ExitStack() as stack:
        ph_ffn1a(S, NCView(nc, ""), stack, aps, T, D, DI)
        S.finalize()
    return nc


def ph_ffn1b(S, nc, stack, aps, T, D, LAT):
    BD = D // 128
    BL = LAT // 128
    TT = T // 128
    xnT_h = aps["xnT"]
    gate = aps["gate"]
    fc1 = aps["fc1_bf"]
    expw = aps["expw_bf"]
    fc2 = aps["fc2_bf"]
    identf_h = aps["c_identf32"]
    RT = aps["RT"]
    sbt = lambda name, shp, dt: stack.enter_context(
        nc.sbuf_tensor(name, shp, dt))
    identf = sbt("identf", [128, 128], F32)
    S.dma("sync", identf[:, :], identf_h, label="ldidf")
    gb = sbt("gb", [128, BD * 8], BF16)
    xlT = sbt("xlT", [128, BL * T], BF16)
    lgT = sbt("lgT", [8, T], F32)
    LGt = sbt("LGt", [128, 8], F32)
    rw = sbt("rw", [128, TT], F32)
    wst = sbt("wst1", [128, 8], F32)
    E8 = sbt("E8b", [128, 8], F32)
    M8 = sbt("M8b", [128, 8], F32)
    stack2 = ExitStack()
    sb2 = lambda name, shp, dt: stack2.enter_context(
        nc.sbuf_tensor(name, shp, dt))
    xcb = sb2("xcb", [128, BD * 512], BF16)
    fb = [sb2(f"fb{j}", [128, BD * 128], BF16) for j in range(2)]
    pg = stack.enter_context(nc.psum_tensor("pg", [128, 512], F32))
    pf = [stack.enter_context(nc.psum_tensor(f"pf{j}", [128, 512], F32))
          for j in range(2)]
    xc3 = xcb[:, :].rearrange("p (k t) -> p k t", t=512)
    xl3 = xlT[:, :].rearrange("p (k t) -> p k t", t=T)
    S.dma("gpsimd", gb[:, :].rearrange("p (k j) -> p k j", j=8),
          gate.rearrange("(k p) j -> p k j", p=128),
          reads=[rg(gate)], writes=[rg(gb)], label="ldg")
    for (t0, tn) in _chunks(T, 512):
        S.dma("sync", xc3[:, :, 0:tn],
              xnT_h.rearrange("p (k t) -> p k t", t=T)[:, :, t0:t0 + tn],
              reads=[rg(xnT_h)], writes=[rg(xcb)], label=f"f2.ldx{t0}")
        for k in range(BD):
            S.mm(lambda e, k=k, tn=tn: e.matmul(
                    pg[0:8, 0:tn], gb[:, k * 8:(k + 1) * 8],
                    xc3[:, k, 0:tn],
                    start=(k == 0), stop=(k == BD - 1)),
                 reads=[rg(gb), rg(xcb)], psum_rgn=rg(pg),
                 start=(k == 0), stop=(k == BD - 1),
                 label=f"f2.g{t0}.{k}")
        S.op("vector",
             lambda e, t0=t0, tn=tn: e.tensor_copy(
                 lgT[:, t0:t0 + tn], pg[0:8, 0:tn]),
             reads=[rg(pg)], writes=[rg(lgT, t0)], label=f"f2.gev{t0}")
        for lb in range(BL):
            w = fb[lb % 2]
            S.dma("sync", w[:, :],
                  fc1[:, lb * BD * 128:(lb + 1) * BD * 128],
                  reads=[rg(fc1, lb)], writes=[rg(w)],
                  label=f"f2.ldf{t0}.{lb}")
            p = pf[lb % 2]
            for k in range(BD):
                S.mm(lambda e, w=w, k=k, tn=tn, p=p: e.matmul(
                        p[:, 0:tn], w[:, k * 128:(k + 1) * 128],
                        xc3[:, k, 0:tn],
                        start=(k == 0), stop=(k == BD - 1)),
                     reads=[rg(w), rg(xcb)], psum_rgn=rg(p),
                     start=(k == 0), stop=(k == BD - 1),
                     label=f"f2.fc1.{t0}.{lb}.{k}")
            S.op("vector",
                 lambda e, lb=lb, t0=t0, tn=tn, p=p: e.tensor_copy(
                     xl3[:, lb, t0:t0 + tn], p[:, 0:tn]),
                 reads=[rg(p)], writes=[rg(xlT, lb)],
                 label=f"f2.fev{t0}.{lb}")
    stack2.close()
    S.barrier()
    eoT = sbt("eoT", [128, BL * T], BF16)
    eb = [sbt(f"eb{j}", [128, BL * 128], BF16) for j in range(2)]
    eo3 = eoT[:, :].rearrange("p (k t) -> p k t", t=T)
    for i in range(TT):
        S.op("tensor",
             lambda e, i=i: e.transpose(
                 pg[:, 0:8], lgT[:, i * 128:(i + 1) * 128],
                 identf[0:8, 0:8]),
             reads=[rg(lgT, kk) for kk in range(0, T, 512)]
             + [rg(identf)],
             writes=[rg(pg)], label=f"rw.tr{i}")
        S.op("vector", lambda e: e.tensor_copy(LGt[:, :], pg[:, 0:8]),
             reads=[rg(pg)], writes=[rg(LGt)], label=f"rw.ev{i}")
        S.op("vector",
             lambda e: e.reduce_max(wst[:, 0:1], LGt[:, :], axis=AX.X),
             reads=[rg(LGt)], writes=[rg(wst, 0)], label=f"rw.m1{i}")
        S.op("vector",
             lambda e: e.tensor_scalar(E8[:, :], LGt[:, :], wst[:, 0:1],
                                       None, ALU.is_ge),
             reads=[rg(LGt), rg(wst, 0)], writes=[rg(E8)],
             label=f"rw.eq{i}")
        S.op("vector",
             lambda e: e.reduce_sum(wst[:, 1:2], E8[:, :], axis=AX.X),
             reads=[rg(E8)], writes=[rg(wst, 1)], label=f"rw.cnt{i}")
        S.op("vector",
             lambda e: e.scalar_tensor_tensor(M8[:, :], E8[:, :], -1e30,
                                              LGt[:, :], ALU.mult,
                                              ALU.add),
             reads=[rg(E8), rg(LGt)], writes=[rg(M8)], label=f"rw.mk{i}")
        S.op("vector",
             lambda e: e.reduce_max(wst[:, 2:3], M8[:, :], axis=AX.X),
             reads=[rg(M8)], writes=[rg(wst, 2)], label=f"rw.m2{i}")
        S.op("vector",
             lambda e: e.tensor_scalar(wst[:, 3:4], wst[:, 1:2], 1.5,
                                       None, ALU.is_ge),
             reads=[rg(wst, 1)], writes=[rg(wst, 3)], label=f"rw.g2{i}")
        S.op("vector",
             lambda e: e.tensor_tensor(wst[:, 4:5], wst[:, 0:1],
                                       wst[:, 2:3], ALU.subtract),
             reads=[rg(wst, 0), rg(wst, 2)], writes=[rg(wst, 4)],
             label=f"rw.df{i}")
        S.op("vector",
             lambda e: e.tensor_tensor(wst[:, 5:6], wst[:, 3:4],
                                       wst[:, 4:5], ALU.mult),
             reads=[rg(wst, 3), rg(wst, 4)], writes=[rg(wst, 5)],
             label=f"rw.cr{i}")
        S.op("vector",
             lambda e: e.tensor_tensor(wst[:, 6:7], wst[:, 0:1],
                                       wst[:, 2:3], ALU.add),
             reads=[rg(wst, 0), rg(wst, 2)], writes=[rg(wst, 6)],
             label=f"rw.sm{i}")
        S.op("vector",
             lambda e, i=i: e.tensor_tensor(rw[:, i:i + 1], wst[:, 6:7],
                                            wst[:, 5:6], ALU.add),
             reads=[rg(wst, 6), rg(wst, 5)], writes=[rg(rw, i)],
             label=f"rw.rw{i}")
    for (t0, tn) in _chunks(T, 512):
        for l2 in range(BL):
            w = eb[l2 % 2]
            S.dma("sync", w[:, :],
                  expw[:, l2 * BL * 128:(l2 + 1) * BL * 128],
                  reads=[rg(expw, l2)], writes=[rg(w)],
                  label=f"f2.lde{t0}.{l2}")
            p = pf[l2 % 2]
            for lb in range(BL):
                S.mm(lambda e, w=w, lb=lb, t0=t0, tn=tn, p=p: e.matmul(
                        p[:, 0:tn], w[:, lb * 128:(lb + 1) * 128],
                        xl3[:, lb, t0:t0 + tn],
                        start=(lb == 0), stop=(lb == BL - 1)),
                     reads=[rg(w)] + [rg(xlT, kk) for kk in range(BL)],
                     psum_rgn=rg(p), start=(lb == 0),
                     stop=(lb == BL - 1), label=f"f2.ex{t0}.{l2}.{lb}")
            S.op("vector",
                 lambda e, l2=l2, t0=t0, tn=tn, p=p: e.tensor_copy(
                     eo3[:, l2, t0:t0 + tn], p[:, 0:tn]),
                 reads=[rg(p)], writes=[rg(eoT, l2)],
                 label=f"f2.eev{t0}.{l2}")
    stg = [sbt(f"stg{j}", [128, 512], F32) for j in range(2)]
    si = [0]

    def evac(i, n0, nn, p):
        s = stg[si[0] % 2]
        si[0] += 1
        S.op("vector",
             lambda e, p=p, s=s, nn=nn, i=i: e.tensor_scalar(
                 s[:, 0:nn], p[:, 0:nn], rw[:, i:i + 1], None, ALU.mult),
             reads=[rg(p), rg(rw, i)], writes=[rg(s)],
             label=f"f2.c2ev{i}.{n0}")
        S.dma("sync", RT[i * 128:(i + 1) * 128, n0:n0 + nn], s[:, 0:nn],
              reads=[rg(s)], writes=[rg(RT, (i, n0))],
              label=f"f2.c2st{i}.{n0}")

    ph_mm_T(S, nc, stack, eoT, BL, T, fc2, D, "c2", evac, nchunk=256,
            w_tiled_bf=True)


def build_ffn1b_launch(T, D, LAT):
    nc = bass.Bass("TRN2", target_bir_lowering=False)
    aps = {}
    BD = D // 128
    BL = LAT // 128
    TT = T // 128
    aps["xnT"] = nc.dram_tensor("xnT", [128, BD * T], BF16,
                           kind="ExternalInput").ap()
    aps["gate"] = nc.dram_tensor("gate", [D, 8], F32, kind="ExternalInput").ap()
    aps["fc1_bf"] = nc.dram_tensor("fc1_bf", [128, (D // 128) * LAT], BF16,
                                   kind="ExternalInput").ap()
    aps["expw_bf"] = nc.dram_tensor("expw_bf", [128, (LAT // 128) * LAT],
                                    BF16, kind="ExternalInput").ap()
    aps["fc2_bf"] = nc.dram_tensor("fc2_bf", [128, (LAT // 128) * D], BF16,
                                   kind="ExternalInput").ap()
    aps["c_identf32"] = nc.dram_tensor("c_identf32", [128, 128], F32,
                              kind="ExternalInput").ap()
    aps["RT"] = nc.dram_tensor("RT", [T, D], F32, kind="ExternalOutput").ap()
    S = Sched(nc)
    with ExitStack() as stack:
        ph_ffn1b(S, NCView(nc, ""), stack, aps, T, D, LAT)
        S.finalize()
    return nc


def ph_combine1(S, nc, stack, aps, T, D):
    BD = D // 128
    TT = T // 128
    x1 = aps["x1"]
    attnT_h = aps["attnT"]
    wo = aps["wo_bf"]
    SH = aps["SH"]
    RT = aps["RT"]
    gammaB = aps["gammaB"]
    beta_row = aps["beta_row"]
    OUT = aps["OUT"]
    Y = nc.dram_tensor("Yscr", [T, D], F32, kind="Internal").ap()
    sbt = lambda name, shp, dt: stack.enter_context(
        nc.sbuf_tensor(name, shp, dt))
    C = ConstPool(S, nc, stack)
    btb = sbt("btb", [1, D], BF16)
    ones_bf = sbt("onesb", [1, 128], BF16)
    S.dma("gpsimd", btb[:, :], beta_row, label="ldbt")
    S.op("gpsimd", lambda e: e.memset(ones_bf[:, :], 1.0),
         reads=[], writes=[rg(ones_bf)], label="ones")
    st = sbt("st3", [128, 8], F32)
    my = sbt("my", [128, TT], F32)
    mr = sbt("mr", [128, TT], F32)
    stack2 = ExitStack()
    sb2 = lambda name, shp, dt: stack2.enter_context(
        nc.sbuf_tensor(name, shp, dt))
    ya = [sb2(f"ya{j}", [128, D], F32) for j in range(2)]
    yb = [sb2(f"yb{j}", [128, D], F32) for j in range(2)]
    junk_s = sb2("junks3", [128, D], BF16)
    junk_v = sb2("junkv3", [128, D], BF16)
    for i in range(TT):
        sh = ya[i % 2]
        y = yb[i % 2]
        S.dma("sync", sh[:, :], SH[i * 128:(i + 1) * 128, :],
              reads=[rg(SH)], writes=[rg(sh)], label=f"y.ldsh{i}")
        S.dma("scalar", y[:, :], RT[i * 128:(i + 1) * 128, :],
              reads=[rg(RT)], writes=[rg(y)], label=f"y.ldrt{i}")
        S.op("vector",
             lambda e, sh=sh, y=y: e.tensor_tensor(y[:, :], y[:, :],
                                                   sh[:, :], ALU.add),
             reads=[rg(sh), rg(y)], writes=[rg(y)], label=f"y.add{i}")
        S.op("scalar",
             lambda e, y=y: e.activation(junk_s[:, :], y[:, :], AF.Square,
                                         accum_out=st[:, 1:2]),
             reads=[rg(y)], writes=[rg(junk_s), rg(st, 1)],
             label=f"y.sq{i}")
        S.op("vector",
             lambda e, y=y: e.tensor_scalar(junk_v[:, :], y[:, :], 1.0,
                                            None, ALU.mult, op1=ALU.add,
                                            accum_out=st[:, 0:1]),
             reads=[rg(y)], writes=[rg(junk_v), rg(st, 0)],
             label=f"y.sum{i}")
        S.op("vector",
             lambda e, i=i: e.tensor_scalar(my[:, i:i + 1], st[:, 0:1],
                                            1.0 / D, None, ALU.mult),
             reads=[rg(st, 0)], writes=[rg(my, i)], label=f"y.m{i}")
        S.op("vector",
             lambda e: e.tensor_scalar(st[:, 3:4], st[:, 1:2], 1.0 / D,
                                       None, ALU.mult),
             reads=[rg(st, 1)], writes=[rg(st, 3)], label=f"y.msq{i}")
        S.op("vector",
             lambda e, i=i: e.tensor_tensor(st[:, 4:5], my[:, i:i + 1],
                                            my[:, i:i + 1], ALU.mult),
             reads=[rg(my, i)], writes=[rg(st, 4)], label=f"y.m2{i}")
        S.op("vector",
             lambda e: e.tensor_tensor(st[:, 5:6], st[:, 3:4],
                                       st[:, 4:5], ALU.subtract),
             reads=[rg(st, 3), rg(st, 4)], writes=[rg(st, 5)],
             label=f"y.var{i}")
        S.op("scalar",
             lambda e: e.activation(st[:, 6:7], st[:, 5:6], AF.Ln,
                                    bias=C(LN_EPS)),
             reads=[rg(st, 5), C.r(LN_EPS)], writes=[rg(st, 6)],
             label=f"y.ln{i}")
        S.op("scalar",
             lambda e, i=i: e.activation(mr[:, i:i + 1], st[:, 6:7],
                                         AF.Exp, scale=-0.5),
             reads=[rg(st, 6)], writes=[rg(mr, i)], label=f"y.rstd{i}")
        S.dma("sync", Y[i * 128:(i + 1) * 128, :], y[:, :],
              reads=[rg(y)], writes=[rg(Y, i)], label=f"y.st{i}")
    stack2.close()
    S.barrier()
    aT = sbt("aT", [128, BD * T], BF16)
    gmb = sbt("gmb", [128, D], F32)
    S.dma("sync", aT[:, :], attnT_h, label="ldaT")
    S.dma("sync", gmb[:, :], gammaB, label="ldgm")
    yc = [sbt(f"yc{j}", [128, 256], F32) for j in range(3)]
    xc = [sbt(f"xc{j}", [128, 256], F32) for j in range(3)]
    t1b = [sbt(f"t1{j}", [128, 256], F32) for j in range(2)]
    og = [sbt(f"og{j}", [128, 256], F32) for j in range(3)]
    bi = [0]

    def pre(i, n0, nn):
        yt = yc[bi[0] % 3]
        xt = xc[bi[0] % 3]
        S.dma("sync", yt[:, 0:nn], Y[i * 128:(i + 1) * 128, n0:n0 + nn],
              reads=[rg(Y, i)], writes=[rg(yt)], label=f"b.ldy{i}.{n0}")
        S.dma("scalar", xt[:, 0:nn], x1[i * 128:(i + 1) * 128, n0:n0 + nn],
              reads=[rg(x1)], writes=[rg(xt)], label=f"b.ldx{i}.{n0}")

    def extra(i, n0, nn):
        return [(lambda e, p, n0=n0, nn=nn: e.matmul(
                    p[:, 0:nn], ones_bf[0:1, :], btb[0:1, n0:n0 + nn],
                    start=False, stop=True),
                 [rg(ones_bf), rg(btb)])]

    def evac(i, n0, nn, p):
        yt = yc[bi[0] % 3]
        xt = xc[bi[0] % 3]
        o = og[bi[0] % 3]
        t1 = t1b[bi[0] % 2]
        bi[0] += 1
        S.op("vector",
             lambda e, yt=yt, nn=nn, i=i: e.tensor_scalar(
                 t1[:, 0:nn], yt[:, 0:nn], my[:, i:i + 1],
                 mr[:, i:i + 1], ALU.subtract, ALU.mult),
             reads=[rg(yt), rg(my, i), rg(mr, i)], writes=[rg(t1)],
             label=f"b.t1{i}.{n0}")
        S.op("vector",
             lambda e, nn=nn, n0=n0: e.tensor_tensor(
                 t1[:, 0:nn], t1[:, 0:nn], gmb[:, n0:n0 + nn], ALU.mult),
             reads=[rg(t1), rg(gmb)], writes=[rg(t1)],
             label=f"b.t2{i}.{n0}")
        S.op("vector",
             lambda e, p=p, nn=nn: e.tensor_tensor(
                 t1[:, 0:nn], t1[:, 0:nn], p[:, 0:nn], ALU.add),
             reads=[rg(t1), rg(p)], writes=[rg(t1)],
             label=f"b.t3{i}.{n0}")
        S.op("vector",
             lambda e, xt=xt, o=o, nn=nn: e.tensor_tensor(
                 o[:, 0:nn], t1[:, 0:nn], xt[:, 0:nn], ALU.add),
             reads=[rg(t1), rg(xt)], writes=[rg(o)],
             label=f"b.t4{i}.{n0}")
        S.dma("sync", OUT[i * 128:(i + 1) * 128, n0:n0 + nn], o[:, 0:nn],
              reads=[rg(o)], writes=[rg(OUT, (i, n0))],
              label=f"b.st{i}.{n0}")

    ph_mm_T(S, nc, stack, aT, BD, T, wo, D, "b", evac,
            nchunk=256, extra_mm=extra, pre_cb=pre,
            w_tiled_bf=True, w_tiled_blk=128)


def build_combine1_launch(T, D):
    nc = bass.Bass("TRN2", target_bir_lowering=False)
    aps = {}
    BD = D // 128
    TT = T // 128
    aps["x1"] = nc.dram_tensor("x1", [T, D], F32, kind="ExternalInput").ap()
    aps["attnT"] = nc.dram_tensor("attnT", [128, BD * T], BF16,
                             kind="ExternalInput").ap()
    aps["wo_bf"] = nc.dram_tensor("wo_bf", [128, (D // 128) * D], BF16,
                                  kind="ExternalInput").ap()
    aps["SH"] = nc.dram_tensor("SH", [T, D], F32, kind="ExternalInput").ap()
    aps["RT"] = nc.dram_tensor("RT", [T, D], F32, kind="ExternalInput").ap()
    aps["gammaB"] = nc.dram_tensor("gammaB", [128, D], F32,
                            kind="ExternalInput").ap()
    aps["beta_row"] = nc.dram_tensor("beta_row", [1, D], F32,
                              kind="ExternalInput").ap()
    aps["OUT"] = nc.dram_tensor("OUT", [T, D], F32, kind="ExternalOutput").ap()
    S = Sched(nc)
    with ExitStack() as stack:
        ph_combine1(S, NCView(nc, ""), stack, aps, T, D)
        S.finalize()
    return nc


def ph_pre0_ln(S, nc, stack, aps, T, D):
    identb = stack.enter_context(nc.sbuf_tensor("identb", [128, 128], BF16))
    S.dma("sync", identb[:, :], aps["c_identbf"], label="ldidb")
    C = ConstPool(S, nc, stack)
    ph_ln_to_xnT(S, nc, stack, aps["x"], None, identb, T, D, "ln", C,
                 sink=mk_sink_hbm(S, nc, stack, aps["xnT"], T, "lnsk"))


def ph_anchors(S, nc, stack, aps, T, D, NBAT):
    BD = D // 128
    TA = T // 2
    OT = TA // 128
    NAT = NBAT * OT
    xa_h = aps["x_anch"]
    G0 = aps["G0"]
    b8_h = aps["b8_col"]
    LiT_h = aps["LiT"]
    Li_h = aps["Li"]
    identb_h = aps["c_identbf"]
    identf_h = aps["c_identf32"]
    pmixT_out = aps["pmixT"]
    sb = lambda name, shp, dt=F32: stack.enter_context(
        nc.sbuf_tensor(name, shp, dt))
    identb = sb("identb", [128, 128], BF16)
    identf = sb("identf", [128, 128], F32)
    C = ConstPool(S, nc, stack)
    lnb = mk_ln_bufs(nc, stack, D, "lnb", nps=1)
    fns = []
    fns.append(lambda: (
        S.dma("sync", identb[:, :], identb_h, label="ldidb"),
        S.dma("sync", identf[:, :], identf_h, label="ldidf")))
    Gb = sb("Gb", [128, BD * 64], BF16)
    b8 = sb("b8", [8, 1], F32)
    LiT = sb("LiT_s", [32, 8], BF16)
    Li = sb("Li_s", [8, 32], BF16)
    fns.append(lambda: (
        S.dma("gpsimd", Gb[:, :].rearrange("p (k j) -> p k j", j=64),
              G0.rearrange("(k p) j -> p k j", p=128),
              reads=[rg(G0)], writes=[rg(Gb)], label="ldG"),
        S.dma("sync", b8[:, :], b8_h, label="ldb8"),
        S.dma("gpsimd", LiT[:, :], LiT_h, label="ldLiT"),
        S.dma("gpsimd", Li[:, :], Li_h, label="ldLi")))
    xat = lnb["xa"]
    xab = lnb["xnb"]
    junk_s2 = lnb["junk_s"]
    junk_v2 = lnb["junk_v"]
    ast = lnb["st"]
    xaT = sb("xaT", [128, BD * 128], BF16)
    ptr = lnb["ps"]
    psG = stack.enter_context(nc.psum_tensor("psG", [128, 512], F32))
    HsK = sb("HsK", [8, OT * 128])
    dsK = sb("dsK", [8, OT * 128])
    Hst = sb("Hst", [8, 128])
    dst_t = sb("dst", [8, 128])
    zat = sb("zat", [32, 128], BF16)
    mx = sb("mx", [8, 2])
    red = sb("red", [8, 2])
    fns.append(lambda: S.op("gpsimd", lambda e: e.memset(mx[:, :], 0.0),
                            reads=[], writes=[rg(mx)], label="mx0"))

    def seg_tile(a):
        xt = xat[a % 2]
        xb = xab[a % 2]
        pst = ptr[0]
        S.dma("gpsimd", xt[:, :], xa_h[a * 128:(a + 1) * 128, :],
              reads=[rg(xa_h)], writes=[rg(xt)], label=f"a.ldx{a}")
        S.op("scalar",
             lambda e, xt=xt: e.activation(junk_s2[:, :], xt[:, :],
                                           AF.Square,
                                           accum_out=ast[:, 1:2]),
             reads=[rg(xt)], writes=[rg(junk_s2), rg(ast, 1)],
             label=f"a.sq{a}")
        S.op("vector",
             lambda e, xt=xt: e.tensor_scalar(junk_v2[:, :], xt[:, :],
                                              1.0, None, ALU.mult,
                                              op1=ALU.add,
                                              accum_out=ast[:, 0:1]),
             reads=[rg(xt)], writes=[rg(junk_v2), rg(ast, 0)],
             label=f"a.sum{a}")
        S.op("vector",
             lambda e: e.tensor_scalar(ast[:, 2:3], ast[:, 0:1], 1.0 / D,
                                       None, ALU.mult),
             reads=[rg(ast, 0)], writes=[rg(ast, 2)], label=f"a.m{a}")
        S.op("vector",
             lambda e: e.tensor_scalar(ast[:, 3:4], ast[:, 1:2], 1.0 / D,
                                       None, ALU.mult),
             reads=[rg(ast, 1)], writes=[rg(ast, 3)], label=f"a.msq{a}")
        S.op("vector",
             lambda e: e.tensor_tensor(ast[:, 4:5], ast[:, 2:3],
                                       ast[:, 2:3], ALU.mult),
             reads=[rg(ast, 2)], writes=[rg(ast, 4)], label=f"a.m2{a}")
        S.op("vector",
             lambda e: e.tensor_tensor(ast[:, 5:6], ast[:, 3:4],
                                       ast[:, 4:5], ALU.subtract),
             reads=[rg(ast, 3), rg(ast, 4)], writes=[rg(ast, 5)],
             label=f"a.var{a}")
        S.op("scalar",
             lambda e: e.activation(ast[:, 6:7], ast[:, 5:6], AF.Ln,
                                    bias=C(LN_EPS)),
             reads=[rg(ast, 5), C.r(LN_EPS)], writes=[rg(ast, 6)],
             label=f"a.ln{a}")
        S.op("scalar",
             lambda e: e.activation(ast[:, 7:8], ast[:, 6:7], AF.Exp,
                                    scale=-0.5),
             reads=[rg(ast, 6)], writes=[rg(ast, 7)], label=f"a.rstd{a}")
        S.op("vector",
             lambda e, xt=xt, xb=xb: e.tensor_scalar(
                 xb[:, :], xt[:, :], ast[:, 2:3], ast[:, 7:8],
                 ALU.subtract, ALU.mult),
             reads=[rg(xt), rg(ast, 2), rg(ast, 7)], writes=[rg(xb)],
             label=f"a.nrm{a}")
        for k in range(BD):
            S.op("tensor",
                 lambda e, xb=xb, pst=pst, k=k: e.transpose(
                     pst[:, (k % 8) * 128:(k % 8 + 1) * 128],
                     xb[:, k * 128:(k + 1) * 128], identb[:, :]),
                 reads=[rg(xb), rg(identb)], writes=[rg(pst, k % 8)],
                 label=f"a.tr{a}.{k}")
            if k % 8 == 7 or k == BD - 1:
                k0 = (k // 8) * 8
                nb = k - k0 + 1
                S.op("vector",
                     lambda e, k0=k0, nb=nb, pst=pst: e.tensor_copy(
                         xaT[:, k0 * 128:(k0 + nb) * 128],
                         pst[:, 0:nb * 128]),
                     reads=[rg(pst, kk % 8) for kk in range(k0, k + 1)],
                     writes=[rg(xaT, ("c", k0))],
                     label=f"a.ev{a}.{k0}")
        for k in range(BD):
            S.mm(lambda e, k=k: e.matmul(
                    psG[0:64, 0:128], Gb[:, k * 64:(k + 1) * 64],
                    xaT[:, k * 128:(k + 1) * 128],
                    start=(k == 0), stop=(k == BD - 1)),
                 reads=[rg(Gb)] + [rg(xaT, ("c", kk))
                                   for kk in range(0, BD, 8)],
                 psum_rgn=rg(psG), start=(k == 0), stop=(k == BD - 1),
                 label=f"a.g{a}.{k}")
        own = a < OT
        hdst = HsK[:, a * 128:(a + 1) * 128] if own else Hst[:, :]
        hrgn = rg(HsK, a) if own else rg(Hst)
        S.op("vector",
             lambda e, hdst=hdst: e.tensor_scalar(
                 hdst, psG[0:8, 0:128], b8[:, :], None, ALU.add),
             reads=[rg(psG), rg(b8)], writes=[hrgn], label=f"a.hev{a}")
        S.op("vector",
             lambda e: e.tensor_copy(zat[:, :], psG[32:64, 0:128]),
             reads=[rg(psG)], writes=[rg(zat)], label=f"a.zev{a}")
        S.mm(lambda e: e.matmul(psG[0:8, 128:256], LiT[:, :], zat[:, :],
                                start=True, stop=True),
             reads=[rg(LiT), rg(zat)], psum_rgn=rg(psG, "D"),
             start=True, stop=True, label=f"a.d{a}")
        ddst = dsK[:, a * 128:(a + 1) * 128] if own else dst_t[:, :]
        drgn = rg(dsK, a) if own else rg(dst_t)
        S.op("vector",
             lambda e, ddst=ddst: e.tensor_copy(ddst, psG[0:8, 128:256]),
             reads=[rg(psG, "D")], writes=[drgn], label=f"a.dev{a}")
        S.op("vector",
             lambda e, hdst=hdst: e.tensor_reduce(
                 red[:, 0:1], hdst, AX.X, ALU.max,
                 apply_absolute_value=True),
             reads=[hrgn], writes=[rg(red, 0)], label=f"a.hmx{a}")
        S.op("vector",
             lambda e, ddst=ddst: e.tensor_reduce(
                 red[:, 1:2], ddst, AX.X, ALU.max,
                 apply_absolute_value=True),
             reads=[drgn], writes=[rg(red, 1)], label=f"a.dmx{a}")
        S.op("vector",
             lambda e: e.tensor_tensor(mx[:, :], mx[:, :], red[:, :],
                                       ALU.max),
             reads=[rg(red, 0), rg(red, 1), rg(mx)], writes=[rg(mx)],
             label=f"a.mxa{a}")

    for a in range(NAT):
        fns.append(lambda a=a: seg_tile(a))

    ps8 = psG
    mxT = sb("mxT", [2, 8])
    scl = sb("scl", [2, 1])
    sclT = sb("sclT", [1, 2])
    rb = sb("rb", [8, 2])
    ones8 = sb("ones8", [1, 8])
    lgT = sb("lgT", [8, OT * 128])
    tmp8 = sb("tmp8", [8, OT * 128])
    rb1h = sb("rb1h", [8, 1])
    LG = sb("LG", [128, OT * 8])
    WGT = sb("WGT", [128, OT * 8], BF16)
    ps9 = ps8

    def seg_scales():
        S.op("gpsimd", lambda e: e.memset(ones8[:, :], 1.0),
             reads=[], writes=[rg(ones8)], label="ones8")
        S.op("tensor", lambda e: e.transpose(ps8[0:2, 0:8], mx[:, :],
                                             identf[0:8, 0:8]),
             reads=[rg(mx), rg(identf)], writes=[rg(ps8)], label="mxtr")
        S.op("vector", lambda e: e.tensor_copy(mxT[:, :], ps8[0:2, 0:8]),
             reads=[rg(ps8)], writes=[rg(mxT)], label="mxTev")
        S.op("vector",
             lambda e: e.tensor_reduce(scl[:, :], mxT[:, :], AX.X, ALU.max),
             reads=[rg(mxT)], writes=[rg(scl)], label="sclred")
        S.op("vector",
             lambda e: e.tensor_scalar(scl[:, :], scl[:, :], 1e-6, None,
                                       ALU.max),
             reads=[rg(scl)], writes=[rg(scl)], label="scleps")
        S.op("vector", lambda e: e.reciprocal(scl[:, :], scl[:, :]),
             reads=[rg(scl)], writes=[rg(scl)], label="sclrcp")
        S.op("tensor", lambda e: e.transpose(ps8[0:1, 0:2], scl[:, :],
                                             identf[0:2, 0:2]),
             reads=[rg(scl), rg(identf)], writes=[rg(ps8)], label="scltr")
        S.op("vector", lambda e: e.tensor_copy(sclT[:, :], ps8[0:1, 0:2]),
             reads=[rg(ps8)], writes=[rg(sclT)], label="sclTev")
        S.mm(lambda e: e.matmul(ps8[0:8, 0:2], ones8[:, :], sclT[:, :],
                                start=True, stop=True),
             reads=[rg(ones8), rg(sclT)], psum_rgn=rg(ps8),
             start=True, stop=True, label="sclbc")
        S.op("vector", lambda e: e.tensor_copy(rb[:, :], ps8[0:8, 0:2]),
             reads=[rg(ps8)], writes=[rg(rb)], label="rbev")
        S.op("vector",
             lambda e: e.tensor_scalar(tmp8[:, :], HsK[:, :], rb[:, 0:1], 0.5,
                                       ALU.mult, op1=ALU.mult),
             reads=[rg(HsK, kk) for kk in range(OT)] + [rg(rb)],
             writes=[rg(tmp8)], label="lg1")
        S.op("vector",
             lambda e: e.tensor_scalar(rb1h[:, :], rb[:, 1:2], 0.5, None,
                                       ALU.mult),
             reads=[rg(rb)], writes=[rg(rb1h)], label="lg2")
        S.op("vector",
             lambda e: e.scalar_tensor_tensor(lgT[:, :], dsK[:, :],
                                              rb1h[:, :], tmp8[:, :],
                                              ALU.mult, ALU.add),
             reads=[rg(dsK, kk) for kk in range(OT)] + [rg(rb1h), rg(tmp8)],
             writes=[rg(lgT)], label="lg3")
        for a in range(OT):
            S.op("tensor",
                 lambda e, a=a: e.transpose(
                     ps9[:, (a % 8) * 8:(a % 8) * 8 + 8],
                     lgT[:, a * 128:(a + 1) * 128], identf[0:8, 0:8]),
                 reads=[rg(lgT), rg(identf)], writes=[rg(ps9, a % 8)],
                 label=f"w.tr{a}")
            if a % 8 == 7 or a == OT - 1:
                a0 = (a // 8) * 8
                nb = a - a0 + 1
                S.op("vector",
                     lambda e, a0=a0, nb=nb: e.tensor_copy(
                         LG[:, a0 * 8:(a0 + nb) * 8],
                         ps9[:, 0:nb * 8]),
                     reads=[rg(ps9, kk % 8) for kk in range(a0, a + 1)],
                     writes=[rg(LG, ("c", a0))], label=f"w.ev{a0}")

    fns.append(seg_scales)

    wst = sb("wst", [128, 16])
    P8 = sb("P8", [128, 8])
    E8 = sb("E8", [128, 8])
    M8 = sb("M8", [128, 8])
    K8 = sb("K8", [128, 8])
    def seg_rout(a):
        L = LG[:, a * 8:(a + 1) * 8]
        lrg = rg(LG, ("c", (a // 8) * 8))
        S.op("vector",
             lambda e, L=L: e.reduce_max(wst[:, 0:1], L, axis=AX.X),
             reads=[lrg], writes=[rg(wst, 0)], label=f"w.m1{a}")
        S.op("vector",
             lambda e: e.tensor_scalar(wst[:, 1:2], wst[:, 0:1], -1.0,
                                       None, ALU.mult),
             reads=[rg(wst, 0)], writes=[rg(wst, 1)], label=f"w.nm{a}")
        S.op("scalar",
             lambda e, L=L: e.activation(P8[:, :], L, AF.Exp,
                                         bias=wst[:, 1:2],
                                         accum_out=wst[:, 2:3]),
             reads=[lrg, rg(wst, 1)], writes=[rg(P8), rg(wst, 2)],
             label=f"w.exp{a}")
        S.op("vector", lambda e: e.reciprocal(wst[:, 3:4], wst[:, 2:3]),
             reads=[rg(wst, 2)], writes=[rg(wst, 3)], label=f"w.rs{a}")
        S.op("vector",
             lambda e: e.tensor_scalar(P8[:, :], P8[:, :], wst[:, 3:4],
                                       None, ALU.mult),
             reads=[rg(P8), rg(wst, 3)], writes=[rg(P8)],
             label=f"w.pn{a}")
        S.op("vector",
             lambda e: e.reduce_max(wst[:, 4:5], P8[:, :], axis=AX.X),
             reads=[rg(P8)], writes=[rg(wst, 4)], label=f"w.m1p{a}")
        S.op("vector",
             lambda e: e.tensor_scalar(E8[:, :], P8[:, :], wst[:, 4:5],
                                       None, ALU.is_ge),
             reads=[rg(P8), rg(wst, 4)], writes=[rg(E8)],
             label=f"w.eq{a}")
        S.op("vector",
             lambda e: e.reduce_sum(wst[:, 5:6], E8[:, :], axis=AX.X),
             reads=[rg(E8)], writes=[rg(wst, 5)], label=f"w.cnt{a}")
        S.op("vector",
             lambda e: e.scalar_tensor_tensor(M8[:, :], E8[:, :], -1e30,
                                              P8[:, :], ALU.mult,
                                              ALU.add),
             reads=[rg(E8), rg(P8)], writes=[rg(M8)], label=f"w.msk{a}")
        S.op("vector",
             lambda e: e.reduce_max(wst[:, 6:7], M8[:, :], axis=AX.X),
             reads=[rg(M8)], writes=[rg(wst, 6)], label=f"w.m2{a}")
        S.op("vector",
             lambda e: e.tensor_scalar(wst[:, 7:8], wst[:, 5:6], 1.5,
                                       None, ALU.is_ge),
             reads=[rg(wst, 5)], writes=[rg(wst, 7)], label=f"w.ge2{a}")
        S.op("vector",
             lambda e: e.tensor_tensor(wst[:, 8:9], wst[:, 4:5],
                                       wst[:, 6:7], ALU.subtract),
             reads=[rg(wst, 4), rg(wst, 6)], writes=[rg(wst, 8)],
             label=f"w.df{a}")
        S.op("vector",
             lambda e: e.tensor_tensor(wst[:, 9:10], wst[:, 7:8],
                                       wst[:, 8:9], ALU.mult),
             reads=[rg(wst, 7), rg(wst, 8)], writes=[rg(wst, 9)],
             label=f"w.cr{a}")
        S.op("vector",
             lambda e: e.tensor_tensor(wst[:, 10:11], wst[:, 6:7],
                                       wst[:, 9:10], ALU.add),
             reads=[rg(wst, 6), rg(wst, 9)], writes=[rg(wst, 10)],
             label=f"w.th{a}")
        S.op("vector",
             lambda e: e.tensor_scalar(wst[:, 11:12], wst[:, 10:11], 2.0,
                                       None, ALU.mult),
             reads=[rg(wst, 10)], writes=[rg(wst, 11)], label=f"w.t2{a}")
        S.op("scalar",
             lambda e: e.activation(K8[:, :], P8[:, :], AF.Exp,
                                    scale=-2.0, bias=wst[:, 11:12]),
             reads=[rg(P8), rg(wst, 11)], writes=[rg(K8)],
             label=f"w.sg1{a}")
        S.op("vector",
             lambda e: e.tensor_scalar(K8[:, :], K8[:, :], 1.0, None,
                                       ALU.add),
             reads=[rg(K8)], writes=[rg(K8)], label=f"w.sg2{a}")
        S.op("vector", lambda e: e.reciprocal(K8[:, :], K8[:, :]),
             reads=[rg(K8)], writes=[rg(K8)], label=f"w.sg3{a}")
        S.op("vector",
             lambda e: e.tensor_tensor(M8[:, :], P8[:, :], K8[:, :],
                                       ALU.mult),
             reads=[rg(P8), rg(K8)], writes=[rg(M8)], label=f"w.w0{a}")
        S.op("vector",
             lambda e: e.tensor_scalar(E8[:, :], M8[:, :], 1.0, None,
                                       ALU.mult, op1=ALU.add,
                                       accum_out=wst[:, 13:14]),
             reads=[rg(M8)], writes=[rg(E8), rg(wst, 13)],
             label=f"w.ws{a}")
        S.op("vector",
             lambda e: e.tensor_scalar(wst[:, 14:15], wst[:, 13:14],
                                       1e-9, None, ALU.add),
             reads=[rg(wst, 13)], writes=[rg(wst, 14)],
             label=f"w.wse{a}")
        S.op("vector",
             lambda e: e.reciprocal(wst[:, 15:16], wst[:, 14:15]),
             reads=[rg(wst, 14)], writes=[rg(wst, 15)],
             label=f"w.wr{a}")
        S.op("vector",
             lambda e, a=a: e.tensor_scalar(
                 WGT[:, a * 8:(a + 1) * 8], M8[:, :], wst[:, 15:16],
                 None, ALU.mult),
             reads=[rg(M8), rg(wst, 15)], writes=[rg(WGT, a)],
             label=f"w.wf{a}")

    for a in range(OT):
        fns.append(lambda a=a: seg_rout(a))

    wT = sb("wT", [8, OT * 128], BF16)
    wfT = sb("wfT", [8, T], BF16)
    psA = ptr[0]
    pmx_sb = sb("pmx_sb", [32, T])
    psP = psG

    def seg_pmix():
        for a in range(OT):
            S.op("tensor",
                 lambda e, a=a: e.transpose(
                     psA[0:8, (a % 8) * 128:(a % 8 + 1) * 128],
                     WGT[:, a * 8:(a + 1) * 8], identb[:, :]),
                 reads=[rg(WGT, a), rg(identb)], writes=[rg(psA, a % 8)],
                 label=f"p.tr{a}")
            if a % 8 == 7 or a == OT - 1:
                a0 = (a // 8) * 8
                nb = a - a0 + 1
                S.op("vector",
                     lambda e, a0=a0, nb=nb: e.tensor_copy(
                         wT[:, a0 * 128:(a0 + nb) * 128],
                         psA[0:8, 0:nb * 128]),
                     reads=[rg(psA, kk % 8) for kk in range(a0, a + 1)],
                     writes=[rg(wT, ("c", a0))], label=f"p.ev{a0}")
        wf3 = wfT[:, :].rearrange("p (t two) -> p two t", two=2)
        for two in range(2):
            S.op("vector",
                 lambda e, two=two: e.tensor_copy(wf3[:, two, :], wT[:, :]),
                 reads=[rg(wT, ("c", kk)) for kk in range(0, OT, 8)],
                 writes=[rg(wfT, two)], label=f"p.rep{two}")
        for (c0, cn) in _chunks(T, 512):
            S.mm(lambda e, c0=c0, cn=cn: e.matmul(
                    psP[0:32, 0:cn], Li[:, :], wfT[:, c0:c0 + cn],
                    start=True, stop=True),
                 reads=[rg(Li), rg(wfT, 0), rg(wfT, 1)], psum_rgn=rg(psP),
                 start=True, stop=True, label=f"p.mm{c0}")
            S.op("vector",
                 lambda e, c0=c0, cn=cn: e.tensor_copy(
                     pmx_sb[:, c0:c0 + cn], psP[0:32, 0:cn]),
                 reads=[rg(psP)], writes=[rg(pmx_sb)], label=f"p.pev{c0}")
        S.dma("sync", pmixT_out, pmx_sb[:, :], reads=[rg(pmx_sb)],
              writes=[rg(pmixT_out)], label="p.st")

    fns.append(seg_pmix)
    return fns



def build_pre0_launch(T, D, NBAT):
    nc = bass.Bass("TRN2", target_bir_lowering=False)
    aps = {}
    BD = D // 128
    TA = T // 2
    OT = TA // 128
    NAT = NBAT * OT
    NE, RK = 8, 32
    aps["x"] = nc.dram_tensor("x", [T, D], F32, kind="ExternalInput").ap()
    aps["x_anch"] = nc.dram_tensor("x_anch", [NBAT * TA, D], F32,
                          kind="ExternalInput").ap()
    aps["G0"] = nc.dram_tensor("G0", [D, 64], F32, kind="ExternalInput").ap()
    aps["b8_col"] = nc.dram_tensor("b8_col", [8, 1], F32, kind="ExternalInput").ap()
    aps["LiT"] = nc.dram_tensor("LiT", [32, 8], F32, kind="ExternalInput").ap()
    aps["Li"] = nc.dram_tensor("Li", [8, 32], F32, kind="ExternalInput").ap()
    aps["c_identbf"] = nc.dram_tensor("c_identbf", [128, 128], BF16,
                              kind="ExternalInput").ap()
    aps["c_identf32"] = nc.dram_tensor("c_identf32", [128, 128], F32,
                              kind="ExternalInput").ap()
    aps["xnT"] = nc.dram_tensor("xnT", [128, BD * T], BF16,
                             kind="ExternalOutput").ap()
    aps["pmixT"] = nc.dram_tensor("pmixT", [32, T], F32,
                               kind="ExternalOutput").ap()
    S = Sched(nc)
    with ExitStack() as stack:
        ph_pre0_ln(S, NCView(nc, ""), stack, aps, T, D)
        for fn in ph_anchors(S, NCView(nc, "anc_"), stack, aps, T, D, NBAT):
            fn()
        S.finalize()
    return nc


def ph_combine0(S, nc, stack, aps, T, D):
    BD = D // 128
    ACC = aps["ACC"]
    attnT_h = aps["attnT"]
    wc = aps["wc"]
    Bmat = aps["Bmat"]
    zT_h = aps["zT"]
    pmixT_h = aps["pmixT"]
    out1 = aps["out1"]
    Bm = stack.enter_context(nc.sbuf_tensor("Bm", [32, D], BF16))
    zp = stack.enter_context(nc.sbuf_tensor("zp", [32, T], BF16))
    with ExitStack() as zstack:
        zT = zstack.enter_context(nc.sbuf_tensor("zTs", [32, T], F32))
        pmx = zstack.enter_context(nc.sbuf_tensor("pmx", [32, T], F32))
        S.dma("sync", zT[:, :], zT_h, label="ldz")
        S.dma("sync", pmx[:, :], pmixT_h, label="ldpm")
        S.op("vector",
             lambda e: e.scalar_tensor_tensor(zp[:, :], zT[:, :], 0.5,
                                              pmx[:, :], ALU.mult, ALU.mult),
             reads=[rg(zT), rg(pmx)], writes=[rg(zp)], label="zp")
    aT = stack.enter_context(nc.sbuf_tensor("aT", [128, BD * T], BF16))
    ac = [stack.enter_context(nc.sbuf_tensor(f"ac{j}", [128, 256], F32))
          for j in range(3)]
    og = [stack.enter_context(nc.sbuf_tensor(f"og{j}", [128, 256], F32))
          for j in range(3)]
    S.dma("sync", aT[:, :], attnT_h, reads=[rg(attnT_h), rg(zp)],
          writes=[rg(aT)], label="ldaT")
    S.dma("gpsimd", Bm[:, :], Bmat, label="ldB")
    bi = [0]

    def pre(i, n0, nn):
        at = ac[bi[0] % 3]
        S.dma("sync", at[:, 0:nn], ACC[i * 128:(i + 1) * 128, n0:n0 + nn],
              reads=[rg(ACC)], writes=[rg(at)], label=f"c0.lda{i}.{n0}")

    def extra(i, n0, nn):
        return [(lambda e, p, i=i, n0=n0, nn=nn: e.matmul(
                    p[:, 0:nn], zp[:, i * 128:(i + 1) * 128],
                    Bm[:, n0:n0 + nn], start=False, stop=True),
                 [rg(zp), rg(Bm)])]

    def evac(i, n0, nn, p):
        at = ac[bi[0] % 3]
        o = og[bi[0] % 3]
        bi[0] += 1
        S.op("vector",
             lambda e, p=p, at=at, o=o, nn=nn: e.tensor_tensor(
                 o[:, 0:nn], p[:, 0:nn], at[:, 0:nn], ALU.add),
             reads=[rg(p), rg(at)], writes=[rg(o)], label=f"c0.ev{i}.{n0}")
        S.dma("sync", out1[i * 128:(i + 1) * 128, n0:n0 + nn], o[:, 0:nn],
              reads=[rg(o)], writes=[rg(out1, (i, n0))],
              label=f"c0.st{i}.{n0}")

    ph_mm_T(S, nc, stack, aT, BD, T, wc, D, "c0", evac,
            nchunk=256, extra_mm=extra, pre_cb=pre)


def build_combine0_launch(T, D):
    nc = bass.Bass("TRN2", target_bir_lowering=False)
    aps = {}
    BD = D // 128
    aps["ACC"] = nc.dram_tensor("ACC", [T, D], F32, kind="ExternalInput").ap()
    aps["attnT"] = nc.dram_tensor("attnT", [128, BD * T], BF16,
                             kind="ExternalInput").ap()
    aps["wc"] = nc.dram_tensor("wc", [D, D], F32, kind="ExternalInput").ap()
    aps["Bmat"] = nc.dram_tensor("Bmat", [32, D], F32, kind="ExternalInput").ap()
    aps["zT"] = nc.dram_tensor("zT", [32, T], F32, kind="ExternalInput").ap()
    aps["pmixT"] = nc.dram_tensor("pmixT", [32, T], F32, kind="ExternalInput").ap()
    aps["out1"] = nc.dram_tensor("out1", [T, D], F32, kind="ExternalOutput").ap()
    S = Sched(nc)
    with ExitStack() as stack:
        ph_combine0(S, NCView(nc, ""), stack, aps, T, D)
        S.finalize()
    return nc

def ph_attn(S, nc, stack, aps, T, NH, slopes):
    TT = T // 128
    qkT = aps["qkT"]
    kramp_h = aps["c_kramp"]
    maskd_h = aps["c_maskdiag"]
    ident_h = aps["c_identbf"]
    attnT_h = aps["attnT"]
    sbt = lambda n, shp, dt: stack.enter_context(nc.sbuf_tensor(n, shp, dt))
    pst = lambda n, shp, dt: stack.enter_context(nc.psum_tensor(n, shp, dt))
    kT = sbt("kT", [128, T], BF16)
    vF = sbt("vF", [128, T], BF16)
    vT = sbt("vT", [128, T], BF16)
    kramp = sbt("kramp", [128, T], F32)
    maskd = sbt("maskd", [128, 128], F32)
    identb = sbt("identb", [128, 128], BF16)
    ones_cb = sbt("ones_cb", [128, 1], BF16)
    ones_rf = sbt("ones_rf", [1, 128], F32)
    qb = [sbt(f"qb{j}", [128, T], BF16) for j in range(2)]
    q2 = [sbt(f"q2{j}", [128, T], BF16) for j in range(2)]
    Ssb = [sbt(f"Ssb{j}", [128, T], F32) for j in range(2)]
    Psb = [sbt(f"Psb{j}", [128, T], BF16) for j in range(4)]
    PT = [sbt(f"PT{j}", [128, T], BF16) for j in range(2)]
    stq = [sbt(f"ast{j}", [128, 8], F32) for j in range(4)]
    nb = [sbt(f"nb{j}", [1, 8], F32) for j in range(2)]
    negB = [sbt(f"negB{j}", [128, 1], F32) for j in range(2)]
    ah = [sbt(f"ah{j}", [128, T], BF16) for j in range(2)]
    scw = min(T, 2048)
    scb = pst("scb", [128, scw], F32)
    sc = [scb[:, c0:c0 + cn] for (c0, cn) in _chunks(scw, 512)]
    tp = [pst("tp0", [128, 1024], BF16)] * 2
    av = [pst("av0", [128, 512], F32)] * 2
    S.dma("sync", kT[:, :], qkT[:, NH * T:(NH + 1) * T], label="ldk")
    S.dma("sync", vF[:, :], qkT[:, (NH + 1) * T:(NH + 2) * T], label="ldv")
    S.dma("sync", kramp[:, :], kramp_h, label="ldkr")
    S.dma("sync", maskd[:, :], maskd_h, label="ldmd")
    S.dma("sync", identb[:, :], ident_h, label="ldid")
    S.op("gpsimd", lambda e: e.memset(ones_cb[:, :], 1.0),
         reads=[], writes=[rg(ones_cb)], label="onescb")
    S.op("gpsimd", lambda e: e.memset(ones_rf[:, :], 1.0),
         reads=[], writes=[rg(ones_rf)], label="onesrf")
    qiota = sbt("qiota", [128, 1], F32)
    S.op("gpsimd", lambda e: e.iota(qiota[:, :], [[0, 1]], base=0,
                                    channel_multiplier=1,
                                    allow_small_or_imprecise_dtypes=True),
         reads=[], writes=[rg(qiota)], label="qiota")

    def sq_norms_max(src, dst_sq, out_st, out_col, tag):
        S.op("vector",
             lambda e: e.tensor_tensor(dst_sq[:, :], src[:, :], src[:, :],
                                       ALU.mult),
             reads=[rg(src)], writes=[rg(dst_sq)], label=f"{tag}.sq")
        for ci, (c0, cn) in enumerate(_chunks(T, 512)):
            S.mm(lambda e, ci=ci, c0=c0, cn=cn: e.matmul(
                    sc[ci][0:1, 0:cn], ones_cb[:, :], dst_sq[:, c0:c0 + cn],
                    start=True, stop=True),
                 reads=[rg(ones_cb), rg(dst_sq)], psum_rgn=rg(scb, ci),
                 start=True, stop=True, label=f"{tag}.nm{ci}")
            S.op("vector",
                 lambda e, ci=ci, cn=cn: e.reduce_max(
                     out_st[0:1, out_col + ci:out_col + ci + 1],
                     sc[ci][0:1, 0:cn], axis=AX.X),
                 reads=[rg(scb, ci)], writes=[rg(out_st, out_col + ci)],
                 label=f"{tag}.mx{ci}")
        nchk = len(_chunks(T, 512))
        S.op("vector",
             lambda e, nchk=nchk: e.tensor_reduce(
                 out_st[0:1, out_col:out_col + 1],
                 out_st[0:1, out_col:out_col + nchk], AX.X, ALU.max),
             reads=[rg(out_st, out_col + ci) for ci in range(nchk)],
             writes=[rg(out_st, out_col)], label=f"{tag}.mxf")

    def transpose_blocks(src_sb, dst_sb, nblk, tpp, tag, evac_eng):
        for cb in range(nblk):
            S.op("tensor",
                 lambda e, cb=cb: e.transpose(
                     tpp[:, (cb % 8) * 128:(cb % 8 + 1) * 128],
                     src_sb[:, cb * 128:(cb + 1) * 128], identb[:, :]),
                 reads=[rg(src_sb), rg(identb)], writes=[rg(tpp, cb % 8)],
                 label=f"{tag}.tr{cb}")
            if cb % 8 == 7 or cb == nblk - 1:
                k0 = (cb // 8) * 8
                nbk = cb - k0 + 1
                if evac_eng == "scalar":
                    fn = lambda e, k0=k0, nbk=nbk: e.copy(
                        dst_sb[:, k0 * 128:(k0 + nbk) * 128],
                        tpp[:, 0:nbk * 128])
                else:
                    fn = lambda e, k0=k0, nbk=nbk: e.tensor_copy(
                        dst_sb[:, k0 * 128:(k0 + nbk) * 128],
                        tpp[:, 0:nbk * 128])
                S.op(evac_eng, fn,
                     reads=[rg(tpp, kk % 8) for kk in range(k0, cb + 1)],
                     writes=[rg(dst_sb, ("c", k0))],
                     label=f"{tag}.ev{k0}")

    transpose_blocks(vF, vT, TT, tp[0], "vtr", "vector")
    sq_norms_max(kT, q2[0], nb[0], 0, "knrm")

    def emit_A(h, qt, slope, q, hp):
        it = h * TT + qt
        Sb = Ssb[it % 2]
        Pb, st = Psb[it % 4], stq[it % 4]
        W = (qt + 1) * 128
        chs = _chunks(W, 512)
        nchs = len(chs)
        for ci, (c0, cn) in enumerate(chs):
            p = sc[ci]
            S.mm(lambda e, p=p, q=q, qt=qt, c0=c0, cn=cn: e.matmul(
                    p[:, 0:cn], q[:, qt * 128:(qt + 1) * 128],
                    kT[:, c0:c0 + cn], start=True, stop=True),
                 reads=[rg(q), rg(kT)], psum_rgn=rg(scb, ci),
                 start=True, stop=True, label=f"sc{h}.{qt}.{ci}")
        if slope != 0.0:
            S.op("vector",
                 lambda e, W=W, Sb=Sb: e.scalar_tensor_tensor(
                     Sb[:, 0:W], kramp[:, 0:W], slope,
                     scb[:, 0:W], ALU.mult, ALU.add),
                 reads=[rg(kramp)] + [rg(scb, ci) for ci in range(nchs)],
                 writes=[rg(Sb)] + [rg(Sb, ci) for ci in range(nchs)],
                 label=f"sev{h}.{qt}")
            S.op("vector",
                 lambda e, W=W, Sb=Sb: e.tensor_tensor(
                     Sb[:, W - 128:W], Sb[:, W - 128:W], maskd[:, :],
                     ALU.add),
                 reads=[rg(Sb, nchs - 1), rg(maskd)],
                 writes=[rg(Sb, ("d", nchs - 1))],
                 label=f"msk{h}.{qt}")
            S.op("vector",
                 lambda e, st=st, hp=hp, slope=slope:
                 e.scalar_tensor_tensor(st[:, 6:7], qiota[:, :], -slope,
                                        negB[hp][:, :], ALU.mult,
                                        ALU.add),
                 reads=[rg(qiota), rg(negB[hp])], writes=[rg(st, 6)],
                 label=f"bi1{h}.{qt}")
            S.op("vector",
                 lambda e, st=st, slope=slope, qt=qt: e.tensor_scalar(
                     st[:, 6:7], st[:, 6:7], -slope * qt * 128.0, None,
                     ALU.add),
                 reads=[rg(st, 6)], writes=[rg(st, 6)],
                 label=f"bi2{h}.{qt}")
        else:
            S.op("vector",
                 lambda e, W=W: e.tensor_tensor(
                     scb[:, W - 128:W], scb[:, W - 128:W],
                     maskd[:, :], ALU.add),
                 reads=[rg(scb, nchs - 1), rg(maskd)],
                 writes=[rg(scb, ("d", nchs - 1))],
                 label=f"msk{h}.{qt}")
            S.op("scalar",
                 lambda e, W=W, Pb=Pb, st=st, hp=hp: e.activation(
                     Pb[:, 0:W], scb[:, 0:W], AF.Exp,
                     bias=negB[hp][:, :], accum_out=st[:, 1:2]),
                 reads=[rg(negB[hp]), rg(scb, ("d", nchs - 1))]
                 + [rg(scb, ci) for ci in range(nchs)],
                 writes=[rg(Pb), rg(st, 1)], label=f"exp{h}.{qt}")

    def emit_Bx(h, qt, slope):
        if slope == 0.0:
            return
        it = h * TT + qt
        Sb = Ssb[it % 2]
        Pb, st = Psb[it % 4], stq[it % 4]
        W = (qt + 1) * 128
        nchs = len(_chunks(W, 512))
        S.op("scalar",
             lambda e, W=W, Sb=Sb, Pb=Pb, st=st: e.activation(
                 Pb[:, 0:W], Sb[:, 0:W], AF.Exp,
                 bias=st[:, 6:7], accum_out=st[:, 1:2]),
             reads=[rg(Sb), rg(Sb, ("d", nchs - 1)), rg(st, 6)]
             + [rg(Sb, ci) for ci in range(nchs)],
             writes=[rg(Pb), rg(st, 1)], label=f"exp{h}.{qt}")

    def emit_C(h, qt, slope):
        it = h * TT + qt
        Pb, st = Psb[it % 4], stq[it % 4]
        W = (qt + 1) * 128
        nchs = len(_chunks(W, 512))
        S.op("vector",
             lambda e, st=st: e.reciprocal(st[:, 0:1], st[:, 1:2]),
             reads=[rg(st, 1)], writes=[rg(st, 0)],
             label=f"rs{h}.{qt}")
        S.op("vector",
             lambda e, W=W, Pb=Pb, st=st: e.tensor_scalar(
                 Pb[:, 0:W], Pb[:, 0:W], st[:, 0:1], None, ALU.mult),
             reads=[rg(Pb), rg(st, 0)]
             + [rg(Pb, ci) for ci in range(nchs)],
             writes=[rg(Pb)] + [rg(Pb, ci) for ci in range(nchs)],
             label=f"nrm{h}.{qt}")

    def emit_D(h, qt, slope):
        it = h * TT + qt
        par = it % 2
        Pb, PTb = Psb[it % 4], PT[par]
        evac_eng = "scalar" if par == 0 else "vector"
        transpose_blocks(Pb, PTb, qt + 1, tp[par], f"ptr{h}.{qt}", evac_eng)
        pav = av[par]
        for cb in range(qt + 1):
            S.mm(lambda e, pav=pav, cb=cb, qt=qt, PTb=PTb: e.matmul(
                    pav[:, 0:128], vT[:, cb * 128:(cb + 1) * 128],
                    PTb[:, cb * 128:(cb + 1) * 128],
                    start=(cb == 0), stop=(cb == qt)),
                 reads=[rg(vT)] + [rg(PTb, ("c", kk))
                                   for kk in range(0, qt + 1, 8)],
                 psum_rgn=rg(pav), start=(cb == 0), stop=(cb == qt),
                 label=f"av{h}.{qt}.{cb}")

    def emit_E(h, qt, slope):
        it = h * TT + qt
        par = it % 2
        a = ah[h % 2]
        pav = av[par]
        eng = "scalar" if par == 0 else "vector"
        if eng == "scalar":
            fn = lambda e, a=a, pav=pav, qt=qt: e.copy(
                a[:, qt * 128:(qt + 1) * 128], pav[:, 0:128])
        else:
            fn = lambda e, a=a, pav=pav, qt=qt: e.tensor_copy(
                a[:, qt * 128:(qt + 1) * 128], pav[:, 0:128])
        S.op(eng, fn, reads=[rg(pav)], writes=[rg(a, qt)],
             label=f"aev{h}.{qt}")
        if qt == TT - 1:
            S.dma("sync", attnT_h[:, h * T:(h + 1) * T], a[:, :],
                  reads=[rg(a, qq) for qq in range(TT)],
                  writes=[rg(attnT_h, h)], label=f"sta{h}")

    def emit_head_prologue(h):
        hp = h % 2
        q = qb[hp]
        S.dma("sync", q[:, :], qkT[:, h * T:(h + 1) * T], label=f"ldq{h}")
        sq_norms_max(q, q2[hp], nb[hp], 4, f"qn{h}")
        S.op("vector",
             lambda e, hp=hp: e.tensor_tensor(nb[hp][0:1, 1:2],
                                              nb[hp][0:1, 4:5],
                                              nb[0][0:1, 0:1], ALU.mult),
             reads=[rg(nb[hp], 4), rg(nb[0], 0)], writes=[rg(nb[hp], 1)],
             label=f"bm{h}")
        S.op("vector",
             lambda e, hp=hp: e.tensor_scalar(nb[hp][0:1, 1:2],
                                              nb[hp][0:1, 1:2], 1e-30, None,
                                              ALU.add),
             reads=[rg(nb[hp], 1)], writes=[rg(nb[hp], 1)], label=f"bz{h}")
        S.op("scalar",
             lambda e, hp=hp: e.activation(nb[hp][0:1, 2:3],
                                           nb[hp][0:1, 1:2], AF.Ln),
             reads=[rg(nb[hp], 1)], writes=[rg(nb[hp], 2)], label=f"bl{h}")
        S.op("scalar",
             lambda e, hp=hp: e.activation(nb[hp][0:1, 3:4],
                                           nb[hp][0:1, 2:3], AF.Exp,
                                           scale=0.5),
             reads=[rg(nb[hp], 2)], writes=[rg(nb[hp], 3)], label=f"be{h}")
        S.op("vector",
             lambda e, hp=hp: e.tensor_scalar(nb[hp][0:1, 3:4],
                                              nb[hp][0:1, 3:4], -1.0, -2.0,
                                              ALU.mult, op1=ALU.add),
             reads=[rg(nb[hp], 3)], writes=[rg(nb[hp], 3)], label=f"bn{h}")
        S.mm(lambda e, hp=hp: e.matmul(scb[0:128, 0:1], ones_rf[:, :],
                                       nb[hp][0:1, 3:4],
                                       start=True, stop=True),
             reads=[rg(ones_rf), rg(nb[hp], 3)], psum_rgn=rg(scb, 0),
             start=True, stop=True, label=f"bb{h}")
        S.op("vector",
             lambda e, hp=hp: e.tensor_copy(negB[hp][:, :],
                                            scb[0:128, 0:1]),
             reads=[rg(scb, 0)], writes=[rg(negB[hp])], label=f"bv{h}")
        return q, hp

    stages = [(emit_Bx, 1), (emit_C, 2), (emit_E, 4), (emit_D, 3)]
    its = []
    for h in range(NH):
        for qt in range(TT):
            its.append((h, qt, float(slopes[h])))
    total = len(its)
    for g in range(total + 5):
        if g < total:
            h, qt, slope = its[g]
            if qt == 0:
                q, hp = emit_head_prologue(h)
            emit_A(h, qt, slope, qb[h % 2], h % 2)
        for fn, lag in stages:
            jj = g - lag
            if 0 <= jj < total:
                fn(*its[jj])


def ph_attn2(S, nc, stack, aps, T, NH, slopes):
    TT = T // 128
    qkT = aps["qkT"]
    maskd_h = aps["c_maskdiag"]
    ident_h = aps["c_identbf"]
    attnT_h = aps["attnT"]
    has_alibi = any(float(s) != 0.0 for s in slopes)
    sbt = lambda n, shp, dt: stack.enter_context(nc.sbuf_tensor(n, shp, dt))
    pst = lambda n, shp, dt: stack.enter_context(nc.psum_tensor(n, shp, dt))
    kT = sbt("kT", [128, T], BF16)
    vF = sbt("vF", [128, T], BF16)
    vT = sbt("vT", [128, T], BF16)
    maskd = sbt("maskd", [128, 128], F32)
    identb = sbt("identb", [128, 128], BF16)
    ones_cb = sbt("ones_cb", [128, 1], BF16)
    ones_rf = sbt("ones_rf", [1, 128], F32)
    qb = [sbt(f"qb{j}", [128, T], BF16) for j in range(2)]
    q2 = [sbt(f"q2{j}", [128, T], BF16) for j in range(2)]
    Psb = [sbt(f"Psb{j}", [128, T], BF16) for j in range(4)]
    PT = [sbt(f"PT{j}", [128, T], BF16) for j in range(2)]
    stq = [sbt(f"ast{j}", [128, 8], F32) for j in range(4)]
    nb = [sbt(f"nb{j}", [1, 8], F32) for j in range(2)]
    negBall = sbt("negBall", [128, NH], F32)
    ah = [sbt(f"ah{j}", [128, T], BF16) for j in range(2)]
    scb = pst("scb", [128, 1024], F32)
    sc = [scb[:, 0:512], scb[:, 512:1024]]
    tp = [pst(f"tp{j}", [128, 1024], BF16) for j in range(2)]
    av = [pst(f"av{j}", [128, 512], F32) for j in range(2)]
    if has_alibi:
        alib3_h = aps["alib3"]
        alb = [sbt(f"alb{j}", [3, T], BF16) for j in range(2)]
        ones3 = sbt("ones3", [3, 128], BF16)
        S.op("gpsimd", lambda e: e.memset(ones3[:, :], 1.0),
             reads=[], writes=[rg(ones3)], label="ones3")
    S.dma("sync", kT[:, :], qkT[:, NH * T:(NH + 1) * T], label="ldk")
    S.dma("sync", vF[:, :], qkT[:, (NH + 1) * T:(NH + 2) * T], label="ldv")
    S.dma("sync", maskd[:, :], maskd_h, label="ldmd")
    S.dma("sync", identb[:, :], ident_h, label="ldid")
    S.op("gpsimd", lambda e: e.memset(ones_cb[:, :], 1.0),
         reads=[], writes=[rg(ones_cb)], label="onescb")
    S.op("gpsimd", lambda e: e.memset(ones_rf[:, :], 1.0),
         reads=[], writes=[rg(ones_rf)], label="onesrf")
    qiota = sbt("qiota", [128, 1], F32)
    S.op("gpsimd", lambda e: e.iota(qiota[:, :], [[0, 1]], base=0,
                                    channel_multiplier=1,
                                    allow_small_or_imprecise_dtypes=True),
         reads=[], writes=[rg(qiota)], label="qiota")

    def sq_norms_max(src, dst_sq, out_st, out_col, tag):
        S.op("vector",
             lambda e: e.tensor_tensor(dst_sq[:, :], src[:, :], src[:, :],
                                       ALU.mult),
             reads=[rg(src)], writes=[rg(dst_sq)], label=f"{tag}.sq")
        for ci, (c0, cn) in enumerate(_chunks(T, 512)):
            S.mm(lambda e, ci=ci, c0=c0, cn=cn: e.matmul(
                    sc[ci % 2][0:1, 0:cn], ones_cb[:, :],
                    dst_sq[:, c0:c0 + cn], start=True, stop=True),
                 reads=[rg(ones_cb), rg(dst_sq)], psum_rgn=rg(scb, ci % 2),
                 start=True, stop=True, label=f"{tag}.nm{ci}")
            S.op("vector",
                 lambda e, ci=ci, cn=cn: e.reduce_max(
                     out_st[0:1, out_col + ci:out_col + ci + 1],
                     sc[ci % 2][0:1, 0:cn], axis=AX.X),
                 reads=[rg(scb, ci % 2)], writes=[rg(out_st, out_col + ci)],
                 label=f"{tag}.mx{ci}")
        nchk = len(_chunks(T, 512))
        S.op("vector",
             lambda e, nchk=nchk: e.tensor_reduce(
                 out_st[0:1, out_col:out_col + 1],
                 out_st[0:1, out_col:out_col + nchk], AX.X, ALU.max),
             reads=[rg(out_st, out_col + ci) for ci in range(nchk)],
             writes=[rg(out_st, out_col)], label=f"{tag}.mxf")

    def transpose_blocks(src_sb, dst_sb, nblk, base_par, tag):
        for cb in range(nblk):
            tpp = tp[(base_par + (cb // 8)) % 2]
            S.op("tensor",
                 lambda e, cb=cb, tpp=tpp: e.transpose(
                     tpp[:, (cb % 8) * 128:(cb % 8 + 1) * 128],
                     src_sb[:, cb * 128:(cb + 1) * 128], identb[:, :]),
                 reads=[rg(src_sb), rg(identb)], writes=[rg(tpp, cb % 8)],
                 label=f"{tag}.tr{cb}")
            if cb % 8 == 7 or cb == nblk - 1:
                k0 = (cb // 8) * 8
                nbk = cb - k0 + 1
                evac_eng = "scalar" if (base_par + (cb // 8)) % 2 == 0 \
                    else "vector"
                if evac_eng == "scalar":
                    fn = lambda e, k0=k0, nbk=nbk, tpp=tpp: e.copy(
                        dst_sb[:, k0 * 128:(k0 + nbk) * 128],
                        tpp[:, 0:nbk * 128])
                else:
                    fn = lambda e, k0=k0, nbk=nbk, tpp=tpp: e.tensor_copy(
                        dst_sb[:, k0 * 128:(k0 + nbk) * 128],
                        tpp[:, 0:nbk * 128])
                S.op(evac_eng, fn,
                     reads=[rg(tpp, kk % 8) for kk in range(k0, cb + 1)],
                     writes=[rg(dst_sb, ("c", k0))],
                     label=f"{tag}.ev{k0}")

    transpose_blocks(vF, vT, TT, 0, "vtr")
    sq_norms_max(kT, q2[0], nb[0], 0, "knrm")
    for h in range(NH):
        hp = h % 2
        q = qb[hp]
        S.dma("sync", q[:, :], qkT[:, h * T:(h + 1) * T], label=f"nq{h}")
        sq_norms_max(q, q2[hp], nb[hp], 4, f"qn{h}")
        S.op("vector",
             lambda e, hp=hp: e.tensor_tensor(nb[hp][0:1, 1:2],
                                              nb[hp][0:1, 4:5],
                                              nb[0][0:1, 0:1], ALU.mult),
             reads=[rg(nb[hp], 4), rg(nb[0], 0)], writes=[rg(nb[hp], 1)],
             label=f"bm{h}")
        S.op("vector",
             lambda e, hp=hp: e.tensor_scalar(nb[hp][0:1, 1:2],
                                              nb[hp][0:1, 1:2], 1e-30, None,
                                              ALU.add),
             reads=[rg(nb[hp], 1)], writes=[rg(nb[hp], 1)], label=f"bz{h}")
        S.op("scalar",
             lambda e, hp=hp: e.activation(nb[hp][0:1, 2:3],
                                           nb[hp][0:1, 1:2], AF.Ln),
             reads=[rg(nb[hp], 1)], writes=[rg(nb[hp], 2)], label=f"bl{h}")
        S.op("scalar",
             lambda e, hp=hp: e.activation(nb[hp][0:1, 3:4],
                                           nb[hp][0:1, 2:3], AF.Exp,
                                           scale=0.5),
             reads=[rg(nb[hp], 2)], writes=[rg(nb[hp], 3)], label=f"be{h}")
        S.op("vector",
             lambda e, hp=hp: e.tensor_scalar(nb[hp][0:1, 3:4],
                                              nb[hp][0:1, 3:4], -1.0, -2.0,
                                              ALU.mult, op1=ALU.add),
             reads=[rg(nb[hp], 3)], writes=[rg(nb[hp], 3)], label=f"bn{h}")
        S.mm(lambda e, hp=hp: e.matmul(scb[0:128, 0:1], ones_rf[:, :],
                                       nb[hp][0:1, 3:4],
                                       start=True, stop=True),
             reads=[rg(ones_rf), rg(nb[hp], 3)], psum_rgn=rg(scb, 0),
             start=True, stop=True, label=f"bb{h}")
        S.op("vector",
             lambda e, h=h: e.tensor_copy(negBall[:, h:h + 1],
                                          scb[0:128, 0:1]),
             reads=[rg(scb, 0)], writes=[rg(negBall, h)], label=f"bv{h}")

    def emit_A(h, qt, slope, q, hp):
        it = h * TT + qt
        st = stq[it % 4]
        Pb = Psb[it % 4]
        W = (qt + 1) * 128
        chs = _chunks(W, 512)
        nchs = len(chs)
        if slope != 0.0:
            S.op("vector",
                 lambda e, st=st, h=h, slope=slope:
                 e.scalar_tensor_tensor(st[:, 6:7], qiota[:, :], -slope,
                                        negBall[:, h:h + 1], ALU.mult,
                                        ALU.add),
                 reads=[rg(qiota), rg(negBall, h)], writes=[rg(st, 6)],
                 label=f"bi1{h}.{qt}")
            S.op("vector",
                 lambda e, st=st, slope=slope, qt=qt: e.tensor_scalar(
                     st[:, 6:7], st[:, 6:7], -slope * qt * 128.0, None,
                     ALU.add),
                 reads=[rg(st, 6)], writes=[rg(st, 6)],
                 label=f"bi2{h}.{qt}")
            bias = st[:, 6:7]
            bias_r = rg(st, 6)
        else:
            bias = negBall[:, h:h + 1]
            bias_r = rg(negBall, h)
        for ci, (c0, cn) in enumerate(chs):
            b = (it + ci) % 2
            p = sc[b]
            S.mm(lambda e, p=p, q=q, qt=qt, c0=c0, cn=cn: e.matmul(
                    p[:, 0:cn], q[:, qt * 128:(qt + 1) * 128],
                    kT[:, c0:c0 + cn], start=True, stop=(slope == 0.0)),
                 reads=[rg(q), rg(kT)], psum_rgn=rg(scb, b),
                 start=True, stop=(slope == 0.0), label=f"sc{h}.{qt}.{ci}")
            if slope != 0.0:
                S.mm(lambda e, p=p, hp=hp, c0=c0, cn=cn: e.matmul(
                        p[:, 0:cn], ones3[:, :],
                        alb[hp][:, c0:c0 + cn],
                        start=False, stop=True),
                     reads=[rg(ones3), rg(alb[hp])], psum_rgn=rg(scb, b),
                     start=False, stop=True, label=f"al{h}.{qt}.{ci}")
            last = ci == nchs - 1
            if last:
                S.op("vector",
                     lambda e, p=p, cn=cn: e.tensor_tensor(
                         p[:, cn - 128:cn], p[:, cn - 128:cn],
                         maskd[:, :], ALU.add),
                     reads=[rg(scb, b), rg(maskd)],
                     writes=[rg(scb, ("d", b))],
                     label=f"msk{h}.{qt}")
            S.op("scalar",
                 lambda e, p=p, Pb=Pb, st=st, ci=ci, c0=c0, cn=cn,
                 bias=bias: e.activation(
                     Pb[:, c0:c0 + cn], p[:, 0:cn], AF.Exp,
                     bias=bias, accum_out=st[:, 1 + ci:2 + ci]),
                 reads=[bias_r, rg(scb, b)]
                 + ([rg(scb, ("d", b))] if last else []),
                 writes=[rg(Pb, ("e", ci)), rg(st, 1 + ci)],
                 label=f"exp{h}.{qt}.{ci}")

    def emit_C(h, qt, slope):
        it = h * TT + qt
        Pb, st = Psb[it % 4], stq[it % 4]
        W = (qt + 1) * 128
        nchs = len(_chunks(W, 512))
        if nchs > 1:
            S.op("vector",
                 lambda e, st=st, nchs=nchs: e.tensor_reduce(
                     st[:, 5:6], st[:, 1:1 + nchs], AX.X, ALU.add),
                 reads=[rg(st, 1 + ci) for ci in range(nchs)],
                 writes=[rg(st, 5)], label=f"sms{h}.{qt}")
            sum_col, sum_r = 5, rg(st, 5)
        else:
            sum_col, sum_r = 1, rg(st, 1)
        S.op("vector",
             lambda e, st=st, sum_col=sum_col: e.reciprocal(
                 st[:, 0:1], st[:, sum_col:sum_col + 1]),
             reads=[sum_r], writes=[rg(st, 0)],
             label=f"rs{h}.{qt}")
        S.op("vector",
             lambda e, W=W, Pb=Pb, st=st: e.tensor_scalar(
                 Pb[:, 0:W], Pb[:, 0:W], st[:, 0:1], None, ALU.mult),
             reads=[rg(Pb), rg(st, 0)],
             writes=[rg(Pb)],
             label=f"nrm{h}.{qt}")

    def emit_D(h, qt, slope):
        it = h * TT + qt
        par = it % 2
        Pb, PTb = Psb[it % 4], PT[par]
        transpose_blocks(Pb, PTb, qt + 1, par, f"ptr{h}.{qt}")
        pav = av[par]
        for cb in range(qt + 1):
            S.mm(lambda e, pav=pav, cb=cb, qt=qt, PTb=PTb: e.matmul(
                    pav[:, 0:128], vT[:, cb * 128:(cb + 1) * 128],
                    PTb[:, cb * 128:(cb + 1) * 128],
                    start=(cb == 0), stop=(cb == qt)),
                 reads=[rg(vT)] + [rg(PTb, ("c", kk))
                                   for kk in range(0, qt + 1, 8)],
                 psum_rgn=rg(pav), start=(cb == 0), stop=(cb == qt),
                 label=f"av{h}.{qt}.{cb}")

    def emit_E(h, qt, slope):
        it = h * TT + qt
        par = it % 2
        a = ah[h % 2]
        pav = av[par]
        eng = "scalar" if par == 0 else "vector"
        if eng == "scalar":
            fn = lambda e, a=a, pav=pav, qt=qt: e.copy(
                a[:, qt * 128:(qt + 1) * 128], pav[:, 0:128])
        else:
            fn = lambda e, a=a, pav=pav, qt=qt: e.tensor_copy(
                a[:, qt * 128:(qt + 1) * 128], pav[:, 0:128])
        S.op(eng, fn, reads=[rg(pav)], writes=[rg(a, qt)],
             label=f"aev{h}.{qt}")
        if qt == TT - 1:
            S.dma("sync", attnT_h[:, h * T:(h + 1) * T], a[:, :],
                  reads=[rg(a, qq) for qq in range(TT)],
                  writes=[rg(attnT_h, h)], label=f"sta{h}")

    def emit_head_prologue(h):
        hp = h % 2
        q = qb[hp]
        S.dma("sync", q[:, :], qkT[:, h * T:(h + 1) * T], label=f"ldq{h}")
        if has_alibi and float(slopes[h]) != 0.0:
            S.dma("sync", alb[hp][:, :], alib3_h[3 * h:3 * h + 3, :],
                  reads=[rg(alib3_h)], writes=[rg(alb[hp])],
                  label=f"ldal{h}")
        return q, hp

    stages = [(emit_C, 1), (emit_D, 2), (emit_E, 3)]
    its = []
    for h in range(NH):
        for qt in range(TT):
            its.append((h, qt, float(slopes[h])))
    total = len(its)
    qhp = {}
    for g in range(total + 5):
        if g < total:
            h, qt, slope = its[g]
            if g == 0:
                qhp[0] = emit_head_prologue(0)
            if qt == TT - 2 and h + 1 < NH:
                qhp[h + 1] = emit_head_prologue(h + 1)
            q, hp = qhp[h]
            emit_A(h, qt, slope, q, hp)
        for fn, lag in stages:
            jj = g - lag
            if 0 <= jj < total:
                fn(*its[jj])


def build_attn_launch(T, NH, slopes):
    nc = bass.Bass("TRN2", target_bir_lowering=False)
    aps = {}
    TT = T // 128
    aps["qkT"] = nc.dram_tensor("qkT", [128, (NH + 2) * T], BF16,
                         kind="ExternalInput").ap()
    aps["c_kramp"] = nc.dram_tensor("c_kramp", [128, T], F32,
                             kind="ExternalInput").ap()
    aps["c_maskdiag"] = nc.dram_tensor("c_maskdiag", [128, 128], F32,
                             kind="ExternalInput").ap()
    aps["c_identbf"] = nc.dram_tensor("c_identbf", [128, 128], BF16,
                             kind="ExternalInput").ap()
    aps["attnT"] = nc.dram_tensor("attnT", [128, NH * T], BF16,
                             kind="ExternalOutput").ap()
    S = Sched(nc)
    with ExitStack() as stack:
        ph_attn(S, NCView(nc, ""), stack, aps, T, NH, slopes)
        S.finalize()
    return nc


def ph_lnL(S, nc, stack, aps, T, D):
    BD = D // 128
    x = aps["x"]
    ident = aps["c_identbf"]
    xnT_h = aps["xnT"]
    ident_bf = stack.enter_context(nc.sbuf_tensor("idb", [128, 128], BF16))
    S.dma("sync", ident_bf[:, :], ident, label="ldid")
    C = ConstPool(S, nc, stack)
    ph_ln_to_xnT(S, nc, stack, x, None, ident_bf, T, D, "ln", C,
                 sink=mk_sink_hbm(S, nc, stack, xnT_h, T, "lnsk"))


def build_ln_launch(T, D):
    nc = bass.Bass("TRN2", target_bir_lowering=False)
    aps = {}
    BD = D // 128
    aps["x"] = nc.dram_tensor("x", [T, D], F32, kind="ExternalInput").ap()
    aps["c_identbf"] = nc.dram_tensor("c_identbf", [128, 128], BF16,
                           kind="ExternalInput").ap()
    aps["xnT"] = nc.dram_tensor("xnT", [128, BD * T], BF16,
                           kind="ExternalOutput").ap()
    S = Sched(nc)
    with ExitStack() as stack:
        ph_lnL(S, NCView(nc, ""), stack, aps, T, D)
        S.finalize()
    return nc


def ph_qkv(S, nc, stack, aps, T, D, NB):
    BD = D // 128
    c1 = float(128 ** -0.5)
    xnT_h = aps["xnT"]
    wqkv = aps["wqkv"]
    qkT_h = aps["qkT"]
    xnT_sb = stack.enter_context(
        nc.sbuf_tensor("xnT_sb", [128, BD * T], BF16))
    stg = [stack.enter_context(nc.sbuf_tensor(f"stg{j}", [128, 512], BF16))
           for j in range(2)]
    S.dma("sync", xnT_sb[:, :], xnT_h, label="ldxnT")
    si = [0]

    def evac(m, c0, cn, p):
        s = stg[si[0] % 2]
        si[0] += 1
        scale = c1 if m < NB - 2 else 1.0
        S.op("vector",
             lambda e, s=s, p=p, cn=cn, scale=scale: e.tensor_scalar(
                 s[:, 0:cn], p[:, 0:cn], scale, None, ALU.mult),
             reads=[rg(p)], writes=[rg(s)], label=f"qkv.ev{m}.{c0}")
        S.dma("sync", qkT_h[:, m * T + c0:m * T + c0 + cn], s[:, 0:cn],
              reads=[rg(s)], writes=[rg(qkT_h, m)], label=f"qkv.st{m}.{c0}")

    ph_mm_F(S, nc, stack, wqkv, D, NB * 128, xnT_sb, T, "qkv", evac)


def build_qkv_launch(T, D, NB):
    nc = bass.Bass("TRN2", target_bir_lowering=False)
    aps = {}
    BD = D // 128
    aps["xnT"] = nc.dram_tensor("xnT", [128, BD * T], BF16,
                           kind="ExternalInput").ap()
    aps["wqkv"] = nc.dram_tensor("wqkv", [D, NB * 128], F32,
                          kind="ExternalInput").ap()
    aps["qkT"] = nc.dram_tensor("qkT", [128, NB * T], BF16,
                           kind="ExternalOutput").ap()
    c1 = float(128 ** -0.5)
    S = Sched(nc)
    with ExitStack() as stack:
        ph_qkv(S, NCView(nc, ""), stack, aps, T, D, NB)
        S.finalize()
    return nc


def ph_h0(S, nc, stack, aps, T, D):
    BD = D // 128
    xnT_h = aps["xnT"]
    WT = aps["WT"]
    b_row = aps["b_row"]
    A0 = aps["A0"]
    x = aps["x"]
    ACC = aps["ACC"]
    zT_h = aps["zT"]
    xnT_sb = stack.enter_context(
        nc.sbuf_tensor("xnT_sb", [128, BD * T], BF16))
    ones_bf = stack.enter_context(nc.sbuf_tensor("onesb", [1, 128], BF16))
    b_bf = stack.enter_context(nc.sbuf_tensor("b_bf", [1, D], BF16))
    ab = stack.enter_context(nc.sbuf_tensor("ab", [128, BD * 32], BF16))
    zps = stack.enter_context(nc.psum_tensor("zps", [32, 512], F32))
    zT_sb = stack.enter_context(nc.sbuf_tensor("zT_sb", [32, T], F32))
    xc = [stack.enter_context(nc.sbuf_tensor(f"xc{j}", [128, 256], F32))
          for j in range(3)]
    og = [stack.enter_context(nc.sbuf_tensor(f"og{j}", [128, 256], F32))
          for j in range(3)]
    S.dma("sync", xnT_sb[:, :], xnT_h, label="ldxnT")
    S.dma("gpsimd", b_bf[:, :], b_row, label="ldb")
    S.dma("gpsimd",
          ab[:, :].rearrange("p (k j) -> p k j", j=32),
          A0.rearrange("(k p) j -> p k j", p=128),
          reads=[rg(A0)], writes=[rg(ab)], label="lda")
    S.op("gpsimd", lambda e: e.memset(ones_bf[:, :], 1.0),
         reads=[], writes=[rg(ones_bf)], label="ones")
    xnT3 = xnT_sb[:, :].rearrange("p (k t) -> p k t", t=T)
    for (c0, cn) in _chunks(T, 512):
        for k in range(BD):
            S.mm(lambda e, k=k, c0=c0, cn=cn: e.matmul(
                    zps[:, 0:cn], ab[:, k * 32:(k + 1) * 32],
                    xnT3[:, k, c0:c0 + cn],
                    start=(k == 0), stop=(k == BD - 1)),
                 reads=[rg(ab), rg(xnT_sb)], psum_rgn=rg(zps),
                 start=(k == 0), stop=(k == BD - 1), label=f"z.mm{c0}.{k}")
        S.op("vector",
             lambda e, c0=c0, cn=cn: e.tensor_copy(
                 zT_sb[:, c0:c0 + cn], zps[0:32, 0:cn]),
             reads=[rg(zps)], writes=[rg(zT_sb)], label=f"z.ev{c0}")
    S.dma("sync", zT_h, zT_sb[:, :], reads=[rg(zT_sb)], writes=[rg(zT_h)],
          label="z.st")
    bi = [0]

    def pre(i, n0, nn):
        xt = xc[bi[0] % 3]
        S.dma("sync", xt[:, 0:nn], x[i * 128:(i + 1) * 128, n0:n0 + nn],
              reads=[rg(x)], writes=[rg(xt)], label=f"h.ldx{i}.{n0}")

    def extra(i, n0, nn):
        return [(lambda e, p, n0=n0, nn=nn: e.matmul(
                    p[:, 0:nn], ones_bf[0:1, :], b_bf[0:1, n0:n0 + nn],
                    start=False, stop=True),
                 [rg(ones_bf), rg(b_bf)])]

    def evac(i, n0, nn, p):
        xt = xc[bi[0] % 3]
        o = og[bi[0] % 3]
        bi[0] += 1
        S.op("vector",
             lambda e, p=p, xt=xt, o=o, nn=nn: e.tensor_tensor(
                 o[:, 0:nn], p[:, 0:nn], xt[:, 0:nn], ALU.add),
             reads=[rg(p), rg(xt)], writes=[rg(o)],
             label=f"h.ev{i}.{n0}")
        S.dma("sync", ACC[i * 128:(i + 1) * 128, n0:n0 + nn], o[:, 0:nn],
              reads=[rg(o)], writes=[rg(ACC, (i, n0))],
              label=f"h.st{i}.{n0}")

    ph_mm_T(S, nc, stack, xnT_sb, BD, T, WT, D, "h", evac,
            nchunk=256, extra_mm=extra, pre_cb=pre)


def build_h0_launch(T, D):
    nc = bass.Bass("TRN2", target_bir_lowering=False)
    aps = {}
    BD = D // 128
    aps["xnT"] = nc.dram_tensor("xnT", [128, BD * T], BF16,
                           kind="ExternalInput").ap()
    aps["WT"] = nc.dram_tensor("WT", [D, D], F32, kind="ExternalInput").ap()
    aps["b_row"] = nc.dram_tensor("b_row", [1, D], F32, kind="ExternalInput").ap()
    aps["A0"] = nc.dram_tensor("A0", [D, 32], F32, kind="ExternalInput").ap()
    aps["x"] = nc.dram_tensor("x", [T, D], F32, kind="ExternalInput").ap()
    aps["ACC"] = nc.dram_tensor("ACC", [T, D], F32, kind="ExternalOutput").ap()
    aps["zT"] = nc.dram_tensor("zT", [32, T], F32, kind="ExternalOutput").ap()
    S = Sched(nc)
    with ExitStack() as stack:
        ph_h0(S, NCView(nc, ""), stack, aps, T, D)
        S.finalize()
    return nc



# version 23
# speedup vs baseline: 1.0427x; 1.0132x over previous
import numpy as np
import ml_dtypes
from contextlib import ExitStack

import concourse.bass as bass
import concourse.mybir as mybir
from concourse.bass_utils import run_bass_kernel_spmd

F32 = mybir.dt.float32
BF16 = mybir.dt.bfloat16
AF = mybir.ActivationFunctionType
ALU = mybir.AluOpType
AX = mybir.AxisListType

ENGINES = ("tensor", "vector", "scalar", "gpsimd", "sync")


def _rname(x):
    if isinstance(x, str):
        return x
    name = getattr(x, "name", None)
    if name is None:
        t = getattr(x, "tensor", None)
        name = t.name
    return name


def rg(t, key=None):
    return (_rname(t), key)


class _Rec:
    __slots__ = ("engine", "fn", "waits", "clock", "completion", "mark",
                 "dma_inc", "label", "seq")

    def __init__(self, engine, fn, label):
        self.engine = engine
        self.fn = fn
        self.waits = {}
        self.clock = {}
        self.completion = None
        self.mark = False
        self.dma_inc = None
        self.label = label
        self.seq = -1


class Sched:
    DMA_SLOTS = 8

    def __init__(self, nc):
        self.nc = nc
        self.recs = []
        self.wr = {}
        self.rd = {}
        self.eng_count = {e: 0 for e in ENGINES}
        self.last_clock = {e: {} for e in ENGINES}
        self.pending_barrier = {e: None for e in ENGINES}
        self.dma_slot_i = {}
        self.dma_slot_uses = {}
        self.open_group_reads = None
        self.group_reads = []
        self.group_psum = None

    def _conflict_writers(self, name, key):
        d = self.wr.get(name)
        if not d:
            return ()
        if key is None:
            return list(d.values())
        out = []
        if key in d:
            out.append(d[key])
        if None in d:
            out.append(d[None])
        return out

    def _conflict_readers(self, name, key):
        d = self.rd.get(name)
        if not d:
            return ()
        if key is None:
            out = []
            for lst in d.values():
                out.extend(lst)
            return out
        out = list(d.get(key, ()))
        out.extend(d.get(None, ()))
        return out

    def _edge(self, rec, prod):
        if prod is None or prod is rec:
            return
        comp = prod.completion
        assert comp is not None, (
            f"dependency on {prod.label} which has no completion event "
            f"(needed by {rec.label})")
        sem, val = comp
        c = rec.clock
        if c.get(sem, 0) < val:
            w = rec.waits
            if w.get(sem, 0) < val:
                w[sem] = val
        for k, v in prod.clock.items():
            if c.get(k, 0) < v:
                c[k] = v
        if c.get(sem, 0) < val:
            c[sem] = val

    def _reads(self, rec, reads, register=True):
        for name, key in reads:
            if self.group_psum is not None:
                gname, gkey = self.group_psum
                assert not (name == gname and (key == gkey or key is None or
                                               gkey is None)), \
                    f"{rec.label}: read of open matmul-group psum {name}/{key}"
            for w in self._conflict_writers(name, key):
                self._edge(rec, w)
            if register:
                self.rd.setdefault(name, {}).setdefault(key, []).append(rec)

    def _writes(self, rec, writes, register=True):
        for name, key in writes:
            if self.open_group_reads is not None:
                assert (name, key) not in self.open_group_reads and \
                       (name, None) not in self.open_group_reads, \
                    f"write to {name}/{key} while matmul group reading it is open"
            for w in self._conflict_writers(name, key):
                self._edge(rec, w)
            for r in self._conflict_readers(name, key):
                self._edge(rec, r)
            if register:
                wd = self.wr.setdefault(name, {})
                rd = self.rd.setdefault(name, {})
                if key is None:
                    wd.clear()
                    rd.clear()
                else:
                    rd.pop(key, None)
                wd[key] = rec

    def _new_rec(self, engine, fn, label):
        rec = _Rec(engine, fn, label)
        rec.clock = dict(self.last_clock[engine])
        pb = self.pending_barrier[engine]
        if pb is not None:
            for comp in pb:
                sem, val = comp
                if rec.clock.get(sem, 0) < val:
                    rec.waits[sem] = max(rec.waits.get(sem, 0), val)
                    rec.clock[sem] = val
            self.pending_barrier[engine] = None
        return rec

    def op(self, engine, fn, reads=(), writes=(), label=""):
        rec = self._new_rec(engine, fn, label)
        self._reads(rec, reads)
        self._writes(rec, writes, register=False)
        self.eng_count[engine] += 1
        rec.completion = (("E", engine), self.eng_count[engine])
        rec.mark = True
        for name, key in writes:
            wd = self.wr.setdefault(name, {})
            rd = self.rd.setdefault(name, {})
            if key is None:
                wd.clear()
                rd.clear()
            else:
                rd.pop(key, None)
            wd[key] = rec
        self.last_clock[engine] = rec.clock
        rec.seq = len(self.recs)
        self.recs.append(rec)
        self._bg_tick()
        return rec

    def dma(self, engine, out_ap, in_ap, reads=None, writes=None, label="", **kw):
        if reads is None:
            reads = [rg(in_ap)]
        if writes is None:
            writes = [rg(out_ap)]
        slot = self.dma_slot_i.get(engine, 0)
        self.dma_slot_i[engine] = (slot + 1) % self.DMA_SLOTS
        uses = self.dma_slot_uses.get((engine, slot), 0) + 1
        self.dma_slot_uses[(engine, slot)] = uses
        semkey = ("D", engine, slot)

        def fn(eng, out_ap=out_ap, in_ap=in_ap, kw=kw):
            return eng.dma_start(out_ap, in_ap, **kw)

        rec = self._new_rec(engine, fn, label)
        if uses > 1 and rec.clock.get(semkey, 0) < 16 * (uses - 1):
            rec.waits[semkey] = max(rec.waits.get(semkey, 0), 16 * (uses - 1))
            rec.clock[semkey] = 16 * (uses - 1)
        rec.completion = (semkey, 16 * uses)
        rec.dma_inc = (semkey, 16)
        self._reads(rec, reads)
        self._writes(rec, writes)
        self.last_clock[rec.engine] = rec.clock
        rec.seq = len(self.recs)
        self.recs.append(rec)
        self._bg_tick()
        return rec

    def mm(self, fn, reads=(), psum_rgn=None, start=False, stop=False, label=""):
        rec = self._new_rec("tensor", fn, label)
        if start:
            assert self.group_psum is None, "nested matmul group"
            self.group_psum = psum_rgn
            self.group_reads = []
            self.open_group_reads = set()
            self._writes(rec, [psum_rgn], register=False)
        self._reads(rec, reads, register=False)
        self.group_reads.extend(reads)
        self.open_group_reads.update(reads)
        if stop:
            self.eng_count["tensor"] += 1
            rec.completion = (("E", "tensor"), self.eng_count["tensor"])
            rec.mark = True
            name, key = self.group_psum
            self.wr.setdefault(name, {})[key] = rec
            self.rd.setdefault(name, {}).pop(key, None)
            for rname, rkey in self.group_reads:
                self.rd.setdefault(rname, {}).setdefault(rkey, []).append(rec)
            self.group_psum = None
            self.group_reads = []
            self.open_group_reads = None
        self.last_clock[rec.engine] = rec.clock
        rec.seq = len(self.recs)
        self.recs.append(rec)
        return rec

    def add_background(self, fns, every=8):
        from collections import deque
        if not hasattr(self, "_bg"):
            self._bg = deque()
            self._bg_count = 0
            self._bg_active = False
        self._bg_every = every
        self._bg.extend(fns)

    def _bg_tick(self):
        if not getattr(self, "_bg", None) or getattr(self, "_bg_active", False):
            return
        self._bg_count += 1
        if self._bg_count % self._bg_every == 0:
            self._bg_pending = True
        if getattr(self, "_bg_pending", False) and self.group_psum is None:
            self._bg_active = True
            try:
                fn = self._bg.popleft()
                fn()
            finally:
                self._bg_active = False
            self._bg_pending = False

    def drain_background(self):
        while getattr(self, "_bg", None):
            self._bg_active = True
            try:
                fn = self._bg.popleft()
                fn()
            finally:
                self._bg_active = False

    def barrier(self):
        comps = []
        for e in ENGINES:
            if self.eng_count[e] > 0:
                comps.append((("E", e), self.eng_count[e]))
        for (eng, slot), uses in self.dma_slot_uses.items():
            comps.append((("D", eng, slot), 16 * uses))
        for e in ENGINES:
            prev = self.pending_barrier[e] or []
            self.pending_barrier[e] = list(prev) + comps

    def finalize(self):
        nc = self.nc
        semkeys = set()
        for rec in self.recs:
            semkeys.update(rec.waits.keys())
            if rec.completion is not None:
                semkeys.add(rec.completion[0])
        final_waits = {}
        for e in ENGINES:
            if self.eng_count[e] > 0 and e != "sync":
                final_waits[("E", e)] = self.eng_count[e]
                semkeys.add(("E", e))
        for (eng, slot), uses in self.dma_slot_uses.items():
            final_waits[("D", eng, slot)] = 16 * uses
            semkeys.add(("D", eng, slot))

        with ExitStack() as stack:
            sems = {}
            for k in sorted(semkeys, key=repr):
                sems[k] = stack.enter_context(
                    nc.semaphore("s_" + "_".join(str(p) for p in k)))
            block = stack.enter_context(nc.Block())
            by_eng = {e: [r for r in self.recs if r.engine == e]
                      for e in ENGINES}

            def make_section(e):
                recs = by_eng[e]

                def section(eng):
                    for rec in recs:
                        for sem, val in sorted(rec.waits.items(), key=repr):
                            eng.wait_ge(sems[sem], val)
                        try:
                            ins = rec.fn(eng)
                        except Exception:
                            print(f"[Sched] failing instruction: {rec.label}")
                            raise
                        if rec.dma_inc is not None:
                            ins.then_inc(sems[rec.dma_inc[0]], rec.dma_inc[1])
                        elif rec.mark:
                            ins.then_inc(sems[rec.completion[0]], 1)
                    if e == "sync":
                        for sem, val in sorted(final_waits.items(), key=repr):
                            eng.wait_ge(sems[sem], val)
                return section

            for e in ENGINES:
                if by_eng[e] or e == "sync":
                    getattr(block, e)(make_section(e))


class ConstPool:

    def __init__(self, S, nc, stack, n=16, tag="cst"):
        self.S = S
        self.t = stack.enter_context(nc.sbuf_tensor(tag, [128, n], F32))
        self.cols = {}
        self.n = n

    def __call__(self, value):
        value = float(value)
        if value not in self.cols:
            j = len(self.cols)
            assert j < self.n
            self.cols[value] = j
            self.S.op("vector",
                      lambda e, j=j, value=value: e.memset(self.t[:, j:j + 1],
                                                           value),
                      reads=[], writes=[rg(self.t, j)],
                      label=f"cst.{value}")
        j = self.cols[value]
        return self.t[:, j:j + 1]

    def r(self, value):
        self(value)
        return (self.t.name, self.cols[float(value)])



LN_EPS = 1e-5


def mk_ln_bufs(nc, stack, D, tag, nps=2):
    return dict(
        xa=[stack.enter_context(nc.sbuf_tensor(f"{tag}_x{j}", [128, D], F32))
            for j in range(2)],
        junk_s=stack.enter_context(
            nc.sbuf_tensor(f"{tag}_junks", [128, D], BF16)),
        junk_v=stack.enter_context(
            nc.sbuf_tensor(f"{tag}_junkv", [128, D], BF16)),
        xnb=[stack.enter_context(
            nc.sbuf_tensor(f"{tag}_xnb{j}", [128, D], BF16))
            for j in range(2)],
        st=stack.enter_context(nc.sbuf_tensor(f"{tag}_st", [128, 8], F32)),
        ps=[stack.enter_context(
            nc.psum_tensor(f"{tag}_pt{j}", [128, 1024], BF16))
            for j in range(nps)],
    )


def mk_sink_sbuf(S, xnT_sb, T, tag):
    def sink(i, k0, nblk, pst, reads):
        dst = xnT_sb[:, :].rearrange("p (k t) -> p k t", t=T)[
            :, k0:k0 + nblk, i * 128:(i + 1) * 128]
        src = pst[:, :].rearrange("p (b j) -> p b j", j=128)[:, 0:nblk, :]
        S.op("vector",
             lambda e, dst=dst, src=src: e.tensor_copy(dst, src),
             reads=reads, writes=[rg(xnT_sb, ("c", i))],
             label=f"{tag}.ev{i}.{k0}")
    return sink


def mk_sink_hbm(S, nc, stack, xnT_hbm, T, tag):
    stg = [stack.enter_context(
        nc.sbuf_tensor(f"{tag}_sg{j}", [128, 1024], BF16)) for j in range(2)]
    cnt = [0]

    def sink(i, k0, nblk, pst, reads):
        s = stg[cnt[0] % 2]
        cnt[0] += 1
        S.op("vector",
             lambda e, s=s, pst=pst, nblk=nblk: e.tensor_copy(
                 s[:, 0:nblk * 128], pst[:, 0:nblk * 128]),
             reads=reads, writes=[rg(s)], label=f"{tag}.ev{i}.{k0}")
        dst = xnT_hbm.rearrange("p (k t) -> p k t", t=T)[
            :, k0:k0 + nblk, i * 128:(i + 1) * 128]
        src = s[:, :].rearrange("p (b j) -> p b j", j=128)[:, 0:nblk, :]
        S.dma("sync", dst, src, reads=[rg(s)],
              writes=[rg(xnT_hbm, ("c", i, k0))], label=f"{tag}.sx{i}.{k0}")
    return sink


def ph_ln_to_xnT(S, nc, stack, x_hbm, xnT_sb, ident_bf, T, D, tag, C,
                 eps=LN_EPS, bufs=None, sink=None):
    BD = D // 128
    TT = T // 128
    if bufs is None:
        bufs = mk_ln_bufs(nc, stack, D, tag)
    if sink is None:
        sink = mk_sink_sbuf(S, xnT_sb, T, tag)
    xa = bufs["xa"]
    junk_s = bufs["junk_s"]
    junk_v = bufs["junk_v"]
    xnb = bufs["xnb"]
    st = bufs["st"]
    ps = bufs["ps"]
    for i in range(TT):
        xt = xa[i % 2]
        xn = xnb[i % 2]
        pst = ps[i % 2]
        S.dma("sync", xt[:, :], x_hbm[i * 128:(i + 1) * 128, :],
              label=f"{tag}.ldx{i}")
        S.op("scalar",
             lambda e, xt=xt: e.activation(junk_s[:, :], xt[:, :], AF.Square,
                                           accum_out=st[:, 1:2]),
             reads=[rg(xt)], writes=[rg(junk_s), rg(st, 1)],
             label=f"{tag}.sq{i}")
        S.op("vector",
             lambda e, xt=xt: e.tensor_scalar(junk_v[:, :], xt[:, :], 1.0, None,
                                              ALU.mult, op1=ALU.add,
                                              accum_out=st[:, 0:1]),
             reads=[rg(xt)], writes=[rg(junk_v), rg(st, 0)],
             label=f"{tag}.sum{i}")
        S.op("vector",
             lambda e: e.tensor_scalar(st[:, 2:3], st[:, 0:1], 1.0 / D, None,
                                       ALU.mult),
             reads=[rg(st, 0)], writes=[rg(st, 2)], label=f"{tag}.m{i}")
        S.op("vector",
             lambda e: e.tensor_scalar(st[:, 3:4], st[:, 1:2], 1.0 / D, None,
                                       ALU.mult),
             reads=[rg(st, 1)], writes=[rg(st, 3)], label=f"{tag}.msq{i}")
        S.op("vector",
             lambda e: e.tensor_tensor(st[:, 4:5], st[:, 2:3], st[:, 2:3],
                                       ALU.mult),
             reads=[rg(st, 2)], writes=[rg(st, 4)], label=f"{tag}.m2{i}")
        S.op("vector",
             lambda e: e.tensor_tensor(st[:, 5:6], st[:, 3:4], st[:, 4:5],
                                       ALU.subtract),
             reads=[rg(st, 3), rg(st, 4)], writes=[rg(st, 5)],
             label=f"{tag}.var{i}")
        S.op("scalar",
             lambda e: e.activation(st[:, 6:7], st[:, 5:6], AF.Ln, bias=C(eps)),
             reads=[rg(st, 5), C.r(eps)], writes=[rg(st, 6)],
             label=f"{tag}.ln{i}")
        S.op("scalar",
             lambda e: e.activation(st[:, 7:8], st[:, 6:7], AF.Exp, scale=-0.5),
             reads=[rg(st, 6)], writes=[rg(st, 7)], label=f"{tag}.rstd{i}")
        S.op("vector",
             lambda e, xt=xt, xn=xn: e.tensor_scalar(
                 xn[:, :], xt[:, :], st[:, 2:3], st[:, 7:8],
                 ALU.subtract, ALU.mult),
             reads=[rg(xt), rg(st, 2), rg(st, 7)], writes=[rg(xn)],
             label=f"{tag}.norm{i}")
        for k in range(BD):
            S.op("tensor",
                 lambda e, xn=xn, pst=pst, k=k: e.transpose(
                     pst[:, (k % 8) * 128:(k % 8 + 1) * 128],
                     xn[:, k * 128:(k + 1) * 128], ident_bf[:, :]),
                 reads=[rg(xn), rg(ident_bf)], writes=[rg(pst, k % 8)],
                 label=f"{tag}.tr{i}.{k}")
            if k % 8 == 7 or k == BD - 1:
                k0 = (k // 8) * 8
                nblk = k - k0 + 1
                sink(i, k0, nblk, pst,
                     [rg(pst, kk % 8) for kk in range(k0, k + 1)])


def _chunks(total, size):
    out = []
    c0 = 0
    while c0 < total:
        out.append((c0, min(size, total - c0)))
        c0 += size
    return out


def ph_mm_F(S, nc, stack, w_hbm, Din, ncols, xnT_sb, T, tag, evac_cb,
            col_scale=None, nchunk=512):
    BK = Din // 128
    NB = ncols // 128
    wb = [stack.enter_context(nc.sbuf_tensor(f"{tag}_wb{j}", [128, BK * 128],
                                             BF16)) for j in range(2)]
    ps = [stack.enter_context(nc.psum_tensor(f"{tag}_ps{j}", [128, 512], F32))
          for j in range(2)]
    xnT3 = xnT_sb[:, :].rearrange("p (k t) -> p k t", t=T)
    pi = 0
    for m in range(NB):
        w = wb[m % 2]
        src = w_hbm[:, m * 128:(m + 1) * 128].rearrange("(k p) j -> p k j",
                                                        p=128)
        dst = w[:, :].rearrange("p (k j) -> p k j", j=128)
        S.dma("gpsimd", dst, src, reads=[rg(w_hbm)], writes=[rg(w)],
              label=f"{tag}.ldw{m}")
        for (c0, cn) in _chunks(T, nchunk):
            p = ps[pi % 2]
            pi += 1
            for k in range(BK):
                S.mm(lambda e, w=w, k=k, c0=c0, cn=cn, p=p: e.matmul(
                        p[:, 0:cn], w[:, k * 128:(k + 1) * 128],
                        xnT3[:, k, c0:c0 + cn],
                        start=(k == 0), stop=(k == BK - 1)),
                     reads=[rg(w), rg(xnT_sb)], psum_rgn=rg(p),
                     start=(k == 0), stop=(k == BK - 1),
                     label=f"{tag}.mm{m}.{c0}.{k}")
            evac_cb(m, c0, cn, p)


def ph_mm_T(S, nc, stack, actT_sb, BK, T, w_hbm, Dout, tag, evac_cb,
            nchunk=256, extra_mm=None, w_rows0=0, pre_cb=None,
            w_tiled_bf=False, w_tiled_blk=None):
    wb = [stack.enter_context(nc.sbuf_tensor(f"{tag}_wb{j}",
                                             [128, BK * nchunk], BF16))
          for j in range(2)]
    ps = [stack.enter_context(nc.psum_tensor(f"{tag}_ps{j}", [128, 512], F32))
          for j in range(2)]
    act3 = actT_sb[:, :].rearrange("p (k t) -> p k t", t=T)
    TT = T // 128
    pi = 0
    for ci, (n0, nn) in enumerate(_chunks(Dout, nchunk)):
        w = wb[ci % 2]
        if w_tiled_bf:
            slab = BK * nchunk
            S.dma("scalar", w[:, 0:BK * nn],
                  w_hbm[:, ci * slab:ci * slab + BK * nn],
                  reads=[rg(w_hbm, ci * (nchunk // (w_tiled_blk or nchunk))
                            + kk)
                         for kk in range(nchunk // (w_tiled_blk or nchunk))],
                  writes=[rg(w)],
                  label=f"{tag}.ldw{ci}")
            blk = w_tiled_blk or nchunk
            w4 = w[:, :].rearrange("p (s k j) -> p s k j", k=BK, j=blk)
        else:
            src = w_hbm[w_rows0:w_rows0 + BK * 128, n0:n0 + nn].rearrange(
                "(k p) j -> p k j", p=128)
            dst = w[:, :].rearrange("p (k j) -> p k j", j=nchunk)[:, :, 0:nn]
            S.dma("gpsimd", dst, src, reads=[rg(w_hbm)], writes=[rg(w)],
                  label=f"{tag}.ldw{ci}")
        for i in range(TT):
            p = ps[pi % 2]
            pi += 1
            if pre_cb is not None:
                pre_cb(i, n0, nn)
            extras = extra_mm(i, n0, nn) if extra_mm is not None else []
            for k in range(BK):
                last = (k == BK - 1) and not extras
                if w_tiled_bf and (w_tiled_blk or nchunk) != nchunk:
                    blk = w_tiled_blk
                    rhs = w4[:, 0:nn // blk, k, :]
                else:
                    rhs = w[:, :].rearrange("p (k j) -> p k j", j=nchunk)[
                        :, k, 0:nn]
                S.mm(lambda e, w=w, k=k, i=i, nn=nn, p=p, last=last, rhs=rhs:
                     e.matmul(
                        p[:, 0:nn],
                        act3[:, k, i * 128:(i + 1) * 128],
                        rhs,
                        start=(k == 0), stop=last),
                     reads=[rg(w), rg(actT_sb)], psum_rgn=rg(p),
                     start=(k == 0), stop=last,
                     label=f"{tag}.mm{ci}.{i}.{k}")
            for j, (fn, rds) in enumerate(extras):
                last = (j == len(extras) - 1)
                S.mm(lambda e, fn=fn, p=p: fn(e, p),
                     reads=rds, psum_rgn=rg(p), start=False, stop=last,
                     label=f"{tag}.xmm{ci}.{i}.{j}")
            evac_cb(i, n0, nn, p)




class NCView:

    def __init__(self, nc, pfx):
        self._nc = nc
        self._pfx = pfx

    def sbuf_tensor(self, name, *a, **k):
        return self._nc.sbuf_tensor(self._pfx + name, *a, **k)

    def psum_tensor(self, name, *a, **k):
        return self._nc.psum_tensor(self._pfx + name, *a, **k)

    def dram_tensor(self, name, *a, **k):
        return self._nc.dram_tensor(self._pfx + name, *a, **k)



def _alibi_slopes(n):
    import math
    cp2 = 2 ** int(math.floor(math.log2(n)))
    base = 2.0 ** (-2.0 ** (-(math.log2(cp2) - 3)))
    slopes = base ** np.arange(1, cp2 + 1, dtype=np.float64)
    if cp2 != n:
        extra_base = 2.0 ** (-2.0 ** (-(math.log2(2 * cp2) - 3)))
        rem = min(cp2, n - cp2)
        slopes = np.concatenate(
            [slopes, extra_base ** np.arange(1, 1 + 2 * rem, 2,
                                             dtype=np.float64)])
    return slopes.astype(np.float32)


def _alib3_rows(T, slopes):
    H = len(slopes)
    out = np.zeros((3 * H, T), dtype=ml_dtypes.bfloat16)
    k = np.arange(T, dtype=np.float64)
    for h, s in enumerate(slopes):
        v = np.float64(np.float32(s)) * k
        hi = v.astype(ml_dtypes.bfloat16)
        r1 = v - hi.astype(np.float64)
        mid = r1.astype(ml_dtypes.bfloat16)
        lo = (r1 - mid.astype(np.float64)).astype(ml_dtypes.bfloat16)
        out[3 * h] = hi
        out[3 * h + 1] = mid
        out[3 * h + 2] = lo
    return out


_BUILD_CACHE = {}


def _get_nc(build_fn, *args):
    key = (build_fn.__name__,) + tuple(args)
    if key not in _BUILD_CACHE:
        _BUILD_CACHE[key] = build_fn(*args)
    return _BUILD_CACHE[key]


def _chain(inputs, dims, run):
    T, D, NH, DI, LAT = (dims["T"], dims["D"], dims["NH"], dims["DI"],
                         dims["LAT"])
    NB = NH + 2
    BD = D // 128
    x = np.ascontiguousarray(inputs["L0_attn_hidden_states"],
                             dtype=np.float32)
    NC = x.shape[0]
    cn = np.ascontiguousarray
    f32 = np.float32
    identbf = np.eye(128, dtype=ml_dtypes.bfloat16)
    identf32 = np.eye(128, dtype=f32)
    kramp = cn(np.broadcast_to(np.arange(T, dtype=f32), (128, T)))
    maskdiag = np.where(np.arange(128)[:, None] >= np.arange(128)[None, :],
                        0.0, -1e30).astype(f32)
    W0 = inputs["L0_ffn_W"].astype(f32)
    WT0 = cn(W0.T)
    A0 = inputs["L0_ffn_A"].astype(f32)
    b0 = inputs["L0_ffn_b"].astype(f32)
    LiMEs = inputs["L0_ffn_LiMEs"].astype(f32)
    G0 = cn(np.concatenate(
        [WT0[:, :8], np.zeros((D, 24), f32), A0], axis=1))
    WQKV0 = cn(inputs["L0_attn_w_qkv"].astype(f32))
    WQKV1 = cn(np.concatenate([inputs["L1_attn_wq"], inputs["L1_attn_wk"],
                               inputs["L1_attn_wv"]], axis=1).astype(f32))
    wc0 = cn(inputs["L0_attn_w_c"].astype(f32))
    Bm0 = cn(inputs["L0_ffn_Bmat"].astype(f32))
    wo1 = cn(inputs["L1_attn_wo"].astype(f32))
    up1 = cn(inputs["L1_ffn_up_w"].astype(f32))
    dn1 = cn(inputs["L1_ffn_down_w"].astype(f32))
    gt1 = cn(inputs["L1_ffn_gate_w"].astype(f32))
    f11 = cn(inputs["L1_ffn_fc1_w"].astype(f32))
    ew1 = cn(inputs["L1_ffn_expert_w"].astype(f32))
    f21 = cn(inputs["L1_ffn_fc2_w"].astype(f32))
    gammaB = cn(np.broadcast_to(inputs["L1_ffn_ln_gamma"].astype(f32),
                                (128, D)))
    beta_row = cn(inputs["L1_ffn_ln_beta"].astype(f32)[None, :])
    slopes0 = _alibi_slopes(NH)
    slopes1 = np.zeros(NH, f32)
    xa = cn(x[:, 1::2, :])

    def rn(build_fn, args, maps):
        return run(_get_nc(build_fn, *args), maps)

    r = rn(build_pre0_launch, (T, D, NC),
           [{"x": x[c], "x_anch": cn(np.concatenate(
                 [xa[c:c + 1], xa[:c], xa[c + 1:]]).reshape(-1, D)),
             "G0": G0, "b8_col": cn(b0[:8, None]),
             "LiT": cn(LiMEs.T), "Li": LiMEs,
             "c_identbf": identbf, "c_identf32": identf32}
            for c in range(NC)])
    xnT0 = [r[c]["xnT"] for c in range(NC)]
    pmixT = [r[c]["pmixT"] for c in range(NC)]
    r = rn(build_qkv_launch, (T, D, NB),
           [{"xnT": xnT0[c], "wqkv": WQKV0} for c in range(NC)])
    qkT = [r[c]["qkT"] for c in range(NC)]
    r = rn(build_attn_launch, (T, NH, tuple(float(s) for s in slopes0)),
           [{"qkT": qkT[c], "c_kramp": kramp, "c_maskdiag": maskdiag,
             "c_identbf": identbf} for c in range(NC)])
    attnT = [r[c]["attnT"] for c in range(NC)]
    r = rn(build_h0_launch, (T, D),
           [{"xnT": xnT0[c], "WT": WT0, "b_row": cn(b0[None, :]), "A0": A0,
             "x": x[c]} for c in range(NC)])
    ACC = [r[c]["ACC"] for c in range(NC)]
    zT = [r[c]["zT"] for c in range(NC)]
    r = rn(build_combine0_launch, (T, D),
           [{"ACC": ACC[c], "attnT": attnT[c], "wc": wc0, "Bmat": Bm0,
             "zT": zT[c], "pmixT": pmixT[c]} for c in range(NC)])
    out1 = [r[c]["out1"] for c in range(NC)]
    r = rn(build_ln_launch, (T, D),
           [{"x": out1[c], "c_identbf": identbf} for c in range(NC)])
    xnT1 = [r[c]["xnT"] for c in range(NC)]
    r = rn(build_qkv_launch, (T, D, NB),
           [{"xnT": xnT1[c], "wqkv": WQKV1} for c in range(NC)])
    qkT1 = [r[c]["qkT"] for c in range(NC)]
    r = rn(build_attn_launch, (T, NH, tuple(float(s) for s in slopes1)),
           [{"qkT": qkT1[c], "c_kramp": kramp, "c_maskdiag": maskdiag,
             "c_identbf": identbf} for c in range(NC)])
    attn1T = [r[c]["attnT"] for c in range(NC)]
    r = rn(build_wcast_launch, (D, DI, LAT),
           [{"up": up1, "down": dn1, "fc1": f11, "expw": ew1, "fc2": f21,
             "wo": wo1} for c in range(NC)])
    wb = r
    r = rn(build_ffn1a_launch, (T, D, DI),
           [{"xnT": xnT1[c], "up_bf": wb[c]["up_bf"],
             "down_bf": wb[c]["down_bf"]} for c in range(NC)])
    SH = [r[c]["SH"] for c in range(NC)]
    r = rn(build_ffn1b_launch, (T, D, LAT),
           [{"xnT": xnT1[c], "gate": gt1, "fc1_bf": wb[c]["fc1_bf"],
             "expw_bf": wb[c]["expw_bf"], "fc2_bf": wb[c]["fc2_bf"],
             "c_identf32": identf32} for c in range(NC)])
    RT = [r[c]["RT"] for c in range(NC)]
    r = rn(build_combine1_launch, (T, D),
           [{"x1": out1[c], "attnT": attn1T[c], "wo_bf": wb[c]["wo_bf"],
             "SH": SH[c], "RT": RT[c], "gammaB": gammaB,
             "beta_row": beta_row} for c in range(NC)])
    return np.stack([r[c]["OUT"] for c in range(NC)]).astype(np.float32)


def _dev_run(nc, in_maps):
    res = run_bass_kernel_spmd(nc, in_maps, core_ids=list(range(len(in_maps))))
    return res.results


def build_fused(T, D, NH, DI, LAT, NBAT, slopes0, slopes1):
    nc = bass.Bass("TRN2", target_bir_lowering=False)
    BD = D // 128
    NBq = NH + 2
    EI, EO, IN = "ExternalInput", "ExternalOutput", "Internal"

    def dt(n, shp, dty, kind):
        return nc.dram_tensor(n, shp, dty, kind=kind).ap()

    A = {
        "x": dt("x", [T, D], F32, EI),
        "x_anch": dt("x_anch", [NBAT * (T // 2), D], F32, EI),
        "G0": dt("G0", [D, 64], F32, EI),
        "b8_col": dt("b8_col", [8, 1], F32, EI),
        "LiT": dt("LiT", [32, 8], F32, EI),
        "Li": dt("Li", [8, 32], F32, EI),
        "c_identbf": dt("c_identbf", [128, 128], BF16, EI),
        "c_identf32": dt("c_identf32", [128, 128], F32, EI),
        "c_maskdiag": dt("c_maskdiag", [128, 128], F32, EI),
        "alib3": dt("alib3", [3 * NH, T], BF16, EI),
        "wqkv0": dt("wqkv0", [D, NBq * 128], F32, EI),
        "wqkv1": dt("wqkv1", [D, NBq * 128], F32, EI),
        "WT": dt("WT", [D, D], F32, EI),
        "b_row": dt("b_row", [1, D], F32, EI),
        "A0": dt("A0", [D, 32], F32, EI),
        "wc": dt("wc", [D, D], F32, EI),
        "Bmat": dt("Bmat", [32, D], F32, EI),
        "up": dt("up", [D, DI], F32, EI),
        "down": dt("down", [DI, D], F32, EI),
        "gate": dt("gate", [D, 8], F32, EI),
        "fc1": dt("fc1", [D, LAT], F32, EI),
        "expw": dt("expw", [LAT, LAT], F32, EI),
        "fc2": dt("fc2", [LAT, D], F32, EI),
        "wo": dt("wo", [D, D], F32, EI),
        "gammaB": dt("gammaB", [128, D], F32, EI),
        "beta_row": dt("beta_row", [1, D], F32, EI),
        "OUT": dt("OUT", [T, D], F32, EO),
        "xnT0": dt("xnT0", [128, BD * T], BF16, IN),
        "pmixT": dt("pmixT", [32, T], F32, IN),
        "qkT0": dt("qkT0", [128, NBq * T], BF16, IN),
        "attnT0": dt("attnT0", [128, BD * T], BF16, IN),
        "ACC": dt("ACC", [T, D], F32, IN),
        "zT": dt("zT", [32, T], F32, IN),
        "out1": dt("out1", [T, D], F32, IN),
        "xnT1": dt("xnT1", [128, BD * T], BF16, IN),
        "qkT1": dt("qkT1", [128, NBq * T], BF16, IN),
        "attn1T": dt("attn1T", [128, BD * T], BF16, IN),
        "SH": dt("SH", [T, D], F32, IN),
        "RT": dt("RT", [T, D], F32, IN),
        "WT_bf": dt("WT_bf", [128, (D // 128) * D], BF16, IN),
        "wc_bf": dt("wc_bf", [128, (D // 128) * D], BF16, IN),
        "up_bf": dt("up_bf", [128, (D // 128) * DI], BF16, IN),
        "down_bf": dt("down_bf", [128, (DI // 128) * D], BF16, IN),
        "fc1_bf": dt("fc1_bf", [128, (D // 128) * LAT], BF16, IN),
        "expw_bf": dt("expw_bf", [128, (LAT // 128) * LAT], BF16, IN),
        "fc2_bf": dt("fc2_bf", [128, (LAT // 128) * D], BF16, IN),
        "wo_bf": dt("wo_bf", [128, (D // 128) * D], BF16, IN),
    }
    S = Sched(nc)
    pc = [0]

    def P(fn, aps, *args):
        pc[0] += 1
        with ExitStack() as st:
            fn(S, NCView(nc, f"p{pc[0]}_"), st, aps, *args)
        S.barrier()

    g = lambda *names: {n: A[m] for n, m in
                        (nm.split(":") if ":" in nm else (nm, nm)
                         for nm in names)}
    P(ph_pre0_ln, g("x", "c_identbf", "xnT:xnT0"), T, D)
    P(ph_qkv, g("xnT:xnT0", "wqkv:wqkv0", "qkT:qkT0"), T, D, NBq)
    anch_stack = ExitStack()
    anch_fns = ph_anchors(S, NCView(nc, "anc_"), anch_stack,
                          g("x_anch", "G0", "b8_col", "LiT", "Li",
                            "c_identbf", "c_identf32", "pmixT"), T, D, NBAT)
    wc0_stack = ExitStack()
    wc0_fns = ph_wcast(S, NCView(nc, "wcz_"), wc0_stack, [
        (A["WT"], A["WT_bf"], D, D, 128),
        (A["wc"], A["wc_bf"], D, D, 128),
    ], tag="wc0")
    bg0 = []
    for j in range(max(len(anch_fns), len(wc0_fns))):
        if j < len(anch_fns):
            bg0.append(anch_fns[j])
        if j < len(wc0_fns):
            bg0.append(wc0_fns[j])
    S.add_background(bg0, every=55)
    P(ph_attn2, g("qkT:qkT0", "alib3", "c_maskdiag", "c_identbf",
                  "attnT:attnT0"), T, NH, slopes0)
    S.drain_background()
    S.barrier()
    wc0_stack.close()
    anch_stack.close()
    wc_stack = ExitStack()
    wc_fns = ph_wcast(S, NCView(nc, "wcv_"), wc_stack, [
        (A["up"], A["up_bf"], D, DI, 128),
        (A["down"], A["down_bf"], DI, D, 64),
        (A["fc1"], A["fc1_bf"], D, LAT, 128),
        (A["expw"], A["expw_bf"], LAT, LAT, 128),
        (A["fc2"], A["fc2_bf"], LAT, D, 256),
        (A["wo"], A["wo_bf"], D, D, 128),
    ])
    S.add_background(wc_fns, every=32)
    P(ph_h0, g("xnT:xnT0", "WT:WT_bf", "b_row", "A0", "x", "ACC", "zT"),
      T, D)
    P(ph_combine0, g("ACC", "attnT:attnT0", "wc:wc_bf", "Bmat", "zT",
                     "pmixT", "out1"), T, D)
    P(ph_lnL, g("x:out1", "c_identbf", "xnT:xnT1"), T, D)
    P(ph_qkv, g("xnT:xnT1", "wqkv:wqkv1", "qkT:qkT1"), T, D, NBq)
    P(ph_attn2, g("qkT:qkT1", "c_maskdiag", "c_identbf",
                  "attnT:attn1T"), T, NH, slopes1)
    S.drain_background()
    S.barrier()
    wc_stack.close()
    P(ph_ffn1a, g("xnT:xnT1", "up_bf", "down_bf", "SH"), T, D, DI)
    P(ph_ffn1b, g("xnT:xnT1", "gate", "fc1_bf", "expw_bf", "fc2_bf",
                  "c_identf32", "RT"), T, D, LAT)
    P(ph_combine1, g("x1:out1", "attnT:attn1T", "wo_bf", "SH", "RT",
                     "gammaB", "beta_row", "OUT"), T, D)
    S.finalize()
    return nc


def _fused_chain(inputs, dims, run):
    T, D, NH, DI, LAT = (dims["T"], dims["D"], dims["NH"], dims["DI"],
                         dims["LAT"])
    x = np.ascontiguousarray(inputs["L0_attn_hidden_states"],
                             dtype=np.float32)
    NC = x.shape[0]
    cn = np.ascontiguousarray
    f32 = np.float32
    W0 = inputs["L0_ffn_W"].astype(f32)
    WT0 = cn(W0.T)
    A0 = inputs["L0_ffn_A"].astype(f32)
    b0 = inputs["L0_ffn_b"].astype(f32)
    LiMEs = inputs["L0_ffn_LiMEs"].astype(f32)
    xa = cn(x[:, 1::2, :])
    nc = _get_nc(build_fused, T, D, NH, DI, LAT, NC,
                 tuple(float(s) for s in _alibi_slopes(NH)),
                 tuple(0.0 for _ in range(NH)))
    shared = {
        "G0": cn(np.concatenate([WT0[:, :8], np.zeros((D, 24), f32), A0],
                                axis=1)),
        "b8_col": cn(b0[:8, None]),
        "LiT": cn(LiMEs.T),
        "Li": LiMEs,
        "c_identbf": np.eye(128, dtype=ml_dtypes.bfloat16),
        "c_identf32": np.eye(128, dtype=f32),
        "alib3": _alib3_rows(T, _alibi_slopes(NH)),
        "c_maskdiag": np.where(
            np.arange(128)[:, None] >= np.arange(128)[None, :],
            0.0, -1e30).astype(f32),
        "wqkv0": cn(inputs["L0_attn_w_qkv"].astype(f32)),
        "wqkv1": cn(np.concatenate(
            [inputs["L1_attn_wq"], inputs["L1_attn_wk"],
             inputs["L1_attn_wv"]], axis=1).astype(f32)),
        "WT": WT0,
        "b_row": cn(b0[None, :]),
        "A0": cn(A0),
        "wc": cn(inputs["L0_attn_w_c"].astype(f32)),
        "Bmat": cn(inputs["L0_ffn_Bmat"].astype(f32)),
        "up": cn(inputs["L1_ffn_up_w"].astype(f32)),
        "down": cn(inputs["L1_ffn_down_w"].astype(f32)),
        "gate": cn(inputs["L1_ffn_gate_w"].astype(f32)),
        "fc1": cn(inputs["L1_ffn_fc1_w"].astype(f32)),
        "expw": cn(inputs["L1_ffn_expert_w"].astype(f32)),
        "fc2": cn(inputs["L1_ffn_fc2_w"].astype(f32)),
        "wo": cn(inputs["L1_attn_wo"].astype(f32)),
        "gammaB": cn(np.broadcast_to(
            inputs["L1_ffn_ln_gamma"].astype(f32), (128, D))),
        "beta_row": cn(inputs["L1_ffn_ln_beta"].astype(f32)[None, :]),
    }
    in_maps = []
    for c in range(NC):
        m = dict(shared)
        m["x"] = x[c]
        m["x_anch"] = cn(np.concatenate(
            [xa[c:c + 1], xa[:c], xa[c + 1:]]).reshape(-1, D))
        in_maps.append(m)
    r = run(nc, in_maps)
    return np.stack([r[c]["OUT"] for c in range(NC)]).astype(np.float32)


def kernel(**inputs):
    import os
    dims = dict(T=2048, D=4096, NH=32, DI=8192, LAT=2048)
    if os.environ.get("KERNEL_UNFUSED"):
        return _chain(inputs, dims, _dev_run)
    return _fused_chain(inputs, dims, _dev_run)


def ph_wcast(S, nc, stack, specs, tag="wc"):
    bufs = [stack.enter_context(
        nc.sbuf_tensor(f"{tag}_b{j}", [128, 4096], BF16)) for j in range(2)]
    cnt = [0]
    fns = []
    pend = [None]

    def emit_slab(src, dst, blk, slab, ib, si):
        b = bufs[cnt[0] % 2]
        cnt[0] += 1
        if pend[0] is not None:
            pend[0]()
        S.dma("gpsimd",
              b[:, 0:slab].rearrange("p (k j) -> p k j", j=blk),
              src[:, ib * blk:(ib + 1) * blk].rearrange(
                  "(k p) j -> p k j", p=128),
              reads=[rg(src)], writes=[rg(b)],
              label=f"{tag}.c{si}.{ib}")
        pend[0] = lambda: S.dma(
            "gpsimd", dst[:, ib * slab:(ib + 1) * slab], b[:, 0:slab],
            reads=[rg(b)], writes=[rg(dst, ib)],
            label=f"{tag}.s{si}.{ib}")

    def flush():
        if pend[0] is not None:
            pend[0]()
            pend[0] = None

    for si, (src, dst, Din, ncols, blk) in enumerate(specs):
        BK = Din // 128
        slab = BK * blk
        assert slab <= 4096, (slab, si)
        for ib in range(ncols // blk):
            fns.append(lambda src=src, dst=dst, blk=blk, slab=slab, ib=ib,
                       si=si: emit_slab(src, dst, blk, slab, ib, si))
    fns.append(flush)
    return fns


def ph_ffn1a(S, nc, stack, aps, T, D, DI):
    BD = D // 128
    BI = DI // 128
    xnT_h = aps["xnT"]
    up = aps["up_bf"]
    down = aps["down_bf"]
    SH = aps["SH"]
    sbt = lambda name, shp, dt: stack.enter_context(
        nc.sbuf_tensor(name, shp, dt))
    xcb = sbt("xcb", [128, BD * 512], BF16)
    hTn = sbt("hTn", [128, BI * 512], BF16)
    h1 = sbt("h1", [128, 512], BF16)
    ub = [sbt(f"ub{j}", [128, BD * 128], BF16) for j in range(2)]
    db = [sbt(f"db{j}", [128, BI * 256], BF16) for j in range(2)]
    stg = [sbt(f"stg{j}", [128, 256], F32) for j in range(2)]
    pu = [stack.enter_context(nc.psum_tensor(f"pu{j}", [128, 512], F32))
          for j in range(2)]
    pd = [stack.enter_context(nc.psum_tensor(f"pd{j}", [128, 512], F32))
          for j in range(2)]
    xc3 = xcb[:, :].rearrange("p (k t) -> p k t", t=512)
    hT3 = hTn[:, :].rearrange("p (k t) -> p k t", t=512)
    si = [0]
    for (t0, tn) in _chunks(T, 512):
        S.dma("sync", xc3[:, :, 0:tn],
              xnT_h.rearrange("p (k t) -> p k t", t=T)[:, :, t0:t0 + tn],
              reads=[rg(xnT_h)], writes=[rg(xcb)], label=f"f1.ldx{t0}")
        for ib in range(BI):
            w = ub[ib % 2]
            S.dma("sync", w[:, :],
                  up[:, ib * BD * 128:(ib + 1) * BD * 128],
                  reads=[rg(up, ib)], writes=[rg(w)],
                  label=f"f1.ldu{t0}.{ib}")
            p = pu[ib % 2]
            for k in range(BD):
                S.mm(lambda e, w=w, k=k, tn=tn, p=p: e.matmul(
                        p[:, 0:tn], w[:, k * 128:(k + 1) * 128],
                        xc3[:, k, 0:tn],
                        start=(k == 0), stop=(k == BD - 1)),
                     reads=[rg(w), rg(xcb)], psum_rgn=rg(p),
                     start=(k == 0), stop=(k == BD - 1),
                     label=f"f1.up{t0}.{ib}.{k}")
            S.op("scalar",
                 lambda e, p=p, tn=tn: e.activation(h1[:, 0:tn],
                                                    p[:, 0:tn], AF.Relu),
                 reads=[rg(p)], writes=[rg(h1)], label=f"f1.rl{t0}.{ib}")
            S.op("vector",
                 lambda e, ib=ib, tn=tn: e.tensor_tensor(
                     hT3[:, ib, 0:tn], h1[:, 0:tn], h1[:, 0:tn],
                     ALU.mult),
                 reads=[rg(h1)], writes=[rg(hTn, ib)],
                 label=f"f1.sq{t0}.{ib}")
        for mi, (m0, mn) in enumerate(_chunks(D, 256)):
            dbv = db[mi % 2]
            s0 = (m0 // 64) * BI * 64
            S.dma("sync", dbv[:, 0:BI * mn], down[:, s0:s0 + BI * mn],
                  reads=[rg(down, m0 // 64 + i) for i in range(mn // 64)],
                  writes=[rg(dbv)],
                  label=f"f1.ldd{t0}.{m0}")
            db4 = dbv[:, :].rearrange("p (s k j) -> p s k j", k=BI, j=64)
            for tt in range(tn // 128):
                p = pd[tt % 2]
                for ib in range(BI):
                    S.mm(lambda e, ib=ib, tt=tt, mn=mn, p=p, db4=db4:
                         e.matmul(
                            p[:, 0:mn],
                            hT3[:, ib, tt * 128:(tt + 1) * 128],
                            db4[:, 0:mn // 64, ib, :],
                            start=(ib == 0), stop=(ib == BI - 1)),
                         reads=[rg(dbv)] + [rg(hTn, kk)
                                           for kk in range(BI)],
                         psum_rgn=rg(p), start=(ib == 0),
                         stop=(ib == BI - 1),
                         label=f"f1.dn{t0}.{m0}.{tt}.{ib}")
                s = stg[si[0] % 2]
                si[0] += 1
                S.op("vector",
                     lambda e, p=p, s=s, mn=mn: e.tensor_copy(
                         s[:, 0:mn], p[:, 0:mn]),
                     reads=[rg(p)], writes=[rg(s)],
                     label=f"f1.ev{t0}.{m0}.{tt}")
                S.dma("sync",
                      SH[t0 + tt * 128:t0 + (tt + 1) * 128, m0:m0 + mn],
                      s[:, 0:mn], reads=[rg(s)],
                      writes=[rg(SH, (t0, m0, tt))],
                      label=f"f1.st{t0}.{m0}.{tt}")


def build_wcast_launch(D, DI, LAT):
    nc = bass.Bass("TRN2", target_bir_lowering=False)
    aps = {}
    aps["up"] = nc.dram_tensor("up", [D, DI], F32, kind="ExternalInput").ap()
    aps["down"] = nc.dram_tensor("down", [DI, D], F32,
                                 kind="ExternalInput").ap()
    aps["fc1"] = nc.dram_tensor("fc1", [D, LAT], F32,
                                kind="ExternalInput").ap()
    aps["expw"] = nc.dram_tensor("expw", [LAT, LAT], F32,
                                 kind="ExternalInput").ap()
    aps["fc2"] = nc.dram_tensor("fc2", [LAT, D], F32,
                                kind="ExternalInput").ap()
    aps["wo"] = nc.dram_tensor("wo", [D, D], F32,
                               kind="ExternalInput").ap()
    outs = {
        "up_bf": nc.dram_tensor("up_bf", [128, (D // 128) * DI], BF16,
                                kind="ExternalOutput").ap(),
        "down_bf": nc.dram_tensor("down_bf", [128, (DI // 128) * D], BF16,
                                  kind="ExternalOutput").ap(),
        "fc1_bf": nc.dram_tensor("fc1_bf", [128, (D // 128) * LAT], BF16,
                                 kind="ExternalOutput").ap(),
        "expw_bf": nc.dram_tensor("expw_bf", [128, (LAT // 128) * LAT], BF16,
                                  kind="ExternalOutput").ap(),
        "fc2_bf": nc.dram_tensor("fc2_bf", [128, (LAT // 128) * D], BF16,
                                 kind="ExternalOutput").ap(),
        "wo_bf": nc.dram_tensor("wo_bf", [128, (D // 128) * D], BF16,
                                kind="ExternalOutput").ap(),
    }
    S = Sched(nc)
    with ExitStack() as stack:
        for fn in ph_wcast(S, NCView(nc, ""), stack, [
                (aps["up"], outs["up_bf"], D, DI, 128),
                (aps["down"], outs["down_bf"], DI, D, 64),
                (aps["fc1"], outs["fc1_bf"], D, LAT, 128),
                (aps["expw"], outs["expw_bf"], LAT, LAT, 128),
                (aps["fc2"], outs["fc2_bf"], LAT, D, 256),
                (aps["wo"], outs["wo_bf"], D, D, 128)]):
            fn()
        S.finalize()
    return nc


def build_ffn1a_launch(T, D, DI):
    nc = bass.Bass("TRN2", target_bir_lowering=False)
    aps = {}
    BD = D // 128
    BI = DI // 128
    aps["xnT"] = nc.dram_tensor("xnT", [128, BD * T], BF16,
                           kind="ExternalInput").ap()
    aps["up_bf"] = nc.dram_tensor("up_bf", [128, (D // 128) * DI], BF16,
                                  kind="ExternalInput").ap()
    aps["down_bf"] = nc.dram_tensor("down_bf", [128, (DI // 128) * D], BF16,
                                    kind="ExternalInput").ap()
    aps["SH"] = nc.dram_tensor("SH", [T, D], F32, kind="ExternalOutput").ap()
    S = Sched(nc)
    with ExitStack() as stack:
        ph_ffn1a(S, NCView(nc, ""), stack, aps, T, D, DI)
        S.finalize()
    return nc


def ph_ffn1b(S, nc, stack, aps, T, D, LAT):
    BD = D // 128
    BL = LAT // 128
    TT = T // 128
    xnT_h = aps["xnT"]
    gate = aps["gate"]
    fc1 = aps["fc1_bf"]
    expw = aps["expw_bf"]
    fc2 = aps["fc2_bf"]
    identf_h = aps["c_identf32"]
    RT = aps["RT"]
    sbt = lambda name, shp, dt: stack.enter_context(
        nc.sbuf_tensor(name, shp, dt))
    identf = sbt("identf", [128, 128], F32)
    S.dma("sync", identf[:, :], identf_h, label="ldidf")
    gb = sbt("gb", [128, BD * 8], BF16)
    xlT = sbt("xlT", [128, BL * T], BF16)
    lgT = sbt("lgT", [8, T], F32)
    LGt = sbt("LGt", [128, 8], F32)
    rw = sbt("rw", [128, TT], F32)
    wst = sbt("wst1", [128, 8], F32)
    E8 = sbt("E8b", [128, 8], F32)
    M8 = sbt("M8b", [128, 8], F32)
    stack2 = ExitStack()
    sb2 = lambda name, shp, dt: stack2.enter_context(
        nc.sbuf_tensor(name, shp, dt))
    xcb = sb2("xcb", [128, BD * 512], BF16)
    fb = [sb2(f"fb{j}", [128, BD * 128], BF16) for j in range(2)]
    pg = stack.enter_context(nc.psum_tensor("pg", [128, 512], F32))
    pf = [stack.enter_context(nc.psum_tensor(f"pf{j}", [128, 512], F32))
          for j in range(2)]
    xc3 = xcb[:, :].rearrange("p (k t) -> p k t", t=512)
    xl3 = xlT[:, :].rearrange("p (k t) -> p k t", t=T)
    S.dma("gpsimd", gb[:, :].rearrange("p (k j) -> p k j", j=8),
          gate.rearrange("(k p) j -> p k j", p=128),
          reads=[rg(gate)], writes=[rg(gb)], label="ldg")
    for (t0, tn) in _chunks(T, 512):
        S.dma("sync", xc3[:, :, 0:tn],
              xnT_h.rearrange("p (k t) -> p k t", t=T)[:, :, t0:t0 + tn],
              reads=[rg(xnT_h)], writes=[rg(xcb)], label=f"f2.ldx{t0}")
        for k in range(BD):
            S.mm(lambda e, k=k, tn=tn: e.matmul(
                    pg[0:8, 0:tn], gb[:, k * 8:(k + 1) * 8],
                    xc3[:, k, 0:tn],
                    start=(k == 0), stop=(k == BD - 1)),
                 reads=[rg(gb), rg(xcb)], psum_rgn=rg(pg),
                 start=(k == 0), stop=(k == BD - 1),
                 label=f"f2.g{t0}.{k}")
        S.op("vector",
             lambda e, t0=t0, tn=tn: e.tensor_copy(
                 lgT[:, t0:t0 + tn], pg[0:8, 0:tn]),
             reads=[rg(pg)], writes=[rg(lgT, t0)], label=f"f2.gev{t0}")
        for lb in range(BL):
            w = fb[lb % 2]
            S.dma("sync", w[:, :],
                  fc1[:, lb * BD * 128:(lb + 1) * BD * 128],
                  reads=[rg(fc1, lb)], writes=[rg(w)],
                  label=f"f2.ldf{t0}.{lb}")
            p = pf[lb % 2]
            for k in range(BD):
                S.mm(lambda e, w=w, k=k, tn=tn, p=p: e.matmul(
                        p[:, 0:tn], w[:, k * 128:(k + 1) * 128],
                        xc3[:, k, 0:tn],
                        start=(k == 0), stop=(k == BD - 1)),
                     reads=[rg(w), rg(xcb)], psum_rgn=rg(p),
                     start=(k == 0), stop=(k == BD - 1),
                     label=f"f2.fc1.{t0}.{lb}.{k}")
            S.op("vector",
                 lambda e, lb=lb, t0=t0, tn=tn, p=p: e.tensor_copy(
                     xl3[:, lb, t0:t0 + tn], p[:, 0:tn]),
                 reads=[rg(p)], writes=[rg(xlT, lb)],
                 label=f"f2.fev{t0}.{lb}")
    stack2.close()
    S.barrier()
    eoT = sbt("eoT", [128, BL * T], BF16)
    eb = [sbt(f"eb{j}", [128, BL * 128], BF16) for j in range(2)]
    eo3 = eoT[:, :].rearrange("p (k t) -> p k t", t=T)
    for i in range(TT):
        S.op("tensor",
             lambda e, i=i: e.transpose(
                 pg[:, 0:8], lgT[:, i * 128:(i + 1) * 128],
                 identf[0:8, 0:8]),
             reads=[rg(lgT, kk) for kk in range(0, T, 512)]
             + [rg(identf)],
             writes=[rg(pg)], label=f"rw.tr{i}")
        S.op("vector", lambda e: e.tensor_copy(LGt[:, :], pg[:, 0:8]),
             reads=[rg(pg)], writes=[rg(LGt)], label=f"rw.ev{i}")
        S.op("vector",
             lambda e: e.reduce_max(wst[:, 0:1], LGt[:, :], axis=AX.X),
             reads=[rg(LGt)], writes=[rg(wst, 0)], label=f"rw.m1{i}")
        S.op("vector",
             lambda e: e.tensor_scalar(E8[:, :], LGt[:, :], wst[:, 0:1],
                                       None, ALU.is_ge),
             reads=[rg(LGt), rg(wst, 0)], writes=[rg(E8)],
             label=f"rw.eq{i}")
        S.op("vector",
             lambda e: e.reduce_sum(wst[:, 1:2], E8[:, :], axis=AX.X),
             reads=[rg(E8)], writes=[rg(wst, 1)], label=f"rw.cnt{i}")
        S.op("vector",
             lambda e: e.scalar_tensor_tensor(M8[:, :], E8[:, :], -1e30,
                                              LGt[:, :], ALU.mult,
                                              ALU.add),
             reads=[rg(E8), rg(LGt)], writes=[rg(M8)], label=f"rw.mk{i}")
        S.op("vector",
             lambda e: e.reduce_max(wst[:, 2:3], M8[:, :], axis=AX.X),
             reads=[rg(M8)], writes=[rg(wst, 2)], label=f"rw.m2{i}")
        S.op("vector",
             lambda e: e.tensor_scalar(wst[:, 3:4], wst[:, 1:2], 1.5,
                                       None, ALU.is_ge),
             reads=[rg(wst, 1)], writes=[rg(wst, 3)], label=f"rw.g2{i}")
        S.op("vector",
             lambda e: e.tensor_tensor(wst[:, 4:5], wst[:, 0:1],
                                       wst[:, 2:3], ALU.subtract),
             reads=[rg(wst, 0), rg(wst, 2)], writes=[rg(wst, 4)],
             label=f"rw.df{i}")
        S.op("vector",
             lambda e: e.tensor_tensor(wst[:, 5:6], wst[:, 3:4],
                                       wst[:, 4:5], ALU.mult),
             reads=[rg(wst, 3), rg(wst, 4)], writes=[rg(wst, 5)],
             label=f"rw.cr{i}")
        S.op("vector",
             lambda e: e.tensor_tensor(wst[:, 6:7], wst[:, 0:1],
                                       wst[:, 2:3], ALU.add),
             reads=[rg(wst, 0), rg(wst, 2)], writes=[rg(wst, 6)],
             label=f"rw.sm{i}")
        S.op("vector",
             lambda e, i=i: e.tensor_tensor(rw[:, i:i + 1], wst[:, 6:7],
                                            wst[:, 5:6], ALU.add),
             reads=[rg(wst, 6), rg(wst, 5)], writes=[rg(rw, i)],
             label=f"rw.rw{i}")
    for (t0, tn) in _chunks(T, 512):
        for l2 in range(BL):
            w = eb[l2 % 2]
            S.dma("sync", w[:, :],
                  expw[:, l2 * BL * 128:(l2 + 1) * BL * 128],
                  reads=[rg(expw, l2)], writes=[rg(w)],
                  label=f"f2.lde{t0}.{l2}")
            p = pf[l2 % 2]
            for lb in range(BL):
                S.mm(lambda e, w=w, lb=lb, t0=t0, tn=tn, p=p: e.matmul(
                        p[:, 0:tn], w[:, lb * 128:(lb + 1) * 128],
                        xl3[:, lb, t0:t0 + tn],
                        start=(lb == 0), stop=(lb == BL - 1)),
                     reads=[rg(w)] + [rg(xlT, kk) for kk in range(BL)],
                     psum_rgn=rg(p), start=(lb == 0),
                     stop=(lb == BL - 1), label=f"f2.ex{t0}.{l2}.{lb}")
            S.op("vector",
                 lambda e, l2=l2, t0=t0, tn=tn, p=p: e.tensor_copy(
                     eo3[:, l2, t0:t0 + tn], p[:, 0:tn]),
                 reads=[rg(p)], writes=[rg(eoT, l2)],
                 label=f"f2.eev{t0}.{l2}")
    stg = [sbt(f"stg{j}", [128, 512], F32) for j in range(2)]
    si = [0]

    def evac(i, n0, nn, p):
        s = stg[si[0] % 2]
        si[0] += 1
        S.op("vector",
             lambda e, p=p, s=s, nn=nn, i=i: e.tensor_scalar(
                 s[:, 0:nn], p[:, 0:nn], rw[:, i:i + 1], None, ALU.mult),
             reads=[rg(p), rg(rw, i)], writes=[rg(s)],
             label=f"f2.c2ev{i}.{n0}")
        S.dma("sync", RT[i * 128:(i + 1) * 128, n0:n0 + nn], s[:, 0:nn],
              reads=[rg(s)], writes=[rg(RT, (i, n0))],
              label=f"f2.c2st{i}.{n0}")

    ph_mm_T(S, nc, stack, eoT, BL, T, fc2, D, "c2", evac, nchunk=256,
            w_tiled_bf=True)


def build_ffn1b_launch(T, D, LAT):
    nc = bass.Bass("TRN2", target_bir_lowering=False)
    aps = {}
    BD = D // 128
    BL = LAT // 128
    TT = T // 128
    aps["xnT"] = nc.dram_tensor("xnT", [128, BD * T], BF16,
                           kind="ExternalInput").ap()
    aps["gate"] = nc.dram_tensor("gate", [D, 8], F32, kind="ExternalInput").ap()
    aps["fc1_bf"] = nc.dram_tensor("fc1_bf", [128, (D // 128) * LAT], BF16,
                                   kind="ExternalInput").ap()
    aps["expw_bf"] = nc.dram_tensor("expw_bf", [128, (LAT // 128) * LAT],
                                    BF16, kind="ExternalInput").ap()
    aps["fc2_bf"] = nc.dram_tensor("fc2_bf", [128, (LAT // 128) * D], BF16,
                                   kind="ExternalInput").ap()
    aps["c_identf32"] = nc.dram_tensor("c_identf32", [128, 128], F32,
                              kind="ExternalInput").ap()
    aps["RT"] = nc.dram_tensor("RT", [T, D], F32, kind="ExternalOutput").ap()
    S = Sched(nc)
    with ExitStack() as stack:
        ph_ffn1b(S, NCView(nc, ""), stack, aps, T, D, LAT)
        S.finalize()
    return nc


def ph_combine1(S, nc, stack, aps, T, D):
    BD = D // 128
    TT = T // 128
    x1 = aps["x1"]
    attnT_h = aps["attnT"]
    wo = aps["wo_bf"]
    SH = aps["SH"]
    RT = aps["RT"]
    gammaB = aps["gammaB"]
    beta_row = aps["beta_row"]
    OUT = aps["OUT"]
    Y = nc.dram_tensor("Yscr", [T, D], F32, kind="Internal").ap()
    sbt = lambda name, shp, dt: stack.enter_context(
        nc.sbuf_tensor(name, shp, dt))
    C = ConstPool(S, nc, stack)
    btb = sbt("btb", [1, D], BF16)
    ones_bf = sbt("onesb", [1, 128], BF16)
    S.dma("gpsimd", btb[:, :], beta_row, label="ldbt")
    S.op("gpsimd", lambda e: e.memset(ones_bf[:, :], 1.0),
         reads=[], writes=[rg(ones_bf)], label="ones")
    st = sbt("st3", [128, 8], F32)
    my = sbt("my", [128, TT], F32)
    mr = sbt("mr", [128, TT], F32)
    stack2 = ExitStack()
    sb2 = lambda name, shp, dt: stack2.enter_context(
        nc.sbuf_tensor(name, shp, dt))
    ya = [sb2(f"ya{j}", [128, D], F32) for j in range(2)]
    yb = [sb2(f"yb{j}", [128, D], F32) for j in range(2)]
    junk_s = sb2("junks3", [128, D], BF16)
    junk_v = sb2("junkv3", [128, D], BF16)
    for i in range(TT):
        sh = ya[i % 2]
        y = yb[i % 2]
        S.dma("sync", sh[:, :], SH[i * 128:(i + 1) * 128, :],
              reads=[rg(SH)], writes=[rg(sh)], label=f"y.ldsh{i}")
        S.dma("scalar", y[:, :], RT[i * 128:(i + 1) * 128, :],
              reads=[rg(RT)], writes=[rg(y)], label=f"y.ldrt{i}")
        S.op("vector",
             lambda e, sh=sh, y=y: e.tensor_tensor(y[:, :], y[:, :],
                                                   sh[:, :], ALU.add),
             reads=[rg(sh), rg(y)], writes=[rg(y)], label=f"y.add{i}")
        S.op("scalar",
             lambda e, y=y: e.activation(junk_s[:, :], y[:, :], AF.Square,
                                         accum_out=st[:, 1:2]),
             reads=[rg(y)], writes=[rg(junk_s), rg(st, 1)],
             label=f"y.sq{i}")
        S.op("vector",
             lambda e, y=y: e.tensor_scalar(junk_v[:, :], y[:, :], 1.0,
                                            None, ALU.mult, op1=ALU.add,
                                            accum_out=st[:, 0:1]),
             reads=[rg(y)], writes=[rg(junk_v), rg(st, 0)],
             label=f"y.sum{i}")
        S.op("vector",
             lambda e, i=i: e.tensor_scalar(my[:, i:i + 1], st[:, 0:1],
                                            1.0 / D, None, ALU.mult),
             reads=[rg(st, 0)], writes=[rg(my, i)], label=f"y.m{i}")
        S.op("vector",
             lambda e: e.tensor_scalar(st[:, 3:4], st[:, 1:2], 1.0 / D,
                                       None, ALU.mult),
             reads=[rg(st, 1)], writes=[rg(st, 3)], label=f"y.msq{i}")
        S.op("vector",
             lambda e, i=i: e.tensor_tensor(st[:, 4:5], my[:, i:i + 1],
                                            my[:, i:i + 1], ALU.mult),
             reads=[rg(my, i)], writes=[rg(st, 4)], label=f"y.m2{i}")
        S.op("vector",
             lambda e: e.tensor_tensor(st[:, 5:6], st[:, 3:4],
                                       st[:, 4:5], ALU.subtract),
             reads=[rg(st, 3), rg(st, 4)], writes=[rg(st, 5)],
             label=f"y.var{i}")
        S.op("scalar",
             lambda e: e.activation(st[:, 6:7], st[:, 5:6], AF.Ln,
                                    bias=C(LN_EPS)),
             reads=[rg(st, 5), C.r(LN_EPS)], writes=[rg(st, 6)],
             label=f"y.ln{i}")
        S.op("scalar",
             lambda e, i=i: e.activation(mr[:, i:i + 1], st[:, 6:7],
                                         AF.Exp, scale=-0.5),
             reads=[rg(st, 6)], writes=[rg(mr, i)], label=f"y.rstd{i}")
        S.dma("sync", Y[i * 128:(i + 1) * 128, :], y[:, :],
              reads=[rg(y)], writes=[rg(Y, i)], label=f"y.st{i}")
    stack2.close()
    S.barrier()
    aT = sbt("aT", [128, BD * T], BF16)
    gmb = sbt("gmb", [128, D], F32)
    S.dma("sync", aT[:, :], attnT_h, label="ldaT")
    S.dma("sync", gmb[:, :], gammaB, label="ldgm")
    yc = [sbt(f"yc{j}", [128, 256], F32) for j in range(3)]
    xc = [sbt(f"xc{j}", [128, 256], F32) for j in range(3)]
    t1b = [sbt(f"t1{j}", [128, 256], F32) for j in range(2)]
    og = [sbt(f"og{j}", [128, 256], F32) for j in range(3)]
    bi = [0]

    def pre(i, n0, nn):
        yt = yc[bi[0] % 3]
        xt = xc[bi[0] % 3]
        S.dma("sync", yt[:, 0:nn], Y[i * 128:(i + 1) * 128, n0:n0 + nn],
              reads=[rg(Y, i)], writes=[rg(yt)], label=f"b.ldy{i}.{n0}")
        S.dma("scalar", xt[:, 0:nn], x1[i * 128:(i + 1) * 128, n0:n0 + nn],
              reads=[rg(x1)], writes=[rg(xt)], label=f"b.ldx{i}.{n0}")

    def extra(i, n0, nn):
        return [(lambda e, p, n0=n0, nn=nn: e.matmul(
                    p[:, 0:nn], ones_bf[0:1, :], btb[0:1, n0:n0 + nn],
                    start=False, stop=True),
                 [rg(ones_bf), rg(btb)])]

    def evac(i, n0, nn, p):
        yt = yc[bi[0] % 3]
        xt = xc[bi[0] % 3]
        o = og[bi[0] % 3]
        t1 = t1b[bi[0] % 2]
        bi[0] += 1
        S.op("vector",
             lambda e, yt=yt, nn=nn, i=i: e.tensor_scalar(
                 t1[:, 0:nn], yt[:, 0:nn], my[:, i:i + 1],
                 mr[:, i:i + 1], ALU.subtract, ALU.mult),
             reads=[rg(yt), rg(my, i), rg(mr, i)], writes=[rg(t1)],
             label=f"b.t1{i}.{n0}")
        S.op("vector",
             lambda e, nn=nn, n0=n0: e.tensor_tensor(
                 t1[:, 0:nn], t1[:, 0:nn], gmb[:, n0:n0 + nn], ALU.mult),
             reads=[rg(t1), rg(gmb)], writes=[rg(t1)],
             label=f"b.t2{i}.{n0}")
        S.op("vector",
             lambda e, p=p, nn=nn: e.tensor_tensor(
                 t1[:, 0:nn], t1[:, 0:nn], p[:, 0:nn], ALU.add),
             reads=[rg(t1), rg(p)], writes=[rg(t1)],
             label=f"b.t3{i}.{n0}")
        S.op("vector",
             lambda e, xt=xt, o=o, nn=nn: e.tensor_tensor(
                 o[:, 0:nn], t1[:, 0:nn], xt[:, 0:nn], ALU.add),
             reads=[rg(t1), rg(xt)], writes=[rg(o)],
             label=f"b.t4{i}.{n0}")
        S.dma("sync", OUT[i * 128:(i + 1) * 128, n0:n0 + nn], o[:, 0:nn],
              reads=[rg(o)], writes=[rg(OUT, (i, n0))],
              label=f"b.st{i}.{n0}")

    ph_mm_T(S, nc, stack, aT, BD, T, wo, D, "b", evac,
            nchunk=256, extra_mm=extra, pre_cb=pre,
            w_tiled_bf=True, w_tiled_blk=128)


def build_combine1_launch(T, D):
    nc = bass.Bass("TRN2", target_bir_lowering=False)
    aps = {}
    BD = D // 128
    TT = T // 128
    aps["x1"] = nc.dram_tensor("x1", [T, D], F32, kind="ExternalInput").ap()
    aps["attnT"] = nc.dram_tensor("attnT", [128, BD * T], BF16,
                             kind="ExternalInput").ap()
    aps["wo_bf"] = nc.dram_tensor("wo_bf", [128, (D // 128) * D], BF16,
                                  kind="ExternalInput").ap()
    aps["SH"] = nc.dram_tensor("SH", [T, D], F32, kind="ExternalInput").ap()
    aps["RT"] = nc.dram_tensor("RT", [T, D], F32, kind="ExternalInput").ap()
    aps["gammaB"] = nc.dram_tensor("gammaB", [128, D], F32,
                            kind="ExternalInput").ap()
    aps["beta_row"] = nc.dram_tensor("beta_row", [1, D], F32,
                              kind="ExternalInput").ap()
    aps["OUT"] = nc.dram_tensor("OUT", [T, D], F32, kind="ExternalOutput").ap()
    S = Sched(nc)
    with ExitStack() as stack:
        ph_combine1(S, NCView(nc, ""), stack, aps, T, D)
        S.finalize()
    return nc


def ph_pre0_ln(S, nc, stack, aps, T, D):
    identb = stack.enter_context(nc.sbuf_tensor("identb", [128, 128], BF16))
    S.dma("sync", identb[:, :], aps["c_identbf"], label="ldidb")
    C = ConstPool(S, nc, stack)
    ph_ln_to_xnT(S, nc, stack, aps["x"], None, identb, T, D, "ln", C,
                 sink=mk_sink_hbm(S, nc, stack, aps["xnT"], T, "lnsk"))


def ph_anchors(S, nc, stack, aps, T, D, NBAT):
    BD = D // 128
    TA = T // 2
    OT = TA // 128
    NAT = NBAT * OT
    xa_h = aps["x_anch"]
    G0 = aps["G0"]
    b8_h = aps["b8_col"]
    LiT_h = aps["LiT"]
    Li_h = aps["Li"]
    identb_h = aps["c_identbf"]
    identf_h = aps["c_identf32"]
    pmixT_out = aps["pmixT"]
    sb = lambda name, shp, dt=F32: stack.enter_context(
        nc.sbuf_tensor(name, shp, dt))
    identb = sb("identb", [128, 128], BF16)
    identf = sb("identf", [128, 128], F32)
    C = ConstPool(S, nc, stack)
    lnb = mk_ln_bufs(nc, stack, D, "lnb", nps=1)
    fns = []
    fns.append(lambda: (
        S.dma("sync", identb[:, :], identb_h, label="ldidb"),
        S.dma("sync", identf[:, :], identf_h, label="ldidf")))
    Gb = sb("Gb", [128, BD * 64], BF16)
    b8 = sb("b8", [8, 1], F32)
    LiT = sb("LiT_s", [32, 8], BF16)
    Li = sb("Li_s", [8, 32], BF16)
    fns.append(lambda: (
        S.dma("gpsimd", Gb[:, :].rearrange("p (k j) -> p k j", j=64),
              G0.rearrange("(k p) j -> p k j", p=128),
              reads=[rg(G0)], writes=[rg(Gb)], label="ldG"),
        S.dma("sync", b8[:, :], b8_h, label="ldb8"),
        S.dma("gpsimd", LiT[:, :], LiT_h, label="ldLiT"),
        S.dma("gpsimd", Li[:, :], Li_h, label="ldLi")))
    xat = lnb["xa"]
    xab = lnb["xnb"]
    junk_s2 = lnb["junk_s"]
    junk_v2 = lnb["junk_v"]
    ast = lnb["st"]
    xaT = sb("xaT", [128, BD * 128], BF16)
    ptr = lnb["ps"]
    psG = stack.enter_context(nc.psum_tensor("psG", [128, 512], F32))
    HsK = sb("HsK", [8, OT * 128])
    dsK = sb("dsK", [8, OT * 128])
    Hst = sb("Hst", [8, 128])
    dst_t = sb("dst", [8, 128])
    zat = sb("zat", [32, 128], BF16)
    mx = sb("mx", [8, 2])
    red = sb("red", [8, 2])
    fns.append(lambda: S.op("gpsimd", lambda e: e.memset(mx[:, :], 0.0),
                            reads=[], writes=[rg(mx)], label="mx0"))

    def seg_tile(a):
        xt = xat[a % 2]
        xb = xab[a % 2]
        pst = ptr[0]
        S.dma("gpsimd", xt[:, :], xa_h[a * 128:(a + 1) * 128, :],
              reads=[rg(xa_h)], writes=[rg(xt)], label=f"a.ldx{a}")
        S.op("scalar",
             lambda e, xt=xt: e.activation(junk_s2[:, :], xt[:, :],
                                           AF.Square,
                                           accum_out=ast[:, 1:2]),
             reads=[rg(xt)], writes=[rg(junk_s2), rg(ast, 1)],
             label=f"a.sq{a}")
        S.op("vector",
             lambda e, xt=xt: e.tensor_scalar(junk_v2[:, :], xt[:, :],
                                              1.0, None, ALU.mult,
                                              op1=ALU.add,
                                              accum_out=ast[:, 0:1]),
             reads=[rg(xt)], writes=[rg(junk_v2), rg(ast, 0)],
             label=f"a.sum{a}")
        S.op("vector",
             lambda e: e.tensor_scalar(ast[:, 2:3], ast[:, 0:1], 1.0 / D,
                                       None, ALU.mult),
             reads=[rg(ast, 0)], writes=[rg(ast, 2)], label=f"a.m{a}")
        S.op("vector",
             lambda e: e.tensor_scalar(ast[:, 3:4], ast[:, 1:2], 1.0 / D,
                                       None, ALU.mult),
             reads=[rg(ast, 1)], writes=[rg(ast, 3)], label=f"a.msq{a}")
        S.op("vector",
             lambda e: e.tensor_tensor(ast[:, 4:5], ast[:, 2:3],
                                       ast[:, 2:3], ALU.mult),
             reads=[rg(ast, 2)], writes=[rg(ast, 4)], label=f"a.m2{a}")
        S.op("vector",
             lambda e: e.tensor_tensor(ast[:, 5:6], ast[:, 3:4],
                                       ast[:, 4:5], ALU.subtract),
             reads=[rg(ast, 3), rg(ast, 4)], writes=[rg(ast, 5)],
             label=f"a.var{a}")
        S.op("scalar",
             lambda e: e.activation(ast[:, 6:7], ast[:, 5:6], AF.Ln,
                                    bias=C(LN_EPS)),
             reads=[rg(ast, 5), C.r(LN_EPS)], writes=[rg(ast, 6)],
             label=f"a.ln{a}")
        S.op("scalar",
             lambda e: e.activation(ast[:, 7:8], ast[:, 6:7], AF.Exp,
                                    scale=-0.5),
             reads=[rg(ast, 6)], writes=[rg(ast, 7)], label=f"a.rstd{a}")
        S.op("vector",
             lambda e, xt=xt, xb=xb: e.tensor_scalar(
                 xb[:, :], xt[:, :], ast[:, 2:3], ast[:, 7:8],
                 ALU.subtract, ALU.mult),
             reads=[rg(xt), rg(ast, 2), rg(ast, 7)], writes=[rg(xb)],
             label=f"a.nrm{a}")
        for k in range(BD):
            S.op("tensor",
                 lambda e, xb=xb, pst=pst, k=k: e.transpose(
                     pst[:, (k % 8) * 128:(k % 8 + 1) * 128],
                     xb[:, k * 128:(k + 1) * 128], identb[:, :]),
                 reads=[rg(xb), rg(identb)], writes=[rg(pst, k % 8)],
                 label=f"a.tr{a}.{k}")
            if k % 8 == 7 or k == BD - 1:
                k0 = (k // 8) * 8
                nb = k - k0 + 1
                S.op("vector",
                     lambda e, k0=k0, nb=nb, pst=pst: e.tensor_copy(
                         xaT[:, k0 * 128:(k0 + nb) * 128],
                         pst[:, 0:nb * 128]),
                     reads=[rg(pst, kk % 8) for kk in range(k0, k + 1)],
                     writes=[rg(xaT, ("c", k0))],
                     label=f"a.ev{a}.{k0}")
        for k in range(BD):
            S.mm(lambda e, k=k: e.matmul(
                    psG[0:64, 0:128], Gb[:, k * 64:(k + 1) * 64],
                    xaT[:, k * 128:(k + 1) * 128],
                    start=(k == 0), stop=(k == BD - 1)),
                 reads=[rg(Gb)] + [rg(xaT, ("c", kk))
                                   for kk in range(0, BD, 8)],
                 psum_rgn=rg(psG), start=(k == 0), stop=(k == BD - 1),
                 label=f"a.g{a}.{k}")
        own = a < OT
        hdst = HsK[:, a * 128:(a + 1) * 128] if own else Hst[:, :]
        hrgn = rg(HsK, a) if own else rg(Hst)
        S.op("vector",
             lambda e, hdst=hdst: e.tensor_scalar(
                 hdst, psG[0:8, 0:128], b8[:, :], None, ALU.add),
             reads=[rg(psG), rg(b8)], writes=[hrgn], label=f"a.hev{a}")
        S.op("vector",
             lambda e: e.tensor_copy(zat[:, :], psG[32:64, 0:128]),
             reads=[rg(psG)], writes=[rg(zat)], label=f"a.zev{a}")
        S.mm(lambda e: e.matmul(psG[0:8, 128:256], LiT[:, :], zat[:, :],
                                start=True, stop=True),
             reads=[rg(LiT), rg(zat)], psum_rgn=rg(psG, "D"),
             start=True, stop=True, label=f"a.d{a}")
        ddst = dsK[:, a * 128:(a + 1) * 128] if own else dst_t[:, :]
        drgn = rg(dsK, a) if own else rg(dst_t)
        S.op("vector",
             lambda e, ddst=ddst: e.tensor_copy(ddst, psG[0:8, 128:256]),
             reads=[rg(psG, "D")], writes=[drgn], label=f"a.dev{a}")
        S.op("vector",
             lambda e, hdst=hdst: e.tensor_reduce(
                 red[:, 0:1], hdst, AX.X, ALU.max,
                 apply_absolute_value=True),
             reads=[hrgn], writes=[rg(red, 0)], label=f"a.hmx{a}")
        S.op("vector",
             lambda e, ddst=ddst: e.tensor_reduce(
                 red[:, 1:2], ddst, AX.X, ALU.max,
                 apply_absolute_value=True),
             reads=[drgn], writes=[rg(red, 1)], label=f"a.dmx{a}")
        S.op("vector",
             lambda e: e.tensor_tensor(mx[:, :], mx[:, :], red[:, :],
                                       ALU.max),
             reads=[rg(red, 0), rg(red, 1), rg(mx)], writes=[rg(mx)],
             label=f"a.mxa{a}")

    for a in range(NAT):
        fns.append(lambda a=a: seg_tile(a))

    ps8 = psG
    mxT = sb("mxT", [2, 8])
    scl = sb("scl", [2, 1])
    sclT = sb("sclT", [1, 2])
    rb = sb("rb", [8, 2])
    ones8 = sb("ones8", [1, 8])
    lgT = sb("lgT", [8, OT * 128])
    tmp8 = sb("tmp8", [8, OT * 128])
    rb1h = sb("rb1h", [8, 1])
    LG = sb("LG", [128, OT * 8])
    WGT = sb("WGT", [128, OT * 8], BF16)
    ps9 = ps8

    def seg_scales():
        S.op("gpsimd", lambda e: e.memset(ones8[:, :], 1.0),
             reads=[], writes=[rg(ones8)], label="ones8")
        S.op("tensor", lambda e: e.transpose(ps8[0:2, 0:8], mx[:, :],
                                             identf[0:8, 0:8]),
             reads=[rg(mx), rg(identf)], writes=[rg(ps8)], label="mxtr")
        S.op("vector", lambda e: e.tensor_copy(mxT[:, :], ps8[0:2, 0:8]),
             reads=[rg(ps8)], writes=[rg(mxT)], label="mxTev")
        S.op("vector",
             lambda e: e.tensor_reduce(scl[:, :], mxT[:, :], AX.X, ALU.max),
             reads=[rg(mxT)], writes=[rg(scl)], label="sclred")
        S.op("vector",
             lambda e: e.tensor_scalar(scl[:, :], scl[:, :], 1e-6, None,
                                       ALU.max),
             reads=[rg(scl)], writes=[rg(scl)], label="scleps")
        S.op("vector", lambda e: e.reciprocal(scl[:, :], scl[:, :]),
             reads=[rg(scl)], writes=[rg(scl)], label="sclrcp")
        S.op("tensor", lambda e: e.transpose(ps8[0:1, 0:2], scl[:, :],
                                             identf[0:2, 0:2]),
             reads=[rg(scl), rg(identf)], writes=[rg(ps8)], label="scltr")
        S.op("vector", lambda e: e.tensor_copy(sclT[:, :], ps8[0:1, 0:2]),
             reads=[rg(ps8)], writes=[rg(sclT)], label="sclTev")
        S.mm(lambda e: e.matmul(ps8[0:8, 0:2], ones8[:, :], sclT[:, :],
                                start=True, stop=True),
             reads=[rg(ones8), rg(sclT)], psum_rgn=rg(ps8),
             start=True, stop=True, label="sclbc")
        S.op("vector", lambda e: e.tensor_copy(rb[:, :], ps8[0:8, 0:2]),
             reads=[rg(ps8)], writes=[rg(rb)], label="rbev")
        S.op("vector",
             lambda e: e.tensor_scalar(tmp8[:, :], HsK[:, :], rb[:, 0:1], 0.5,
                                       ALU.mult, op1=ALU.mult),
             reads=[rg(HsK, kk) for kk in range(OT)] + [rg(rb)],
             writes=[rg(tmp8)], label="lg1")
        S.op("vector",
             lambda e: e.tensor_scalar(rb1h[:, :], rb[:, 1:2], 0.5, None,
                                       ALU.mult),
             reads=[rg(rb)], writes=[rg(rb1h)], label="lg2")
        S.op("vector",
             lambda e: e.scalar_tensor_tensor(lgT[:, :], dsK[:, :],
                                              rb1h[:, :], tmp8[:, :],
                                              ALU.mult, ALU.add),
             reads=[rg(dsK, kk) for kk in range(OT)] + [rg(rb1h), rg(tmp8)],
             writes=[rg(lgT)], label="lg3")
        for a in range(OT):
            S.op("tensor",
                 lambda e, a=a: e.transpose(
                     ps9[:, (a % 8) * 8:(a % 8) * 8 + 8],
                     lgT[:, a * 128:(a + 1) * 128], identf[0:8, 0:8]),
                 reads=[rg(lgT), rg(identf)], writes=[rg(ps9, a % 8)],
                 label=f"w.tr{a}")
            if a % 8 == 7 or a == OT - 1:
                a0 = (a // 8) * 8
                nb = a - a0 + 1
                S.op("vector",
                     lambda e, a0=a0, nb=nb: e.tensor_copy(
                         LG[:, a0 * 8:(a0 + nb) * 8],
                         ps9[:, 0:nb * 8]),
                     reads=[rg(ps9, kk % 8) for kk in range(a0, a + 1)],
                     writes=[rg(LG, ("c", a0))], label=f"w.ev{a0}")

    fns.append(seg_scales)

    wst = sb("wst", [128, 16])
    P8 = sb("P8", [128, 8])
    E8 = sb("E8", [128, 8])
    M8 = sb("M8", [128, 8])
    K8 = sb("K8", [128, 8])
    def seg_rout(a):
        L = LG[:, a * 8:(a + 1) * 8]
        lrg = rg(LG, ("c", (a // 8) * 8))
        S.op("vector",
             lambda e, L=L: e.reduce_max(wst[:, 0:1], L, axis=AX.X),
             reads=[lrg], writes=[rg(wst, 0)], label=f"w.m1{a}")
        S.op("vector",
             lambda e: e.tensor_scalar(wst[:, 1:2], wst[:, 0:1], -1.0,
                                       None, ALU.mult),
             reads=[rg(wst, 0)], writes=[rg(wst, 1)], label=f"w.nm{a}")
        S.op("scalar",
             lambda e, L=L: e.activation(P8[:, :], L, AF.Exp,
                                         bias=wst[:, 1:2],
                                         accum_out=wst[:, 2:3]),
             reads=[lrg, rg(wst, 1)], writes=[rg(P8), rg(wst, 2)],
             label=f"w.exp{a}")
        S.op("vector", lambda e: e.reciprocal(wst[:, 3:4], wst[:, 2:3]),
             reads=[rg(wst, 2)], writes=[rg(wst, 3)], label=f"w.rs{a}")
        S.op("vector",
             lambda e: e.tensor_scalar(P8[:, :], P8[:, :], wst[:, 3:4],
                                       None, ALU.mult),
             reads=[rg(P8), rg(wst, 3)], writes=[rg(P8)],
             label=f"w.pn{a}")
        S.op("vector",
             lambda e: e.reduce_max(wst[:, 4:5], P8[:, :], axis=AX.X),
             reads=[rg(P8)], writes=[rg(wst, 4)], label=f"w.m1p{a}")
        S.op("vector",
             lambda e: e.tensor_scalar(E8[:, :], P8[:, :], wst[:, 4:5],
                                       None, ALU.is_ge),
             reads=[rg(P8), rg(wst, 4)], writes=[rg(E8)],
             label=f"w.eq{a}")
        S.op("vector",
             lambda e: e.reduce_sum(wst[:, 5:6], E8[:, :], axis=AX.X),
             reads=[rg(E8)], writes=[rg(wst, 5)], label=f"w.cnt{a}")
        S.op("vector",
             lambda e: e.scalar_tensor_tensor(M8[:, :], E8[:, :], -1e30,
                                              P8[:, :], ALU.mult,
                                              ALU.add),
             reads=[rg(E8), rg(P8)], writes=[rg(M8)], label=f"w.msk{a}")
        S.op("vector",
             lambda e: e.reduce_max(wst[:, 6:7], M8[:, :], axis=AX.X),
             reads=[rg(M8)], writes=[rg(wst, 6)], label=f"w.m2{a}")
        S.op("vector",
             lambda e: e.tensor_scalar(wst[:, 7:8], wst[:, 5:6], 1.5,
                                       None, ALU.is_ge),
             reads=[rg(wst, 5)], writes=[rg(wst, 7)], label=f"w.ge2{a}")
        S.op("vector",
             lambda e: e.tensor_tensor(wst[:, 8:9], wst[:, 4:5],
                                       wst[:, 6:7], ALU.subtract),
             reads=[rg(wst, 4), rg(wst, 6)], writes=[rg(wst, 8)],
             label=f"w.df{a}")
        S.op("vector",
             lambda e: e.tensor_tensor(wst[:, 9:10], wst[:, 7:8],
                                       wst[:, 8:9], ALU.mult),
             reads=[rg(wst, 7), rg(wst, 8)], writes=[rg(wst, 9)],
             label=f"w.cr{a}")
        S.op("vector",
             lambda e: e.tensor_tensor(wst[:, 10:11], wst[:, 6:7],
                                       wst[:, 9:10], ALU.add),
             reads=[rg(wst, 6), rg(wst, 9)], writes=[rg(wst, 10)],
             label=f"w.th{a}")
        S.op("vector",
             lambda e: e.tensor_scalar(wst[:, 11:12], wst[:, 10:11], 2.0,
                                       None, ALU.mult),
             reads=[rg(wst, 10)], writes=[rg(wst, 11)], label=f"w.t2{a}")
        S.op("scalar",
             lambda e: e.activation(K8[:, :], P8[:, :], AF.Exp,
                                    scale=-2.0, bias=wst[:, 11:12]),
             reads=[rg(P8), rg(wst, 11)], writes=[rg(K8)],
             label=f"w.sg1{a}")
        S.op("vector",
             lambda e: e.tensor_scalar(K8[:, :], K8[:, :], 1.0, None,
                                       ALU.add),
             reads=[rg(K8)], writes=[rg(K8)], label=f"w.sg2{a}")
        S.op("vector", lambda e: e.reciprocal(K8[:, :], K8[:, :]),
             reads=[rg(K8)], writes=[rg(K8)], label=f"w.sg3{a}")
        S.op("vector",
             lambda e: e.tensor_tensor(M8[:, :], P8[:, :], K8[:, :],
                                       ALU.mult),
             reads=[rg(P8), rg(K8)], writes=[rg(M8)], label=f"w.w0{a}")
        S.op("vector",
             lambda e: e.tensor_scalar(E8[:, :], M8[:, :], 1.0, None,
                                       ALU.mult, op1=ALU.add,
                                       accum_out=wst[:, 13:14]),
             reads=[rg(M8)], writes=[rg(E8), rg(wst, 13)],
             label=f"w.ws{a}")
        S.op("vector",
             lambda e: e.tensor_scalar(wst[:, 14:15], wst[:, 13:14],
                                       1e-9, None, ALU.add),
             reads=[rg(wst, 13)], writes=[rg(wst, 14)],
             label=f"w.wse{a}")
        S.op("vector",
             lambda e: e.reciprocal(wst[:, 15:16], wst[:, 14:15]),
             reads=[rg(wst, 14)], writes=[rg(wst, 15)],
             label=f"w.wr{a}")
        S.op("vector",
             lambda e, a=a: e.tensor_scalar(
                 WGT[:, a * 8:(a + 1) * 8], M8[:, :], wst[:, 15:16],
                 None, ALU.mult),
             reads=[rg(M8), rg(wst, 15)], writes=[rg(WGT, a)],
             label=f"w.wf{a}")

    for a in range(OT):
        fns.append(lambda a=a: seg_rout(a))

    wT = sb("wT", [8, OT * 128], BF16)
    wfT = sb("wfT", [8, T], BF16)
    psA = ptr[0]
    pmx_sb = sb("pmx_sb", [32, T])
    psP = psG

    def seg_pmix():
        for a in range(OT):
            S.op("tensor",
                 lambda e, a=a: e.transpose(
                     psA[0:8, (a % 8) * 128:(a % 8 + 1) * 128],
                     WGT[:, a * 8:(a + 1) * 8], identb[:, :]),
                 reads=[rg(WGT, a), rg(identb)], writes=[rg(psA, a % 8)],
                 label=f"p.tr{a}")
            if a % 8 == 7 or a == OT - 1:
                a0 = (a // 8) * 8
                nb = a - a0 + 1
                S.op("vector",
                     lambda e, a0=a0, nb=nb: e.tensor_copy(
                         wT[:, a0 * 128:(a0 + nb) * 128],
                         psA[0:8, 0:nb * 128]),
                     reads=[rg(psA, kk % 8) for kk in range(a0, a + 1)],
                     writes=[rg(wT, ("c", a0))], label=f"p.ev{a0}")
        wf3 = wfT[:, :].rearrange("p (t two) -> p two t", two=2)
        for two in range(2):
            S.op("vector",
                 lambda e, two=two: e.tensor_copy(wf3[:, two, :], wT[:, :]),
                 reads=[rg(wT, ("c", kk)) for kk in range(0, OT, 8)],
                 writes=[rg(wfT, two)], label=f"p.rep{two}")
        for (c0, cn) in _chunks(T, 512):
            S.mm(lambda e, c0=c0, cn=cn: e.matmul(
                    psP[0:32, 0:cn], Li[:, :], wfT[:, c0:c0 + cn],
                    start=True, stop=True),
                 reads=[rg(Li), rg(wfT, 0), rg(wfT, 1)], psum_rgn=rg(psP),
                 start=True, stop=True, label=f"p.mm{c0}")
            S.op("vector",
                 lambda e, c0=c0, cn=cn: e.tensor_copy(
                     pmx_sb[:, c0:c0 + cn], psP[0:32, 0:cn]),
                 reads=[rg(psP)], writes=[rg(pmx_sb)], label=f"p.pev{c0}")
        S.dma("sync", pmixT_out, pmx_sb[:, :], reads=[rg(pmx_sb)],
              writes=[rg(pmixT_out)], label="p.st")

    fns.append(seg_pmix)
    return fns



def build_pre0_launch(T, D, NBAT):
    nc = bass.Bass("TRN2", target_bir_lowering=False)
    aps = {}
    BD = D // 128
    TA = T // 2
    OT = TA // 128
    NAT = NBAT * OT
    NE, RK = 8, 32
    aps["x"] = nc.dram_tensor("x", [T, D], F32, kind="ExternalInput").ap()
    aps["x_anch"] = nc.dram_tensor("x_anch", [NBAT * TA, D], F32,
                          kind="ExternalInput").ap()
    aps["G0"] = nc.dram_tensor("G0", [D, 64], F32, kind="ExternalInput").ap()
    aps["b8_col"] = nc.dram_tensor("b8_col", [8, 1], F32, kind="ExternalInput").ap()
    aps["LiT"] = nc.dram_tensor("LiT", [32, 8], F32, kind="ExternalInput").ap()
    aps["Li"] = nc.dram_tensor("Li", [8, 32], F32, kind="ExternalInput").ap()
    aps["c_identbf"] = nc.dram_tensor("c_identbf", [128, 128], BF16,
                              kind="ExternalInput").ap()
    aps["c_identf32"] = nc.dram_tensor("c_identf32", [128, 128], F32,
                              kind="ExternalInput").ap()
    aps["xnT"] = nc.dram_tensor("xnT", [128, BD * T], BF16,
                             kind="ExternalOutput").ap()
    aps["pmixT"] = nc.dram_tensor("pmixT", [32, T], F32,
                               kind="ExternalOutput").ap()
    S = Sched(nc)
    with ExitStack() as stack:
        ph_pre0_ln(S, NCView(nc, ""), stack, aps, T, D)
        for fn in ph_anchors(S, NCView(nc, "anc_"), stack, aps, T, D, NBAT):
            fn()
        S.finalize()
    return nc


def ph_combine0(S, nc, stack, aps, T, D):
    BD = D // 128
    ACC = aps["ACC"]
    attnT_h = aps["attnT"]
    wc = aps["wc"]
    Bmat = aps["Bmat"]
    zT_h = aps["zT"]
    pmixT_h = aps["pmixT"]
    out1 = aps["out1"]
    Bm = stack.enter_context(nc.sbuf_tensor("Bm", [32, D], BF16))
    zp = stack.enter_context(nc.sbuf_tensor("zp", [32, T], BF16))
    with ExitStack() as zstack:
        zT = zstack.enter_context(nc.sbuf_tensor("zTs", [32, T], F32))
        pmx = zstack.enter_context(nc.sbuf_tensor("pmx", [32, T], F32))
        S.dma("sync", zT[:, :], zT_h, label="ldz")
        S.dma("sync", pmx[:, :], pmixT_h, label="ldpm")
        S.op("vector",
             lambda e: e.scalar_tensor_tensor(zp[:, :], zT[:, :], 0.5,
                                              pmx[:, :], ALU.mult, ALU.mult),
             reads=[rg(zT), rg(pmx)], writes=[rg(zp)], label="zp")
    aT = stack.enter_context(nc.sbuf_tensor("aT", [128, BD * T], BF16))
    ac = [stack.enter_context(nc.sbuf_tensor(f"ac{j}", [128, 256], F32))
          for j in range(3)]
    og = [stack.enter_context(nc.sbuf_tensor(f"og{j}", [128, 256], F32))
          for j in range(3)]
    S.dma("sync", aT[:, :], attnT_h, reads=[rg(attnT_h), rg(zp)],
          writes=[rg(aT)], label="ldaT")
    S.dma("gpsimd", Bm[:, :], Bmat, label="ldB")
    bi = [0]

    def pre(i, n0, nn):
        at = ac[bi[0] % 3]
        S.dma("sync", at[:, 0:nn], ACC[i * 128:(i + 1) * 128, n0:n0 + nn],
              reads=[rg(ACC)], writes=[rg(at)], label=f"c0.lda{i}.{n0}")

    def extra(i, n0, nn):
        return [(lambda e, p, i=i, n0=n0, nn=nn: e.matmul(
                    p[:, 0:nn], zp[:, i * 128:(i + 1) * 128],
                    Bm[:, n0:n0 + nn], start=False, stop=True),
                 [rg(zp), rg(Bm)])]

    def evac(i, n0, nn, p):
        at = ac[bi[0] % 3]
        o = og[bi[0] % 3]
        bi[0] += 1
        S.op("vector",
             lambda e, p=p, at=at, o=o, nn=nn: e.tensor_tensor(
                 o[:, 0:nn], p[:, 0:nn], at[:, 0:nn], ALU.add),
             reads=[rg(p), rg(at)], writes=[rg(o)], label=f"c0.ev{i}.{n0}")
        S.dma("sync", out1[i * 128:(i + 1) * 128, n0:n0 + nn], o[:, 0:nn],
              reads=[rg(o)], writes=[rg(out1, (i, n0))],
              label=f"c0.st{i}.{n0}")

    ph_mm_T(S, nc, stack, aT, BD, T, wc, D, "c0", evac,
            nchunk=256, extra_mm=extra, pre_cb=pre,
            w_tiled_bf=True, w_tiled_blk=128)


def build_combine0_launch(T, D):
    nc = bass.Bass("TRN2", target_bir_lowering=False)
    aps = {}
    BD = D // 128
    aps["ACC"] = nc.dram_tensor("ACC", [T, D], F32, kind="ExternalInput").ap()
    aps["attnT"] = nc.dram_tensor("attnT", [128, BD * T], BF16,
                             kind="ExternalInput").ap()
    aps["wc"] = nc.dram_tensor("wc", [D, D], F32, kind="ExternalInput").ap()
    aps["Bmat"] = nc.dram_tensor("Bmat", [32, D], F32, kind="ExternalInput").ap()
    aps["zT"] = nc.dram_tensor("zT", [32, T], F32, kind="ExternalInput").ap()
    aps["pmixT"] = nc.dram_tensor("pmixT", [32, T], F32, kind="ExternalInput").ap()
    aps["out1"] = nc.dram_tensor("out1", [T, D], F32, kind="ExternalOutput").ap()
    S = Sched(nc)
    with ExitStack() as stack:
        ph_combine0(S, NCView(nc, ""), stack, aps, T, D)
        S.finalize()
    return nc

def ph_attn(S, nc, stack, aps, T, NH, slopes):
    TT = T // 128
    qkT = aps["qkT"]
    kramp_h = aps["c_kramp"]
    maskd_h = aps["c_maskdiag"]
    ident_h = aps["c_identbf"]
    attnT_h = aps["attnT"]
    sbt = lambda n, shp, dt: stack.enter_context(nc.sbuf_tensor(n, shp, dt))
    pst = lambda n, shp, dt: stack.enter_context(nc.psum_tensor(n, shp, dt))
    kT = sbt("kT", [128, T], BF16)
    vF = sbt("vF", [128, T], BF16)
    vT = sbt("vT", [128, T], BF16)
    kramp = sbt("kramp", [128, T], F32)
    maskd = sbt("maskd", [128, 128], F32)
    identb = sbt("identb", [128, 128], BF16)
    ones_cb = sbt("ones_cb", [128, 1], BF16)
    ones_rf = sbt("ones_rf", [1, 128], F32)
    qb = [sbt(f"qb{j}", [128, T], BF16) for j in range(2)]
    q2 = [sbt(f"q2{j}", [128, T], BF16) for j in range(2)]
    Ssb = [sbt(f"Ssb{j}", [128, T], F32) for j in range(2)]
    Psb = [sbt(f"Psb{j}", [128, T], BF16) for j in range(4)]
    PT = [sbt(f"PT{j}", [128, T], BF16) for j in range(2)]
    stq = [sbt(f"ast{j}", [128, 8], F32) for j in range(4)]
    nb = [sbt(f"nb{j}", [1, 8], F32) for j in range(2)]
    negB = [sbt(f"negB{j}", [128, 1], F32) for j in range(2)]
    ah = [sbt(f"ah{j}", [128, T], BF16) for j in range(2)]
    scw = min(T, 2048)
    scb = pst("scb", [128, scw], F32)
    sc = [scb[:, c0:c0 + cn] for (c0, cn) in _chunks(scw, 512)]
    tp = [pst("tp0", [128, 1024], BF16)] * 2
    av = [pst("av0", [128, 512], F32)] * 2
    S.dma("sync", kT[:, :], qkT[:, NH * T:(NH + 1) * T], label="ldk")
    S.dma("sync", vF[:, :], qkT[:, (NH + 1) * T:(NH + 2) * T], label="ldv")
    S.dma("sync", kramp[:, :], kramp_h, label="ldkr")
    S.dma("sync", maskd[:, :], maskd_h, label="ldmd")
    S.dma("sync", identb[:, :], ident_h, label="ldid")
    S.op("gpsimd", lambda e: e.memset(ones_cb[:, :], 1.0),
         reads=[], writes=[rg(ones_cb)], label="onescb")
    S.op("gpsimd", lambda e: e.memset(ones_rf[:, :], 1.0),
         reads=[], writes=[rg(ones_rf)], label="onesrf")
    qiota = sbt("qiota", [128, 1], F32)
    S.op("gpsimd", lambda e: e.iota(qiota[:, :], [[0, 1]], base=0,
                                    channel_multiplier=1,
                                    allow_small_or_imprecise_dtypes=True),
         reads=[], writes=[rg(qiota)], label="qiota")

    def sq_norms_max(src, dst_sq, out_st, out_col, tag):
        S.op("vector",
             lambda e: e.tensor_tensor(dst_sq[:, :], src[:, :], src[:, :],
                                       ALU.mult),
             reads=[rg(src)], writes=[rg(dst_sq)], label=f"{tag}.sq")
        for ci, (c0, cn) in enumerate(_chunks(T, 512)):
            S.mm(lambda e, ci=ci, c0=c0, cn=cn: e.matmul(
                    sc[ci][0:1, 0:cn], ones_cb[:, :], dst_sq[:, c0:c0 + cn],
                    start=True, stop=True),
                 reads=[rg(ones_cb), rg(dst_sq)], psum_rgn=rg(scb, ci),
                 start=True, stop=True, label=f"{tag}.nm{ci}")
            S.op("vector",
                 lambda e, ci=ci, cn=cn: e.reduce_max(
                     out_st[0:1, out_col + ci:out_col + ci + 1],
                     sc[ci][0:1, 0:cn], axis=AX.X),
                 reads=[rg(scb, ci)], writes=[rg(out_st, out_col + ci)],
                 label=f"{tag}.mx{ci}")
        nchk = len(_chunks(T, 512))
        S.op("vector",
             lambda e, nchk=nchk: e.tensor_reduce(
                 out_st[0:1, out_col:out_col + 1],
                 out_st[0:1, out_col:out_col + nchk], AX.X, ALU.max),
             reads=[rg(out_st, out_col + ci) for ci in range(nchk)],
             writes=[rg(out_st, out_col)], label=f"{tag}.mxf")

    def transpose_blocks(src_sb, dst_sb, nblk, tpp, tag, evac_eng):
        for cb in range(nblk):
            S.op("tensor",
                 lambda e, cb=cb: e.transpose(
                     tpp[:, (cb % 8) * 128:(cb % 8 + 1) * 128],
                     src_sb[:, cb * 128:(cb + 1) * 128], identb[:, :]),
                 reads=[rg(src_sb), rg(identb)], writes=[rg(tpp, cb % 8)],
                 label=f"{tag}.tr{cb}")
            if cb % 8 == 7 or cb == nblk - 1:
                k0 = (cb // 8) * 8
                nbk = cb - k0 + 1
                if evac_eng == "scalar":
                    fn = lambda e, k0=k0, nbk=nbk: e.copy(
                        dst_sb[:, k0 * 128:(k0 + nbk) * 128],
                        tpp[:, 0:nbk * 128])
                else:
                    fn = lambda e, k0=k0, nbk=nbk: e.tensor_copy(
                        dst_sb[:, k0 * 128:(k0 + nbk) * 128],
                        tpp[:, 0:nbk * 128])
                S.op(evac_eng, fn,
                     reads=[rg(tpp, kk % 8) for kk in range(k0, cb + 1)],
                     writes=[rg(dst_sb, ("c", k0))],
                     label=f"{tag}.ev{k0}")

    transpose_blocks(vF, vT, TT, tp[0], "vtr", "vector")
    sq_norms_max(kT, q2[0], nb[0], 0, "knrm")

    def emit_A(h, qt, slope, q, hp):
        it = h * TT + qt
        Sb = Ssb[it % 2]
        Pb, st = Psb[it % 4], stq[it % 4]
        W = (qt + 1) * 128
        chs = _chunks(W, 512)
        nchs = len(chs)
        for ci, (c0, cn) in enumerate(chs):
            p = sc[ci]
            S.mm(lambda e, p=p, q=q, qt=qt, c0=c0, cn=cn: e.matmul(
                    p[:, 0:cn], q[:, qt * 128:(qt + 1) * 128],
                    kT[:, c0:c0 + cn], start=True, stop=True),
                 reads=[rg(q), rg(kT)], psum_rgn=rg(scb, ci),
                 start=True, stop=True, label=f"sc{h}.{qt}.{ci}")
        if slope != 0.0:
            S.op("vector",
                 lambda e, W=W, Sb=Sb: e.scalar_tensor_tensor(
                     Sb[:, 0:W], kramp[:, 0:W], slope,
                     scb[:, 0:W], ALU.mult, ALU.add),
                 reads=[rg(kramp)] + [rg(scb, ci) for ci in range(nchs)],
                 writes=[rg(Sb)] + [rg(Sb, ci) for ci in range(nchs)],
                 label=f"sev{h}.{qt}")
            S.op("vector",
                 lambda e, W=W, Sb=Sb: e.tensor_tensor(
                     Sb[:, W - 128:W], Sb[:, W - 128:W], maskd[:, :],
                     ALU.add),
                 reads=[rg(Sb, nchs - 1), rg(maskd)],
                 writes=[rg(Sb, ("d", nchs - 1))],
                 label=f"msk{h}.{qt}")
            S.op("vector",
                 lambda e, st=st, hp=hp, slope=slope:
                 e.scalar_tensor_tensor(st[:, 6:7], qiota[:, :], -slope,
                                        negB[hp][:, :], ALU.mult,
                                        ALU.add),
                 reads=[rg(qiota), rg(negB[hp])], writes=[rg(st, 6)],
                 label=f"bi1{h}.{qt}")
            S.op("vector",
                 lambda e, st=st, slope=slope, qt=qt: e.tensor_scalar(
                     st[:, 6:7], st[:, 6:7], -slope * qt * 128.0, None,
                     ALU.add),
                 reads=[rg(st, 6)], writes=[rg(st, 6)],
                 label=f"bi2{h}.{qt}")
        else:
            S.op("vector",
                 lambda e, W=W: e.tensor_tensor(
                     scb[:, W - 128:W], scb[:, W - 128:W],
                     maskd[:, :], ALU.add),
                 reads=[rg(scb, nchs - 1), rg(maskd)],
                 writes=[rg(scb, ("d", nchs - 1))],
                 label=f"msk{h}.{qt}")
            S.op("scalar",
                 lambda e, W=W, Pb=Pb, st=st, hp=hp: e.activation(
                     Pb[:, 0:W], scb[:, 0:W], AF.Exp,
                     bias=negB[hp][:, :], accum_out=st[:, 1:2]),
                 reads=[rg(negB[hp]), rg(scb, ("d", nchs - 1))]
                 + [rg(scb, ci) for ci in range(nchs)],
                 writes=[rg(Pb), rg(st, 1)], label=f"exp{h}.{qt}")

    def emit_Bx(h, qt, slope):
        if slope == 0.0:
            return
        it = h * TT + qt
        Sb = Ssb[it % 2]
        Pb, st = Psb[it % 4], stq[it % 4]
        W = (qt + 1) * 128
        nchs = len(_chunks(W, 512))
        S.op("scalar",
             lambda e, W=W, Sb=Sb, Pb=Pb, st=st: e.activation(
                 Pb[:, 0:W], Sb[:, 0:W], AF.Exp,
                 bias=st[:, 6:7], accum_out=st[:, 1:2]),
             reads=[rg(Sb), rg(Sb, ("d", nchs - 1)), rg(st, 6)]
             + [rg(Sb, ci) for ci in range(nchs)],
             writes=[rg(Pb), rg(st, 1)], label=f"exp{h}.{qt}")

    def emit_C(h, qt, slope):
        it = h * TT + qt
        Pb, st = Psb[it % 4], stq[it % 4]
        W = (qt + 1) * 128
        nchs = len(_chunks(W, 512))
        S.op("vector",
             lambda e, st=st: e.reciprocal(st[:, 0:1], st[:, 1:2]),
             reads=[rg(st, 1)], writes=[rg(st, 0)],
             label=f"rs{h}.{qt}")
        S.op("vector",
             lambda e, W=W, Pb=Pb, st=st: e.tensor_scalar(
                 Pb[:, 0:W], Pb[:, 0:W], st[:, 0:1], None, ALU.mult),
             reads=[rg(Pb), rg(st, 0)]
             + [rg(Pb, ci) for ci in range(nchs)],
             writes=[rg(Pb)] + [rg(Pb, ci) for ci in range(nchs)],
             label=f"nrm{h}.{qt}")

    def emit_D(h, qt, slope):
        it = h * TT + qt
        par = it % 2
        Pb, PTb = Psb[it % 4], PT[par]
        evac_eng = "scalar" if par == 0 else "vector"
        transpose_blocks(Pb, PTb, qt + 1, tp[par], f"ptr{h}.{qt}", evac_eng)
        pav = av[par]
        for cb in range(qt + 1):
            S.mm(lambda e, pav=pav, cb=cb, qt=qt, PTb=PTb: e.matmul(
                    pav[:, 0:128], vT[:, cb * 128:(cb + 1) * 128],
                    PTb[:, cb * 128:(cb + 1) * 128],
                    start=(cb == 0), stop=(cb == qt)),
                 reads=[rg(vT)] + [rg(PTb, ("c", kk))
                                   for kk in range(0, qt + 1, 8)],
                 psum_rgn=rg(pav), start=(cb == 0), stop=(cb == qt),
                 label=f"av{h}.{qt}.{cb}")

    def emit_E(h, qt, slope):
        it = h * TT + qt
        par = it % 2
        a = ah[h % 2]
        pav = av[par]
        eng = "scalar" if par == 0 else "vector"
        if eng == "scalar":
            fn = lambda e, a=a, pav=pav, qt=qt: e.copy(
                a[:, qt * 128:(qt + 1) * 128], pav[:, 0:128])
        else:
            fn = lambda e, a=a, pav=pav, qt=qt: e.tensor_copy(
                a[:, qt * 128:(qt + 1) * 128], pav[:, 0:128])
        S.op(eng, fn, reads=[rg(pav)], writes=[rg(a, qt)],
             label=f"aev{h}.{qt}")
        if qt == TT - 1:
            S.dma("sync", attnT_h[:, h * T:(h + 1) * T], a[:, :],
                  reads=[rg(a, qq) for qq in range(TT)],
                  writes=[rg(attnT_h, h)], label=f"sta{h}")

    def emit_head_prologue(h):
        hp = h % 2
        q = qb[hp]
        S.dma("sync", q[:, :], qkT[:, h * T:(h + 1) * T], label=f"ldq{h}")
        sq_norms_max(q, q2[hp], nb[hp], 4, f"qn{h}")
        S.op("vector",
             lambda e, hp=hp: e.tensor_tensor(nb[hp][0:1, 1:2],
                                              nb[hp][0:1, 4:5],
                                              nb[0][0:1, 0:1], ALU.mult),
             reads=[rg(nb[hp], 4), rg(nb[0], 0)], writes=[rg(nb[hp], 1)],
             label=f"bm{h}")
        S.op("vector",
             lambda e, hp=hp: e.tensor_scalar(nb[hp][0:1, 1:2],
                                              nb[hp][0:1, 1:2], 1e-30, None,
                                              ALU.add),
             reads=[rg(nb[hp], 1)], writes=[rg(nb[hp], 1)], label=f"bz{h}")
        S.op("scalar",
             lambda e, hp=hp: e.activation(nb[hp][0:1, 2:3],
                                           nb[hp][0:1, 1:2], AF.Ln),
             reads=[rg(nb[hp], 1)], writes=[rg(nb[hp], 2)], label=f"bl{h}")
        S.op("scalar",
             lambda e, hp=hp: e.activation(nb[hp][0:1, 3:4],
                                           nb[hp][0:1, 2:3], AF.Exp,
                                           scale=0.5),
             reads=[rg(nb[hp], 2)], writes=[rg(nb[hp], 3)], label=f"be{h}")
        S.op("vector",
             lambda e, hp=hp: e.tensor_scalar(nb[hp][0:1, 3:4],
                                              nb[hp][0:1, 3:4], -1.0, -2.0,
                                              ALU.mult, op1=ALU.add),
             reads=[rg(nb[hp], 3)], writes=[rg(nb[hp], 3)], label=f"bn{h}")
        S.mm(lambda e, hp=hp: e.matmul(scb[0:128, 0:1], ones_rf[:, :],
                                       nb[hp][0:1, 3:4],
                                       start=True, stop=True),
             reads=[rg(ones_rf), rg(nb[hp], 3)], psum_rgn=rg(scb, 0),
             start=True, stop=True, label=f"bb{h}")
        S.op("vector",
             lambda e, hp=hp: e.tensor_copy(negB[hp][:, :],
                                            scb[0:128, 0:1]),
             reads=[rg(scb, 0)], writes=[rg(negB[hp])], label=f"bv{h}")
        return q, hp

    stages = [(emit_Bx, 1), (emit_C, 2), (emit_E, 4), (emit_D, 3)]
    its = []
    for h in range(NH):
        for qt in range(TT):
            its.append((h, qt, float(slopes[h])))
    total = len(its)
    for g in range(total + 5):
        if g < total:
            h, qt, slope = its[g]
            if qt == 0:
                q, hp = emit_head_prologue(h)
            emit_A(h, qt, slope, qb[h % 2], h % 2)
        for fn, lag in stages:
            jj = g - lag
            if 0 <= jj < total:
                fn(*its[jj])


def ph_attn2(S, nc, stack, aps, T, NH, slopes):
    TT = T // 128
    qkT = aps["qkT"]
    maskd_h = aps["c_maskdiag"]
    ident_h = aps["c_identbf"]
    attnT_h = aps["attnT"]
    has_alibi = any(float(s) != 0.0 for s in slopes)
    sbt = lambda n, shp, dt: stack.enter_context(nc.sbuf_tensor(n, shp, dt))
    pst = lambda n, shp, dt: stack.enter_context(nc.psum_tensor(n, shp, dt))
    kT = sbt("kT", [128, T], BF16)
    vF = sbt("vF", [128, T], BF16)
    vT = sbt("vT", [128, T], BF16)
    maskd = sbt("maskd", [128, 128], F32)
    identb = sbt("identb", [128, 128], BF16)
    ones_cb = sbt("ones_cb", [128, 1], BF16)
    ones_rf = sbt("ones_rf", [1, 128], F32)
    qb = [sbt(f"qb{j}", [128, T], BF16) for j in range(2)]
    q2 = [sbt(f"q2{j}", [128, T], BF16) for j in range(2)]
    Psb = [sbt(f"Psb{j}", [128, T], BF16) for j in range(4)]
    PT = [sbt(f"PT{j}", [128, T], BF16) for j in range(2)]
    stq = [sbt(f"ast{j}", [128, 8], F32) for j in range(4)]
    nb = [sbt(f"nb{j}", [1, 8], F32) for j in range(2)]
    negBall = sbt("negBall", [128, NH], F32)
    ah = [sbt(f"ah{j}", [128, T], BF16) for j in range(2)]
    scb = pst("scb", [128, 1024], F32)
    sc = [scb[:, 0:512], scb[:, 512:1024]]
    tp = [pst(f"tp{j}", [128, 1024], BF16) for j in range(2)]
    av = [pst(f"av{j}", [128, 512], F32) for j in range(2)]
    if has_alibi:
        alib3_h = aps["alib3"]
        alb = [sbt(f"alb{j}", [3, T], BF16) for j in range(2)]
        ones3 = sbt("ones3", [3, 128], BF16)
        S.op("gpsimd", lambda e: e.memset(ones3[:, :], 1.0),
             reads=[], writes=[rg(ones3)], label="ones3")
    S.dma("sync", kT[:, :], qkT[:, NH * T:(NH + 1) * T], label="ldk")
    S.dma("sync", vF[:, :], qkT[:, (NH + 1) * T:(NH + 2) * T], label="ldv")
    S.dma("sync", maskd[:, :], maskd_h, label="ldmd")
    S.dma("sync", identb[:, :], ident_h, label="ldid")
    S.op("gpsimd", lambda e: e.memset(ones_cb[:, :], 1.0),
         reads=[], writes=[rg(ones_cb)], label="onescb")
    S.op("gpsimd", lambda e: e.memset(ones_rf[:, :], 1.0),
         reads=[], writes=[rg(ones_rf)], label="onesrf")
    qiota = sbt("qiota", [128, 1], F32)
    S.op("gpsimd", lambda e: e.iota(qiota[:, :], [[0, 1]], base=0,
                                    channel_multiplier=1,
                                    allow_small_or_imprecise_dtypes=True),
         reads=[], writes=[rg(qiota)], label="qiota")

    def sq_norms_max(src, dst_sq, out_st, out_col, tag):
        S.op("vector",
             lambda e: e.tensor_tensor(dst_sq[:, :], src[:, :], src[:, :],
                                       ALU.mult),
             reads=[rg(src)], writes=[rg(dst_sq)], label=f"{tag}.sq")
        for ci, (c0, cn) in enumerate(_chunks(T, 512)):
            S.mm(lambda e, ci=ci, c0=c0, cn=cn: e.matmul(
                    sc[ci % 2][0:1, 0:cn], ones_cb[:, :],
                    dst_sq[:, c0:c0 + cn], start=True, stop=True),
                 reads=[rg(ones_cb), rg(dst_sq)], psum_rgn=rg(scb, ci % 2),
                 start=True, stop=True, label=f"{tag}.nm{ci}")
            S.op("vector",
                 lambda e, ci=ci, cn=cn: e.reduce_max(
                     out_st[0:1, out_col + ci:out_col + ci + 1],
                     sc[ci % 2][0:1, 0:cn], axis=AX.X),
                 reads=[rg(scb, ci % 2)], writes=[rg(out_st, out_col + ci)],
                 label=f"{tag}.mx{ci}")
        nchk = len(_chunks(T, 512))
        S.op("vector",
             lambda e, nchk=nchk: e.tensor_reduce(
                 out_st[0:1, out_col:out_col + 1],
                 out_st[0:1, out_col:out_col + nchk], AX.X, ALU.max),
             reads=[rg(out_st, out_col + ci) for ci in range(nchk)],
             writes=[rg(out_st, out_col)], label=f"{tag}.mxf")

    def transpose_blocks(src_sb, dst_sb, nblk, base_par, tag):
        for cb in range(nblk):
            tpp = tp[(base_par + (cb // 8)) % 2]
            S.op("tensor",
                 lambda e, cb=cb, tpp=tpp: e.transpose(
                     tpp[:, (cb % 8) * 128:(cb % 8 + 1) * 128],
                     src_sb[:, cb * 128:(cb + 1) * 128], identb[:, :]),
                 reads=[rg(src_sb), rg(identb)], writes=[rg(tpp, cb % 8)],
                 label=f"{tag}.tr{cb}")
            if cb % 8 == 7 or cb == nblk - 1:
                k0 = (cb // 8) * 8
                nbk = cb - k0 + 1
                evac_eng = "scalar" if (base_par + (cb // 8)) % 2 == 0 \
                    else "vector"
                if evac_eng == "scalar":
                    fn = lambda e, k0=k0, nbk=nbk, tpp=tpp: e.copy(
                        dst_sb[:, k0 * 128:(k0 + nbk) * 128],
                        tpp[:, 0:nbk * 128])
                else:
                    fn = lambda e, k0=k0, nbk=nbk, tpp=tpp: e.tensor_copy(
                        dst_sb[:, k0 * 128:(k0 + nbk) * 128],
                        tpp[:, 0:nbk * 128])
                S.op(evac_eng, fn,
                     reads=[rg(tpp, kk % 8) for kk in range(k0, cb + 1)],
                     writes=[rg(dst_sb, ("c", k0))],
                     label=f"{tag}.ev{k0}")

    transpose_blocks(vF, vT, TT, 0, "vtr")
    sq_norms_max(kT, q2[0], nb[0], 0, "knrm")
    for h in range(NH):
        hp = h % 2
        q = qb[hp]
        S.dma("sync", q[:, :], qkT[:, h * T:(h + 1) * T], label=f"nq{h}")
        sq_norms_max(q, q2[hp], nb[hp], 4, f"qn{h}")
        S.op("vector",
             lambda e, hp=hp: e.tensor_tensor(nb[hp][0:1, 1:2],
                                              nb[hp][0:1, 4:5],
                                              nb[0][0:1, 0:1], ALU.mult),
             reads=[rg(nb[hp], 4), rg(nb[0], 0)], writes=[rg(nb[hp], 1)],
             label=f"bm{h}")
        S.op("vector",
             lambda e, hp=hp: e.tensor_scalar(nb[hp][0:1, 1:2],
                                              nb[hp][0:1, 1:2], 1e-30, None,
                                              ALU.add),
             reads=[rg(nb[hp], 1)], writes=[rg(nb[hp], 1)], label=f"bz{h}")
        S.op("scalar",
             lambda e, hp=hp: e.activation(nb[hp][0:1, 2:3],
                                           nb[hp][0:1, 1:2], AF.Ln),
             reads=[rg(nb[hp], 1)], writes=[rg(nb[hp], 2)], label=f"bl{h}")
        S.op("scalar",
             lambda e, hp=hp: e.activation(nb[hp][0:1, 3:4],
                                           nb[hp][0:1, 2:3], AF.Exp,
                                           scale=0.5),
             reads=[rg(nb[hp], 2)], writes=[rg(nb[hp], 3)], label=f"be{h}")
        S.op("vector",
             lambda e, hp=hp: e.tensor_scalar(nb[hp][0:1, 3:4],
                                              nb[hp][0:1, 3:4], -1.0, -2.0,
                                              ALU.mult, op1=ALU.add),
             reads=[rg(nb[hp], 3)], writes=[rg(nb[hp], 3)], label=f"bn{h}")
        S.mm(lambda e, hp=hp: e.matmul(scb[0:128, 0:1], ones_rf[:, :],
                                       nb[hp][0:1, 3:4],
                                       start=True, stop=True),
             reads=[rg(ones_rf), rg(nb[hp], 3)], psum_rgn=rg(scb, 0),
             start=True, stop=True, label=f"bb{h}")
        S.op("vector",
             lambda e, h=h: e.tensor_copy(negBall[:, h:h + 1],
                                          scb[0:128, 0:1]),
             reads=[rg(scb, 0)], writes=[rg(negBall, h)], label=f"bv{h}")

    def emit_A(h, qt, slope, q, hp):
        it = h * TT + qt
        st = stq[it % 4]
        Pb = Psb[it % 4]
        W = (qt + 1) * 128
        chs = _chunks(W, 512)
        nchs = len(chs)
        if slope != 0.0:
            S.op("vector",
                 lambda e, st=st, h=h, slope=slope:
                 e.scalar_tensor_tensor(st[:, 6:7], qiota[:, :], -slope,
                                        negBall[:, h:h + 1], ALU.mult,
                                        ALU.add),
                 reads=[rg(qiota), rg(negBall, h)], writes=[rg(st, 6)],
                 label=f"bi1{h}.{qt}")
            S.op("vector",
                 lambda e, st=st, slope=slope, qt=qt: e.tensor_scalar(
                     st[:, 6:7], st[:, 6:7], -slope * qt * 128.0, None,
                     ALU.add),
                 reads=[rg(st, 6)], writes=[rg(st, 6)],
                 label=f"bi2{h}.{qt}")
            bias = st[:, 6:7]
            bias_r = rg(st, 6)
        else:
            bias = negBall[:, h:h + 1]
            bias_r = rg(negBall, h)
        for ci, (c0, cn) in enumerate(chs):
            b = (it + ci) % 2
            p = sc[b]
            S.mm(lambda e, p=p, q=q, qt=qt, c0=c0, cn=cn: e.matmul(
                    p[:, 0:cn], q[:, qt * 128:(qt + 1) * 128],
                    kT[:, c0:c0 + cn], start=True, stop=(slope == 0.0)),
                 reads=[rg(q), rg(kT)], psum_rgn=rg(scb, b),
                 start=True, stop=(slope == 0.0), label=f"sc{h}.{qt}.{ci}")
            if slope != 0.0:
                S.mm(lambda e, p=p, hp=hp, c0=c0, cn=cn: e.matmul(
                        p[:, 0:cn], ones3[:, :],
                        alb[hp][:, c0:c0 + cn],
                        start=False, stop=True),
                     reads=[rg(ones3), rg(alb[hp])], psum_rgn=rg(scb, b),
                     start=False, stop=True, label=f"al{h}.{qt}.{ci}")
            last = ci == nchs - 1
            if last:
                S.op("vector",
                     lambda e, p=p, cn=cn: e.tensor_tensor(
                         p[:, cn - 128:cn], p[:, cn - 128:cn],
                         maskd[:, :], ALU.add),
                     reads=[rg(scb, b), rg(maskd)],
                     writes=[rg(scb, ("d", b))],
                     label=f"msk{h}.{qt}")
            S.op("scalar",
                 lambda e, p=p, Pb=Pb, st=st, ci=ci, c0=c0, cn=cn,
                 bias=bias: e.activation(
                     Pb[:, c0:c0 + cn], p[:, 0:cn], AF.Exp,
                     bias=bias, accum_out=st[:, 1 + ci:2 + ci]),
                 reads=[bias_r, rg(scb, b)]
                 + ([rg(scb, ("d", b))] if last else []),
                 writes=[rg(Pb, ("e", ci)), rg(st, 1 + ci)],
                 label=f"exp{h}.{qt}.{ci}")

    def emit_C(h, qt, slope):
        it = h * TT + qt
        Pb, st = Psb[it % 4], stq[it % 4]
        W = (qt + 1) * 128
        nchs = len(_chunks(W, 512))
        if nchs > 1:
            S.op("vector",
                 lambda e, st=st, nchs=nchs: e.tensor_reduce(
                     st[:, 5:6], st[:, 1:1 + nchs], AX.X, ALU.add),
                 reads=[rg(st, 1 + ci) for ci in range(nchs)],
                 writes=[rg(st, 5)], label=f"sms{h}.{qt}")
            sum_col, sum_r = 5, rg(st, 5)
        else:
            sum_col, sum_r = 1, rg(st, 1)
        S.op("vector",
             lambda e, st=st, sum_col=sum_col: e.reciprocal(
                 st[:, 0:1], st[:, sum_col:sum_col + 1]),
             reads=[sum_r], writes=[rg(st, 0)],
             label=f"rs{h}.{qt}")
        S.op("vector",
             lambda e, W=W, Pb=Pb, st=st: e.tensor_scalar(
                 Pb[:, 0:W], Pb[:, 0:W], st[:, 0:1], None, ALU.mult),
             reads=[rg(Pb), rg(st, 0)],
             writes=[rg(Pb)],
             label=f"nrm{h}.{qt}")

    def emit_D(h, qt, slope):
        it = h * TT + qt
        par = it % 2
        Pb, PTb = Psb[it % 4], PT[par]
        transpose_blocks(Pb, PTb, qt + 1, par, f"ptr{h}.{qt}")
        pav = av[par]
        for cb in range(qt + 1):
            S.mm(lambda e, pav=pav, cb=cb, qt=qt, PTb=PTb: e.matmul(
                    pav[:, 0:128], vT[:, cb * 128:(cb + 1) * 128],
                    PTb[:, cb * 128:(cb + 1) * 128],
                    start=(cb == 0), stop=(cb == qt)),
                 reads=[rg(vT)] + [rg(PTb, ("c", kk))
                                   for kk in range(0, qt + 1, 8)],
                 psum_rgn=rg(pav), start=(cb == 0), stop=(cb == qt),
                 label=f"av{h}.{qt}.{cb}")

    def emit_E(h, qt, slope):
        it = h * TT + qt
        par = it % 2
        a = ah[h % 2]
        pav = av[par]
        eng = "scalar" if par == 0 else "vector"
        if eng == "scalar":
            fn = lambda e, a=a, pav=pav, qt=qt: e.copy(
                a[:, qt * 128:(qt + 1) * 128], pav[:, 0:128])
        else:
            fn = lambda e, a=a, pav=pav, qt=qt: e.tensor_copy(
                a[:, qt * 128:(qt + 1) * 128], pav[:, 0:128])
        S.op(eng, fn, reads=[rg(pav)], writes=[rg(a, qt)],
             label=f"aev{h}.{qt}")
        if qt == TT - 1:
            S.dma("sync", attnT_h[:, h * T:(h + 1) * T], a[:, :],
                  reads=[rg(a, qq) for qq in range(TT)],
                  writes=[rg(attnT_h, h)], label=f"sta{h}")

    def emit_head_prologue(h):
        hp = h % 2
        q = qb[hp]
        S.dma("sync", q[:, :], qkT[:, h * T:(h + 1) * T], label=f"ldq{h}")
        if has_alibi and float(slopes[h]) != 0.0:
            S.dma("sync", alb[hp][:, :], alib3_h[3 * h:3 * h + 3, :],
                  reads=[rg(alib3_h)], writes=[rg(alb[hp])],
                  label=f"ldal{h}")
        return q, hp

    stages = [(emit_C, 1), (emit_D, 2), (emit_E, 3)]
    its = []
    for h in range(NH):
        for qt in range(TT):
            its.append((h, qt, float(slopes[h])))
    total = len(its)
    qhp = {}
    for g in range(total + 5):
        for fn, lag in stages:
            jj = g - lag
            if 0 <= jj < total:
                fn(*its[jj])
        if g < total:
            h, qt, slope = its[g]
            if g == 0:
                qhp[0] = emit_head_prologue(0)
            if qt == TT - 2 and h + 1 < NH:
                qhp[h + 1] = emit_head_prologue(h + 1)
            q, hp = qhp[h]
            emit_A(h, qt, slope, q, hp)


def build_attn_launch(T, NH, slopes):
    nc = bass.Bass("TRN2", target_bir_lowering=False)
    aps = {}
    TT = T // 128
    aps["qkT"] = nc.dram_tensor("qkT", [128, (NH + 2) * T], BF16,
                         kind="ExternalInput").ap()
    aps["c_kramp"] = nc.dram_tensor("c_kramp", [128, T], F32,
                             kind="ExternalInput").ap()
    aps["c_maskdiag"] = nc.dram_tensor("c_maskdiag", [128, 128], F32,
                             kind="ExternalInput").ap()
    aps["c_identbf"] = nc.dram_tensor("c_identbf", [128, 128], BF16,
                             kind="ExternalInput").ap()
    aps["attnT"] = nc.dram_tensor("attnT", [128, NH * T], BF16,
                             kind="ExternalOutput").ap()
    S = Sched(nc)
    with ExitStack() as stack:
        ph_attn(S, NCView(nc, ""), stack, aps, T, NH, slopes)
        S.finalize()
    return nc


def ph_lnL(S, nc, stack, aps, T, D):
    BD = D // 128
    x = aps["x"]
    ident = aps["c_identbf"]
    xnT_h = aps["xnT"]
    ident_bf = stack.enter_context(nc.sbuf_tensor("idb", [128, 128], BF16))
    S.dma("sync", ident_bf[:, :], ident, label="ldid")
    C = ConstPool(S, nc, stack)
    ph_ln_to_xnT(S, nc, stack, x, None, ident_bf, T, D, "ln", C,
                 sink=mk_sink_hbm(S, nc, stack, xnT_h, T, "lnsk"))


def build_ln_launch(T, D):
    nc = bass.Bass("TRN2", target_bir_lowering=False)
    aps = {}
    BD = D // 128
    aps["x"] = nc.dram_tensor("x", [T, D], F32, kind="ExternalInput").ap()
    aps["c_identbf"] = nc.dram_tensor("c_identbf", [128, 128], BF16,
                           kind="ExternalInput").ap()
    aps["xnT"] = nc.dram_tensor("xnT", [128, BD * T], BF16,
                           kind="ExternalOutput").ap()
    S = Sched(nc)
    with ExitStack() as stack:
        ph_lnL(S, NCView(nc, ""), stack, aps, T, D)
        S.finalize()
    return nc


def ph_qkv(S, nc, stack, aps, T, D, NB):
    BD = D // 128
    c1 = float(128 ** -0.5)
    xnT_h = aps["xnT"]
    wqkv = aps["wqkv"]
    qkT_h = aps["qkT"]
    xnT_sb = stack.enter_context(
        nc.sbuf_tensor("xnT_sb", [128, BD * T], BF16))
    stg = [stack.enter_context(nc.sbuf_tensor(f"stg{j}", [128, 512], BF16))
           for j in range(2)]
    S.dma("sync", xnT_sb[:, :], xnT_h, label="ldxnT")
    si = [0]

    def evac(m, c0, cn, p):
        s = stg[si[0] % 2]
        si[0] += 1
        scale = c1 if m < NB - 2 else 1.0
        S.op("vector",
             lambda e, s=s, p=p, cn=cn, scale=scale: e.tensor_scalar(
                 s[:, 0:cn], p[:, 0:cn], scale, None, ALU.mult),
             reads=[rg(p)], writes=[rg(s)], label=f"qkv.ev{m}.{c0}")
        S.dma("sync", qkT_h[:, m * T + c0:m * T + c0 + cn], s[:, 0:cn],
              reads=[rg(s)], writes=[rg(qkT_h, m)], label=f"qkv.st{m}.{c0}")

    ph_mm_F(S, nc, stack, wqkv, D, NB * 128, xnT_sb, T, "qkv", evac)


def build_qkv_launch(T, D, NB):
    nc = bass.Bass("TRN2", target_bir_lowering=False)
    aps = {}
    BD = D // 128
    aps["xnT"] = nc.dram_tensor("xnT", [128, BD * T], BF16,
                           kind="ExternalInput").ap()
    aps["wqkv"] = nc.dram_tensor("wqkv", [D, NB * 128], F32,
                          kind="ExternalInput").ap()
    aps["qkT"] = nc.dram_tensor("qkT", [128, NB * T], BF16,
                           kind="ExternalOutput").ap()
    c1 = float(128 ** -0.5)
    S = Sched(nc)
    with ExitStack() as stack:
        ph_qkv(S, NCView(nc, ""), stack, aps, T, D, NB)
        S.finalize()
    return nc


def ph_h0(S, nc, stack, aps, T, D):
    BD = D // 128
    xnT_h = aps["xnT"]
    WT = aps["WT"]
    b_row = aps["b_row"]
    A0 = aps["A0"]
    x = aps["x"]
    ACC = aps["ACC"]
    zT_h = aps["zT"]
    xnT_sb = stack.enter_context(
        nc.sbuf_tensor("xnT_sb", [128, BD * T], BF16))
    ones_bf = stack.enter_context(nc.sbuf_tensor("onesb", [1, 128], BF16))
    b_bf = stack.enter_context(nc.sbuf_tensor("b_bf", [1, D], BF16))
    ab = stack.enter_context(nc.sbuf_tensor("ab", [128, BD * 32], BF16))
    zps = stack.enter_context(nc.psum_tensor("zps", [32, 512], F32))
    zT_sb = stack.enter_context(nc.sbuf_tensor("zT_sb", [32, T], F32))
    xc = [stack.enter_context(nc.sbuf_tensor(f"xc{j}", [128, 256], F32))
          for j in range(3)]
    og = [stack.enter_context(nc.sbuf_tensor(f"og{j}", [128, 256], F32))
          for j in range(3)]
    S.dma("sync", xnT_sb[:, :], xnT_h, label="ldxnT")
    S.dma("gpsimd", b_bf[:, :], b_row, label="ldb")
    S.dma("gpsimd",
          ab[:, :].rearrange("p (k j) -> p k j", j=32),
          A0.rearrange("(k p) j -> p k j", p=128),
          reads=[rg(A0)], writes=[rg(ab)], label="lda")
    S.op("gpsimd", lambda e: e.memset(ones_bf[:, :], 1.0),
         reads=[], writes=[rg(ones_bf)], label="ones")
    xnT3 = xnT_sb[:, :].rearrange("p (k t) -> p k t", t=T)
    for (c0, cn) in _chunks(T, 512):
        for k in range(BD):
            S.mm(lambda e, k=k, c0=c0, cn=cn: e.matmul(
                    zps[:, 0:cn], ab[:, k * 32:(k + 1) * 32],
                    xnT3[:, k, c0:c0 + cn],
                    start=(k == 0), stop=(k == BD - 1)),
                 reads=[rg(ab), rg(xnT_sb)], psum_rgn=rg(zps),
                 start=(k == 0), stop=(k == BD - 1), label=f"z.mm{c0}.{k}")
        S.op("vector",
             lambda e, c0=c0, cn=cn: e.tensor_copy(
                 zT_sb[:, c0:c0 + cn], zps[0:32, 0:cn]),
             reads=[rg(zps)], writes=[rg(zT_sb)], label=f"z.ev{c0}")
    S.dma("sync", zT_h, zT_sb[:, :], reads=[rg(zT_sb)], writes=[rg(zT_h)],
          label="z.st")
    bi = [0]

    def pre(i, n0, nn):
        xt = xc[bi[0] % 3]
        S.dma("sync", xt[:, 0:nn], x[i * 128:(i + 1) * 128, n0:n0 + nn],
              reads=[rg(x)], writes=[rg(xt)], label=f"h.ldx{i}.{n0}")

    def extra(i, n0, nn):
        return [(lambda e, p, n0=n0, nn=nn: e.matmul(
                    p[:, 0:nn], ones_bf[0:1, :], b_bf[0:1, n0:n0 + nn],
                    start=False, stop=True),
                 [rg(ones_bf), rg(b_bf)])]

    def evac(i, n0, nn, p):
        xt = xc[bi[0] % 3]
        o = og[bi[0] % 3]
        bi[0] += 1
        S.op("vector",
             lambda e, p=p, xt=xt, o=o, nn=nn: e.tensor_tensor(
                 o[:, 0:nn], p[:, 0:nn], xt[:, 0:nn], ALU.add),
             reads=[rg(p), rg(xt)], writes=[rg(o)],
             label=f"h.ev{i}.{n0}")
        S.dma("sync", ACC[i * 128:(i + 1) * 128, n0:n0 + nn], o[:, 0:nn],
              reads=[rg(o)], writes=[rg(ACC, (i, n0))],
              label=f"h.st{i}.{n0}")

    ph_mm_T(S, nc, stack, xnT_sb, BD, T, WT, D, "h", evac,
            nchunk=256, extra_mm=extra, pre_cb=pre,
            w_tiled_bf=True, w_tiled_blk=128)


def build_h0_launch(T, D):
    nc = bass.Bass("TRN2", target_bir_lowering=False)
    aps = {}
    BD = D // 128
    aps["xnT"] = nc.dram_tensor("xnT", [128, BD * T], BF16,
                           kind="ExternalInput").ap()
    aps["WT"] = nc.dram_tensor("WT", [D, D], F32, kind="ExternalInput").ap()
    aps["b_row"] = nc.dram_tensor("b_row", [1, D], F32, kind="ExternalInput").ap()
    aps["A0"] = nc.dram_tensor("A0", [D, 32], F32, kind="ExternalInput").ap()
    aps["x"] = nc.dram_tensor("x", [T, D], F32, kind="ExternalInput").ap()
    aps["ACC"] = nc.dram_tensor("ACC", [T, D], F32, kind="ExternalOutput").ap()
    aps["zT"] = nc.dram_tensor("zT", [32, T], F32, kind="ExternalOutput").ap()
    S = Sched(nc)
    with ExitStack() as stack:
        ph_h0(S, NCView(nc, ""), stack, aps, T, D)
        S.finalize()
    return nc

